# version 03
# baseline (speedup 1.0000x reference)
import numpy as np
import ml_dtypes
from contextlib import ExitStack

import concourse.bass as bass
import concourse.mybir as mybir
from concourse.bass_utils import run_bass_kernel_spmd

F32 = mybir.dt.float32
BF16 = mybir.dt.bfloat16
I32 = mybir.dt.int32
AF = mybir.ActivationFunctionType
ALU = mybir.AluOpType
AX = mybir.AxisListType.X
NPBF16 = ml_dtypes.bfloat16

D = 1024
T = 4096
TL = 2048
HL = 8
HD = 64
EPS_LN = 1e-5

ENGINES = ("tensor", "vector", "scalar", "sync", "gpsimd")

class Sched:
    def __init__(self):
        self.q = {e: [] for e in ENGINES}
        self.cnt = {e: 0 for e in ENGINES}

    def emit(self, eng, fn, waits=(), inc=1):
        self.cnt[eng] += inc
        tok = (eng, self.cnt[eng])
        self.q[eng].append((fn, tuple(waits), inc))
        return tok

    def dma(self, eng, fn, waits=()):
        return self.emit(eng, fn, waits, inc=16)

    def build(self, nc, ctx):
        sems = {e: ctx.enter_context(nc.semaphore(f"s_{e}")) for e in ENGINES}
        block = ctx.enter_context(nc.Block())
        sched = self

        def replay(eng_name, h):
            hw = {e: 0 for e in ENGINES}
            cnt_self = 0
            for fn, waits, inc in sched.q[eng_name]:
                if inc == 16 and cnt_self > hw[eng_name]:
                    h.wait_ge(sems[eng_name], cnt_self)
                    hw[eng_name] = cnt_self
                for (we, wv) in waits:
                    if wv > hw[we]:
                        h.wait_ge(sems[we], wv)
                        hw[we] = wv
                cnt_self += inc
                inst = fn(h)
                if inst is not None:
                    inst.then_inc(sems[eng_name], inc)
                else:
                    h.nop().then_inc(sems[eng_name], inc)

        @block.tensor
        def _(h):
            replay("tensor", h)

        @block.vector
        def _(h):
            replay("vector", h)

        @block.scalar
        def _(h):
            replay("scalar", h)
            for e in ENGINES:
                if sched.cnt[e] > 0 and e != "scalar":
                    h.wait_ge(sems[e], sched.cnt[e])

        @block.sync
        def _(h):
            replay("sync", h)

        @block.gpsimd
        def _(h):
            replay("gpsimd", h)



def emit_ln(s, sb, x_dram, xn_dram, ntiles, pre_waits=()):
    xt, sq, xnb, st = sb["ln_xt"], sb["ln_sq"], sb["ln_xn"], sb["ln_st"]
    out_toks = []
    last = [None, None]
    for i in range(ntiles):
        b = i % 2
        w = list(pre_waits) + ([last[b]] if last[b] else [])
        tld = s.dma("sync", lambda h, i=i, b=b: h.dma_start(
            out=xt[b][:], in_=x_dram[i * 128:(i + 1) * 128, :]), waits=w)
        tsq = s.emit("scalar", lambda h, b=b: h.square(sq[:], xt[b][:]), waits=[tld])
        t1 = s.emit("vector", lambda h, b=b: h.reduce_sum(st[:, 0:1], xt[b][:], AX), waits=[tld])
        t2 = s.emit("vector", lambda h: h.reduce_sum(st[:, 1:2], sq[:], AX), waits=[tsq, t1])
        tm = s.emit("vector", lambda h: h.tensor_scalar(st[:, 2:3], st[:, 0:1], 1.0 / D, None, ALU.mult), waits=[t2])
        tm2 = s.emit("vector", lambda h: h.tensor_scalar(st[:, 3:4], st[:, 2:3], st[:, 2:3], None, ALU.mult), waits=[tm])
        tv = s.emit("vector", lambda h: h.scalar_tensor_tensor(
            st[:, 4:5], st[:, 1:2], 1.0 / D, st[:, 3:4], ALU.mult, ALU.subtract), waits=[tm2])
        tve = s.emit("vector", lambda h: h.tensor_scalar(st[:, 5:6], st[:, 4:5], EPS_LN, None, ALU.add), waits=[tv])
        tsd = s.emit("scalar", lambda h: h.sqrt(st[:, 6:7], st[:, 5:6]), waits=[tve])
        trs = s.emit("vector", lambda h: h.reciprocal(st[:, 7:8], st[:, 6:7]), waits=[tsd])
        txn = s.emit("vector", lambda h, b=b: h.tensor_scalar(
            xnb[:], xt[b][:], st[:, 2:3], st[:, 7:8], ALU.subtract, ALU.mult),
            waits=[trs] + ([sb["_ln_prev_store"]] if "_ln_prev_store" in sb else []))
        tst = s.dma("sync", lambda h, i=i: h.dma_start(
            out=xn_dram[i * 128:(i + 1) * 128, :], in_=xnb[:]), waits=[txn])
        last[b] = txn
        sb["_ln_prev_store"] = tst
        out_toks.append(tst)
    return out_toks


def emit_transpose_in(s, sb_fm, xn_dram, Tp, waits):
    toks = []
    for j in range(D // 128):
        t = s.dma("sync", lambda h, j=j: h.dma_start_transpose(
            sb_fm[:, j * Tp:(j + 1) * Tp], xn_dram[:, j * 128:(j + 1) * 128]), waits=waits)
        toks.append(t)
    return toks


def emit_load_w_bf16(s, dst, src_dram, kchunks, ncols, waits=()):
    toks = []
    for kc in range(kchunks):
        t = s.dma("gpsimd", lambda h, kc=kc: h.dma_start(
            out=dst[:, kc * ncols:(kc + 1) * ncols],
            in_=src_dram[kc * 128:(kc + 1) * 128, :]), waits=waits)
        toks.append(t)
    return toks



def build_dn_a(debug=False):
    nc = bass.Bass("TRN2", target_bir_lowering=False)
    xb = nc.dram_tensor("xb", [T, D], F32, kind="ExternalInput").ap()
    wq = nc.dram_tensor("wq", [D, 512], F32, kind="ExternalInput").ap()
    wk = nc.dram_tensor("wk", [D, 512], F32, kind="ExternalInput").ap()
    wv = nc.dram_tensor("wv", [D, 512], F32, kind="ExternalInput").ap()
    wbe = nc.dram_tensor("wbe", [D, 8], F32, kind="ExternalInput").ap()
    bq_col = nc.dram_tensor("bq_col", [128, 4], F32, kind="ExternalInput").ap()
    bk_row = nc.dram_tensor("bk_row", [1, 512], F32, kind="ExternalInput").ap()
    bv_row = nc.dram_tensor("bv_row", [1, 512], F32, kind="ExternalInput").ap()
    bbe_row = nc.dram_tensor("bbe_row", [1, 8], F32, kind="ExternalInput").ap()
    ident = nc.dram_tensor("ident", [128, 128], BF16, kind="ExternalInput").ap()
    stril = nc.dram_tensor("stril", [128, 128], BF16, kind="ExternalInput").ap()
    triui = nc.dram_tensor("triui", [128, 128], BF16, kind="ExternalInput").ap()
    xn_scr = nc.dram_tensor("xn_scr", [T, D], BF16, kind="Internal").ap()
    o_out = nc.dram_tensor("o_out", [128, 4 * T], BF16, kind="ExternalOutput").ap()
    if debug:
        dbg_q = nc.dram_tensor("dbg_q", [128, 4 * T], BF16, kind="ExternalOutput").ap()
        dbg_ktm = nc.dram_tensor("dbg_ktm", [128, 32 * 512], BF16, kind="ExternalOutput").ap()
        dbg_vtm = nc.dram_tensor("dbg_vtm", [128, 32 * 512], BF16, kind="ExternalOutput").ap()
        dbg_beta = nc.dram_tensor("dbg_beta", [128, 32 * 8], F32, kind="ExternalOutput").ap()
        dbg_kfm = nc.dram_tensor("dbg_kfm", [128, 4 * T], BF16, kind="ExternalOutput").ap()
        dbg_xn = nc.dram_tensor("dbg_xn", [128, D], BF16, kind="ExternalOutput").ap()

    s = Sched()
    with ExitStack() as ctx:
        E = ctx.enter_context
        sb = {}
        identt = E(nc.sbuf_tensor("identt", [128, 128], BF16))
        strilt = E(nc.sbuf_tensor("strilt", [128, 128], BF16))
        triuit = E(nc.sbuf_tensor("triuit", [128, 128], BF16))
        cpt = E(nc.sbuf_tensor("cpt", [128, 8 * 144], F32))
        ones1 = E(nc.sbuf_tensor("ones1", [1, 128], BF16))
        bqc = E(nc.sbuf_tensor("bqc", [128, 4], F32))
        bkr = E(nc.sbuf_tensor("bkr", [1, 512], BF16))
        bvr = E(nc.sbuf_tensor("bvr", [1, 512], BF16))
        bber = E(nc.sbuf_tensor("bber", [1, 8], BF16))
        q_fm = E(nc.sbuf_tensor("q_fm", [128, 4 * T], BF16))
        k_tm = E(nc.sbuf_tensor("k_tm", [128, 32 * 512], BF16))
        v_tm = E(nc.sbuf_tensor("v_tm", [128, 32 * 512], BF16))
        beta = E(nc.sbuf_tensor("beta", [128, 32 * 8], F32))
        nbeta = E(nc.sbuf_tensor("nbeta", [128, 32 * 8], F32))

        t_c = []
        t_c.append(s.dma("sync", lambda h: h.dma_start(out=identt[:], in_=ident[:, :])))
        t_c.append(s.dma("sync", lambda h: h.dma_start(out=strilt[:], in_=stril[:, :])))
        t_c.append(s.dma("sync", lambda h: h.dma_start(out=triuit[:], in_=triui[:, :])))
        t_c.append(s.emit("gpsimd", lambda h: h.memset(ones1[:], 1.0)))
        t_c.append(s.dma("sync", lambda h: h.dma_start(out=bqc[:], in_=bq_col[:, :])))
        t_c.append(s.dma("gpsimd", lambda h: h.dma_start(out=bkr[:], in_=bk_row[:, :])))
        t_c.append(s.dma("gpsimd", lambda h: h.dma_start(out=bvr[:], in_=bv_row[:, :])))
        t_c.append(s.dma("gpsimd", lambda h: h.dma_start(out=bber[:], in_=bbe_row[:, :])))

        with ExitStack() as ctx2:
            E2 = ctx2.enter_context
            xn_fm = E2(nc.sbuf_tensor("xn_fm", [128, 8 * T], BF16))
            sb["ln_xt"] = [E2(nc.sbuf_tensor(f"ln_xt{i}", [128, D], F32)) for i in range(2)]
            sb["ln_sq"] = E2(nc.sbuf_tensor("ln_sq", [128, D], F32))
            sb["ln_xn"] = E2(nc.sbuf_tensor("ln_xnb", [128, D], BF16))
            sb["ln_st"] = E2(nc.sbuf_tensor("ln_st", [128, 8], F32))
            wqt = E2(nc.sbuf_tensor("wqt", [128, 8 * 512], BF16))
            wkt = E2(nc.sbuf_tensor("wkt", [128, 8 * 512], BF16))
            wvt = E2(nc.sbuf_tensor("wvt", [128, 8 * 512], BF16))
            wbet = E2(nc.sbuf_tensor("wbet", [128, 8 * 8], BF16))
            ksq = sb["ln_sq"]
            kst = sb["ln_xt"][0]

            def krw(kb):
                return (sb["ln_xt"][0] if kb == 0 else sb["ln_sq"])[:, 512:1024]
            ps_a = E2(nc.psum_tensor("ps_a", [128, 512], F32))
            ps_b = E2(nc.psum_tensor("ps_b", [128, 512], F32))
            ps_c = E2(nc.psum_tensor("ps_c", [128, 512], F32))
            ps_s = E2(nc.psum_tensor("ps_s", [128, 8], F32))

            ln_toks = emit_ln(s, sb, xb, xn_scr, T // 128)
            tr_toks = emit_transpose_in(s, xn_fm, xn_scr, T, ln_toks)
            wq_t = emit_load_w_bf16(s, wqt, wq, 8, 512)
            wk_t = emit_load_w_bf16(s, wkt, wk, 8, 512)
            wv_t = emit_load_w_bf16(s, wvt, wv, 8, 512)
            wbe_t = emit_load_w_bf16(s, wbet, wbe, 8, 8)

            ev_prev = {}
            for oc in range(4):
                for tc in range(T // 512):
                    ps = [ps_a, ps_b][tc % 2]
                    w = tr_toks + wq_t
                    if (id(ps),) in ev_prev:
                        w = w + [ev_prev[(id(ps),)]]
                    tok = None
                    for kc in range(8):
                        tok = s.emit("tensor", lambda h, kc=kc, oc=oc, tc=tc, ps=ps: h.matmul(
                            ps[:], wqt[:, kc * 512 + oc * 128: kc * 512 + (oc + 1) * 128],
                            xn_fm[:, kc * T + tc * 512: kc * T + (tc + 1) * 512],
                            start=(kc == 0), stop=(kc == 7)),
                            waits=(w if kc == 0 else []))
                    tev = s.emit("scalar", lambda h, oc=oc, tc=tc, ps=ps: h.activation(
                        q_fm[:, oc * T + tc * 512: oc * T + (tc + 1) * 512], ps[:],
                        AF.Identity, bias=bqc[:, oc:oc + 1], scale=1.0), waits=[tok])
                    ev_prev[(id(ps),)] = tev

            for tt in range(T // 128):
                for which in ("k", "v", "be"):
                    wt = {"k": wkt, "v": wvt, "be": wbet}[which]
                    br = {"k": bkr, "v": bvr, "be": bber}[which]
                    ncols = 8 if which == "be" else 512
                    ps = ps_s if which == "be" else [ps_a, ps_b][tt % 2]
                    w = tr_toks + {"k": wk_t, "v": wv_t, "be": wbe_t}[which] + t_c
                    key = (id(ps),)
                    if key in ev_prev:
                        w = w + [ev_prev[key]]
                    tok = None
                    for kc in range(8):
                        tok = s.emit("tensor", lambda h, kc=kc, tt=tt, wt=wt, ncols=ncols, ps=ps: h.matmul(
                            ps[:, 0:ncols],
                            xn_fm[:, kc * T + tt * 128: kc * T + (tt + 1) * 128],
                            wt[:, kc * ncols:(kc + 1) * ncols],
                            start=(kc == 0), stop=False), waits=(w if kc == 0 else []))
                    tok = s.emit("tensor", lambda h, br=br, ncols=ncols, ps=ps: h.matmul(
                        ps[:, 0:ncols], ones1[:], br[:, 0:ncols], start=False, stop=True))
                    if which == "v":
                        tev = s.emit("vector", lambda h, tt=tt, ps=ps: h.tensor_copy(
                            v_tm[:, tt * 512:(tt + 1) * 512], ps[:]), waits=[tok])
                        ev_prev[(id(ps),)] = tev
                    elif which == "be":
                        tsg = s.emit("scalar", lambda h, tt=tt: h.activation(
                            beta[:, tt * 8:(tt + 1) * 8], ps_s[:], AF.Sigmoid,
                            bias=0.0, scale=1.0), waits=[tok])
                        tnb = s.emit("vector", lambda h, tt=tt: h.tensor_scalar(
                            nbeta[:, tt * 8:(tt + 1) * 8], beta[:, tt * 8:(tt + 1) * 8],
                            -1.0, None, ALU.mult), waits=[tsg])
                        ev_prev[(id(ps_s),)] = tnb
                    else:
                        kb = tt % 2
                        tkr = s.emit("scalar", lambda h, kb=kb, ps=ps: h.copy(
                            krw(kb), ps[:]),
                            waits=[tok] + ([ev_prev[("kraw", kb)]] if ("kraw", kb) in ev_prev else []))
                        ev_prev[(id(ps),)] = tkr
                        tq = s.emit("scalar", lambda h, kb=kb: h.square(ksq[:, 0:512], krw(kb)), waits=[tkr])
                        trd = s.emit("vector", lambda h: h.tensor_reduce(
                            kst[:, 0:8], ksq[:, 0:512].rearrange("p (a m) -> p a m", a=8), AX, ALU.add), waits=[tq])
                        tsd = s.emit("scalar", lambda h: h.sqrt(kst[:, 8:16], kst[:, 0:8]), waits=[trd])
                        tep = s.emit("vector", lambda h: h.tensor_scalar(
                            kst[:, 8:16], kst[:, 8:16], 1e-6, None, ALU.add), waits=[tsd])
                        trc = s.emit("vector", lambda h: h.reciprocal(kst[:, 16:24], kst[:, 8:16]), waits=[tep])
                        tkn = s.emit("vector", lambda h, tt=tt, kb=kb: h.tensor_tensor(
                            k_tm[:, tt * 512:(tt + 1) * 512].rearrange("p (a m) -> p a m", a=8),
                            krw(kb).rearrange("p (a m) -> p a m", a=8),
                            kst[:, 16:24].unsqueeze(2).broadcast_to([128, 8, 64]),
                            ALU.mult), waits=[trc])
                        ev_prev[("kraw", kb)] = tkn

        k_fm = E(nc.sbuf_tensor("k_fm", [128, 4 * T], BF16))
        o_fm = E(nc.sbuf_tensor("o_fm", [128, 4 * T], BF16))
        Nt = E(nc.sbuf_tensor("Nt", [128, 128], BF16))
        NTt = E(nc.sbuf_tensor("NTt", [128, 128], BF16))
        P2 = E(nc.sbuf_tensor("P2", [128, 128], BF16))
        PT2 = E(nc.sbuf_tensor("PT2", [128, 128], BF16))
        P4 = E(nc.sbuf_tensor("P4", [128, 128], BF16))
        PT4 = E(nc.sbuf_tensor("PT4", [128, 128], BF16))
        P8 = E(nc.sbuf_tensor("P8", [128, 128], BF16))
        PT8 = E(nc.sbuf_tensor("PT8", [128, 128], BF16))
        PT16 = E(nc.sbuf_tensor("PT16", [128, 128], BF16))
        GTm = E(nc.sbuf_tensor("GTm", [128, 128], BF16))
        Rt = E(nc.sbuf_tensor("Rt", [128, 64], BF16))
        Vb = E(nc.sbuf_tensor("Vb", [128, 64], BF16))
        Xa = E(nc.sbuf_tensor("Xa", [128, 64], BF16))
        Xb2 = E(nc.sbuf_tensor("Xb2", [128, 64], BF16))
        Ut = E(nc.sbuf_tensor("Ut", [128, 64], BF16))
        Sf = [E(nc.sbuf_tensor(f"Sf{i}", [128, 4 * 64], F32)) for i in range(2)]
        Sb = E(nc.sbuf_tensor("Sb", [128, 4 * 64], BF16))
        pm1 = E(nc.psum_tensor("pm1", [128, 128], F32))
        pmt = E(nc.psum_tensor("pmt", [128, 128], BF16))
        pm2 = E(nc.psum_tensor("pm2", [128, 128], F32))
        pu = E(nc.psum_tensor("pu", [128, 64], F32))
        po = E(nc.psum_tensor("po", [64, 128], F32))
        pst = E(nc.psum_tensor("pst", [128, 64], F32))

        bar0 = [(e, s.cnt[e]) for e in ENGINES]
        tz0 = s.emit("gpsimd", lambda h: h.memset(Sf[0][:], 0.0), waits=bar0)
        tz1 = s.emit("gpsimd", lambda h: h.memset(Sf[1][:], 0.0), waits=bar0)
        tz2 = s.emit("gpsimd", lambda h: h.memset(Sb[:], 0.0), waits=bar0)

        last_pmt = None
        for tt in range(T // 128):
            for cc in range(4):
                wl = list(bar0) + ([last_pmt] if last_pmt else [])
                ttr = s.emit("tensor", lambda h, tt=tt, cc=cc: h.transpose(
                    pmt[:], k_tm[:, tt * 512 + cc * 128: tt * 512 + (cc + 1) * 128], identt[:]),
                    waits=wl)
                tev = s.emit("vector", lambda h, tt=tt, cc=cc: h.tensor_copy(
                    k_fm[:, cc * T + tt * 128: cc * T + (tt + 1) * 128], pmt[:]), waits=[ttr])
                last_pmt = tev

        def kfm_h(h_, c):
            tile = h_ // 2
            r0 = (h_ % 2) * 64
            return k_fm[r0:r0 + 64, tile * T + c * 128: tile * T + (c + 1) * 128]

        def qfm_h(h_, c):
            tile = h_ // 2
            r0 = (h_ % 2) * 64
            return q_fm[r0:r0 + 64, tile * T + c * 128: tile * T + (c + 1) * 128]

        def ktm_h(h_, c):
            return k_tm[:, c * 512 + h_ * 64: c * 512 + (h_ + 1) * 64]

        def vtm_h(h_, c):
            return v_tm[:, c * 512 + h_ * 64: c * 512 + (h_ + 1) * 64]

        def ofm_h(h_, c):
            tile = h_ // 2
            r0 = (h_ % 2) * 64
            return o_fm[r0:r0 + 64, tile * T + c * 128: tile * T + (c + 1) * 128]

        def beta_ap(h_, c):
            return beta[:, c * 8 + h_: c * 8 + h_ + 1]

        def nbeta_ap(h_, c):
            return nbeta[:, c * 8 + h_: c * 8 + h_ + 1]

        last_o = None
        for c in range(T // 128):
            par = c % 2
            for h_ in range(HL):
                ibar = [("vector", s.cnt["vector"]), ("scalar", s.cnt["scalar"]),
                        ("gpsimd", s.cnt["gpsimd"])]
                r0 = (h_ % 2) * 64
                cs = slice((h_ // 2) * 64, (h_ // 2) * 64 + 64)
                Sbf_h = Sb[r0:r0 + 64, cs]
                Sf_old = Sf[par][r0:r0 + 64, cs]
                Sf_new = Sf[1 - par][r0:r0 + 64, cs]
                t1 = s.emit("tensor", lambda h, h_=h_, c=c: h.matmul(
                    pm1[:], kfm_h(h_, c), kfm_h(h_, c), start=True, stop=True), waits=ibar)
                tN = s.emit("vector", lambda h, h_=h_, c=c: h.scalar_tensor_tensor(
                    Nt[:], pm1[:], beta_ap(h_, c), strilt[:], ALU.mult, ALU.mult), waits=[t1])
                tTr = s.emit("tensor", lambda h: h.transpose(pmt[:], Nt[:], identt[:]), waits=[tN])
                tNT = s.emit("scalar", lambda h: h.copy(NTt[:], pmt[:]), waits=[tTr])
                tU = s.emit("tensor", lambda h, h_=h_, c=c, Sbf_h=Sbf_h: h.matmul(
                    pu[:], kfm_h(h_, c), Sbf_h, start=True, stop=True), waits=[tN])
                tVb = s.emit("vector", lambda h, h_=h_, c=c: h.tensor_scalar(
                    Vb[:], vtm_h(h_, c), beta_ap(h_, c), None, ALU.mult), waits=[tU])
                tR = s.emit("vector", lambda h, h_=h_, c=c: h.scalar_tensor_tensor(
                    Rt[:], pu[:], nbeta_ap(h_, c), Vb[:], ALU.mult, ALU.add), waits=[tU, tVb])

                def sqr(lhs, rhs, out, wl):
                    tm = s.emit("tensor", lambda h, lhs=lhs, rhs=rhs: h.matmul(
                        pm2[:], lhs[:], rhs[:], start=True, stop=True), waits=wl)
                    te = s.emit("vector", lambda h, out=out: h.tensor_copy(out[:], pm2[:]), waits=[tm])
                    return te
                e2 = sqr(NTt, Nt, P2, [tNT, tN])
                eT2 = sqr(Nt, NTt, PT2, [e2])
                e4 = sqr(PT2, P2, P4, [eT2])
                eT4 = sqr(P2, PT2, PT4, [e4])
                e8 = sqr(PT4, P4, P8, [eT4])
                eT8 = sqr(P4, PT4, PT8, [e8])
                eT16 = sqr(P8, PT8, PT16, [eT8])

                cur = Rt
                lastap = eT16
                for it in range(7):
                    nxt = Xa if cur is not Xa else Xb2
                    tm = s.emit("tensor", lambda h, cur=cur: h.matmul(
                        pu[:], PT16[:], cur[:], start=True, stop=True), waits=[lastap, tR])
                    lastap = s.emit("vector", lambda h, nxt=nxt: h.tensor_tensor(
                        nxt[:], pu[:], Rt[:], ALU.add), waits=[tm])
                    cur = nxt
                for mat in (PT8, PT4, PT2):
                    nxt = Xa if cur is not Xa else Xb2
                    tm = s.emit("tensor", lambda h, mat=mat, cur=cur: h.matmul(
                        pu[:], mat[:], cur[:], start=True, stop=True), waits=[lastap])
                    lastap = s.emit("vector", lambda h, nxt=nxt, cur=cur: h.tensor_tensor(
                        nxt[:], pu[:], cur[:], ALU.add), waits=[tm])
                    cur = nxt
                tm = s.emit("tensor", lambda h, cur=cur: h.matmul(
                    pu[:], NTt[:], cur[:], start=True, stop=True), waits=[lastap])
                tUf = s.emit("vector", lambda h, cur=cur: h.tensor_tensor(
                    Ut[:], cur[:], pu[:], ALU.subtract), waits=[tm])

                tG = s.emit("tensor", lambda h, h_=h_, c=c: h.matmul(
                    pm1[:], kfm_h(h_, c), qfm_h(h_, c), start=True, stop=True), waits=[tN])
                tGm = s.emit("vector", lambda h: h.tensor_tensor(
                    GTm[:], pm1[:], triuit[:], ALU.mult), waits=[tG])

                tO1 = s.emit("tensor", lambda h, h_=h_, c=c, Sbf_h=Sbf_h: h.matmul(
                    po[:], Sbf_h, qfm_h(h_, c), start=True, stop=False), waits=[tGm])
                tO2 = s.emit("tensor", lambda h: h.matmul(
                    po[:], Ut[:], GTm[:], start=False, stop=True), waits=[tUf, tGm])
                tOe = s.emit("scalar", lambda h, h_=h_, c=c: h.copy(ofm_h(h_, c), po[:]), waits=[tO2])

                tS = s.emit("tensor", lambda h, h_=h_, c=c, r0=r0: h.matmul(
                    pst[r0:r0 + 64, :], ktm_h(h_, c), Ut[:], start=True, stop=True,
                    tile_position=(0, r0)), waits=[tUf])
                tSa = s.emit("vector", lambda h, Sf_old=Sf_old, Sf_new=Sf_new, r0=r0: h.tensor_tensor(
                    Sf_new, pst[r0:r0 + 64, :], Sf_old, ALU.add), waits=[tS, tz0, tz1])
                tSb = s.emit("scalar", lambda h, Sf_new=Sf_new, Sbf_h=Sbf_h: h.copy(
                    Sbf_h, Sf_new), waits=[tSa, tz2, tOe])
                last_o = tOe

        if debug:
            dw = [(e, s.cnt[e]) for e in ENGINES]
            s.dma("sync", lambda h: h.dma_start(out=dbg_q[:, :], in_=q_fm[:]), waits=dw)
            s.dma("sync", lambda h: h.dma_start(out=dbg_ktm[:, :], in_=k_tm[:]), waits=dw)
            s.dma("sync", lambda h: h.dma_start(out=dbg_vtm[:, :], in_=v_tm[:]), waits=dw)
            s.dma("sync", lambda h: h.dma_start(out=dbg_beta[:, :], in_=beta[:]), waits=dw)
            s.dma("sync", lambda h: h.dma_start(out=dbg_kfm[:, :], in_=k_fm[:]), waits=dw)
            s.dma("sync", lambda h: h.dma_start(out=dbg_xn[:, :], in_=xn_scr[0:128, :]), waits=dw)
        ow = [last_o, ("scalar", s.cnt["scalar"])]
        for j in range(4):
            s.dma("sync", lambda h, j=j: h.dma_start(
                out=o_out[:, j * T:(j + 1) * T], in_=o_fm[:, j * T:(j + 1) * T]), waits=ow)

        s.build(nc, ctx)
    return nc



def build_proj_residual():
    nc = bass.Bass("TRN2", target_bir_lowering=False)
    x_fm = nc.dram_tensor("x_fm", [128, 8 * TL], F32, kind="ExternalInput").ap()
    o_in = nc.dram_tensor("o_in", [128, 8 * TL], BF16, kind="ExternalInput").ap()
    w = nc.dram_tensor("w", [D, D], F32, kind="ExternalInput").ap()
    b_col = nc.dram_tensor("b_col", [128, 8], F32, kind="ExternalInput").ap()
    xo = nc.dram_tensor("xo", [128, 8 * TL], F32, kind="ExternalOutput").ap()

    s = Sched()
    with ExitStack() as ctx:
        E = ctx.enter_context
        xt = E(nc.sbuf_tensor("xt", [128, 8 * TL], F32))
        ot = E(nc.sbuf_tensor("ot", [128, 8 * TL], BF16))
        wt = E(nc.sbuf_tensor("wt", [128, 8 * D], BF16))
        bt = E(nc.sbuf_tensor("bt", [128, 8], F32))
        xn = E(nc.sbuf_tensor("xn", [128, 8 * TL], F32))
        pss = [E(nc.psum_tensor(f"ps{i}", [128, 512], F32)) for i in range(2)]

        tx = s.dma("sync", lambda h: h.dma_start(out=xt[:], in_=x_fm[:, :]))
        to = s.dma("sync", lambda h: h.dma_start(out=ot[:], in_=o_in[:, :]))
        tb = s.dma("sync", lambda h: h.dma_start(out=bt[:], in_=b_col[:, :]))
        tw = emit_load_w_bf16(s, wt, w, 8, D)
        ev = {}
        for oc in range(8):
            for tc in range(TL // 512):
                ps = pss[tc % 2]
                wl = [tx, to, tb] + tw + ([ev[id(ps)]] if id(ps) in ev else [])
                tok = None
                for kc in range(8):
                    tok = s.emit("tensor", lambda h, kc=kc, oc=oc, tc=tc, ps=ps: h.matmul(
                        ps[:], wt[:, kc * D + oc * 128: kc * D + (oc + 1) * 128],
                        ot[:, kc * TL + tc * 512: kc * TL + (tc + 1) * 512],
                        start=(kc == 0), stop=(kc == 7)), waits=(wl if kc == 0 else []))
                te = s.emit("vector", lambda h, oc=oc, tc=tc, ps=ps: h.scalar_tensor_tensor(
                    xn[:, oc * TL + tc * 512: oc * TL + (tc + 1) * 512], ps[:],
                    bt[:, oc:oc + 1], xt[:, oc * TL + tc * 512: oc * TL + (tc + 1) * 512],
                    ALU.add, ALU.add), waits=[tok])
                ev[id(ps)] = te
        wall = list(ev.values())
        for j in range(8):
            s.dma("sync", lambda h, j=j: h.dma_start(
                out=xo[:, j * TL:(j + 1) * TL], in_=xn[:, j * TL:(j + 1) * TL]), waits=wall)
        s.build(nc, ctx)
    return nc



def fm(x):
    Tp = x.shape[0]
    return np.ascontiguousarray(x.T.reshape(D // 128, 128, Tp).transpose(1, 0, 2).reshape(128, -1))

def unfm(y, Tp):
    return np.ascontiguousarray(y.reshape(128, D // 128, Tp).transpose(1, 0, 2).reshape(D, Tp).T)

def col128(b):
    return np.ascontiguousarray(b.reshape(-1, 128).T)

IDENT = np.eye(128, dtype=NPBF16)
STRIL = np.tril(np.ones((128, 128), np.float32), -1).astype(NPBF16)
TRIUI = np.triu(np.ones((128, 128), np.float32), 0).astype(NPBF16)

_PROGS = {}

def prog(name, builder):
    if name not in _PROGS:
        _PROGS[name] = builder()
    return _PROGS[name]


def o_heads_to_fm(o_parts):
    return np.concatenate(o_parts, axis=1)


def run_deltanet_sublayer(x, ins):
    in_maps = []
    for c in range(8):
        b, p = c // 2, c % 2
        hs = p * HL * HD
        in_maps.append(dict(
            xb=np.ascontiguousarray(x[b]),
            wq=np.ascontiguousarray(ins["L0_attn_a_Wq"][:, hs:hs + 512]),
            wk=np.ascontiguousarray(ins["L0_attn_a_Wk"][:, hs:hs + 512]),
            wv=np.ascontiguousarray(ins["L0_attn_a_Wv"][:, hs:hs + 512]),
            wbe=np.ascontiguousarray(ins["L0_attn_a_Wbeta"][:, p * 8:p * 8 + 8]),
            bq_col=col128(ins["L0_attn_a_bq"][hs:hs + 512]),
            bk_row=ins["L0_attn_a_bk"][hs:hs + 512].reshape(1, 512),
            bv_row=ins["L0_attn_a_bv"][hs:hs + 512].reshape(1, 512),
            bbe_row=ins["L0_attn_a_bbeta"][p * 8:p * 8 + 8].reshape(1, 8),
            ident=IDENT, stril=STRIL, triui=TRIUI,
        ))
    res = run_bass_kernel_spmd(prog("dn_a", build_dn_a), in_maps, core_ids=list(range(8)))
    o_full = [o_heads_to_fm([res.results[2 * b]["o_out"], res.results[2 * b + 1]["o_out"]])
              for b in range(4)]
    in_maps = []
    for c in range(8):
        b, p = c // 2, c % 2
        in_maps.append(dict(
            x_fm=fm(x[b, p * TL:(p + 1) * TL]),
            o_in=np.ascontiguousarray(o_full[b][:, [j * T + p * TL + t for j in range(8) for t in range(0)]])
            if False else np.ascontiguousarray(
                o_full[b].reshape(128, 8, T)[:, :, p * TL:(p + 1) * TL].reshape(128, 8 * TL)),
            w=ins["L0_attn_a_Wo"],
            b_col=col128(ins["L0_attn_a_bo"]),
        ))
    res = run_bass_kernel_spmd(prog("dn_b", build_proj_residual), in_maps, core_ids=list(range(8)))
    xn = np.empty_like(x)
    for c in range(8):
        b, p = c // 2, c % 2
        xn[b, p * TL:(p + 1) * TL] = unfm(res.results[c]["xo"], TL)
    return xn





def build_moe():
    nc = bass.Bass("TRN2", target_bir_lowering=False)
    x_tm = nc.dram_tensor("x_tm", [TL, D], F32, kind="ExternalInput").ap()
    wrT = nc.dram_tensor("wrT", [D, 16], F32, kind="ExternalInput").ap()
    wmhT = nc.dram_tensor("wmhT", [D, D], F32, kind="ExternalInput").ap()
    wmgT = nc.dram_tensor("wmgT", [D, D], F32, kind="ExternalInput").ap()
    kf = nc.dram_tensor("kf", [8, 128, 4096], F32, kind="ExternalInput").ap()
    vf = nc.dram_tensor("vf", [8, 4096, 128], F32, kind="ExternalInput").ap()
    oh = nc.dram_tensor("oh", [16, 16 * 128], BF16, kind="ExternalInput").ap()
    ident = nc.dram_tensor("ident", [128, 128], BF16, kind="ExternalInput").ap()
    xn_scr = nc.dram_tensor("xn_scr", [TL, D], BF16, kind="Internal").ap()
    xo = nc.dram_tensor("xo", [TL, D], F32, kind="ExternalOutput").ap()

    NT_ = TL // 128
    s = Sched()
    with ExitStack() as ctx:
        E = ctx.enter_context
        sb = {}
        oht = E(nc.sbuf_tensor("oht", [16, 16 * 128], BF16))
        identt = E(nc.sbuf_tensor("identt", [128, 128], BF16))
        gateT = E(nc.sbuf_tensor("gateT", [16, TL], BF16))
        gA = E(nc.sbuf_tensor("gA", [128, 16], F32))
        gB = E(nc.sbuf_tensor("gB", [128, 16], F32))
        gC = E(nc.sbuf_tensor("gC", [128, 4], F32))
        gD = E(nc.sbuf_tensor("gD", [128, 16], F32))
        gbf = E(nc.sbuf_tensor("gbf", [128, 16], BF16))
        q_fm = E(nc.sbuf_tensor("q_fm", [128, 8 * TL], BF16))
        ps_lg = E(nc.psum_tensor("ps_lg", [128, 16], F32))
        ps_tr = E(nc.psum_tensor("ps_tr", [16, 128], BF16))
        ps_sc = [E(nc.psum_tensor(f"ps_sc{i}", [128, 512], F32)) for i in range(2)]
        ps_y = E(nc.psum_tensor("ps_y", [128, 512], F32))
        ps_w = E(nc.psum_tensor("ps_w", [128, 512], F32))

        t_oh = s.dma("sync", lambda h: h.dma_start(out=oht[:], in_=oh[:, :]))
        t_id = s.dma("sync", lambda h: h.dma_start(out=identt[:], in_=ident[:, :]))

        with ExitStack() as ctx2:
            E2 = ctx2.enter_context
            xn_fm = E2(nc.sbuf_tensor("xn_fm", [128, 8 * TL], BF16))
            sb["ln_xt"] = [E2(nc.sbuf_tensor(f"ln_xt{i}", [128, D], F32)) for i in range(2)]
            sb["ln_sq"] = E2(nc.sbuf_tensor("ln_sq", [128, D], F32))
            sb["ln_xn"] = E2(nc.sbuf_tensor("ln_xnb", [128, D], BF16))
            sb["ln_st"] = E2(nc.sbuf_tensor("ln_st", [128, 8], F32))
            wrTt = E2(nc.sbuf_tensor("wrTt", [128, 8 * 16], BF16))
            wmt = E2(nc.sbuf_tensor("wmt", [128, 8 * D], BF16))

            ln_toks = emit_ln(s, sb, x_tm, xn_scr, NT_)
            tr_toks = emit_transpose_in(s, xn_fm, xn_scr, TL, ln_toks)
            wr_t = emit_load_w_bf16(s, wrTt, wrT, 8, 16)
            wm_t = emit_load_w_bf16(s, wmt, wmhT, 8, D)

            prev_lg = None
            prev_lg_s = None
            prev_tr = None
            for i in range(NT_):
                wl = tr_toks + wr_t + ([prev_lg, prev_lg_s] if prev_lg else [])
                tok = None
                for kc in range(8):
                    tok = s.emit("tensor", lambda h, kc=kc, i=i: h.matmul(
                        ps_lg[:], xn_fm[:, kc * TL + i * 128: kc * TL + (i + 1) * 128],
                        wrTt[:, kc * 16:(kc + 1) * 16],
                        start=(kc == 0), stop=(kc == 7)), waits=(wl if kc == 0 else []))
                tm1 = s.emit("vector", lambda h: h.reduce_max(gC[:, 0:1], ps_lg[:], AX), waits=[tok])
                teq = s.emit("vector", lambda h: h.tensor_scalar(gA[:], ps_lg[:], gC[:, 0:1], None, ALU.is_equal), waits=[tm1])
                tng = s.emit("vector", lambda h: h.scalar_tensor_tensor(
                    gB[:], gA[:], -1e30, ps_lg[:], ALU.mult, ALU.add), waits=[teq])
                tm2 = s.emit("vector", lambda h: h.reduce_max(gC[:, 1:2], gB[:], AX), waits=[tng])
                tse = s.emit("vector", lambda h: h.tensor_scalar(gA[:], ps_lg[:], gC[:, 1:2], None, ALU.is_ge), waits=[tm2])
                tsg = s.emit("scalar", lambda h: h.activation(gB[:], ps_lg[:], AF.Sigmoid, bias=0.0, scale=1.0), waits=[tse])
                tgt = s.emit("vector", lambda h: h.tensor_tensor(gD[:], gA[:], gB[:], ALU.mult), waits=[tsg])
                prev_lg = tgt
                prev_lg_s = tsg
                tgb = s.emit("vector", lambda h: h.tensor_copy(gbf[:], gD[:]), waits=[tgt])
                ttr = s.emit("tensor", lambda h: h.transpose(ps_tr[:], gbf[:], identt[:]),
                             waits=[tgb, t_id] + ([prev_tr] if prev_tr else []))
                tte = s.emit("vector", lambda h, i=i: h.tensor_copy(
                    gateT[:, i * 128:(i + 1) * 128], ps_tr[:]), waits=[ttr])
                prev_tr = tte

            prev_ps = {}
            for oc in range(8):
                for tc in range(TL // 512):
                    ps = ps_sc[tc % 2]
                    wl = tr_toks + wm_t + ([prev_ps[id(ps)]] if id(ps) in prev_ps else [])
                    tok = None
                    for kc in range(8):
                        tok = s.emit("tensor", lambda h, kc=kc, oc=oc, tc=tc, ps=ps: h.matmul(
                            ps[:], wmt[:, kc * D + oc * 128: kc * D + (oc + 1) * 128],
                            xn_fm[:, kc * TL + tc * 512: kc * TL + (tc + 1) * 512],
                            start=(kc == 0), stop=(kc == 7)), waits=(wl if kc == 0 else []))
                    te = s.emit("scalar", lambda h, oc=oc, tc=tc, ps=ps: h.copy(
                        q_fm[:, oc * TL + tc * 512: oc * TL + (tc + 1) * 512], ps[:]), waits=[tok])
                    prev_ps[id(ps)] = te

        bar0 = [(e, s.cnt[e]) for e in ENGINES]
        y_fm = E(nc.sbuf_tensor("y_fm", [128, 8 * TL], BF16))
        Bt = E(nc.sbuf_tensor("Bt", [128, 16 * 1024], BF16))
        kft = [E(nc.sbuf_tensor(f"kft{i}", [128, 4096], BF16)) for i in range(2)]
        vft = [E(nc.sbuf_tensor(f"vft{i}", [128, 4096], BF16)) for i in range(2)]
        ar = [E(nc.sbuf_tensor(f"ar{i}", [128, 512], BF16)) for i in range(2)]
        ag = [E(nc.sbuf_tensor(f"ag{i}", [128, 512], BF16)) for i in range(2)]

        kvctr = [0]
        y_toks = []
        prev_act = {}
        prev_ag = {}
        prev_ty = None
        prev_B = None
        last_B = None
        kv_tok = [None, None]
        for tcg in range(2):
            exp_done = [(e, s.cnt[e]) for e in ENGINES]
            for e in range(16):
                for tci in range(2):
                    tc = 2 * tcg + tci
                    tm = s.emit("tensor", lambda h, e=e, tc=tc: h.matmul(
                        ps_w[:], oht[:, e * 128:(e + 1) * 128],
                        gateT[:, tc * 512:(tc + 1) * 512],
                        start=True, stop=True),
                        waits=bar0 + exp_done + [t_oh] + ([prev_B] if prev_B else []))
                    te = s.emit("vector", lambda h, e=e, tci=tci: h.tensor_copy(
                        Bt[:, e * 1024 + tci * 512: e * 1024 + (tci + 1) * 512], ps_w[:]), waits=[tm])
                    prev_B = te
            last_B = prev_B
            for h_ in range(8):
                hb = h_ % 2
                wl = bar0 + ([kv_tok[hb]] if kv_tok[hb] else [])
                tk = s.dma("gpsimd", lambda h, h_=h_, hb=hb: h.dma_start(
                    out=kft[hb][:], in_=kf[h_, :, :]), waits=wl)
                tv = s.dma("gpsimd", lambda h, h_=h_, hb=hb: h.dma_start(
                    out=vft[hb][:].rearrange("p (b d) -> p b d", b=32),
                    in_=vf[h_, :, :].rearrange("(b p) d -> p b d", p=128)), waits=wl)
                for tci in range(2):
                    tc = 2 * tcg + tci
                    for eb in range(32):
                        pp = ps_sc[eb % 2]
                        ab = eb % 2
                        wl2 = list(bar0) + [tk, tv, last_B] + ([prev_act.get(id(pp))] if prev_act.get(id(pp)) else [])
                        tsc = s.emit("tensor", lambda h, h_=h_, hb=hb, tc=tc, eb=eb, pp=pp: h.matmul(
                            pp[:], kft[hb][:, eb * 128:(eb + 1) * 128],
                            q_fm[:, h_ * TL + tc * 512: h_ * TL + (tc + 1) * 512],
                            start=True, stop=True), waits=wl2)
                        tge = s.emit("scalar", lambda h, ab=ab, pp=pp: h.activation(
                            ar[ab][:], pp[:], AF.Gelu, bias=0.0, scale=1.0),
                            waits=[tsc] + ([prev_ag.get(ab)] if prev_ag.get(ab) else []))
                        tag = s.emit("vector", lambda h, ab=ab, eb=eb, tci=tci: h.tensor_tensor(
                            ag[ab][:], ar[ab][:],
                            Bt[:, (eb // 2) * 1024 + tci * 512: (eb // 2) * 1024 + (tci + 1) * 512],
                            ALU.mult), waits=[tge])
                        prev_act[id(pp)] = tag
                        prev_ag[ab] = tag
                        tym = s.emit("tensor", lambda h, hb=hb, ab=ab, eb=eb: h.matmul(
                            ps_y[:], vft[hb][:, eb * 128:(eb + 1) * 128], ag[ab][:],
                            start=(eb == 0), stop=(eb == 31)),
                            waits=[tag] + ([prev_ty] if (eb == 0 and prev_ty) else []))
                    ty = s.emit("scalar", lambda h, h_=h_, tc=tc: h.copy(
                        y_fm[:, h_ * TL + tc * 512: h_ * TL + (tc + 1) * 512], ps_y[:]), waits=[tym])
                    prev_ty = ty
                    y_toks.append(ty)
                kv_tok[hb] = ty

        with ExitStack() as ctx3:
            E3 = ctx3.enter_context
            wgt = E3(nc.sbuf_tensor("wgt", [128, 8 * D], BF16))
            xtile = [E3(nc.sbuf_tensor(f"xtile{i}", [128, D], F32)) for i in range(2)]
            otile = [E3(nc.sbuf_tensor(f"otile{i}", [128, D], F32)) for i in range(2)]
            wg_t = emit_load_w_bf16(s, wgt, wmgT, 8, D, waits=bar0)
            prev_x = [None, None]
            prev_o = [None, None]
            prev_wg = None
            for i in range(NT_):
                ib = i % 2
                tx = s.dma("sync", lambda h, i=i, ib=ib: h.dma_start(
                    out=xtile[ib][:], in_=x_tm[i * 128:(i + 1) * 128, :]),
                    waits=([prev_x[ib]] if prev_x[ib] else []))
                for oc in range(2):
                    wl = y_toks[-1:] + [last_B] + wg_t + [tx]
                    if prev_wg:
                        wl = wl + [prev_wg]
                    tok = None
                    for kc in range(8):
                        tok = s.emit("tensor", lambda h, kc=kc, i=i, oc=oc: h.matmul(
                            ps_w[:], y_fm[:, kc * TL + i * 128: kc * TL + (i + 1) * 128],
                            wgt[:, kc * D + oc * 512: kc * D + (oc + 1) * 512],
                            start=(kc == 0), stop=(kc == 7)), waits=(wl if kc == 0 else []))
                    te = s.emit("vector", lambda h, i=i, oc=oc, ib=ib: h.tensor_tensor(
                        otile[ib][:, oc * 512:(oc + 1) * 512], ps_w[:],
                        xtile[ib][:, oc * 512:(oc + 1) * 512], ALU.add), waits=[tok])
                    prev_wg = te
                tst = s.dma("sync", lambda h, i=i, ib=ib: h.dma_start(
                    out=xo[i * 128:(i + 1) * 128, :], in_=otile[ib][:]), waits=[prev_wg])
                prev_x[ib] = prev_wg
                prev_o[ib] = tst
            s.build(nc, ctx)
    return nc


OH16 = np.zeros((16, 16 * 128), NPBF16)
for _e in range(16):
    OH16[_e, _e * 128:(_e + 1) * 128] = 1


def run_moe_sublayer(x, ins, pfx="L0_ffn"):
    in_maps = []
    for c in range(8):
        b, p = c // 2, c % 2
        in_maps.append(dict(
            x_tm=np.ascontiguousarray(x[b, p * TL:(p + 1) * TL]),
            wrT=np.ascontiguousarray(ins[pfx + "_w_router"].T),
            wmhT=np.ascontiguousarray(ins[pfx + "_w_mh"].T),
            wmgT=np.ascontiguousarray(ins[pfx + "_w_mg"].T),
            kf=np.ascontiguousarray(ins[pfx + "_k_ffwd"].transpose(0, 3, 1, 2).reshape(8, 128, 4096)),
            vf=np.ascontiguousarray(ins[pfx + "_v_ffwd"].reshape(8, 4096, 128)),
            oh=OH16, ident=IDENT,
        ))
    res = run_bass_kernel_spmd(prog("moe", build_moe), in_maps, core_ids=list(range(8)))
    xn = np.empty_like(x)
    for c in range(8):
        b, p = c // 2, c % 2
        xn[b, p * TL:(p + 1) * TL] = res.results[c]["xo"]
    return xn





def build_attn(kind):
    rope = kind == "rope"
    nex = 1 if rope else 2
    scale = 0.125 if rope else 0.03125
    nc = bass.Bass("TRN2", target_bir_lowering=False)
    xb = nc.dram_tensor("xb", [T, D], F32, kind="ExternalInput").ap()
    wq = nc.dram_tensor("wq", [D, 512], F32, kind="ExternalInput").ap()
    wk = nc.dram_tensor("wk", [D, 512], F32, kind="ExternalInput").ap()
    wv = nc.dram_tensor("wv", [D, 512], F32, kind="ExternalInput").ap()
    negm = nc.dram_tensor("negm", [128, 128], BF16, kind="ExternalInput").ap()
    if rope:
        bq_col = nc.dram_tensor("bq_col", [64, 8], F32, kind="ExternalInput").ap()
        bk_col = nc.dram_tensor("bk_col", [64, 8], F32, kind="ExternalInput").ap()
        bv_row = nc.dram_tensor("bv_row", [1, 512], F32, kind="ExternalInput").ap()
        rc_t = nc.dram_tensor("rc_t", [64, T], BF16, kind="ExternalInput").ap()
        rs_t = nc.dram_tensor("rs_t", [64, T], BF16, kind="ExternalInput").ap()
        pswap = nc.dram_tensor("pswap", [64, 64], BF16, kind="ExternalInput").ap()
    else:
        m_col = nc.dram_tensor("m_col", [128, 8], F32, kind="ExternalInput").ap()
        iota_r = nc.dram_tensor("iota_r", [1, T], F32, kind="ExternalInput").ap()
        iota2d = nc.dram_tensor("iota2d", [128, 32], F32, kind="ExternalInput").ap()
    xn_scr = nc.dram_tensor("xn_scr", [T, D], BF16, kind="Internal").ap()
    o_out = nc.dram_tensor("o_out", [128, 32 * 512], BF16, kind="ExternalOutput").ap()

    NB = T // 128
    s = Sched()
    with ExitStack() as ctx:
        E = ctx.enter_context
        sb = {}
        xn_fm = E(nc.sbuf_tensor("xn_fm", [128, 8 * T], BF16))
        v_tm = E(nc.sbuf_tensor("v_tm", [128, NB * 520], BF16))
        negmt = E(nc.sbuf_tensor("negmt", [128, 128], BF16))
        Qh = E(nc.sbuf_tensor("Qh", [64 + nex, T], BF16))
        Kh = E(nc.sbuf_tensor("Kh", [64 + nex, T], BF16))
        wqh = [E(nc.sbuf_tensor(f"wqh{i}", [128, 8 * 64], BF16)) for i in range(2)]
        wkh = [E(nc.sbuf_tensor(f"wkh{i}", [128, 8 * 64], BF16)) for i in range(2)]
        qraw = E(nc.sbuf_tensor("qraw", [64, 512], BF16))
        rtm01 = E(nc.sbuf_tensor("rtm01", [64, 512], BF16))
        rtm23 = E(nc.sbuf_tensor("rtm23", [64, 512], BF16))
        sq64 = E(nc.sbuf_tensor("sq64", [64, 512], BF16))
        ones64 = E(nc.sbuf_tensor("ones64", [64, 1], BF16))
        kmx = E(nc.sbuf_tensor("kmx", [1, 16], F32))
        rowa = E(nc.sbuf_tensor("rowa", [1, 512], F32))
        rowb = E(nc.sbuf_tensor("rowb", [1, 512], F32))
        rowc = E(nc.sbuf_tensor("rowc", [1, 512], F32))
        mbf = E(nc.sbuf_tensor("mbf", [1, T], BF16))

        if rope:
            rct = E(nc.sbuf_tensor("rct", [64, T], BF16))
            rst = E(nc.sbuf_tensor("rst", [64, T], BF16))
            pswt = E(nc.sbuf_tensor("pswt", [64, 64], BF16))
            swp = E(nc.sbuf_tensor("swp", [64, 512], BF16))
            bqc = E(nc.sbuf_tensor("bqc", [64, 8], F32))
            bkc = E(nc.sbuf_tensor("bkc", [64, 8], F32))
            bvr = E(nc.sbuf_tensor("bvr", [1, 512], BF16))
            ones1 = E(nc.sbuf_tensor("ones1", [1, 128], BF16))
        else:
            mbf2 = E(nc.sbuf_tensor("mbf2", [1, T], BF16))
            mrowf = E(nc.sbuf_tensor("mrowf", [1, 512], F32))
            mct = E(nc.sbuf_tensor("mct", [128, 8], F32))
            iort = E(nc.sbuf_tensor("iort", [1, T], F32))
            io2t = E(nc.sbuf_tensor("io2t", [128, 32], F32))
            msfull = E(nc.sbuf_tensor("msfull", [128, 32], F32))
            mrl = E(nc.sbuf_tensor("mrl", [128, 8], F32))
        ps_sc = [E(nc.psum_tensor(f"ps_sc{i}", [128, 512], F32)) for i in range(2)]
        ps_o = [E(nc.psum_tensor(f"ps_o{i}", [128, 512], F32)) for i in range(4)]
        ps_n = E(nc.psum_tensor("ps_n", [1, 512], F32))

        t_c = [s.dma("sync", lambda h: h.dma_start(out=negmt[:], in_=negm[:, :]))]
        t_c.append(s.emit("gpsimd", lambda h: h.memset(ones64[:], 1.0)))
        t_c.append(s.emit("gpsimd", lambda h: h.memset(Kh[64:64 + nex, :], 1.0)))
        t_c.append(s.emit("gpsimd", lambda h: h.memset(v_tm[:], 1.0)))
        if rope:
            t_c.append(s.dma("sync", lambda h: h.dma_start(out=rct[:], in_=rc_t[:, :])))
            t_c.append(s.dma("sync", lambda h: h.dma_start(out=rst[:], in_=rs_t[:, :])))
            t_c.append(s.dma("sync", lambda h: h.dma_start(out=pswt[:], in_=pswap[:, :])))
            t_c.append(s.dma("sync", lambda h: h.dma_start(out=bqc[:], in_=bq_col[:, :])))
            t_c.append(s.dma("sync", lambda h: h.dma_start(out=bkc[:], in_=bk_col[:, :])))
            t_c.append(s.dma("gpsimd", lambda h: h.dma_start(out=bvr[:], in_=bv_row[:, :])))
            t_c.append(s.emit("gpsimd", lambda h: h.memset(ones1[:], 1.0)))
        else:
            t_c.append(s.dma("sync", lambda h: h.dma_start(out=mct[:], in_=m_col[:, :])))
            t_c.append(s.dma("sync", lambda h: h.dma_start(out=iort[:], in_=iota_r[:, :])))
            t_c.append(s.dma("sync", lambda h: h.dma_start(out=io2t[:], in_=iota2d[:, :])))
            t_c.append(s.emit("vector", lambda h: h.tensor_relu(mrl[:], mct[:]), waits=t_c[-3:]))

        with ExitStack() as ctx2:
            E2 = ctx2.enter_context
            sb["ln_xt"] = [E2(nc.sbuf_tensor("ln_xt0", [128, D], F32))] * 2
            sb["ln_xt"] = [sb["ln_xt"][0], sb["ln_xt"][0]]
            sb["ln_sq"] = E2(nc.sbuf_tensor("ln_sq", [128, D], F32))
            sb["ln_xn"] = E2(nc.sbuf_tensor("ln_xnb", [128, D], BF16))
            sb["ln_st"] = E2(nc.sbuf_tensor("ln_st", [128, 8], F32))
            wvt = E2(nc.sbuf_tensor("wvt", [128, 8 * 512], BF16))
            ln_toks = emit_ln(s, sb, xb, xn_scr, NB)
            tr_toks = emit_transpose_in(s, xn_fm, xn_scr, T, ln_toks)
            w_v = emit_load_w_bf16(s, wvt, wv, 8, 512)

            prev_v = {}
            for tt in range(NB):
                ps = ps_sc[tt % 2]
                wl = tr_toks + w_v + t_c + ([prev_v[id(ps)]] if id(ps) in prev_v else [])
                tok = None
                for kc in range(8):
                    tok = s.emit("tensor", lambda h, kc=kc, tt=tt, ps=ps: h.matmul(
                        ps[:], xn_fm[:, kc * T + tt * 128: kc * T + (tt + 1) * 128],
                        wvt[:, kc * 512:(kc + 1) * 512],
                        start=(kc == 0), stop=(False if rope else kc == 7)),
                        waits=(wl if kc == 0 else []))
                if rope:
                    tok = s.emit("tensor", lambda h, ps=ps: h.matmul(
                        ps[:], ones1[:], bvr[:], start=False, stop=True))
                te = s.emit("vector", lambda h, tt=tt, ps=ps: h.tensor_copy(
                    v_tm[:, tt * 520: tt * 520 + 520].rearrange("p (a m) -> p a m", a=8)[:, :, 0:64],
                    ps[:].rearrange("p (a m) -> p a m", a=8)), waits=[tok])
                prev_v[id(ps)] = te

        bar0 = [(e, s.cnt[e]) for e in ENGINES]
        o_buf = E(nc.sbuf_tensor("o_buf", [128, NB * 512], BF16))
        pt = [E(nc.sbuf_tensor(f"pt{i}", [128, 512], BF16)) for i in range(2)]
        dtile = E(nc.sbuf_tensor("dtile", [128, 128], F32))
        rcp = E(nc.sbuf_tensor("rcp", [128, 4], F32))

        prev = {}
        wtok = [None, None]
        for h_ in range(HL):
            hb = h_ % 2
            wl = list(bar0) + ([wtok[hb]] if wtok[hb] else [])
            twq = s.dma("gpsimd", lambda h, h_=h_, hb=hb: h.dma_start(
                out=wqh[hb][:], in_=wq[:, h_ * 64:(h_ + 1) * 64].rearrange("(kc p) c -> p kc c", p=128)),
                waits=wl)
            twk = s.dma("gpsimd", lambda h, h_=h_, hb=hb: h.dma_start(
                out=wkh[hb][:], in_=wk[:, h_ * 64:(h_ + 1) * 64].rearrange("(kc p) c -> p kc c", p=128)),
                waits=wl)
            for side, wt_, dst in (("q", wqh, Qh), ("k", wkh, Kh)):
                tws = twq if side == "q" else twk
                for tc in range(8):
                    ps = ps_sc[tc % 2]
                    wl = list(bar0) + [tws] + ([prev[id(ps)]] if id(ps) in prev else [])
                    tok = None
                    for kc in range(8):
                        tok = s.emit("tensor", lambda h, kc=kc, tc=tc, hb=hb, wt_=wt_, ps=ps: h.matmul(
                            ps[0:64, :], wt_[hb][:, kc * 64:(kc + 1) * 64],
                            xn_fm[:, kc * T + tc * 512: kc * T + (tc + 1) * 512],
                            start=(kc == 0), stop=(kc == 7)), waits=(wl if kc == 0 else []))
                    if rope:
                        bcol = bqc if side == "q" else bkc
                        tev = s.emit("scalar", lambda h, ps=ps, bcol=bcol, h_=h_: h.activation(
                            qraw[:], ps[0:64, :], AF.Identity, bias=bcol[:, h_:h_ + 1], scale=1.0),
                            waits=[tok] + ([prev["qraw"]] if "qraw" in prev else []))
                        tsw = s.emit("tensor", lambda h, ps=ps: h.matmul(
                            ps[64:128, 0:512], pswt[:], qraw[:], start=True, stop=True,
                            tile_position=(0, 64)), waits=[tev])
                        tse = s.emit("scalar", lambda h, ps=ps: h.copy(swp[:], ps[64:128, 0:512]), waits=[tsw])
                        t1 = s.emit("vector", lambda h, tc=tc: h.tensor_tensor(
                            rtm01[:], qraw[:], rct[:, tc * 512:(tc + 1) * 512], ALU.mult), waits=[tev])
                        t2 = s.emit("vector", lambda h, tc=tc: h.tensor_tensor(
                            rtm23[:], swp[:], rst[:, tc * 512:(tc + 1) * 512], ALU.mult), waits=[tse, t1])
                        t6 = s.emit("vector", lambda h, tc=tc, dst=dst: h.tensor_tensor(
                            dst[0:64, tc * 512:(tc + 1) * 512], rtm01[:], rtm23[:], ALU.add), waits=[t2])
                        prev["qraw"] = t6
                        prev[id(ps)] = t6
                    else:
                        tev = s.emit("scalar", lambda h, ps=ps, tc=tc, dst=dst: h.copy(
                            dst[0:64, tc * 512:(tc + 1) * 512], ps[0:64, :]), waits=[tok])
                        prev[id(ps)] = tev
            pk_done = [("vector" if rope else "scalar", s.cnt["vector" if rope else "scalar"])]

            nrm_ev = []
            for tc in range(8):
                tsq = s.emit("scalar", lambda h, tc=tc: h.square(
                    sq64[:], Kh[0:64, tc * 512:(tc + 1) * 512]), waits=pk_done + nrm_ev[-1:])
                tmm = s.emit("tensor", lambda h: h.matmul(
                    ps_n[:], ones64[:], sq64[:], start=True, stop=True),
                    waits=[tsq] + nrm_ev[-1:] + pk_done)
                tev = s.emit("vector", lambda h, tc=tc: h.reduce_max(
                    kmx[:, tc:tc + 1], ps_n[:], AX), waits=[tmm])
                nrm_ev.append(tev)
            tkx = s.emit("vector", lambda h: h.reduce_max(kmx[:, 8:9], kmx[:, 0:8], AX), waits=nrm_ev[-1:])
            mrow_done = []
            mdma = []
            for tc in range(8):
                tsq = s.emit("scalar", lambda h, tc=tc: h.square(
                    sq64[:], Qh[0:64, tc * 512:(tc + 1) * 512]),
                    waits=pk_done + mrow_done[-1:] + nrm_ev[-1:])
                tmm = s.emit("tensor", lambda h: h.matmul(
                    ps_n[:], ones64[:], sq64[:], start=True, stop=True),
                    waits=[tsq] + mrow_done[-1:] + nrm_ev[-1:])
                tq2 = s.emit("vector", lambda h: h.tensor_scalar(
                    rowa[:], ps_n[:], kmx[:, 8:9], None, ALU.mult), waits=[tmm, tkx])
                tsd = s.emit("scalar", lambda h: h.sqrt(rowb[:], rowa[:]), waits=[tq2])
                tlast = tsd
                if not rope:
                    tit = s.emit("vector", lambda h, h_=h_, tc=tc: h.tensor_scalar(
                        rowc[:], iort[:, tc * 512:(tc + 1) * 512], mrl[0:1, h_:h_ + 1], 32.0,
                        ALU.mult, ALU.mult), waits=t_c + [tsd])
                    tlast = s.emit("vector", lambda h: h.tensor_tensor(
                        rowb[:], rowb[:], rowc[:], ALU.add), waits=[tit])
                tng = s.emit("vector", lambda h: h.tensor_scalar(
                    rowa[:], rowb[:], 1.0, -1.0, ALU.add, ALU.mult), waits=[tlast])
                tb1 = s.emit("vector", lambda h, tc=tc: h.tensor_copy(
                    mbf[:, tc * 512:(tc + 1) * 512], rowa[:]), waits=[tng])
                mrow_done.append(tb1)
                if not rope:
                    tup = s.emit("vector", lambda h, tc=tc: h.tensor_copy(
                        mrowf[:], mbf[:, tc * 512:(tc + 1) * 512]), waits=[tb1])
                    tm2v = s.emit("vector", lambda h: h.tensor_tensor(
                        rowc[:], rowa[:], mrowf[:], ALU.subtract), waits=[tup])
                    tb2 = s.emit("vector", lambda h, tc=tc: h.tensor_copy(
                        mbf2[:, tc * 512:(tc + 1) * 512], rowc[:]), waits=[tm2v])
                    mrow_done.append(tb2)
            td1 = s.dma("sync", lambda h: h.dma_start(out=Qh[64:65, :], in_=mbf[:, :]),
                        waits=mrow_done + [("tensor", s.cnt["tensor"])])
            mdma = [td1]
            if not rope:
                td2 = s.dma("sync", lambda h: h.dma_start(out=Qh[65:66, :], in_=mbf2[:, :]),
                            waits=mrow_done + [("tensor", s.cnt["tensor"])])
                mdma.append(td2)
                tmsf = s.emit("vector", lambda h, h_=h_: h.tensor_scalar(
                    msfull[:], io2t[:], mct[:, h_:h_ + 1], None, ALU.mult),
                    waits=t_c + mdma + [("scalar", s.cnt["scalar"])])
                mdma.append(tmsf)

            for tcq in range(8):
                for sb_ in range(4 * tcq + 4):
                    pp = ps_sc[sb_ % 2]
                    pb = sb_ % 2
                    wl = list(mdma) + ([prev[("pt", pb)]] if ("pt", pb) in prev else []) + \
                         ([prev[id(pp)]] if id(pp) in prev else [])
                    tmm = s.emit("tensor", lambda h, h_=h_, tcq=tcq, sb_=sb_, pp=pp: h.matmul(
                        pp[:], Kh[0:64 + nex, sb_ * 128:(sb_ + 1) * 128],
                        Qh[0:64 + nex, tcq * 512:(tcq + 1) * 512],
                        start=True, stop=True), waits=wl)
                    if rope:
                        tex = s.emit("scalar", lambda h, pb=pb, pp=pp: h.activation(
                            pt[pb][:], pp[:], AF.Exp, bias=0.0, scale=scale), waits=[tmm])
                    else:
                        tex = s.emit("scalar", lambda h, pb=pb, pp=pp, sb_=sb_: h.activation(
                            pt[pb][:], pp[:], AF.Exp, bias=msfull[:, sb_:sb_ + 1], scale=scale), waits=[tmm])
                    j0 = sb_ - 4 * tcq
                    if 0 <= j0 <= 3:
                        tdt = s.emit("vector", lambda h, pp=pp, j0=j0: h.tensor_tensor(
                            dtile[:], pp[:, j0 * 128:(j0 + 1) * 128], negmt[:], ALU.add),
                            waits=[tmm, tex])
                        if rope:
                            tex2 = s.emit("scalar", lambda h, pb=pb, j0=j0: h.activation(
                                pt[pb][:, j0 * 128:(j0 + 1) * 128], dtile[:], AF.Exp,
                                bias=0.0, scale=scale), waits=[tdt])
                        else:
                            tex2 = s.emit("scalar", lambda h, pb=pb, j0=j0, sb_=sb_: h.activation(
                                pt[pb][:, j0 * 128:(j0 + 1) * 128], dtile[:], AF.Exp,
                                bias=msfull[:, sb_:sb_ + 1], scale=scale), waits=[tdt])
                        tex = tex2
                    prev[id(pp)] = tex
                    tpv = None
                    for j in range(max(0, j0), 4):
                        tpv = s.emit("tensor", lambda h, pb=pb, j=j, sb_=sb_, h_=h_: h.matmul(
                            ps_o[j][:, 0:65], pt[pb][:, j * 128:(j + 1) * 128],
                            v_tm[:, sb_ * 520 + h_ * 65: sb_ * 520 + (h_ + 1) * 65],
                            start=(sb_ == 0), stop=(sb_ == 4 * tcq + j)),
                            waits=[tex] + ([prev[("po", j)]] if (("po", j) in prev and sb_ == 0) else []))
                    prev[("pt", pb)] = tpv if tpv is not None else tex
                for j in range(4):
                    trc = s.emit("vector", lambda h, j=j: h.reciprocal(
                        rcp[:, j:j + 1], ps_o[j][:, 64:65]),
                        waits=[prev[("pt", 0)], prev[("pt", 1)]])
                    tfo = s.emit("vector", lambda h, j=j, tcq=tcq, h_=h_: h.tensor_scalar(
                        o_buf[:, (4 * tcq + j) * 512 + h_ * 64: (4 * tcq + j) * 512 + (h_ + 1) * 64],
                        ps_o[j][:, 0:64], rcp[:, j:j + 1], None, ALU.mult), waits=[trc])
                    prev[("po", j)] = tfo
            wtok[hb] = prev[("pt", 0)]
        ow = [prev[("po", j)] for j in range(4)]
        for part in range(4):
            s.dma("sync", lambda h, part=part: h.dma_start(
                out=o_out[:, part * 8 * 512:(part + 1) * 8 * 512],
                in_=o_buf[:, part * 8 * 512:(part + 1) * 8 * 512]), waits=ow)
        s.build(nc, ctx)
    return nc


NEGM_TILE = np.where(np.arange(128)[:, None] <= np.arange(128)[None, :], 0.0, -1e30).astype(NPBF16)


def rope_tables():
    inv = 1.0 / 10000.0 ** (np.arange(0, 64, 2) / 64.0)
    ang = np.arange(T)[None, :] * np.repeat(inv, 2)[:, None]
    C = np.cos(ang)
    S = np.sin(ang)
    S[0::2, :] *= -1.0
    return C.astype(NPBF16), S.astype(NPBF16)

PSWAP64 = np.zeros((64, 64), np.float32)
for _i in range(0, 64, 2):
    PSWAP64[_i, _i + 1] = 1.0
    PSWAP64[_i + 1, _i] = 1.0
PSWAP64 = PSWAP64.astype(NPBF16)


def perm_eo(w):
    w = w.reshape(w.shape[0], -1, 64)
    return np.ascontiguousarray(np.concatenate([w[:, :, 0::2], w[:, :, 1::2]], axis=2).reshape(w.shape[0], -1))


def perm_eo_b(b):
    b = b.reshape(-1, 64)
    return np.ascontiguousarray(np.concatenate([b[:, 0::2], b[:, 1::2]], axis=1).reshape(-1))


def o_to_tokmajor(o_buf_arr):
    return np.ascontiguousarray(o_buf_arr.reshape(128, 32, 512).transpose(1, 0, 2).reshape(T, 512))


def run_rope_attn_sublayer(x, ins):
    cos_t, sin_t = rope_tables()
    in_maps = []
    for c in range(8):
        b, p = c // 2, c % 2
        hs = p * 512
        wqT = np.ascontiguousarray(ins["L0_attn_b_Wq"].T[:, hs:hs + 512])
        wkT = np.ascontiguousarray(ins["L0_attn_b_Wk"].T[:, hs:hs + 512])
        wvT = np.ascontiguousarray(ins["L0_attn_b_Wv"].T[:, hs:hs + 512])
        bq = ins["L0_attn_b_bq"][hs:hs + 512]
        bk = ins["L0_attn_b_bk"][hs:hs + 512]
        in_maps.append(dict(
            xb=np.ascontiguousarray(x[b]),
            wq=wqT, wk=wkT, wv=wvT,
            bq_col=np.ascontiguousarray(bq.reshape(8, 64).T),
            bk_col=np.ascontiguousarray(bk.reshape(8, 64).T),
            bv_row=ins["L0_attn_b_bv"][hs:hs + 512].reshape(1, 512),
            rc_t=cos_t, rs_t=sin_t, negm=NEGM_TILE, pswap=PSWAP64,
        ))
    res = run_bass_kernel_spmd(prog("attn_rope", lambda: build_attn("rope")), in_maps, core_ids=list(range(8)))
    o_full = [np.concatenate([o_to_tokmajor(res.results[2 * b]["o_out"]),
                              o_to_tokmajor(res.results[2 * b + 1]["o_out"])], axis=1)
              for b in range(4)]
    in_maps = []
    for c in range(8):
        b, p = c // 2, c % 2
        in_maps.append(dict(
            x_fm=fm(x[b, p * TL:(p + 1) * TL]),
            o_in=fm(o_full[b][p * TL:(p + 1) * TL]).astype(NPBF16),
            w=np.ascontiguousarray(ins["L0_attn_b_Wo"].T),
            b_col=col128(ins["L0_attn_b_bo"]),
        ))
    res = run_bass_kernel_spmd(prog("dn_b", build_proj_residual), in_maps, core_ids=list(range(8)))
    xn = np.empty_like(x)
    for c in range(8):
        b, p = c // 2, c % 2
        xn[b, p * TL:(p + 1) * TL] = unfm(res.results[c]["xo"], TL)
    return xn


IDENT1024 = None

def run_alibi_attn_sublayer(x, ins):
    global IDENT1024
    wkqv = ins["L1_attn_a_W_kqv"]
    in_maps = []
    for c in range(8):
        b, p = c // 2, c % 2
        hs = p * 512
        in_maps.append(dict(
            xb=np.ascontiguousarray(x[b]),
            wk=np.ascontiguousarray(wkqv[:, hs:hs + 512]),
            wq=np.ascontiguousarray(wkqv[:, 1024 + hs:1024 + hs + 512]),
            wv=np.ascontiguousarray(wkqv[:, 2048 + hs:2048 + hs + 512]),
            m_col=np.ascontiguousarray(np.broadcast_to(
                ins["L1_attn_a_m"][p * 8:(p + 1) * 8, 0, 0].reshape(1, 8), (128, 8))),
            iota_r=np.arange(T, dtype=np.float32).reshape(1, T),
            iota2d=np.ascontiguousarray((np.arange(128)[:, None] + 128 * np.arange(32)[None, :]).astype(np.float32)),
            negm=NEGM_TILE,
        ))
    res = run_bass_kernel_spmd(prog("attn_alibi", lambda: build_attn("alibi")), in_maps, core_ids=list(range(8)))
    o_full = [np.concatenate([o_to_tokmajor(res.results[2 * b]["o_out"]),
                              o_to_tokmajor(res.results[2 * b + 1]["o_out"])], axis=1)
              for b in range(4)]
    if IDENT1024 is None:
        IDENT1024 = np.eye(D, dtype=np.float32)
    in_maps = []
    for c in range(8):
        b, p = c // 2, c % 2
        in_maps.append(dict(
            x_fm=fm(x[b, p * TL:(p + 1) * TL]),
            o_in=fm(o_full[b][p * TL:(p + 1) * TL]).astype(NPBF16),
            w=IDENT1024,
            b_col=np.zeros((128, 8), np.float32),
        ))
    res = run_bass_kernel_spmd(prog("dn_b", build_proj_residual), in_maps, core_ids=list(range(8)))
    xn = np.empty_like(x)
    for c in range(8):
        b, p = c // 2, c % 2
        xn[b, p * TL:(p + 1) * TL] = unfm(res.results[c]["xo"], TL)
    return xn





def emit_quant(s, qs, src_ap, dst_ap, W, waits=(), eng="vector"):
    nb = W // 16
    bc = lambda t: t[:, 0:nb].unsqueeze(2).broadcast_to([128, nb, 16])
    v3 = lambda t: t[:, 0:W].rearrange("p (a m) -> p a m", a=nb)
    tw = list(waits)
    t0 = s.emit(eng, lambda h: h.scalar_tensor_tensor(qs["qa"][:, 0:W], src_ap, -1.0, src_ap, ALU.mult, ALU.max), waits=tw)
    t1 = s.emit(eng, lambda h: h.tensor_reduce(qs["qm1"][:, 0:nb], v3(qs["qa"]), AX, ALU.max), waits=[t0])
    t2 = s.emit(eng, lambda h: h.tensor_scalar(qs["qm2"][:, 0:nb], qs["qm1"][:, 0:nb], 1.0 / 6.0, 1e-12, ALU.mult, ALU.max), waits=[t1])
    t3 = s.emit(eng, lambda h: h.reciprocal(qs["qm3"][:, 0:nb], qs["qm2"][:, 0:nb]), waits=[t2])
    t4 = s.emit(eng, lambda h: h.tensor_tensor(v3(qs["qv"]), v3(qs["qa"]), bc(qs["qm3"]), ALU.mult), waits=[t3])
    ta = s.emit(eng, lambda h: h.tensor_scalar(qs["qa"][:, 0:W], qs["qv"][:, 0:W], 0.25, None, ALU.is_gt), waits=[t4])
    for c in (0.75, 1.25, 1.75):
        ta = s.emit(eng, lambda h, c=c: h.scalar_tensor_tensor(
            qs["qa"][:, 0:W], qs["qv"][:, 0:W], c, qs["qa"][:, 0:W], ALU.is_gt, ALU.add), waits=[ta])
    tb = s.emit(eng, lambda h: h.tensor_scalar(qs["qb"][:, 0:W], qs["qv"][:, 0:W], 2.5, None, ALU.is_gt), waits=[t4])
    tb = s.emit(eng, lambda h: h.scalar_tensor_tensor(
        qs["qb"][:, 0:W], qs["qv"][:, 0:W], 3.5, qs["qb"][:, 0:W], ALU.is_gt, ALU.add), waits=[tb])
    tc_ = s.emit(eng, lambda h: h.tensor_scalar(qs["qv"][:, 0:W], qs["qv"][:, 0:W], 5.0, None, ALU.is_gt), waits=[tb, ta])
    tq = s.emit(eng, lambda h: h.scalar_tensor_tensor(
        qs["qb"][:, 0:W], qs["qv"][:, 0:W], 2.0, qs["qb"][:, 0:W], ALU.mult, ALU.add), waits=[tc_])
    tq = s.emit(eng, lambda h: h.scalar_tensor_tensor(
        qs["qb"][:, 0:W], qs["qa"][:, 0:W], 0.5, qs["qb"][:, 0:W], ALU.mult, ALU.add), waits=[tq])
    tsg = s.emit("scalar", lambda h: h.sign(qs["qsg"][:, 0:W], src_ap), waits=tw + [t4])
    tm = s.emit(eng, lambda h: h.tensor_tensor(qs["qa"][:, 0:W], qs["qb"][:, 0:W], qs["qsg"][:, 0:W], ALU.mult), waits=[tq, tsg])
    tout = s.emit(eng, lambda h: h.tensor_tensor(
        v3_out(dst_ap, nb), v3(qs["qa"]), bc(qs["qm2"]), ALU.mult), waits=[tm])
    return tout

def v3_out(ap, nb):
    return ap.rearrange("p (a m) -> p a m", a=nb)


def build_ffn1():
    nc = bass.Bass("TRN2", target_bir_lowering=False)
    x_tm = nc.dram_tensor("x_tm", [TL, D], F32, kind="ExternalInput").ap()
    wg = nc.dram_tensor("wg", [2816, D], F32, kind="ExternalInput").ap()
    wu = nc.dram_tensor("wu", [2816, D], F32, kind="ExternalInput").ap()
    wd = nc.dram_tensor("wd", [D, 2816], F32, kind="ExternalInput").ap()
    ident = nc.dram_tensor("ident", [128, 128], BF16, kind="ExternalInput").ap()
    xq_scr = nc.dram_tensor("xq_scr", [TL, D], BF16, kind="Internal").ap()
    wgq_scr = nc.dram_tensor("wgq_scr", [2816, D], BF16, kind="Internal").ap()
    wuq_scr = nc.dram_tensor("wuq_scr", [2816, D], BF16, kind="Internal").ap()
    wdq_scr = nc.dram_tensor("wdq_scr", [D, 2816], BF16, kind="Internal").ap()
    xo = nc.dram_tensor("xo", [TL, D], F32, kind="ExternalOutput").ap()

    NT_ = TL // 128
    NI = 11
    s = Sched()
    with ExitStack() as ctx:
        E = ctx.enter_context
        qs = dict(
            qa=E(nc.sbuf_tensor("qa", [128, 2816], F32)),
            qv=E(nc.sbuf_tensor("qv", [128, 2816], F32)),
            qb=E(nc.sbuf_tensor("qb", [128, 2816], F32)),
            qm1=E(nc.sbuf_tensor("qm1", [128, 176], F32)),
            qm2=E(nc.sbuf_tensor("qm2", [128, 176], F32)),
            qm3=E(nc.sbuf_tensor("qm3", [128, 176], F32)),
            qsg=E(nc.sbuf_tensor("qsg", [128, 2816], BF16)),
        )
        identt = E(nc.sbuf_tensor("identt", [128, 128], BF16))
        wtile = E(nc.sbuf_tensor("wtile", [128, 2816], F32))
        wq_out = E(nc.sbuf_tensor("wq_out", [128, 2816], BF16))
        t_id = s.dma("sync", lambda h: h.dma_start(out=identt[:], in_=ident[:, :]))

        prevq = None
        for wsrc, wdst, nt, wwid in ((wg, wgq_scr, 22, D), (wu, wuq_scr, 22, D), (wd, wdq_scr, 8, 2816)):
            for i in range(nt):
                tld = s.dma("sync", lambda h, wsrc=wsrc, i=i, wwid=wwid: h.dma_start(
                    out=wtile[:, 0:wwid], in_=wsrc[i * 128:(i + 1) * 128, :]),
                    waits=[prevq] if prevq else [])
                tq = emit_quant(s, qs, wtile[:, 0:wwid], wq_out[:, 0:wwid], wwid, waits=[tld])
                tst = s.dma("sync", lambda h, wdst=wdst, i=i, wwid=wwid: h.dma_start(
                    out=wdst[i * 128:(i + 1) * 128, :], in_=wq_out[:, 0:wwid]), waits=[tq])
                prevq = tst

        with ExitStack() as ctx2:
            E2 = ctx2.enter_context
            sb = {}
            sb["ln_xt"] = [E2(nc.sbuf_tensor(f"lnxt{i}", [128, D], F32)) for i in range(2)]
            sb["ln_sq"] = E2(nc.sbuf_tensor("ln_sq", [128, D], F32))
            sb["ln_xn"] = E2(nc.sbuf_tensor("ln_xnb", [128, D], BF16))
            sb["ln_st"] = E2(nc.sbuf_tensor("ln_st", [128, 8], F32))
            xnf = E2(nc.sbuf_tensor("xnf", [128, D], F32))
            last = None
            prevst = None
            for i in range(NT_):
                xt, sq, st = sb["ln_xt"][0], sb["ln_sq"], sb["ln_st"]
                tld = s.dma("sync", lambda h, i=i: h.dma_start(
                    out=xt[:], in_=x_tm[i * 128:(i + 1) * 128, :]), waits=[last] if last else [prevq])
                tsq = s.emit("scalar", lambda h: h.square(sq[:], xt[:]), waits=[tld])
                t1 = s.emit("vector", lambda h: h.reduce_sum(st[:, 0:1], xt[:], AX), waits=[tld])
                t2 = s.emit("vector", lambda h: h.reduce_sum(st[:, 1:2], sq[:], AX), waits=[tsq, t1])
                tm = s.emit("vector", lambda h: h.tensor_scalar(st[:, 2:3], st[:, 0:1], 1.0 / D, None, ALU.mult), waits=[t2])
                tm2 = s.emit("vector", lambda h: h.tensor_scalar(st[:, 3:4], st[:, 2:3], st[:, 2:3], None, ALU.mult), waits=[tm])
                tv = s.emit("vector", lambda h: h.scalar_tensor_tensor(
                    st[:, 4:5], st[:, 1:2], 1.0 / D, st[:, 3:4], ALU.mult, ALU.subtract), waits=[tm2])
                tve = s.emit("vector", lambda h: h.tensor_scalar(st[:, 5:6], st[:, 4:5], EPS_LN, None, ALU.add), waits=[tv])
                tsd = s.emit("scalar", lambda h: h.sqrt(st[:, 6:7], st[:, 5:6]), waits=[tve])
                trs = s.emit("vector", lambda h: h.reciprocal(st[:, 7:8], st[:, 6:7]), waits=[tsd])
                txn = s.emit("vector", lambda h: h.tensor_scalar(
                    xnf[:], xt[:], st[:, 2:3], st[:, 7:8], ALU.subtract, ALU.mult), waits=[trs])
                tq = emit_quant(s, qs, xnf[:], sb["ln_xn"][:], D, waits=[txn])
                tst2 = s.dma("sync", lambda h, i=i: h.dma_start(
                    out=xq_scr[i * 128:(i + 1) * 128, :], in_=sb["ln_xn"][:]), waits=[tq])
                last = txn
                prevst = tst2
            ln_done = [prevst]

        xq_fm = E(nc.sbuf_tensor("xq_fm", [128, 8 * TL], BF16))
        wdqT = E(nc.sbuf_tensor("wdqT", [128, 22 * D], BF16))
        wgT = [E(nc.sbuf_tensor(f"wgT{i}", [128, 8 * 256], BF16)) for i in range(2)]
        wuT = [E(nc.sbuf_tensor(f"wuT{i}", [128, 8 * 256], BF16)) for i in range(2)]
        hqT = E(nc.sbuf_tensor("hqT", [128, 4 * 2816], BF16))
        silu = E(nc.sbuf_tensor("silu", [128, 256], F32))
        upse = E(nc.sbuf_tensor("upse", [128, 256], F32))
        hft = E(nc.sbuf_tensor("hft", [128, 256], F32))
        hqt = E(nc.sbuf_tensor("hqt", [128, 256], BF16))
        xtile = E(nc.sbuf_tensor("xtile", [128, D], F32))
        otile = E(nc.sbuf_tensor("otile", [128, D], F32))
        ps_g = E(nc.psum_tensor("ps_g", [128, 256], F32))
        ps_u = E(nc.psum_tensor("ps_u", [128, 256], F32))
        ps_t = E(nc.psum_tensor("ps_t", [128, 128], BF16))
        ps_d = [E(nc.psum_tensor(f"ps_d{i}", [128, 512], F32)) for i in range(2)]

        tr_x = emit_transpose_in(s, xq_fm, xq_scr, TL, ln_done)
        twd = []
        for i in range(22):
            twd.append(s.dma("sync", lambda h, i=i: h.dma_start_transpose(
                wdqT[:, i * D:(i + 1) * D], wdq_scr[:, i * 128:(i + 1) * 128]), waits=[prevq]))

        prev = {}
        for tg in range(4):
            for Ic in range(NI):
                wb = Ic % 2
                wl = tr_x + ([prev[("wg", wb)]] if ("wg", wb) in prev else [])
                tgw = []
                for kc in range(8):
                    tgw.append(s.dma("sync", lambda h, kc=kc, Ic=Ic, wb=wb: h.dma_start_transpose(
                        wgT[wb][:, kc * 256:(kc + 1) * 256],
                        wgq_scr[Ic * 256:(Ic + 1) * 256, kc * 128:(kc + 1) * 128]), waits=wl))
                    tgw.append(s.dma("sync", lambda h, kc=kc, Ic=Ic, wb=wb: h.dma_start_transpose(
                        wuT[wb][:, kc * 256:(kc + 1) * 256],
                        wuq_scr[Ic * 256:(Ic + 1) * 256, kc * 128:(kc + 1) * 128]), waits=wl))
                for ti in range(4):
                    tt = tg * 4 + ti
                    wl2 = tgw + ([prev["hq"]] if "hq" in prev else [])
                    tokg = toku = None
                    for kc in range(8):
                        tokg = s.emit("tensor", lambda h, kc=kc, tt=tt, wb=wb: h.matmul(
                            ps_g[:], xq_fm[:, kc * TL + tt * 128: kc * TL + (tt + 1) * 128],
                            wgT[wb][:, kc * 256:(kc + 1) * 256],
                            start=(kc == 0), stop=(kc == 7)), waits=(wl2 if kc == 0 else []))
                    for kc in range(8):
                        toku = s.emit("tensor", lambda h, kc=kc, tt=tt, wb=wb: h.matmul(
                            ps_u[:], xq_fm[:, kc * TL + tt * 128: kc * TL + (tt + 1) * 128],
                            wuT[wb][:, kc * 256:(kc + 1) * 256],
                            start=(kc == 0), stop=(kc == 7)), waits=[])
                    tsl = s.emit("scalar", lambda h: h.activation(
                        silu[:], ps_g[:], AF.Silu, bias=0.0, scale=1.0),
                        waits=[tokg] + ([prev["sl"]] if "sl" in prev else []))
                    tue = s.emit("scalar", lambda h: h.copy(upse[:], ps_u[:]),
                        waits=[toku] + ([prev["ue"]] if "ue" in prev else []))
                    prev["sl"] = tsl
                    prev["ue"] = tue
                    th = s.emit("vector", lambda h: h.tensor_tensor(hft[:], silu[:], upse[:], ALU.mult),
                                waits=[tsl, tue])
                    thq = emit_quant(s, qs, hft[:], hqt[:], 256, waits=[th])
                    prev["hq"] = thq
                    for half in range(2):
                        ttr = s.emit("tensor", lambda h, half=half: h.transpose(
                            ps_t[:], hqt[:, half * 128:(half + 1) * 128], identt[:]),
                            waits=[thq, t_id] + ([prev["tre"]] if "tre" in prev else []))
                        tte = s.emit("vector", lambda h, ti=ti, Ic=Ic, half=half: h.tensor_copy(
                            hqT[:, ti * 2816 + Ic * 256 + half * 128: ti * 2816 + Ic * 256 + (half + 1) * 128],
                            ps_t[:]), waits=[ttr])
                        prev["tre"] = tte
                prev[("wg", wb)] = tokg
            for ti in range(4):
                tt = tg * 4 + ti
                tx = s.dma("sync", lambda h, tt=tt: h.dma_start(
                    out=xtile[:], in_=x_tm[tt * 128:(tt + 1) * 128, :]),
                    waits=[prev["xo"]] if "xo" in prev else [])
                for oc in range(2):
                    pd = ps_d[oc]
                    wl3 = twd + [prev["tre"], tx] + ([prev[("pd", oc)]] if ("pd", oc) in prev else [])
                    tok = None
                    for i in range(22):
                        tok = s.emit("tensor", lambda h, i=i, ti=ti, oc=oc, pd=pd: h.matmul(
                            pd[:], hqT[:, ti * 2816 + i * 128: ti * 2816 + i * 128 + 128],
                            wdqT[:, i * D + oc * 512: i * D + (oc + 1) * 512],
                            start=(i == 0), stop=(i == 21)), waits=(wl3 if i == 0 else []))
                    te = s.emit("vector", lambda h, oc=oc, pd=pd: h.tensor_tensor(
                        otile[:, oc * 512:(oc + 1) * 512], pd[:],
                        xtile[:, oc * 512:(oc + 1) * 512], ALU.add), waits=[tok])
                    prev[("pd", oc)] = te
                txo = s.dma("sync", lambda h, tt=tt: h.dma_start(
                    out=xo[tt * 128:(tt + 1) * 128, :], in_=otile[:]),
                    waits=[prev[("pd", 0)], prev[("pd", 1)]])
                prev["xo"] = txo
        s.build(nc, ctx)
    return nc


def run_ffn1_sublayer(x, ins):
    in_maps = []
    for c in range(8):
        b, p = c // 2, c % 2
        in_maps.append(dict(
            x_tm=np.ascontiguousarray(x[b, p * TL:(p + 1) * TL]),
            wg=ins["L1_ffn_w_gate"], wu=ins["L1_ffn_w_up"], wd=ins["L1_ffn_w_down"],
            ident=IDENT,
        ))
    res = run_bass_kernel_spmd(prog("ffn1", build_ffn1), in_maps, core_ids=list(range(8)))
    xn = np.empty_like(x)
    for c in range(8):
        b, p = c // 2, c % 2
        xn[b, p * TL:(p + 1) * TL] = res.results[c]["xo"]
    return xn





def build_rn_a():
    nc = bass.Bass("TRN2", target_bir_lowering=False)
    xb = nc.dram_tensor("xb", [T, D], F32, kind="ExternalInput").ap()
    wq = nc.dram_tensor("wq", [D, 512], F32, kind="ExternalInput").ap()
    wk = nc.dram_tensor("wk", [D, 512], F32, kind="ExternalInput").ap()
    wv = nc.dram_tensor("wv", [D, 512], F32, kind="ExternalInput").ap()
    pkv = nc.dram_tensor("pkv", [64, 8 * 64], F32, kind="ExternalInput").ap()
    aB = nc.dram_tensor("aB", [64, 8 * 512], BF16, kind="ExternalInput").ap()
    bB = nc.dram_tensor("bB", [64, 8 * 128], BF16, kind="ExternalInput").ap()
    triui = nc.dram_tensor("triui", [128, 128], BF16, kind="ExternalInput").ap()
    cp = nc.dram_tensor("cp", [128, 8 * 144], F32, kind="ExternalInput").ap()
    xn_scr = nc.dram_tensor("xn_scr", [T, D], BF16, kind="Internal").ap()
    r_out = nc.dram_tensor("r_out", [128, 32 * 512], BF16, kind="ExternalOutput").ap()
    st_out = nc.dram_tensor("st_out", [1, 4], F32, kind="ExternalOutput").ap()

    NB = T // 128
    GAM = [1.0 - 2.0 ** (-5.0 - g) for g in range(16)]
    s = Sched()
    with ExitStack() as ctx:
        E = ctx.enter_context
        sb = {}
        xn_fm = E(nc.sbuf_tensor("xn_fm", [128, 8 * T], BF16))
        v_tm = E(nc.sbuf_tensor("v_tm", [128, NB * 512], BF16))
        wqh = [E(nc.sbuf_tensor(f"wqh{i}", [128, 8 * 64], BF16)) for i in range(2)]
        wkh = [E(nc.sbuf_tensor(f"wkh{i}", [128, 8 * 64], BF16)) for i in range(2)]
        Qh = E(nc.sbuf_tensor("Qh", [64, T], BF16))
        Qs = E(nc.sbuf_tensor("Qs", [64, T], BF16))
        Ks = E(nc.sbuf_tensor("Ks", [64, T], BF16))
        pkvf = E(nc.sbuf_tensor("pkvf", [64, 8 * 64], F32))
        pkvs = E(nc.sbuf_tensor("pkvs", [64, 8 * 64], BF16))
        aBt = E(nc.sbuf_tensor("aBt", [64, 8 * 512], BF16))
        bBt = E(nc.sbuf_tensor("bBt", [64, 8 * 128], BF16))
        triuit = E(nc.sbuf_tensor("triuit", [128, 128], BF16))
        cpt = E(nc.sbuf_tensor("cpt", [128, 8 * 144], F32))
        ps_sc = [E(nc.psum_tensor(f"ps_sc{i}", [128, 512], F32)) for i in range(2)]
        ps_r = [E(nc.psum_tensor(f"ps_r{i}", [128, 512], F32)) for i in range(4)]
        ps_n = E(nc.psum_tensor("ps_n", [1, 512], F32))

        t_c = [s.dma("sync", lambda h: h.dma_start(out=triuit[:], in_=triui[:, :]))]
        t_c.append(s.dma("sync", lambda h: h.dma_start(out=cpt[:], in_=cp[:, :])))
        t_c.append(s.dma("sync", lambda h: h.dma_start(out=aBt[:], in_=aB[:, :])))
        t_c.append(s.dma("sync", lambda h: h.dma_start(out=bBt[:], in_=bB[:, :])))
        t_c.append(s.dma("sync", lambda h: h.dma_start(out=pkvf[:], in_=pkv[:, :])))
        t_c.append(s.emit("vector", lambda h: h.tensor_scalar(
            pkvs[:], pkvf[:], 1024.0, None, ALU.mult), waits=t_c[-1:]))

        with ExitStack() as ctx2:
            E2 = ctx2.enter_context
            sb["ln_xt"] = [E2(nc.sbuf_tensor(f"lnxt{i}", [128, D], F32)) for i in range(2)]
            sb["ln_sq"] = E2(nc.sbuf_tensor("ln_sq", [128, D], F32))
            sb["ln_xn"] = E2(nc.sbuf_tensor("ln_xnb", [128, D], BF16))
            sb["ln_st"] = E2(nc.sbuf_tensor("ln_st", [128, 8], F32))
            wvt = E2(nc.sbuf_tensor("wvt", [128, 8 * 512], BF16))
            ln_toks = emit_ln(s, sb, xb, xn_scr, NB)
            tr_toks = emit_transpose_in(s, xn_fm, xn_scr, T, ln_toks)
            w_v = emit_load_w_bf16(s, wvt, wv, 8, 512)
            prev_v = {}
            for tt in range(NB):
                ps = ps_sc[tt % 2]
                wl = tr_toks + w_v + ([prev_v[id(ps)]] if id(ps) in prev_v else [])
                tok = None
                for kc in range(8):
                    tok = s.emit("tensor", lambda h, kc=kc, tt=tt, ps=ps: h.matmul(
                        ps[:], xn_fm[:, kc * T + tt * 128: kc * T + (tt + 1) * 128],
                        wvt[:, kc * 512:(kc + 1) * 512],
                        start=(kc == 0), stop=(kc == 7)), waits=(wl if kc == 0 else []))
                te = s.emit("vector", lambda h, tt=tt, ps=ps: h.tensor_copy(
                    v_tm[:, tt * 512:(tt + 1) * 512], ps[:]), waits=[tok])
                prev_v[id(ps)] = te

        bar0 = [(e, s.cnt[e]) for e in ENGINES]
        r_buf = E(nc.sbuf_tensor("r_buf", [128, NB * 512], BF16))
        sqb = E(nc.sbuf_tensor("sqb", [128, 512], BF16))
        ones128 = E(nc.sbuf_tensor("ones128", [128, 1], BF16))
        strow = E(nc.sbuf_tensor("strow", [1, 16], F32))
        pt = [E(nc.sbuf_tensor(f"pt{i}", [128, 512], BF16)) for i in range(2)]
        tz = s.emit("gpsimd", lambda h: h.memset(ones128[:], 1.0), waits=bar0)

        prev = {}
        wtok = [None, None]
        for hh in range(HL):
            hb = hh % 2
            wl = list(bar0) + ([wtok[hb]] if wtok[hb] else [])
            twq = s.dma("gpsimd", lambda h, hh=hh, hb=hb: h.dma_start(
                out=wqh[hb][:], in_=wq[:, hh * 64:(hh + 1) * 64].rearrange("(kc p) c -> p kc c", p=128)),
                waits=wl)
            twk = s.dma("gpsimd", lambda h, hh=hh, hb=hb: h.dma_start(
                out=wkh[hb][:], in_=wk[:, hh * 64:(hh + 1) * 64].rearrange("(kc p) c -> p kc c", p=128)),
                waits=wl)
            for side in ("q", "k"):
                wt_, tws = (wqh, twq) if side == "q" else (wkh, twk)
                for tc in range(8):
                    ps = ps_sc[tc % 2]
                    wl = list(bar0) + [tws] + ([prev[id(ps)]] if id(ps) in prev else [])
                    tok = None
                    for kc in range(8):
                        tok = s.emit("tensor", lambda h, kc=kc, tc=tc, hb=hb, wt_=wt_, ps=ps: h.matmul(
                            ps[0:64, :], wt_[hb][:, kc * 64:(kc + 1) * 64],
                            xn_fm[:, kc * T + tc * 512: kc * T + (tc + 1) * 512],
                            start=(kc == 0), stop=(kc == 7)), waits=(wl if kc == 0 else []))
                    if side == "q":
                        tev = s.emit("scalar", lambda h, ps=ps, tc=tc: h.copy(
                            Qh[:, tc * 512:(tc + 1) * 512], ps[0:64, :]), waits=[tok])
                        tqs = s.emit("vector", lambda h, ps=ps, tc=tc, hh=hh: h.tensor_tensor(
                            Qs[:, tc * 512:(tc + 1) * 512], ps[0:64, :],
                            aBt[:, hh * 512:(hh + 1) * 512], ALU.mult), waits=[tok, tev] + t_c)
                        prev[id(ps)] = tqs
                        prev["qev"] = tev
                    else:
                        tks = s.emit("vector", lambda h, ps=ps, tc=tc, hh=hh: h.tensor_tensor(
                            Ks[:, tc * 512:(tc + 1) * 512].rearrange("p (a m) -> p a m", a=4),
                            ps[0:64, :].rearrange("p (a m) -> p a m", a=4),
                            bBt[:, hh * 128:(hh + 1) * 128].unsqueeze(1).broadcast_to([64, 4, 128]),
                            ALU.mult), waits=[tok] + t_c)
                        prev[id(ps)] = tks
            pk_done = [("vector", s.cnt["vector"]), ("scalar", s.cnt["scalar"])]

            for tcq in range(8):
                for j in range(4):
                    tb_ = 4 * tcq + j
                    tcx = s.emit("tensor", lambda h, j=j, tb_=tb_, hh=hh: h.matmul(
                        ps_r[j][:, 0:64], Qh[:, tb_ * 128:(tb_ + 1) * 128],
                        pkvs[:, hh * 64:(hh + 1) * 64], start=True, stop=False),
                        waits=pk_done + t_c + ([prev[("pr", j)]] if ("pr", j) in prev else []))
                    prev[("prx", j)] = tcx
                for sb_ in range(4 * tcq + 4):
                    pp = ps_sc[sb_ % 2]
                    pb = sb_ % 2
                    wl = pk_done + ([prev[("pt", pb)]] if ("pt", pb) in prev else []) + \
                         ([prev[id(pp)]] if id(pp) in prev else [])
                    tmm = s.emit("tensor", lambda h, tcq=tcq, sb_=sb_, pp=pp: h.matmul(
                        pp[:], Ks[:, sb_ * 128:(sb_ + 1) * 128],
                        Qs[:, tcq * 512:(tcq + 1) * 512],
                        start=True, stop=True), waits=wl)
                    j0 = sb_ - 4 * tcq
                    ci = cp_index(hh, tcq, sb_)
                    tex = s.emit("vector", lambda h, pb=pb, pp=pp, ci=ci: h.tensor_scalar(
                        pt[pb][:], pp[:], cpt[:, ci:ci + 1], None, ALU.mult), waits=[tmm] + t_c)
                    if 0 <= j0 <= 3:
                        tex = s.emit("vector", lambda h, pb=pb, pp=pp, j0=j0, ci=ci: h.scalar_tensor_tensor(
                            pt[pb][:, j0 * 128:(j0 + 1) * 128], pp[:, j0 * 128:(j0 + 1) * 128],
                            cpt[:, ci:ci + 1], triuit[:], ALU.mult, ALU.mult), waits=[tex])
                    prev[id(pp)] = tex
                    tpv = None
                    for j in range(max(0, j0), 4):
                        tpv = s.emit("tensor", lambda h, pb=pb, j=j, sb_=sb_, hh=hh: h.matmul(
                            ps_r[j][:, 0:64], pt[pb][:, j * 128:(j + 1) * 128],
                            v_tm[:, sb_ * 512 + hh * 64: sb_ * 512 + (hh + 1) * 64],
                            start=False, stop=(sb_ == 4 * tcq + j)),
                            waits=[tex, prev[("prx", j)]])
                    prev[("pt", pb)] = tpv if tpv is not None else tex
                for j in range(4):
                    tfo = s.emit("scalar", lambda h, j=j, tcq=tcq, hh=hh: h.copy(
                        r_buf[:, (4 * tcq + j) * 512 + hh * 64: (4 * tcq + j) * 512 + (hh + 1) * 64],
                        ps_r[j][:, 0:64]), waits=[prev[("pt", 0)], prev[("pt", 1)]])
                    prev[("pr", j)] = tfo
            wtok[hb] = prev[("pt", 0)]

        rdone = [prev[("pr", j)] for j in range(4)]
        tok1 = tok2 = None
        for i in range(NB):
            tok1 = s.emit("tensor", lambda h, i=i: h.matmul(
                ps_n[:], ones128[:], r_buf[:, i * 512:(i + 1) * 512],
                start=(i == 0), stop=(i == NB - 1)), waits=(rdone + [tz] if i == 0 else []))
        ts1 = s.emit("vector", lambda h: h.reduce_sum(strow[:, 0:1], ps_n[:], AX), waits=[tok1])
        sq_ev = []
        for i in range(NB):
            tsq = s.emit("scalar", lambda h, i=i: h.square(sqb[:], r_buf[:, i * 512:(i + 1) * 512]),
                         waits=rdone + [ts1] + sq_ev[-1:])
            tok2 = s.emit("tensor", lambda h, i=i: h.matmul(
                ps_n[:], ones128[:], sqb[:], start=(i == 0), stop=(i == NB - 1)),
                waits=[tsq] + ([ts1] if i == 0 else []))
            sq_ev.append(tok2)
        ts2 = s.emit("vector", lambda h: h.reduce_sum(strow[:, 1:2], ps_n[:], AX), waits=[tok2])
        tds = s.dma("sync", lambda h: h.dma_start(out=st_out[0:1, 0:2], in_=strow[:, 0:2]), waits=[ts1, ts2])
        for part in range(4):
            s.dma("sync", lambda h, part=part: h.dma_start(
                out=r_out[:, part * 8 * 512:(part + 1) * 8 * 512],
                in_=r_buf[:, part * 8 * 512:(part + 1) * 8 * 512]), waits=rdone + [ts2])
        s.build(nc, ctx)
    return nc


def cp_index(hh, tcq, sb_):
    return hh * 144 + 2 * tcq * tcq + 2 * tcq + sb_


def make_cp_table(p):
    tab = np.zeros(8 * 144, np.float32)
    for hh in range(8):
        g = 1.0 - 2.0 ** (-5.0 - (hh + 8 * p))
        for tcq in range(8):
            for sb_ in range(4 * tcq + 4):
                e = 512 * tcq - 128 * sb_
                tab[cp_index(hh, tcq, sb_)] = np.float32(g) ** e
    return np.ascontiguousarray(np.broadcast_to(tab, (128, 8 * 144)))


def make_ab_tables(p):
    aB = np.zeros((64, 8 * 512), np.float32)
    bB = np.zeros((64, 8 * 128), np.float32)
    for hh in range(8):
        g = 1.0 - 2.0 ** (-5.0 - (hh + 8 * p))
        aB[:, hh * 512:(hh + 1) * 512] = g ** np.arange(512)[None, :]
        bB[:, hh * 128:(hh + 1) * 128] = g ** (-np.arange(128)[None, :].astype(np.float64))
    return aB.astype(NPBF16), bB.astype(NPBF16)


def run_retnet_sublayer(x, ins):
    wqkv = ins["L1_attn_b_W_qkv"]
    in_maps = []
    for c in range(8):
        b, p = c // 2, c % 2
        hs = p * 512
        aB, bB = make_ab_tables(p)
        in_maps.append(dict(
            xb=np.ascontiguousarray(x[b]),
            wq=np.ascontiguousarray(wqkv[:, hs:hs + 512]),
            wk=np.ascontiguousarray(wqkv[:, 1024 + hs:1024 + hs + 512]),
            wv=np.ascontiguousarray(wqkv[:, 2048 + hs:2048 + hs + 512]),
            pkv=np.ascontiguousarray(ins["L1_attn_b_past_kv"][p * 8:(p + 1) * 8].transpose(1, 0, 2).reshape(64, 512)),
            aB=aB, bB=bB, triui=TRIUI, cp=make_cp_table(p),
        ))
    res = run_bass_kernel_spmd(prog("rn_a", build_rn_a), in_maps, core_ids=list(range(8)))
    r_full = [np.concatenate([o_to_tokmajor(res.results[2 * b]["r_out"]),
                              o_to_tokmajor(res.results[2 * b + 1]["r_out"])], axis=1)
              for b in range(4)]
    gnw_col = col_by_head(ins["L1_attn_b_gn_weight"])
    gnb_col = col_by_head(ins["L1_attn_b_gn_bias"])
    in_maps = []
    for c in range(8):
        b, p = c // 2, c % 2
        stats = np.concatenate([res.results[2 * b]["st_out"][0],
                                res.results[2 * b + 1]["st_out"][0]]).reshape(1, 8)
        in_maps.append(dict(
            x_fm=fm(x[b, p * TL:(p + 1) * TL]),
            r_fm=fm(r_full[b][p * TL:(p + 1) * TL]).astype(NPBF16),
            st2=np.ascontiguousarray(stats.astype(np.float32)),
            gnw_col=gnw_col, gnb_col=gnb_col,
        ))
    res = run_bass_kernel_spmd(prog("rn_b", build_rn_b), in_maps, core_ids=list(range(8)))
    xn = np.empty_like(x)
    for c in range(8):
        b, p = c // 2, c % 2
        xn[b, p * TL:(p + 1) * TL] = unfm(res.results[c]["xo"], TL)
    return xn


def col_by_head(g):
    feat = np.repeat(g, 64)
    return np.ascontiguousarray(feat.reshape(8, 128).T.astype(np.float32))


def build_rn_b():
    nc = bass.Bass("TRN2", target_bir_lowering=False)
    x_fm = nc.dram_tensor("x_fm", [128, 8 * TL], F32, kind="ExternalInput").ap()
    r_fm = nc.dram_tensor("r_fm", [128, 8 * TL], BF16, kind="ExternalInput").ap()
    st2 = nc.dram_tensor("st2", [1, 8], F32, kind="ExternalInput").ap()
    gnw_col = nc.dram_tensor("gnw_col", [128, 8], F32, kind="ExternalInput").ap()
    gnb_col = nc.dram_tensor("gnb_col", [128, 8], F32, kind="ExternalInput").ap()
    bscr = nc.dram_tensor("bscr", [1, 256], F32, kind="Internal").ap()
    xo = nc.dram_tensor("xo", [128, 8 * TL], F32, kind="ExternalOutput").ap()
    NTOT = float(16 * 64 * T)
    s = Sched()
    with ExitStack() as ctx:
        E = ctx.enter_context
        xt = E(nc.sbuf_tensor("xt", [128, 8 * TL], F32))
        rt = E(nc.sbuf_tensor("rt", [128, 8 * TL], BF16))
        stt_ = E(nc.sbuf_tensor("stt_", [1, 16], F32))
        gw = E(nc.sbuf_tensor("gw", [128, 8], F32))
        gb = E(nc.sbuf_tensor("gb", [128, 8], F32))
        row = E(nc.sbuf_tensor("row", [1, 256], F32))
        colb = E(nc.sbuf_tensor("colb", [128, 2], F32))
        alf = E(nc.sbuf_tensor("alf", [128, 8], F32))
        bet = E(nc.sbuf_tensor("bet", [128, 8], F32))
        tmp = E(nc.sbuf_tensor("tmp", [128, 512], F32))
        on1 = E(nc.sbuf_tensor("on1", [1, 256], F32))
        xob = E(nc.sbuf_tensor("xob", [128, 8 * TL], F32))

        tx = s.dma("sync", lambda h: h.dma_start(out=xt[:], in_=x_fm[:, :]))
        tr = s.dma("sync", lambda h: h.dma_start(out=rt[:], in_=r_fm[:, :]))
        tst = s.dma("sync", lambda h: h.dma_start(out=stt_[0:1, 0:8], in_=st2[0:1, :]))
        tgw = s.dma("sync", lambda h: h.dma_start(out=gw[:], in_=gnw_col[:, :]))
        tgb = s.dma("sync", lambda h: h.dma_start(out=gb[:], in_=gnb_col[:, :]))
        to1 = s.emit("gpsimd", lambda h: h.memset(on1[:], 1.0))
        t1 = s.emit("vector", lambda h: h.tensor_tensor(stt_[0:1, 8:9], stt_[0:1, 0:1], stt_[0:1, 4:5], ALU.add), waits=[tst])
        t2 = s.emit("vector", lambda h: h.tensor_tensor(stt_[0:1, 9:10], stt_[0:1, 1:2], stt_[0:1, 5:6], ALU.add), waits=[t1])
        tmn = s.emit("vector", lambda h: h.tensor_scalar(stt_[0:1, 10:11], stt_[0:1, 8:9], 1.0 / NTOT, None, ALU.mult), waits=[t2])
        tm2 = s.emit("vector", lambda h: h.tensor_scalar(stt_[0:1, 11:12], stt_[0:1, 10:11], stt_[0:1, 10:11], None, ALU.mult), waits=[tmn])
        tvr = s.emit("vector", lambda h: h.scalar_tensor_tensor(
            stt_[0:1, 12:13], stt_[0:1, 9:10], 1.0 / NTOT, stt_[0:1, 11:12], ALU.mult, ALU.subtract), waits=[tm2])
        tve = s.emit("vector", lambda h: h.tensor_scalar(stt_[0:1, 13:14], stt_[0:1, 12:13], 1e-5, None, ALU.add), waits=[tvr])
        tsd = s.emit("scalar", lambda h: h.sqrt(stt_[0:1, 14:15], stt_[0:1, 13:14]), waits=[tve])
        trc = s.emit("vector", lambda h: h.reciprocal(stt_[0:1, 15:16], stt_[0:1, 14:15]), waits=[tsd])
        tr1 = s.emit("vector", lambda h: h.tensor_scalar(row[:, 0:128], on1[:, 0:128], stt_[0:1, 15:16], None, ALU.mult), waits=[trc, to1])
        tr2 = s.emit("vector", lambda h: h.tensor_scalar(row[:, 128:256], on1[:, 0:128], stt_[0:1, 10:11], -1.0, ALU.mult, ALU.mult), waits=[tr1])
        td1 = s.dma("sync", lambda h: h.dma_start(out=bscr[0:1, :], in_=row[:, :]), waits=[tr2])
        td2a = s.dma("sync", lambda h: h.dma_start(
            out=colb[:, 0:1], in_=bscr[0:1, 0:128].rearrange("a (p c) -> (a p) c", c=1)), waits=[td1])
        td2 = s.dma("sync", lambda h: h.dma_start(
            out=colb[:, 1:2], in_=bscr[0:1, 128:256].rearrange("a (p c) -> (a p) c", c=1)), waits=[td1])
        ta = s.emit("vector", lambda h: h.tensor_scalar(alf[:], gw[:], colb[:, 0:1], None, ALU.mult), waits=[td2a, td2, tgw])
        tb = s.emit("vector", lambda h: h.scalar_tensor_tensor(
            bet[:], alf[:], colb[:, 1:2], gb[:], ALU.mult, ALU.add), waits=[ta, tgb])
        prev = None
        for oc in range(8):
            for tc in range(TL // 512):
                sl = slice(oc * TL + tc * 512, oc * TL + (tc + 1) * 512)
                tt1 = s.emit("vector", lambda h, sl=sl, oc=oc: h.tensor_scalar(
                    tmp[:], rt[:, sl], alf[:, oc:oc + 1], bet[:, oc:oc + 1], ALU.mult, ALU.add),
                    waits=[tb, tr] + ([prev] if prev else []))
                tt2 = s.emit("vector", lambda h, sl=sl: h.tensor_tensor(
                    xob[:, sl], tmp[:], xt[:, sl], ALU.add), waits=[tt1, tx])
                prev = tt2
        for j2 in range(8):
            s.dma("sync", lambda h, j2=j2: h.dma_start(
                out=xo[:, j2 * TL:(j2 + 1) * TL], in_=xob[:, j2 * TL:(j2 + 1) * TL]), waits=[prev])
        s.build(nc, ctx)
    return nc


def kernel(**inputs):
    ins = {k: np.asarray(v) for k, v in inputs.items()}
    x = np.ascontiguousarray(ins["L0_attn_a_x"], dtype=np.float32)
    x = run_deltanet_sublayer(x, ins)
    x = run_moe_sublayer(x, ins, "L0_ffn")
    x = run_rope_attn_sublayer(x, ins)
    x = run_moe_sublayer(x, ins, "L0_ffn")
    x = run_alibi_attn_sublayer(x, ins)
    x = run_ffn1_sublayer(x, ins)
    x = run_retnet_sublayer(x, ins)
    x = run_ffn1_sublayer(x, ins)
    return x


# version 04
# speedup vs baseline: 1.0531x; 1.0531x over previous
import numpy as np
import ml_dtypes
from contextlib import ExitStack

import concourse.bass as bass
import concourse.mybir as mybir
from concourse.bass_utils import run_bass_kernel_spmd

F32 = mybir.dt.float32
BF16 = mybir.dt.bfloat16
I32 = mybir.dt.int32
AF = mybir.ActivationFunctionType
ALU = mybir.AluOpType
AX = mybir.AxisListType.X
NPBF16 = ml_dtypes.bfloat16

D = 1024
T = 4096
TL = 2048
HL = 8
HD = 64
EPS_LN = 1e-5

ENGINES = ("tensor", "vector", "scalar", "sync", "gpsimd")

class Sched:
    def __init__(self):
        self.q = {e: [] for e in ENGINES}
        self.cnt = {e: 0 for e in ENGINES}

    def emit(self, eng, fn, waits=(), inc=1):
        self.cnt[eng] += inc
        tok = (eng, self.cnt[eng])
        self.q[eng].append((fn, tuple(waits), inc))
        return tok

    def dma(self, eng, fn, waits=()):
        return self.emit(eng, fn, waits, inc=16)

    def build(self, nc, ctx):
        sems = {e: ctx.enter_context(nc.semaphore(f"s_{e}")) for e in ENGINES}
        block = ctx.enter_context(nc.Block())
        sched = self

        def replay(eng_name, h):
            hw = {e: 0 for e in ENGINES}
            cnt_self = 0
            for fn, waits, inc in sched.q[eng_name]:
                if inc == 16 and cnt_self > hw[eng_name]:
                    h.wait_ge(sems[eng_name], cnt_self)
                    hw[eng_name] = cnt_self
                for (we, wv) in waits:
                    if wv > hw[we]:
                        h.wait_ge(sems[we], wv)
                        hw[we] = wv
                cnt_self += inc
                inst = fn(h)
                if inst is not None:
                    inst.then_inc(sems[eng_name], inc)
                else:
                    h.nop().then_inc(sems[eng_name], inc)

        @block.tensor
        def _(h):
            replay("tensor", h)

        @block.vector
        def _(h):
            replay("vector", h)

        @block.scalar
        def _(h):
            replay("scalar", h)
            for e in ENGINES:
                if sched.cnt[e] > 0 and e != "scalar":
                    h.wait_ge(sems[e], sched.cnt[e])

        @block.sync
        def _(h):
            replay("sync", h)

        @block.gpsimd
        def _(h):
            replay("gpsimd", h)



def emit_ln(s, sb, x_dram, xn_dram, ntiles, pre_waits=()):
    xt, sq, xnb, st = sb["ln_xt"], sb["ln_sq"], sb["ln_xn"], sb["ln_st"]
    out_toks = []
    last = [None, None]
    for i in range(ntiles):
        b = i % 2
        w = list(pre_waits) + ([last[b]] if last[b] else [])
        tld = s.dma("sync", lambda h, i=i, b=b: h.dma_start(
            out=xt[b][:], in_=x_dram[i * 128:(i + 1) * 128, :]), waits=w)
        tsq = s.emit("scalar", lambda h, b=b: h.square(sq[:], xt[b][:]), waits=[tld])
        t1 = s.emit("vector", lambda h, b=b: h.reduce_sum(st[:, 0:1], xt[b][:], AX), waits=[tld])
        t2 = s.emit("vector", lambda h: h.reduce_sum(st[:, 1:2], sq[:], AX), waits=[tsq, t1])
        tm = s.emit("vector", lambda h: h.tensor_scalar(st[:, 2:3], st[:, 0:1], 1.0 / D, None, ALU.mult), waits=[t2])
        tm2 = s.emit("vector", lambda h: h.tensor_scalar(st[:, 3:4], st[:, 2:3], st[:, 2:3], None, ALU.mult), waits=[tm])
        tv = s.emit("vector", lambda h: h.scalar_tensor_tensor(
            st[:, 4:5], st[:, 1:2], 1.0 / D, st[:, 3:4], ALU.mult, ALU.subtract), waits=[tm2])
        tve = s.emit("vector", lambda h: h.tensor_scalar(st[:, 5:6], st[:, 4:5], EPS_LN, None, ALU.add), waits=[tv])
        tsd = s.emit("scalar", lambda h: h.sqrt(st[:, 6:7], st[:, 5:6]), waits=[tve])
        trs = s.emit("vector", lambda h: h.reciprocal(st[:, 7:8], st[:, 6:7]), waits=[tsd])
        txn = s.emit("vector", lambda h, b=b: h.tensor_scalar(
            xnb[:], xt[b][:], st[:, 2:3], st[:, 7:8], ALU.subtract, ALU.mult),
            waits=[trs] + ([sb["_ln_prev_store"]] if "_ln_prev_store" in sb else []))
        tst = s.dma("sync", lambda h, i=i: h.dma_start(
            out=xn_dram[i * 128:(i + 1) * 128, :], in_=xnb[:]), waits=[txn])
        last[b] = txn
        sb["_ln_prev_store"] = tst
        out_toks.append(tst)
    return out_toks


def emit_transpose_in(s, sb_fm, xn_dram, Tp, waits):
    toks = []
    for j in range(D // 128):
        t = s.dma("sync", lambda h, j=j: h.dma_start_transpose(
            sb_fm[:, j * Tp:(j + 1) * Tp], xn_dram[:, j * 128:(j + 1) * 128]), waits=waits)
        toks.append(t)
    return toks


def emit_load_w_bf16(s, dst, src_dram, kchunks, ncols, waits=()):
    toks = []
    for kc in range(kchunks):
        t = s.dma("gpsimd", lambda h, kc=kc: h.dma_start(
            out=dst[:, kc * ncols:(kc + 1) * ncols],
            in_=src_dram[kc * 128:(kc + 1) * 128, :]), waits=waits)
        toks.append(t)
    return toks



def build_dn_a(debug=False):
    nc = bass.Bass("TRN2", target_bir_lowering=False)
    xb = nc.dram_tensor("xb", [T, D], F32, kind="ExternalInput").ap()
    wq = nc.dram_tensor("wq", [D, 512], F32, kind="ExternalInput").ap()
    wk = nc.dram_tensor("wk", [D, 512], F32, kind="ExternalInput").ap()
    wv = nc.dram_tensor("wv", [D, 512], F32, kind="ExternalInput").ap()
    wbe = nc.dram_tensor("wbe", [D, 8], F32, kind="ExternalInput").ap()
    bq_col = nc.dram_tensor("bq_col", [128, 4], F32, kind="ExternalInput").ap()
    bk_row = nc.dram_tensor("bk_row", [1, 512], F32, kind="ExternalInput").ap()
    bv_row = nc.dram_tensor("bv_row", [1, 512], F32, kind="ExternalInput").ap()
    bbe_row = nc.dram_tensor("bbe_row", [1, 8], F32, kind="ExternalInput").ap()
    ident = nc.dram_tensor("ident", [128, 128], BF16, kind="ExternalInput").ap()
    stril = nc.dram_tensor("stril", [128, 128], BF16, kind="ExternalInput").ap()
    triui = nc.dram_tensor("triui", [128, 128], BF16, kind="ExternalInput").ap()
    xn_scr = nc.dram_tensor("xn_scr", [T, D], BF16, kind="Internal").ap()
    o_out = nc.dram_tensor("o_out", [128, 4 * T], BF16, kind="ExternalOutput").ap()
    if debug:
        dbg_q = nc.dram_tensor("dbg_q", [128, 4 * T], BF16, kind="ExternalOutput").ap()
        dbg_ktm = nc.dram_tensor("dbg_ktm", [128, 32 * 512], BF16, kind="ExternalOutput").ap()
        dbg_vtm = nc.dram_tensor("dbg_vtm", [128, 32 * 512], BF16, kind="ExternalOutput").ap()
        dbg_beta = nc.dram_tensor("dbg_beta", [128, 32 * 8], F32, kind="ExternalOutput").ap()
        dbg_kfm = nc.dram_tensor("dbg_kfm", [128, 4 * T], BF16, kind="ExternalOutput").ap()
        dbg_xn = nc.dram_tensor("dbg_xn", [128, D], BF16, kind="ExternalOutput").ap()

    s = Sched()
    with ExitStack() as ctx:
        E = ctx.enter_context
        sb = {}
        identt = E(nc.sbuf_tensor("identt", [128, 128], BF16))
        strilt = E(nc.sbuf_tensor("strilt", [128, 128], BF16))
        triuit = E(nc.sbuf_tensor("triuit", [128, 128], BF16))
        cpt = E(nc.sbuf_tensor("cpt", [128, 8 * 144], F32))
        ones1 = E(nc.sbuf_tensor("ones1", [1, 128], BF16))
        bqc = E(nc.sbuf_tensor("bqc", [128, 4], F32))
        bkr = E(nc.sbuf_tensor("bkr", [1, 512], BF16))
        bvr = E(nc.sbuf_tensor("bvr", [1, 512], BF16))
        bber = E(nc.sbuf_tensor("bber", [1, 8], BF16))
        q_fm = E(nc.sbuf_tensor("q_fm", [128, 4 * T], BF16))
        k_tm = E(nc.sbuf_tensor("k_tm", [128, 32 * 512], BF16))
        v_tm = E(nc.sbuf_tensor("v_tm", [128, 32 * 512], BF16))
        beta = E(nc.sbuf_tensor("beta", [128, 32 * 8], F32))
        nbeta = E(nc.sbuf_tensor("nbeta", [128, 32 * 8], F32))

        t_c = []
        t_c.append(s.dma("sync", lambda h: h.dma_start(out=identt[:], in_=ident[:, :])))
        t_c.append(s.dma("sync", lambda h: h.dma_start(out=strilt[:], in_=stril[:, :])))
        t_c.append(s.dma("sync", lambda h: h.dma_start(out=triuit[:], in_=triui[:, :])))
        t_c.append(s.emit("gpsimd", lambda h: h.memset(ones1[:], 1.0)))
        t_c.append(s.dma("sync", lambda h: h.dma_start(out=bqc[:], in_=bq_col[:, :])))
        t_c.append(s.dma("gpsimd", lambda h: h.dma_start(out=bkr[:], in_=bk_row[:, :])))
        t_c.append(s.dma("gpsimd", lambda h: h.dma_start(out=bvr[:], in_=bv_row[:, :])))
        t_c.append(s.dma("gpsimd", lambda h: h.dma_start(out=bber[:], in_=bbe_row[:, :])))

        with ExitStack() as ctx2:
            E2 = ctx2.enter_context
            xn_fm = E2(nc.sbuf_tensor("xn_fm", [128, 8 * T], BF16))
            sb["ln_xt"] = [E2(nc.sbuf_tensor(f"ln_xt{i}", [128, D], F32)) for i in range(2)]
            sb["ln_sq"] = E2(nc.sbuf_tensor("ln_sq", [128, D], F32))
            sb["ln_xn"] = E2(nc.sbuf_tensor("ln_xnb", [128, D], BF16))
            sb["ln_st"] = E2(nc.sbuf_tensor("ln_st", [128, 8], F32))
            wqt = E2(nc.sbuf_tensor("wqt", [128, 8 * 512], BF16))
            wkt = E2(nc.sbuf_tensor("wkt", [128, 8 * 512], BF16))
            wvt = E2(nc.sbuf_tensor("wvt", [128, 8 * 512], BF16))
            wbet = E2(nc.sbuf_tensor("wbet", [128, 8 * 8], BF16))
            ksq = sb["ln_sq"]
            kst = sb["ln_xt"][0]

            def krw(kb):
                return (sb["ln_xt"][0] if kb == 0 else sb["ln_sq"])[:, 512:1024]
            ps_a = E2(nc.psum_tensor("ps_a", [128, 512], F32))
            ps_b = E2(nc.psum_tensor("ps_b", [128, 512], F32))
            ps_c = E2(nc.psum_tensor("ps_c", [128, 512], F32))
            ps_s = E2(nc.psum_tensor("ps_s", [128, 8], F32))

            ln_toks = emit_ln(s, sb, xb, xn_scr, T // 128)
            tr_toks = emit_transpose_in(s, xn_fm, xn_scr, T, ln_toks)
            wq_t = emit_load_w_bf16(s, wqt, wq, 8, 512)
            wk_t = emit_load_w_bf16(s, wkt, wk, 8, 512)
            wv_t = emit_load_w_bf16(s, wvt, wv, 8, 512)
            wbe_t = emit_load_w_bf16(s, wbet, wbe, 8, 8)

            ev_prev = {}
            for oc in range(4):
                for tc in range(T // 512):
                    ps = [ps_a, ps_b][tc % 2]
                    w = tr_toks + wq_t
                    if (id(ps),) in ev_prev:
                        w = w + [ev_prev[(id(ps),)]]
                    tok = None
                    for kc in range(8):
                        tok = s.emit("tensor", lambda h, kc=kc, oc=oc, tc=tc, ps=ps: h.matmul(
                            ps[:], wqt[:, kc * 512 + oc * 128: kc * 512 + (oc + 1) * 128],
                            xn_fm[:, kc * T + tc * 512: kc * T + (tc + 1) * 512],
                            start=(kc == 0), stop=(kc == 7)),
                            waits=(w if kc == 0 else []))
                    tev = s.emit("scalar", lambda h, oc=oc, tc=tc, ps=ps: h.activation(
                        q_fm[:, oc * T + tc * 512: oc * T + (tc + 1) * 512], ps[:],
                        AF.Identity, bias=bqc[:, oc:oc + 1], scale=1.0), waits=[tok])
                    ev_prev[(id(ps),)] = tev

            for tt in range(T // 128):
                for which in ("k", "v", "be"):
                    wt = {"k": wkt, "v": wvt, "be": wbet}[which]
                    br = {"k": bkr, "v": bvr, "be": bber}[which]
                    ncols = 8 if which == "be" else 512
                    ps = ps_s if which == "be" else [ps_a, ps_b][tt % 2]
                    w = tr_toks + {"k": wk_t, "v": wv_t, "be": wbe_t}[which] + t_c
                    key = (id(ps),)
                    if key in ev_prev:
                        w = w + [ev_prev[key]]
                    tok = None
                    for kc in range(8):
                        tok = s.emit("tensor", lambda h, kc=kc, tt=tt, wt=wt, ncols=ncols, ps=ps: h.matmul(
                            ps[:, 0:ncols],
                            xn_fm[:, kc * T + tt * 128: kc * T + (tt + 1) * 128],
                            wt[:, kc * ncols:(kc + 1) * ncols],
                            start=(kc == 0), stop=False), waits=(w if kc == 0 else []))
                    tok = s.emit("tensor", lambda h, br=br, ncols=ncols, ps=ps: h.matmul(
                        ps[:, 0:ncols], ones1[:], br[:, 0:ncols], start=False, stop=True))
                    if which == "v":
                        tev = s.emit("vector", lambda h, tt=tt, ps=ps: h.tensor_copy(
                            v_tm[:, tt * 512:(tt + 1) * 512], ps[:]), waits=[tok])
                        ev_prev[(id(ps),)] = tev
                    elif which == "be":
                        tsg = s.emit("scalar", lambda h, tt=tt: h.activation(
                            beta[:, tt * 8:(tt + 1) * 8], ps_s[:], AF.Sigmoid,
                            bias=0.0, scale=1.0), waits=[tok])
                        tnb = s.emit("vector", lambda h, tt=tt: h.tensor_scalar(
                            nbeta[:, tt * 8:(tt + 1) * 8], beta[:, tt * 8:(tt + 1) * 8],
                            -1.0, None, ALU.mult), waits=[tsg])
                        ev_prev[(id(ps_s),)] = tnb
                    else:
                        kb = tt % 2
                        tkr = s.emit("scalar", lambda h, kb=kb, ps=ps: h.copy(
                            krw(kb), ps[:]),
                            waits=[tok] + ([ev_prev[("kraw", kb)]] if ("kraw", kb) in ev_prev else []))
                        ev_prev[(id(ps),)] = tkr
                        tq = s.emit("scalar", lambda h, kb=kb: h.square(ksq[:, 0:512], krw(kb)), waits=[tkr])
                        trd = s.emit("vector", lambda h: h.tensor_reduce(
                            kst[:, 0:8], ksq[:, 0:512].rearrange("p (a m) -> p a m", a=8), AX, ALU.add), waits=[tq])
                        tsd = s.emit("scalar", lambda h: h.sqrt(kst[:, 8:16], kst[:, 0:8]), waits=[trd])
                        tep = s.emit("vector", lambda h: h.tensor_scalar(
                            kst[:, 8:16], kst[:, 8:16], 1e-6, None, ALU.add), waits=[tsd])
                        trc = s.emit("vector", lambda h: h.reciprocal(kst[:, 16:24], kst[:, 8:16]), waits=[tep])
                        tkn = s.emit("vector", lambda h, tt=tt, kb=kb: h.tensor_tensor(
                            k_tm[:, tt * 512:(tt + 1) * 512].rearrange("p (a m) -> p a m", a=8),
                            krw(kb).rearrange("p (a m) -> p a m", a=8),
                            kst[:, 16:24].unsqueeze(2).broadcast_to([128, 8, 64]),
                            ALU.mult), waits=[trc])
                        ev_prev[("kraw", kb)] = tkn

        k_fm = E(nc.sbuf_tensor("k_fm", [128, 4 * T], BF16))
        o_fm = E(nc.sbuf_tensor("o_fm", [128, 4 * T], BF16))
        Nt = E(nc.sbuf_tensor("Nt", [128, 128], BF16))
        NTt = E(nc.sbuf_tensor("NTt", [128, 128], BF16))
        P2 = E(nc.sbuf_tensor("P2", [128, 128], BF16))
        PT2 = E(nc.sbuf_tensor("PT2", [128, 128], BF16))
        P4 = E(nc.sbuf_tensor("P4", [128, 128], BF16))
        PT4 = E(nc.sbuf_tensor("PT4", [128, 128], BF16))
        P8 = E(nc.sbuf_tensor("P8", [128, 128], BF16))
        PT8 = E(nc.sbuf_tensor("PT8", [128, 128], BF16))
        PT16 = E(nc.sbuf_tensor("PT16", [128, 128], BF16))
        GTm = E(nc.sbuf_tensor("GTm", [128, 128], BF16))
        Rt = E(nc.sbuf_tensor("Rt", [128, 64], BF16))
        Vb = E(nc.sbuf_tensor("Vb", [128, 64], BF16))
        Xa = E(nc.sbuf_tensor("Xa", [128, 64], BF16))
        Xb2 = E(nc.sbuf_tensor("Xb2", [128, 64], BF16))
        Ut = E(nc.sbuf_tensor("Ut", [128, 64], BF16))
        Sf = [E(nc.sbuf_tensor(f"Sf{i}", [128, 4 * 64], F32)) for i in range(2)]
        Sb = E(nc.sbuf_tensor("Sb", [128, 4 * 64], BF16))
        pm1 = E(nc.psum_tensor("pm1", [128, 128], F32))
        pmt = E(nc.psum_tensor("pmt", [128, 128], BF16))
        pm2 = E(nc.psum_tensor("pm2", [128, 128], F32))
        pu = E(nc.psum_tensor("pu", [128, 64], F32))
        po = E(nc.psum_tensor("po", [64, 128], F32))
        pst = E(nc.psum_tensor("pst", [128, 64], F32))

        bar0 = [(e, s.cnt[e]) for e in ENGINES]
        tz0 = s.emit("gpsimd", lambda h: h.memset(Sf[0][:], 0.0), waits=bar0)
        tz1 = s.emit("gpsimd", lambda h: h.memset(Sf[1][:], 0.0), waits=bar0)
        tz2 = s.emit("gpsimd", lambda h: h.memset(Sb[:], 0.0), waits=bar0)

        last_pmt = None
        for tt in range(T // 128):
            for cc in range(4):
                wl = list(bar0) + ([last_pmt] if last_pmt else [])
                ttr = s.emit("tensor", lambda h, tt=tt, cc=cc: h.transpose(
                    pmt[:], k_tm[:, tt * 512 + cc * 128: tt * 512 + (cc + 1) * 128], identt[:]),
                    waits=wl)
                tev = s.emit("vector", lambda h, tt=tt, cc=cc: h.tensor_copy(
                    k_fm[:, cc * T + tt * 128: cc * T + (tt + 1) * 128], pmt[:]), waits=[ttr])
                last_pmt = tev

        def kfm_h(h_, c):
            tile = h_ // 2
            r0 = (h_ % 2) * 64
            return k_fm[r0:r0 + 64, tile * T + c * 128: tile * T + (c + 1) * 128]

        def qfm_h(h_, c):
            tile = h_ // 2
            r0 = (h_ % 2) * 64
            return q_fm[r0:r0 + 64, tile * T + c * 128: tile * T + (c + 1) * 128]

        def ktm_h(h_, c):
            return k_tm[:, c * 512 + h_ * 64: c * 512 + (h_ + 1) * 64]

        def vtm_h(h_, c):
            return v_tm[:, c * 512 + h_ * 64: c * 512 + (h_ + 1) * 64]

        def ofm_h(h_, c):
            tile = h_ // 2
            r0 = (h_ % 2) * 64
            return o_fm[r0:r0 + 64, tile * T + c * 128: tile * T + (c + 1) * 128]

        def beta_ap(h_, c):
            return beta[:, c * 8 + h_: c * 8 + h_ + 1]

        def nbeta_ap(h_, c):
            return nbeta[:, c * 8 + h_: c * 8 + h_ + 1]

        last_o = None
        for c in range(T // 128):
            par = c % 2
            for h_ in range(HL):
                ibar = [("vector", s.cnt["vector"]), ("scalar", s.cnt["scalar"]),
                        ("gpsimd", s.cnt["gpsimd"])]
                r0 = (h_ % 2) * 64
                cs = slice((h_ // 2) * 64, (h_ // 2) * 64 + 64)
                Sbf_h = Sb[r0:r0 + 64, cs]
                Sf_old = Sf[par][r0:r0 + 64, cs]
                Sf_new = Sf[1 - par][r0:r0 + 64, cs]
                t1 = s.emit("tensor", lambda h, h_=h_, c=c: h.matmul(
                    pm1[:], kfm_h(h_, c), kfm_h(h_, c), start=True, stop=True), waits=ibar)
                tN = s.emit("vector", lambda h, h_=h_, c=c: h.scalar_tensor_tensor(
                    Nt[:], pm1[:], beta_ap(h_, c), strilt[:], ALU.mult, ALU.mult), waits=[t1])
                tTr = s.emit("tensor", lambda h: h.transpose(pmt[:], Nt[:], identt[:]), waits=[tN])
                tNT = s.emit("scalar", lambda h: h.copy(NTt[:], pmt[:]), waits=[tTr])
                tU = s.emit("tensor", lambda h, h_=h_, c=c, Sbf_h=Sbf_h: h.matmul(
                    pu[:], kfm_h(h_, c), Sbf_h, start=True, stop=True), waits=[tN])
                tVb = s.emit("vector", lambda h, h_=h_, c=c: h.tensor_scalar(
                    Vb[:], vtm_h(h_, c), beta_ap(h_, c), None, ALU.mult), waits=[tU])
                tR = s.emit("vector", lambda h, h_=h_, c=c: h.scalar_tensor_tensor(
                    Rt[:], pu[:], nbeta_ap(h_, c), Vb[:], ALU.mult, ALU.add), waits=[tU, tVb])

                def sqr(lhs, rhs, out, wl):
                    tm = s.emit("tensor", lambda h, lhs=lhs, rhs=rhs: h.matmul(
                        pm2[:], lhs[:], rhs[:], start=True, stop=True), waits=wl)
                    te = s.emit("vector", lambda h, out=out: h.tensor_copy(out[:], pm2[:]), waits=[tm])
                    return te
                e2 = sqr(NTt, Nt, P2, [tNT, tN])
                eT2 = sqr(Nt, NTt, PT2, [e2])
                e4 = sqr(PT2, P2, P4, [eT2])
                eT4 = sqr(P2, PT2, PT4, [e4])
                e8 = sqr(PT4, P4, P8, [eT4])
                eT8 = sqr(P4, PT4, PT8, [e8])
                eT16 = sqr(P8, PT8, PT16, [eT8])

                cur = Rt
                lastap = eT16
                for it in range(7):
                    nxt = Xa if cur is not Xa else Xb2
                    tm = s.emit("tensor", lambda h, cur=cur: h.matmul(
                        pu[:], PT16[:], cur[:], start=True, stop=True), waits=[lastap, tR])
                    lastap = s.emit("vector", lambda h, nxt=nxt: h.tensor_tensor(
                        nxt[:], pu[:], Rt[:], ALU.add), waits=[tm])
                    cur = nxt
                for mat in (PT8, PT4, PT2):
                    nxt = Xa if cur is not Xa else Xb2
                    tm = s.emit("tensor", lambda h, mat=mat, cur=cur: h.matmul(
                        pu[:], mat[:], cur[:], start=True, stop=True), waits=[lastap])
                    lastap = s.emit("vector", lambda h, nxt=nxt, cur=cur: h.tensor_tensor(
                        nxt[:], pu[:], cur[:], ALU.add), waits=[tm])
                    cur = nxt
                tm = s.emit("tensor", lambda h, cur=cur: h.matmul(
                    pu[:], NTt[:], cur[:], start=True, stop=True), waits=[lastap])
                tUf = s.emit("vector", lambda h, cur=cur: h.tensor_tensor(
                    Ut[:], cur[:], pu[:], ALU.subtract), waits=[tm])

                tG = s.emit("tensor", lambda h, h_=h_, c=c: h.matmul(
                    pm1[:], kfm_h(h_, c), qfm_h(h_, c), start=True, stop=True), waits=[tN])
                tGm = s.emit("vector", lambda h: h.tensor_tensor(
                    GTm[:], pm1[:], triuit[:], ALU.mult), waits=[tG])

                tO1 = s.emit("tensor", lambda h, h_=h_, c=c, Sbf_h=Sbf_h: h.matmul(
                    po[:], Sbf_h, qfm_h(h_, c), start=True, stop=False), waits=[tGm])
                tO2 = s.emit("tensor", lambda h: h.matmul(
                    po[:], Ut[:], GTm[:], start=False, stop=True), waits=[tUf, tGm])
                tOe = s.emit("scalar", lambda h, h_=h_, c=c: h.copy(ofm_h(h_, c), po[:]), waits=[tO2])

                tS = s.emit("tensor", lambda h, h_=h_, c=c, r0=r0: h.matmul(
                    pst[r0:r0 + 64, :], ktm_h(h_, c), Ut[:], start=True, stop=True,
                    tile_position=(0, r0)), waits=[tUf])
                tSa = s.emit("vector", lambda h, Sf_old=Sf_old, Sf_new=Sf_new, r0=r0: h.tensor_tensor(
                    Sf_new, pst[r0:r0 + 64, :], Sf_old, ALU.add), waits=[tS, tz0, tz1])
                tSb = s.emit("scalar", lambda h, Sf_new=Sf_new, Sbf_h=Sbf_h: h.copy(
                    Sbf_h, Sf_new), waits=[tSa, tz2, tOe])
                last_o = tOe

        if debug:
            dw = [(e, s.cnt[e]) for e in ENGINES]
            s.dma("sync", lambda h: h.dma_start(out=dbg_q[:, :], in_=q_fm[:]), waits=dw)
            s.dma("sync", lambda h: h.dma_start(out=dbg_ktm[:, :], in_=k_tm[:]), waits=dw)
            s.dma("sync", lambda h: h.dma_start(out=dbg_vtm[:, :], in_=v_tm[:]), waits=dw)
            s.dma("sync", lambda h: h.dma_start(out=dbg_beta[:, :], in_=beta[:]), waits=dw)
            s.dma("sync", lambda h: h.dma_start(out=dbg_kfm[:, :], in_=k_fm[:]), waits=dw)
            s.dma("sync", lambda h: h.dma_start(out=dbg_xn[:, :], in_=xn_scr[0:128, :]), waits=dw)
        ow = [last_o, ("scalar", s.cnt["scalar"])]
        for j in range(4):
            s.dma("sync", lambda h, j=j: h.dma_start(
                out=o_out[:, j * T:(j + 1) * T], in_=o_fm[:, j * T:(j + 1) * T]), waits=ow)

        s.build(nc, ctx)
    return nc



def build_proj_residual():
    nc = bass.Bass("TRN2", target_bir_lowering=False)
    x_fm = nc.dram_tensor("x_fm", [128, 8 * TL], F32, kind="ExternalInput").ap()
    o_in = nc.dram_tensor("o_in", [128, 8 * TL], BF16, kind="ExternalInput").ap()
    w = nc.dram_tensor("w", [D, D], F32, kind="ExternalInput").ap()
    b_col = nc.dram_tensor("b_col", [128, 8], F32, kind="ExternalInput").ap()
    xo = nc.dram_tensor("xo", [128, 8 * TL], F32, kind="ExternalOutput").ap()

    s = Sched()
    with ExitStack() as ctx:
        E = ctx.enter_context
        xt = E(nc.sbuf_tensor("xt", [128, 8 * TL], F32))
        ot = E(nc.sbuf_tensor("ot", [128, 8 * TL], BF16))
        wt = E(nc.sbuf_tensor("wt", [128, 8 * D], BF16))
        bt = E(nc.sbuf_tensor("bt", [128, 8], F32))
        xn = E(nc.sbuf_tensor("xn", [128, 8 * TL], F32))
        pss = [E(nc.psum_tensor(f"ps{i}", [128, 512], F32)) for i in range(2)]

        tx = s.dma("sync", lambda h: h.dma_start(out=xt[:], in_=x_fm[:, :]))
        to = s.dma("sync", lambda h: h.dma_start(out=ot[:], in_=o_in[:, :]))
        tb = s.dma("sync", lambda h: h.dma_start(out=bt[:], in_=b_col[:, :]))
        tw = emit_load_w_bf16(s, wt, w, 8, D)
        ev = {}
        for oc in range(8):
            for tc in range(TL // 512):
                ps = pss[tc % 2]
                wl = [tx, to, tb] + tw + ([ev[id(ps)]] if id(ps) in ev else [])
                tok = None
                for kc in range(8):
                    tok = s.emit("tensor", lambda h, kc=kc, oc=oc, tc=tc, ps=ps: h.matmul(
                        ps[:], wt[:, kc * D + oc * 128: kc * D + (oc + 1) * 128],
                        ot[:, kc * TL + tc * 512: kc * TL + (tc + 1) * 512],
                        start=(kc == 0), stop=(kc == 7)), waits=(wl if kc == 0 else []))
                te = s.emit("vector", lambda h, oc=oc, tc=tc, ps=ps: h.scalar_tensor_tensor(
                    xn[:, oc * TL + tc * 512: oc * TL + (tc + 1) * 512], ps[:],
                    bt[:, oc:oc + 1], xt[:, oc * TL + tc * 512: oc * TL + (tc + 1) * 512],
                    ALU.add, ALU.add), waits=[tok])
                ev[id(ps)] = te
        wall = list(ev.values())
        for j in range(8):
            s.dma("sync", lambda h, j=j: h.dma_start(
                out=xo[:, j * TL:(j + 1) * TL], in_=xn[:, j * TL:(j + 1) * TL]), waits=wall)
        s.build(nc, ctx)
    return nc



def fm(x):
    Tp = x.shape[0]
    return np.ascontiguousarray(x.T.reshape(D // 128, 128, Tp).transpose(1, 0, 2).reshape(128, -1))

def unfm(y, Tp):
    return np.ascontiguousarray(y.reshape(128, D // 128, Tp).transpose(1, 0, 2).reshape(D, Tp).T)

def col128(b):
    return np.ascontiguousarray(b.reshape(-1, 128).T)

IDENT = np.eye(128, dtype=NPBF16)
STRIL = np.tril(np.ones((128, 128), np.float32), -1).astype(NPBF16)
TRIUI = np.triu(np.ones((128, 128), np.float32), 0).astype(NPBF16)

_PROGS = {}

def prog(name, builder):
    if name not in _PROGS:
        _PROGS[name] = builder()
    return _PROGS[name]


def o_heads_to_fm(o_parts):
    return np.concatenate(o_parts, axis=1)


def run_deltanet_sublayer(x, ins):
    in_maps = []
    for c in range(8):
        b, p = c // 2, c % 2
        hs = p * HL * HD
        in_maps.append(dict(
            xb=np.ascontiguousarray(x[b]),
            wq=np.ascontiguousarray(ins["L0_attn_a_Wq"][:, hs:hs + 512]),
            wk=np.ascontiguousarray(ins["L0_attn_a_Wk"][:, hs:hs + 512]),
            wv=np.ascontiguousarray(ins["L0_attn_a_Wv"][:, hs:hs + 512]),
            wbe=np.ascontiguousarray(ins["L0_attn_a_Wbeta"][:, p * 8:p * 8 + 8]),
            bq_col=col128(ins["L0_attn_a_bq"][hs:hs + 512]),
            bk_row=ins["L0_attn_a_bk"][hs:hs + 512].reshape(1, 512),
            bv_row=ins["L0_attn_a_bv"][hs:hs + 512].reshape(1, 512),
            bbe_row=ins["L0_attn_a_bbeta"][p * 8:p * 8 + 8].reshape(1, 8),
            ident=IDENT, stril=STRIL, triui=TRIUI,
        ))
    res = run_bass_kernel_spmd(prog("dn_a", build_dn_a), in_maps, core_ids=list(range(8)))
    o_full = [o_heads_to_fm([res.results[2 * b]["o_out"], res.results[2 * b + 1]["o_out"]])
              for b in range(4)]
    in_maps = []
    for c in range(8):
        b, p = c // 2, c % 2
        in_maps.append(dict(
            x_fm=fm(x[b, p * TL:(p + 1) * TL]),
            o_in=np.ascontiguousarray(o_full[b][:, [j * T + p * TL + t for j in range(8) for t in range(0)]])
            if False else np.ascontiguousarray(
                o_full[b].reshape(128, 8, T)[:, :, p * TL:(p + 1) * TL].reshape(128, 8 * TL)),
            w=ins["L0_attn_a_Wo"],
            b_col=col128(ins["L0_attn_a_bo"]),
        ))
    res = run_bass_kernel_spmd(prog("dn_b", build_proj_residual), in_maps, core_ids=list(range(8)))
    xn = np.empty_like(x)
    for c in range(8):
        b, p = c // 2, c % 2
        xn[b, p * TL:(p + 1) * TL] = unfm(res.results[c]["xo"], TL)
    return xn





def build_moe():
    nc = bass.Bass("TRN2", target_bir_lowering=False)
    x_tm = nc.dram_tensor("x_tm", [TL, D], F32, kind="ExternalInput").ap()
    wrT = nc.dram_tensor("wrT", [D, 16], F32, kind="ExternalInput").ap()
    wmhT = nc.dram_tensor("wmhT", [D, D], F32, kind="ExternalInput").ap()
    wmgT = nc.dram_tensor("wmgT", [D, D], F32, kind="ExternalInput").ap()
    kf = nc.dram_tensor("kf", [8, 128, 4096], F32, kind="ExternalInput").ap()
    vf = nc.dram_tensor("vf", [8, 4096, 128], F32, kind="ExternalInput").ap()
    oh = nc.dram_tensor("oh", [16, 16 * 128], BF16, kind="ExternalInput").ap()
    ident = nc.dram_tensor("ident", [128, 128], BF16, kind="ExternalInput").ap()
    xn_scr = nc.dram_tensor("xn_scr", [TL, D], BF16, kind="Internal").ap()
    xo = nc.dram_tensor("xo", [TL, D], F32, kind="ExternalOutput").ap()

    NT_ = TL // 128
    s = Sched()
    with ExitStack() as ctx:
        E = ctx.enter_context
        sb = {}
        oht = E(nc.sbuf_tensor("oht", [16, 16 * 128], BF16))
        identt = E(nc.sbuf_tensor("identt", [128, 128], BF16))
        gateT = E(nc.sbuf_tensor("gateT", [16, TL], BF16))
        gA = E(nc.sbuf_tensor("gA", [128, 16], F32))
        gB = E(nc.sbuf_tensor("gB", [128, 16], F32))
        gC = E(nc.sbuf_tensor("gC", [128, 4], F32))
        gD = E(nc.sbuf_tensor("gD", [128, 16], F32))
        gbf = E(nc.sbuf_tensor("gbf", [128, 16], BF16))
        q_fm = E(nc.sbuf_tensor("q_fm", [128, 8 * TL], BF16))
        ps_lg = E(nc.psum_tensor("ps_lg", [128, 16], F32))
        ps_tr = E(nc.psum_tensor("ps_tr", [16, 128], BF16))
        ps_sc = [E(nc.psum_tensor(f"ps_sc{i}", [128, 512], F32)) for i in range(2)]
        ps_y = E(nc.psum_tensor("ps_y", [128, 512], F32))
        ps_w = E(nc.psum_tensor("ps_w", [128, 512], F32))

        t_oh = s.dma("sync", lambda h: h.dma_start(out=oht[:], in_=oh[:, :]))
        t_id = s.dma("sync", lambda h: h.dma_start(out=identt[:], in_=ident[:, :]))

        with ExitStack() as ctx2:
            E2 = ctx2.enter_context
            xn_fm = E2(nc.sbuf_tensor("xn_fm", [128, 8 * TL], BF16))
            sb["ln_xt"] = [E2(nc.sbuf_tensor(f"ln_xt{i}", [128, D], F32)) for i in range(2)]
            sb["ln_sq"] = E2(nc.sbuf_tensor("ln_sq", [128, D], F32))
            sb["ln_xn"] = E2(nc.sbuf_tensor("ln_xnb", [128, D], BF16))
            sb["ln_st"] = E2(nc.sbuf_tensor("ln_st", [128, 8], F32))
            wrTt = E2(nc.sbuf_tensor("wrTt", [128, 8 * 16], BF16))
            wmt = E2(nc.sbuf_tensor("wmt", [128, 8 * D], BF16))

            ln_toks = emit_ln(s, sb, x_tm, xn_scr, NT_)
            tr_toks = emit_transpose_in(s, xn_fm, xn_scr, TL, ln_toks)
            wr_t = emit_load_w_bf16(s, wrTt, wrT, 8, 16)
            wm_t = emit_load_w_bf16(s, wmt, wmhT, 8, D)

            prev_lg = None
            prev_lg_s = None
            prev_tr = None
            for i in range(NT_):
                wl = tr_toks + wr_t + ([prev_lg, prev_lg_s] if prev_lg else [])
                tok = None
                for kc in range(8):
                    tok = s.emit("tensor", lambda h, kc=kc, i=i: h.matmul(
                        ps_lg[:], xn_fm[:, kc * TL + i * 128: kc * TL + (i + 1) * 128],
                        wrTt[:, kc * 16:(kc + 1) * 16],
                        start=(kc == 0), stop=(kc == 7)), waits=(wl if kc == 0 else []))
                tm1 = s.emit("vector", lambda h: h.reduce_max(gC[:, 0:1], ps_lg[:], AX), waits=[tok])
                teq = s.emit("vector", lambda h: h.tensor_scalar(gA[:], ps_lg[:], gC[:, 0:1], None, ALU.is_equal), waits=[tm1])
                tng = s.emit("vector", lambda h: h.scalar_tensor_tensor(
                    gB[:], gA[:], -1e30, ps_lg[:], ALU.mult, ALU.add), waits=[teq])
                tm2 = s.emit("vector", lambda h: h.reduce_max(gC[:, 1:2], gB[:], AX), waits=[tng])
                tse = s.emit("vector", lambda h: h.tensor_scalar(gA[:], ps_lg[:], gC[:, 1:2], None, ALU.is_ge), waits=[tm2])
                tsg = s.emit("scalar", lambda h: h.activation(gB[:], ps_lg[:], AF.Sigmoid, bias=0.0, scale=1.0), waits=[tse])
                tgt = s.emit("vector", lambda h: h.tensor_tensor(gD[:], gA[:], gB[:], ALU.mult), waits=[tsg])
                prev_lg = tgt
                prev_lg_s = tsg
                tgb = s.emit("vector", lambda h: h.tensor_copy(gbf[:], gD[:]), waits=[tgt])
                ttr = s.emit("tensor", lambda h: h.transpose(ps_tr[:], gbf[:], identt[:]),
                             waits=[tgb, t_id] + ([prev_tr] if prev_tr else []))
                tte = s.emit("vector", lambda h, i=i: h.tensor_copy(
                    gateT[:, i * 128:(i + 1) * 128], ps_tr[:]), waits=[ttr])
                prev_tr = tte

            prev_ps = {}
            for oc in range(8):
                for tc in range(TL // 512):
                    ps = ps_sc[tc % 2]
                    wl = tr_toks + wm_t + ([prev_ps[id(ps)]] if id(ps) in prev_ps else [])
                    tok = None
                    for kc in range(8):
                        tok = s.emit("tensor", lambda h, kc=kc, oc=oc, tc=tc, ps=ps: h.matmul(
                            ps[:], wmt[:, kc * D + oc * 128: kc * D + (oc + 1) * 128],
                            xn_fm[:, kc * TL + tc * 512: kc * TL + (tc + 1) * 512],
                            start=(kc == 0), stop=(kc == 7)), waits=(wl if kc == 0 else []))
                    te = s.emit("scalar", lambda h, oc=oc, tc=tc, ps=ps: h.copy(
                        q_fm[:, oc * TL + tc * 512: oc * TL + (tc + 1) * 512], ps[:]), waits=[tok])
                    prev_ps[id(ps)] = te

        bar0 = [(e, s.cnt[e]) for e in ENGINES]
        y_fm = E(nc.sbuf_tensor("y_fm", [128, 8 * TL], BF16))
        Bt = E(nc.sbuf_tensor("Bt", [128, 16 * 1024], BF16))
        kft = [E(nc.sbuf_tensor(f"kft{i}", [128, 4096], BF16)) for i in range(2)]
        vft = [E(nc.sbuf_tensor(f"vft{i}", [128, 4096], BF16)) for i in range(2)]
        ar = [E(nc.sbuf_tensor(f"ar{i}", [128, 512], BF16)) for i in range(2)]
        ag = [E(nc.sbuf_tensor(f"ag{i}", [128, 512], BF16)) for i in range(2)]

        kvctr = [0]
        y_toks = []
        prev_act = {}
        prev_ag = {}
        prev_ty = None
        prev_B = None
        last_B = None
        kv_tok = [None, None]
        for tcg in range(2):
            exp_done = [(e, s.cnt[e]) for e in ENGINES]
            for e in range(16):
                for tci in range(2):
                    tc = 2 * tcg + tci
                    tm = s.emit("tensor", lambda h, e=e, tc=tc: h.matmul(
                        ps_w[:], oht[:, e * 128:(e + 1) * 128],
                        gateT[:, tc * 512:(tc + 1) * 512],
                        start=True, stop=True),
                        waits=bar0 + exp_done + [t_oh] + ([prev_B] if prev_B else []))
                    te = s.emit("vector", lambda h, e=e, tci=tci: h.tensor_copy(
                        Bt[:, e * 1024 + tci * 512: e * 1024 + (tci + 1) * 512], ps_w[:]), waits=[tm])
                    prev_B = te
            last_B = prev_B
            for h_ in range(8):
                hb = h_ % 2
                wl = bar0 + ([kv_tok[hb]] if kv_tok[hb] else [])
                tk = s.dma("gpsimd", lambda h, h_=h_, hb=hb: h.dma_start(
                    out=kft[hb][:], in_=kf[h_, :, :]), waits=wl)
                tv = s.dma("gpsimd", lambda h, h_=h_, hb=hb: h.dma_start(
                    out=vft[hb][:].rearrange("p (b d) -> p b d", b=32),
                    in_=vf[h_, :, :].rearrange("(b p) d -> p b d", p=128)), waits=wl)
                for tci in range(2):
                    tc = 2 * tcg + tci
                    for eb in range(32):
                        pp = ps_sc[eb % 2]
                        ab = eb % 2
                        wl2 = list(bar0) + [tk, tv, last_B] + ([prev_act.get(id(pp))] if prev_act.get(id(pp)) else [])
                        tsc = s.emit("tensor", lambda h, h_=h_, hb=hb, tc=tc, eb=eb, pp=pp: h.matmul(
                            pp[:], kft[hb][:, eb * 128:(eb + 1) * 128],
                            q_fm[:, h_ * TL + tc * 512: h_ * TL + (tc + 1) * 512],
                            start=True, stop=True), waits=wl2)
                        tge = s.emit("scalar", lambda h, ab=ab, pp=pp: h.activation(
                            ar[ab][:], pp[:], AF.Gelu, bias=0.0, scale=1.0),
                            waits=[tsc] + ([prev_ag.get(ab)] if prev_ag.get(ab) else []))
                        tag = s.emit("vector", lambda h, ab=ab, eb=eb, tci=tci: h.tensor_tensor(
                            ag[ab][:], ar[ab][:],
                            Bt[:, (eb // 2) * 1024 + tci * 512: (eb // 2) * 1024 + (tci + 1) * 512],
                            ALU.mult), waits=[tge])
                        prev_act[id(pp)] = tag
                        prev_ag[ab] = tag
                        tym = s.emit("tensor", lambda h, hb=hb, ab=ab, eb=eb: h.matmul(
                            ps_y[:], vft[hb][:, eb * 128:(eb + 1) * 128], ag[ab][:],
                            start=(eb == 0), stop=(eb == 31)),
                            waits=[tag] + ([prev_ty] if (eb == 0 and prev_ty) else []))
                    ty = s.emit("scalar", lambda h, h_=h_, tc=tc: h.copy(
                        y_fm[:, h_ * TL + tc * 512: h_ * TL + (tc + 1) * 512], ps_y[:]), waits=[tym])
                    prev_ty = ty
                    y_toks.append(ty)
                kv_tok[hb] = ty

        with ExitStack() as ctx3:
            E3 = ctx3.enter_context
            wgt = E3(nc.sbuf_tensor("wgt", [128, 8 * D], BF16))
            xtile = [E3(nc.sbuf_tensor(f"xtile{i}", [128, D], F32)) for i in range(2)]
            otile = [E3(nc.sbuf_tensor(f"otile{i}", [128, D], F32)) for i in range(2)]
            wg_t = emit_load_w_bf16(s, wgt, wmgT, 8, D, waits=bar0)
            prev_x = [None, None]
            prev_o = [None, None]
            prev_wg = None
            for i in range(NT_):
                ib = i % 2
                tx = s.dma("sync", lambda h, i=i, ib=ib: h.dma_start(
                    out=xtile[ib][:], in_=x_tm[i * 128:(i + 1) * 128, :]),
                    waits=([prev_x[ib]] if prev_x[ib] else []))
                for oc in range(2):
                    wl = y_toks[-1:] + [last_B] + wg_t + [tx]
                    if prev_wg:
                        wl = wl + [prev_wg]
                    tok = None
                    for kc in range(8):
                        tok = s.emit("tensor", lambda h, kc=kc, i=i, oc=oc: h.matmul(
                            ps_w[:], y_fm[:, kc * TL + i * 128: kc * TL + (i + 1) * 128],
                            wgt[:, kc * D + oc * 512: kc * D + (oc + 1) * 512],
                            start=(kc == 0), stop=(kc == 7)), waits=(wl if kc == 0 else []))
                    te = s.emit("vector", lambda h, i=i, oc=oc, ib=ib: h.tensor_tensor(
                        otile[ib][:, oc * 512:(oc + 1) * 512], ps_w[:],
                        xtile[ib][:, oc * 512:(oc + 1) * 512], ALU.add), waits=[tok])
                    prev_wg = te
                tst = s.dma("sync", lambda h, i=i, ib=ib: h.dma_start(
                    out=xo[i * 128:(i + 1) * 128, :], in_=otile[ib][:]), waits=[prev_wg])
                prev_x[ib] = prev_wg
                prev_o[ib] = tst
            s.build(nc, ctx)
    return nc


OH16 = np.zeros((16, 16 * 128), NPBF16)
for _e in range(16):
    OH16[_e, _e * 128:(_e + 1) * 128] = 1


def run_moe_sublayer(x, ins, pfx="L0_ffn"):
    in_maps = []
    for c in range(8):
        b, p = c // 2, c % 2
        in_maps.append(dict(
            x_tm=np.ascontiguousarray(x[b, p * TL:(p + 1) * TL]),
            wrT=np.ascontiguousarray(ins[pfx + "_w_router"].T),
            wmhT=np.ascontiguousarray(ins[pfx + "_w_mh"].T),
            wmgT=np.ascontiguousarray(ins[pfx + "_w_mg"].T),
            kf=np.ascontiguousarray(ins[pfx + "_k_ffwd"].transpose(0, 3, 1, 2).reshape(8, 128, 4096)),
            vf=np.ascontiguousarray(ins[pfx + "_v_ffwd"].reshape(8, 4096, 128)),
            oh=OH16, ident=IDENT,
        ))
    res = run_bass_kernel_spmd(prog("moe", build_moe), in_maps, core_ids=list(range(8)))
    xn = np.empty_like(x)
    for c in range(8):
        b, p = c // 2, c % 2
        xn[b, p * TL:(p + 1) * TL] = res.results[c]["xo"]
    return xn





def build_attn(kind):
    rope = kind == "rope"
    nex = 1 if rope else 2
    scale = 0.125 if rope else 0.03125
    nc = bass.Bass("TRN2", target_bir_lowering=False)
    xb = nc.dram_tensor("xb", [T, D], F32, kind="ExternalInput").ap()
    wq = nc.dram_tensor("wq", [D, 512], F32, kind="ExternalInput").ap()
    wk = nc.dram_tensor("wk", [D, 512], F32, kind="ExternalInput").ap()
    wv = nc.dram_tensor("wv", [D, 512], F32, kind="ExternalInput").ap()
    negm = nc.dram_tensor("negm", [128, 128], BF16, kind="ExternalInput").ap()
    if rope:
        bq_col = nc.dram_tensor("bq_col", [64, 8], F32, kind="ExternalInput").ap()
        bk_col = nc.dram_tensor("bk_col", [64, 8], F32, kind="ExternalInput").ap()
        bv_row = nc.dram_tensor("bv_row", [1, 512], F32, kind="ExternalInput").ap()
        rc_t = nc.dram_tensor("rc_t", [64, T], BF16, kind="ExternalInput").ap()
        rs_t = nc.dram_tensor("rs_t", [64, T], BF16, kind="ExternalInput").ap()
        pswap = nc.dram_tensor("pswap", [64, 64], BF16, kind="ExternalInput").ap()
    else:
        m_col = nc.dram_tensor("m_col", [128, 8], F32, kind="ExternalInput").ap()
        iota_r = nc.dram_tensor("iota_r", [1, T], F32, kind="ExternalInput").ap()
        iota2d = nc.dram_tensor("iota2d", [128, 32], F32, kind="ExternalInput").ap()
    xn_scr = nc.dram_tensor("xn_scr", [T, D], BF16, kind="Internal").ap()
    o_out = nc.dram_tensor("o_out", [128, 32 * 512], BF16, kind="ExternalOutput").ap()

    NB = T // 128
    s = Sched()
    with ExitStack() as ctx:
        E = ctx.enter_context
        sb = {}
        xn_fm = E(nc.sbuf_tensor("xn_fm", [128, 8 * T], BF16))
        v_tm = E(nc.sbuf_tensor("v_tm", [128, NB * 520], BF16))
        negmt = E(nc.sbuf_tensor("negmt", [128, 128], BF16))
        Qh = E(nc.sbuf_tensor("Qh", [64 + nex, T], BF16))
        Kh = E(nc.sbuf_tensor("Kh", [64 + nex, T], BF16))
        wqh = [E(nc.sbuf_tensor(f"wqh{i}", [128, 8 * 64], BF16)) for i in range(2)]
        wkh = [E(nc.sbuf_tensor(f"wkh{i}", [128, 8 * 64], BF16)) for i in range(2)]
        qraw = E(nc.sbuf_tensor("qraw", [64, 512], BF16))
        rtm01 = E(nc.sbuf_tensor("rtm01", [64, 512], BF16))
        rtm23 = E(nc.sbuf_tensor("rtm23", [64, 512], BF16))
        sq64 = E(nc.sbuf_tensor("sq64", [64, 512], BF16))
        ones64 = E(nc.sbuf_tensor("ones64", [64, 1], BF16))
        kmx = E(nc.sbuf_tensor("kmx", [1, 16], F32))
        rowa = E(nc.sbuf_tensor("rowa", [1, 512], F32))
        rowb = E(nc.sbuf_tensor("rowb", [1, 512], F32))
        rowc = E(nc.sbuf_tensor("rowc", [1, 512], F32))
        mbf = E(nc.sbuf_tensor("mbf", [1, T], BF16))

        if rope:
            rct = E(nc.sbuf_tensor("rct", [64, T], BF16))
            rst = E(nc.sbuf_tensor("rst", [64, T], BF16))
            pswt = E(nc.sbuf_tensor("pswt", [64, 64], BF16))
            swp = E(nc.sbuf_tensor("swp", [64, 512], BF16))
            bqc = E(nc.sbuf_tensor("bqc", [64, 8], F32))
            bkc = E(nc.sbuf_tensor("bkc", [64, 8], F32))
            bvr = E(nc.sbuf_tensor("bvr", [1, 512], BF16))
            ones1 = E(nc.sbuf_tensor("ones1", [1, 128], BF16))
        else:
            mbf2 = E(nc.sbuf_tensor("mbf2", [1, T], BF16))
            mrowf = E(nc.sbuf_tensor("mrowf", [1, 512], F32))
            mct = E(nc.sbuf_tensor("mct", [128, 8], F32))
            iort = E(nc.sbuf_tensor("iort", [1, T], F32))
            io2t = E(nc.sbuf_tensor("io2t", [128, 32], F32))
            msfull = E(nc.sbuf_tensor("msfull", [128, 32], F32))
            mrl = E(nc.sbuf_tensor("mrl", [128, 8], F32))
        ps_sc = [E(nc.psum_tensor(f"ps_sc{i}", [128, 512], F32)) for i in range(2)]
        ps_o = [E(nc.psum_tensor(f"ps_o{i}", [128, 512], F32)) for i in range(4)]
        ps_n = E(nc.psum_tensor("ps_n", [1, 512], F32))

        t_c = [s.dma("sync", lambda h: h.dma_start(out=negmt[:], in_=negm[:, :]))]
        t_c.append(s.emit("gpsimd", lambda h: h.memset(ones64[:], 1.0)))
        t_c.append(s.emit("gpsimd", lambda h: h.memset(Kh[64:64 + nex, :], 1.0)))
        t_c.append(s.emit("gpsimd", lambda h: h.memset(v_tm[:], 1.0)))
        if rope:
            t_c.append(s.dma("sync", lambda h: h.dma_start(out=rct[:], in_=rc_t[:, :])))
            t_c.append(s.dma("sync", lambda h: h.dma_start(out=rst[:], in_=rs_t[:, :])))
            t_c.append(s.dma("sync", lambda h: h.dma_start(out=pswt[:], in_=pswap[:, :])))
            t_c.append(s.dma("sync", lambda h: h.dma_start(out=bqc[:], in_=bq_col[:, :])))
            t_c.append(s.dma("sync", lambda h: h.dma_start(out=bkc[:], in_=bk_col[:, :])))
            t_c.append(s.dma("gpsimd", lambda h: h.dma_start(out=bvr[:], in_=bv_row[:, :])))
            t_c.append(s.emit("gpsimd", lambda h: h.memset(ones1[:], 1.0)))
        else:
            t_c.append(s.dma("sync", lambda h: h.dma_start(out=mct[:], in_=m_col[:, :])))
            t_c.append(s.dma("sync", lambda h: h.dma_start(out=iort[:], in_=iota_r[:, :])))
            t_c.append(s.dma("sync", lambda h: h.dma_start(out=io2t[:], in_=iota2d[:, :])))
            t_c.append(s.emit("vector", lambda h: h.tensor_relu(mrl[:], mct[:]), waits=t_c[-3:]))

        with ExitStack() as ctx2:
            E2 = ctx2.enter_context
            sb["ln_xt"] = [E2(nc.sbuf_tensor("ln_xt0", [128, D], F32))] * 2
            sb["ln_xt"] = [sb["ln_xt"][0], sb["ln_xt"][0]]
            sb["ln_sq"] = E2(nc.sbuf_tensor("ln_sq", [128, D], F32))
            sb["ln_xn"] = E2(nc.sbuf_tensor("ln_xnb", [128, D], BF16))
            sb["ln_st"] = E2(nc.sbuf_tensor("ln_st", [128, 8], F32))
            wvt = E2(nc.sbuf_tensor("wvt", [128, 8 * 512], BF16))
            ln_toks = emit_ln(s, sb, xb, xn_scr, NB)
            tr_toks = emit_transpose_in(s, xn_fm, xn_scr, T, ln_toks)
            w_v = emit_load_w_bf16(s, wvt, wv, 8, 512)

            prev_v = {}
            for tt in range(NB):
                ps = ps_sc[tt % 2]
                wl = tr_toks + w_v + t_c + ([prev_v[id(ps)]] if id(ps) in prev_v else [])
                tok = None
                for kc in range(8):
                    tok = s.emit("tensor", lambda h, kc=kc, tt=tt, ps=ps: h.matmul(
                        ps[:], xn_fm[:, kc * T + tt * 128: kc * T + (tt + 1) * 128],
                        wvt[:, kc * 512:(kc + 1) * 512],
                        start=(kc == 0), stop=(False if rope else kc == 7)),
                        waits=(wl if kc == 0 else []))
                if rope:
                    tok = s.emit("tensor", lambda h, ps=ps: h.matmul(
                        ps[:], ones1[:], bvr[:], start=False, stop=True))
                te = s.emit("vector", lambda h, tt=tt, ps=ps: h.tensor_copy(
                    v_tm[:, tt * 520: tt * 520 + 520].rearrange("p (a m) -> p a m", a=8)[:, :, 0:64],
                    ps[:].rearrange("p (a m) -> p a m", a=8)), waits=[tok])
                prev_v[id(ps)] = te

        bar0 = [(e, s.cnt[e]) for e in ENGINES]
        o_buf = E(nc.sbuf_tensor("o_buf", [128, NB * 512], BF16))
        pt = [E(nc.sbuf_tensor(f"pt{i}", [128, 512], BF16)) for i in range(2)]
        dtile = E(nc.sbuf_tensor("dtile", [128, 128], F32))
        rcp = E(nc.sbuf_tensor("rcp", [128, 4], F32))

        prev = {}
        wtok = [None, None]
        for h_ in range(HL):
            hb = h_ % 2
            wl = list(bar0) + ([wtok[hb]] if wtok[hb] else [])
            twq = s.dma("gpsimd", lambda h, h_=h_, hb=hb: h.dma_start(
                out=wqh[hb][:], in_=wq[:, h_ * 64:(h_ + 1) * 64].rearrange("(kc p) c -> p kc c", p=128)),
                waits=wl)
            twk = s.dma("gpsimd", lambda h, h_=h_, hb=hb: h.dma_start(
                out=wkh[hb][:], in_=wk[:, h_ * 64:(h_ + 1) * 64].rearrange("(kc p) c -> p kc c", p=128)),
                waits=wl)
            for side, wt_, dst in (("q", wqh, Qh), ("k", wkh, Kh)):
                tws = twq if side == "q" else twk
                for tc in range(8):
                    ps = ps_sc[tc % 2]
                    wl = list(bar0) + [tws] + ([prev[id(ps)]] if id(ps) in prev else [])
                    tok = None
                    for kc in range(8):
                        tok = s.emit("tensor", lambda h, kc=kc, tc=tc, hb=hb, wt_=wt_, ps=ps: h.matmul(
                            ps[0:64, :], wt_[hb][:, kc * 64:(kc + 1) * 64],
                            xn_fm[:, kc * T + tc * 512: kc * T + (tc + 1) * 512],
                            start=(kc == 0), stop=(kc == 7)), waits=(wl if kc == 0 else []))
                    if rope:
                        bcol = bqc if side == "q" else bkc
                        tev = s.emit("scalar", lambda h, ps=ps, bcol=bcol, h_=h_: h.activation(
                            qraw[:], ps[0:64, :], AF.Identity, bias=bcol[:, h_:h_ + 1], scale=1.0),
                            waits=[tok] + ([prev["qraw"]] if "qraw" in prev else []))
                        tsw = s.emit("tensor", lambda h, ps=ps: h.matmul(
                            ps[64:128, 0:512], pswt[:], qraw[:], start=True, stop=True,
                            tile_position=(0, 64)), waits=[tev])
                        tse = s.emit("scalar", lambda h, ps=ps: h.copy(swp[:], ps[64:128, 0:512]), waits=[tsw])
                        t1 = s.emit("vector", lambda h, tc=tc: h.tensor_tensor(
                            rtm01[:], qraw[:], rct[:, tc * 512:(tc + 1) * 512], ALU.mult), waits=[tev])
                        t2 = s.emit("vector", lambda h, tc=tc: h.tensor_tensor(
                            rtm23[:], swp[:], rst[:, tc * 512:(tc + 1) * 512], ALU.mult), waits=[tse, t1])
                        t6 = s.emit("vector", lambda h, tc=tc, dst=dst: h.tensor_tensor(
                            dst[0:64, tc * 512:(tc + 1) * 512], rtm01[:], rtm23[:], ALU.add), waits=[t2])
                        prev["qraw"] = t6
                        prev[id(ps)] = t6
                    else:
                        tev = s.emit("scalar", lambda h, ps=ps, tc=tc, dst=dst: h.copy(
                            dst[0:64, tc * 512:(tc + 1) * 512], ps[0:64, :]), waits=[tok])
                        prev[id(ps)] = tev
            pk_done = [("vector" if rope else "scalar", s.cnt["vector" if rope else "scalar"])]

            nrm_ev = []
            for tc in range(8):
                tsq = s.emit("scalar", lambda h, tc=tc: h.square(
                    sq64[:], Kh[0:64, tc * 512:(tc + 1) * 512]), waits=pk_done + nrm_ev[-1:])
                tmm = s.emit("tensor", lambda h: h.matmul(
                    ps_n[:], ones64[:], sq64[:], start=True, stop=True),
                    waits=[tsq] + nrm_ev[-1:] + pk_done)
                tev = s.emit("vector", lambda h, tc=tc: h.reduce_max(
                    kmx[:, tc:tc + 1], ps_n[:], AX), waits=[tmm])
                nrm_ev.append(tev)
            tkx = s.emit("vector", lambda h: h.reduce_max(kmx[:, 8:9], kmx[:, 0:8], AX), waits=nrm_ev[-1:])
            mrow_done = []
            mdma = []
            for tc in range(8):
                tsq = s.emit("scalar", lambda h, tc=tc: h.square(
                    sq64[:], Qh[0:64, tc * 512:(tc + 1) * 512]),
                    waits=pk_done + mrow_done[-1:] + nrm_ev[-1:])
                tmm = s.emit("tensor", lambda h: h.matmul(
                    ps_n[:], ones64[:], sq64[:], start=True, stop=True),
                    waits=[tsq] + mrow_done[-1:] + nrm_ev[-1:])
                tq2 = s.emit("vector", lambda h: h.tensor_scalar(
                    rowa[:], ps_n[:], kmx[:, 8:9], None, ALU.mult), waits=[tmm, tkx])
                tsd = s.emit("scalar", lambda h: h.sqrt(rowb[:], rowa[:]), waits=[tq2])
                tlast = tsd
                if not rope:
                    tit = s.emit("vector", lambda h, h_=h_, tc=tc: h.tensor_scalar(
                        rowc[:], iort[:, tc * 512:(tc + 1) * 512], mrl[0:1, h_:h_ + 1], 32.0,
                        ALU.mult, ALU.mult), waits=t_c + [tsd])
                    tlast = s.emit("vector", lambda h: h.tensor_tensor(
                        rowb[:], rowb[:], rowc[:], ALU.add), waits=[tit])
                tng = s.emit("vector", lambda h: h.tensor_scalar(
                    rowa[:], rowb[:], 1.0, -1.0, ALU.add, ALU.mult), waits=[tlast])
                tb1 = s.emit("vector", lambda h, tc=tc: h.tensor_copy(
                    mbf[:, tc * 512:(tc + 1) * 512], rowa[:]), waits=[tng])
                mrow_done.append(tb1)
                if not rope:
                    tup = s.emit("vector", lambda h, tc=tc: h.tensor_copy(
                        mrowf[:], mbf[:, tc * 512:(tc + 1) * 512]), waits=[tb1])
                    tm2v = s.emit("vector", lambda h: h.tensor_tensor(
                        rowc[:], rowa[:], mrowf[:], ALU.subtract), waits=[tup])
                    tb2 = s.emit("vector", lambda h, tc=tc: h.tensor_copy(
                        mbf2[:, tc * 512:(tc + 1) * 512], rowc[:]), waits=[tm2v])
                    mrow_done.append(tb2)
            td1 = s.dma("sync", lambda h: h.dma_start(out=Qh[64:65, :], in_=mbf[:, :]),
                        waits=mrow_done + [("tensor", s.cnt["tensor"])])
            mdma = [td1]
            if not rope:
                td2 = s.dma("sync", lambda h: h.dma_start(out=Qh[65:66, :], in_=mbf2[:, :]),
                            waits=mrow_done + [("tensor", s.cnt["tensor"])])
                mdma.append(td2)
                tmsf = s.emit("vector", lambda h, h_=h_: h.tensor_scalar(
                    msfull[:], io2t[:], mct[:, h_:h_ + 1], None, ALU.mult),
                    waits=t_c + mdma + [("scalar", s.cnt["scalar"])])
                mdma.append(tmsf)

            for tcq in range(8):
                for sb_ in range(4 * tcq + 4):
                    pp = ps_sc[sb_ % 2]
                    pb = sb_ % 2
                    wl = list(mdma) + ([prev[("pt", pb)]] if ("pt", pb) in prev else []) + \
                         ([prev[id(pp)]] if id(pp) in prev else [])
                    tmm = s.emit("tensor", lambda h, h_=h_, tcq=tcq, sb_=sb_, pp=pp: h.matmul(
                        pp[:], Kh[0:64 + nex, sb_ * 128:(sb_ + 1) * 128],
                        Qh[0:64 + nex, tcq * 512:(tcq + 1) * 512],
                        start=True, stop=True), waits=wl)
                    if rope:
                        tex = s.emit("scalar", lambda h, pb=pb, pp=pp: h.activation(
                            pt[pb][:], pp[:], AF.Exp, bias=0.0, scale=scale), waits=[tmm])
                    else:
                        tex = s.emit("scalar", lambda h, pb=pb, pp=pp, sb_=sb_: h.activation(
                            pt[pb][:], pp[:], AF.Exp, bias=msfull[:, sb_:sb_ + 1], scale=scale), waits=[tmm])
                    j0 = sb_ - 4 * tcq
                    if 0 <= j0 <= 3:
                        tdt = s.emit("vector", lambda h, pp=pp, j0=j0: h.tensor_tensor(
                            dtile[:], pp[:, j0 * 128:(j0 + 1) * 128], negmt[:], ALU.add),
                            waits=[tmm, tex])
                        if rope:
                            tex2 = s.emit("scalar", lambda h, pb=pb, j0=j0: h.activation(
                                pt[pb][:, j0 * 128:(j0 + 1) * 128], dtile[:], AF.Exp,
                                bias=0.0, scale=scale), waits=[tdt])
                        else:
                            tex2 = s.emit("scalar", lambda h, pb=pb, j0=j0, sb_=sb_: h.activation(
                                pt[pb][:, j0 * 128:(j0 + 1) * 128], dtile[:], AF.Exp,
                                bias=msfull[:, sb_:sb_ + 1], scale=scale), waits=[tdt])
                        tex = tex2
                    prev[id(pp)] = tex
                    tpv = None
                    for j in range(max(0, j0), 4):
                        tpv = s.emit("tensor", lambda h, pb=pb, j=j, sb_=sb_, h_=h_: h.matmul(
                            ps_o[j][:, 0:65], pt[pb][:, j * 128:(j + 1) * 128],
                            v_tm[:, sb_ * 520 + h_ * 65: sb_ * 520 + (h_ + 1) * 65],
                            start=(sb_ == 0), stop=(sb_ == 4 * tcq + j)),
                            waits=[tex] + ([prev[("po", j)]] if (("po", j) in prev and sb_ == 0) else []))
                    prev[("pt", pb)] = tpv if tpv is not None else tex
                for j in range(4):
                    trc = s.emit("vector", lambda h, j=j: h.reciprocal(
                        rcp[:, j:j + 1], ps_o[j][:, 64:65]),
                        waits=[prev[("pt", 0)], prev[("pt", 1)]])
                    tfo = s.emit("vector", lambda h, j=j, tcq=tcq, h_=h_: h.tensor_scalar(
                        o_buf[:, (4 * tcq + j) * 512 + h_ * 64: (4 * tcq + j) * 512 + (h_ + 1) * 64],
                        ps_o[j][:, 0:64], rcp[:, j:j + 1], None, ALU.mult), waits=[trc])
                    prev[("po", j)] = tfo
            wtok[hb] = prev[("pt", 0)]
        ow = [prev[("po", j)] for j in range(4)]
        for part in range(4):
            s.dma("sync", lambda h, part=part: h.dma_start(
                out=o_out[:, part * 8 * 512:(part + 1) * 8 * 512],
                in_=o_buf[:, part * 8 * 512:(part + 1) * 8 * 512]), waits=ow)
        s.build(nc, ctx)
    return nc


NEGM_TILE = np.where(np.arange(128)[:, None] <= np.arange(128)[None, :], 0.0, -1e30).astype(NPBF16)


def rope_tables():
    inv = 1.0 / 10000.0 ** (np.arange(0, 64, 2) / 64.0)
    ang = np.arange(T)[None, :] * np.repeat(inv, 2)[:, None]
    C = np.cos(ang)
    S = np.sin(ang)
    S[0::2, :] *= -1.0
    return C.astype(NPBF16), S.astype(NPBF16)

PSWAP64 = np.zeros((64, 64), np.float32)
for _i in range(0, 64, 2):
    PSWAP64[_i, _i + 1] = 1.0
    PSWAP64[_i + 1, _i] = 1.0
PSWAP64 = PSWAP64.astype(NPBF16)


def perm_eo(w):
    w = w.reshape(w.shape[0], -1, 64)
    return np.ascontiguousarray(np.concatenate([w[:, :, 0::2], w[:, :, 1::2]], axis=2).reshape(w.shape[0], -1))


def perm_eo_b(b):
    b = b.reshape(-1, 64)
    return np.ascontiguousarray(np.concatenate([b[:, 0::2], b[:, 1::2]], axis=1).reshape(-1))


def o_to_tokmajor(o_buf_arr):
    return np.ascontiguousarray(o_buf_arr.reshape(128, 32, 512).transpose(1, 0, 2).reshape(T, 512))


def run_rope_attn_sublayer(x, ins):
    cos_t, sin_t = rope_tables()
    in_maps = []
    for c in range(8):
        b, p = c // 2, c % 2
        hs = p * 512
        wqT = np.ascontiguousarray(ins["L0_attn_b_Wq"].T[:, hs:hs + 512])
        wkT = np.ascontiguousarray(ins["L0_attn_b_Wk"].T[:, hs:hs + 512])
        wvT = np.ascontiguousarray(ins["L0_attn_b_Wv"].T[:, hs:hs + 512])
        bq = ins["L0_attn_b_bq"][hs:hs + 512]
        bk = ins["L0_attn_b_bk"][hs:hs + 512]
        in_maps.append(dict(
            xb=np.ascontiguousarray(x[b]),
            wq=wqT, wk=wkT, wv=wvT,
            bq_col=np.ascontiguousarray(bq.reshape(8, 64).T),
            bk_col=np.ascontiguousarray(bk.reshape(8, 64).T),
            bv_row=ins["L0_attn_b_bv"][hs:hs + 512].reshape(1, 512),
            rc_t=cos_t, rs_t=sin_t, negm=NEGM_TILE, pswap=PSWAP64,
        ))
    res = run_bass_kernel_spmd(prog("attn_rope", lambda: build_attn("rope")), in_maps, core_ids=list(range(8)))
    o_full = [np.concatenate([o_to_tokmajor(res.results[2 * b]["o_out"]),
                              o_to_tokmajor(res.results[2 * b + 1]["o_out"])], axis=1)
              for b in range(4)]
    in_maps = []
    for c in range(8):
        b, p = c // 2, c % 2
        in_maps.append(dict(
            x_fm=fm(x[b, p * TL:(p + 1) * TL]),
            o_in=fm(o_full[b][p * TL:(p + 1) * TL]).astype(NPBF16),
            w=np.ascontiguousarray(ins["L0_attn_b_Wo"].T),
            b_col=col128(ins["L0_attn_b_bo"]),
        ))
    res = run_bass_kernel_spmd(prog("dn_b", build_proj_residual), in_maps, core_ids=list(range(8)))
    xn = np.empty_like(x)
    for c in range(8):
        b, p = c // 2, c % 2
        xn[b, p * TL:(p + 1) * TL] = unfm(res.results[c]["xo"], TL)
    return xn


IDENT1024 = None

def run_alibi_attn_sublayer(x, ins):
    global IDENT1024
    wkqv = ins["L1_attn_a_W_kqv"]
    in_maps = []
    for c in range(8):
        b, p = c // 2, c % 2
        hs = p * 512
        in_maps.append(dict(
            xb=np.ascontiguousarray(x[b]),
            wk=np.ascontiguousarray(wkqv[:, hs:hs + 512]),
            wq=np.ascontiguousarray(wkqv[:, 1024 + hs:1024 + hs + 512]),
            wv=np.ascontiguousarray(wkqv[:, 2048 + hs:2048 + hs + 512]),
            m_col=np.ascontiguousarray(np.broadcast_to(
                ins["L1_attn_a_m"][p * 8:(p + 1) * 8, 0, 0].reshape(1, 8), (128, 8))),
            iota_r=np.arange(T, dtype=np.float32).reshape(1, T),
            iota2d=np.ascontiguousarray((np.arange(128)[:, None] + 128 * np.arange(32)[None, :]).astype(np.float32)),
            negm=NEGM_TILE,
        ))
    res = run_bass_kernel_spmd(prog("attn_alibi", lambda: build_attn("alibi")), in_maps, core_ids=list(range(8)))
    o_full = [np.concatenate([o_to_tokmajor(res.results[2 * b]["o_out"]),
                              o_to_tokmajor(res.results[2 * b + 1]["o_out"])], axis=1)
              for b in range(4)]
    if IDENT1024 is None:
        IDENT1024 = np.eye(D, dtype=np.float32)
    in_maps = []
    for c in range(8):
        b, p = c // 2, c % 2
        in_maps.append(dict(
            x_fm=fm(x[b, p * TL:(p + 1) * TL]),
            o_in=fm(o_full[b][p * TL:(p + 1) * TL]).astype(NPBF16),
            w=IDENT1024,
            b_col=np.zeros((128, 8), np.float32),
        ))
    res = run_bass_kernel_spmd(prog("dn_b", build_proj_residual), in_maps, core_ids=list(range(8)))
    xn = np.empty_like(x)
    for c in range(8):
        b, p = c // 2, c % 2
        xn[b, p * TL:(p + 1) * TL] = unfm(res.results[c]["xo"], TL)
    return xn





def emit_quant(s, qs, src_ap, dst_ap, W, waits=(), eng="vector"):
    nb = W // 16
    bc = lambda t: t[:, 0:nb].unsqueeze(2).broadcast_to([128, nb, 16])
    v3 = lambda t: t[:, 0:W].rearrange("p (a m) -> p a m", a=nb)
    tw = list(waits)
    t0 = s.emit(eng, lambda h: h.scalar_tensor_tensor(qs["qa"][:, 0:W], src_ap, -1.0, src_ap, ALU.mult, ALU.max), waits=tw)
    t1 = s.emit(eng, lambda h: h.tensor_reduce(qs["qm1"][:, 0:nb], v3(qs["qa"]), AX, ALU.max), waits=[t0])
    t2 = s.emit(eng, lambda h: h.tensor_scalar(qs["qm2"][:, 0:nb], qs["qm1"][:, 0:nb], 1.0 / 6.0, 1e-12, ALU.mult, ALU.max), waits=[t1])
    t3 = s.emit(eng, lambda h: h.reciprocal(qs["qm3"][:, 0:nb], qs["qm2"][:, 0:nb]), waits=[t2])
    t4 = s.emit(eng, lambda h: h.tensor_tensor(v3(qs["qv"]), v3(qs["qa"]), bc(qs["qm3"]), ALU.mult), waits=[t3])
    ta = s.emit(eng, lambda h: h.tensor_scalar(qs["qa"][:, 0:W], qs["qv"][:, 0:W], 0.25, None, ALU.is_gt), waits=[t4])
    for c in (0.75, 1.25, 1.75):
        ta = s.emit(eng, lambda h, c=c: h.scalar_tensor_tensor(
            qs["qa"][:, 0:W], qs["qv"][:, 0:W], c, qs["qa"][:, 0:W], ALU.is_gt, ALU.add), waits=[ta])
    tb = s.emit(eng, lambda h: h.tensor_scalar(qs["qb"][:, 0:W], qs["qv"][:, 0:W], 2.5, None, ALU.is_gt), waits=[t4])
    tb = s.emit(eng, lambda h: h.scalar_tensor_tensor(
        qs["qb"][:, 0:W], qs["qv"][:, 0:W], 3.5, qs["qb"][:, 0:W], ALU.is_gt, ALU.add), waits=[tb])
    tc_ = s.emit(eng, lambda h: h.tensor_scalar(qs["qv"][:, 0:W], qs["qv"][:, 0:W], 5.0, None, ALU.is_gt), waits=[tb, ta])
    tq = s.emit(eng, lambda h: h.scalar_tensor_tensor(
        qs["qb"][:, 0:W], qs["qv"][:, 0:W], 2.0, qs["qb"][:, 0:W], ALU.mult, ALU.add), waits=[tc_])
    tq = s.emit(eng, lambda h: h.scalar_tensor_tensor(
        qs["qb"][:, 0:W], qs["qa"][:, 0:W], 0.5, qs["qb"][:, 0:W], ALU.mult, ALU.add), waits=[tq])
    tsg = s.emit("scalar", lambda h: h.sign(qs["qsg"][:, 0:W], src_ap), waits=tw + [t4])
    tm = s.emit(eng, lambda h: h.tensor_tensor(qs["qa"][:, 0:W], qs["qb"][:, 0:W], qs["qsg"][:, 0:W], ALU.mult), waits=[tq, tsg])
    tout = s.emit(eng, lambda h: h.tensor_tensor(
        v3_out(dst_ap, nb), v3(qs["qa"]), bc(qs["qm2"]), ALU.mult), waits=[tm])
    return tout

def v3_out(ap, nb):
    return ap.rearrange("p (a m) -> p a m", a=nb)


def build_ffn1(pre=False):
    nc = bass.Bass("TRN2", target_bir_lowering=False)
    x_tm = nc.dram_tensor("x_tm", [TL, D], F32, kind="ExternalInput").ap()
    if not pre:
        wg = nc.dram_tensor("wg", [2816, D], F32, kind="ExternalInput").ap()
        wu = nc.dram_tensor("wu", [2816, D], F32, kind="ExternalInput").ap()
        wd = nc.dram_tensor("wd", [D, 2816], F32, kind="ExternalInput").ap()
    ident = nc.dram_tensor("ident", [128, 128], BF16, kind="ExternalInput").ap()
    xq_scr = nc.dram_tensor("xq_scr", [TL, D], BF16, kind="Internal").ap()
    wq_kind = "ExternalInput" if pre else "ExternalOutput"
    wgq_scr = nc.dram_tensor("wgq_scr", [2816, D], BF16, kind=wq_kind).ap()
    wuq_scr = nc.dram_tensor("wuq_scr", [2816, D], BF16, kind=wq_kind).ap()
    wdq_scr = nc.dram_tensor("wdq_scr", [D, 2816], BF16, kind=wq_kind).ap()
    xo = nc.dram_tensor("xo", [TL, D], F32, kind="ExternalOutput").ap()

    NT_ = TL // 128
    NI = 11
    s = Sched()
    with ExitStack() as ctx:
        E = ctx.enter_context
        qs = dict(
            qa=E(nc.sbuf_tensor("qa", [128, 2816], F32)),
            qv=E(nc.sbuf_tensor("qv", [128, 2816], F32)),
            qb=E(nc.sbuf_tensor("qb", [128, 2816], F32)),
            qm1=E(nc.sbuf_tensor("qm1", [128, 176], F32)),
            qm2=E(nc.sbuf_tensor("qm2", [128, 176], F32)),
            qm3=E(nc.sbuf_tensor("qm3", [128, 176], F32)),
            qsg=E(nc.sbuf_tensor("qsg", [128, 2816], BF16)),
        )
        identt = E(nc.sbuf_tensor("identt", [128, 128], BF16))
        wtile = E(nc.sbuf_tensor("wtile", [128, 2816], F32))
        wq_out = E(nc.sbuf_tensor("wq_out", [128, 2816], BF16))
        t_id = s.dma("sync", lambda h: h.dma_start(out=identt[:], in_=ident[:, :]))

        prevq = None
        if not pre:
            for wsrc, wdst, nt, wwid in ((wg, wgq_scr, 22, D), (wu, wuq_scr, 22, D), (wd, wdq_scr, 8, 2816)):
                for i in range(nt):
                    tld = s.dma("sync", lambda h, wsrc=wsrc, i=i, wwid=wwid: h.dma_start(
                        out=wtile[:, 0:wwid], in_=wsrc[i * 128:(i + 1) * 128, :]),
                        waits=[prevq] if prevq else [])
                    tq = emit_quant(s, qs, wtile[:, 0:wwid], wq_out[:, 0:wwid], wwid, waits=[tld])
                    tst = s.dma("sync", lambda h, wdst=wdst, i=i, wwid=wwid: h.dma_start(
                        out=wdst[i * 128:(i + 1) * 128, :], in_=wq_out[:, 0:wwid]), waits=[tq])
                    prevq = tst

        with ExitStack() as ctx2:
            E2 = ctx2.enter_context
            sb = {}
            sb["ln_xt"] = [E2(nc.sbuf_tensor(f"lnxt{i}", [128, D], F32)) for i in range(2)]
            sb["ln_sq"] = E2(nc.sbuf_tensor("ln_sq", [128, D], F32))
            sb["ln_xn"] = E2(nc.sbuf_tensor("ln_xnb", [128, D], BF16))
            sb["ln_st"] = E2(nc.sbuf_tensor("ln_st", [128, 8], F32))
            xnf = E2(nc.sbuf_tensor("xnf", [128, D], F32))
            last = None
            prevst = None
            for i in range(NT_):
                xt, sq, st = sb["ln_xt"][0], sb["ln_sq"], sb["ln_st"]
                tld = s.dma("sync", lambda h, i=i: h.dma_start(
                    out=xt[:], in_=x_tm[i * 128:(i + 1) * 128, :]),
                    waits=[last] if last else ([prevq] if prevq else []))
                tsq = s.emit("scalar", lambda h: h.square(sq[:], xt[:]), waits=[tld])
                t1 = s.emit("vector", lambda h: h.reduce_sum(st[:, 0:1], xt[:], AX), waits=[tld])
                t2 = s.emit("vector", lambda h: h.reduce_sum(st[:, 1:2], sq[:], AX), waits=[tsq, t1])
                tm = s.emit("vector", lambda h: h.tensor_scalar(st[:, 2:3], st[:, 0:1], 1.0 / D, None, ALU.mult), waits=[t2])
                tm2 = s.emit("vector", lambda h: h.tensor_scalar(st[:, 3:4], st[:, 2:3], st[:, 2:3], None, ALU.mult), waits=[tm])
                tv = s.emit("vector", lambda h: h.scalar_tensor_tensor(
                    st[:, 4:5], st[:, 1:2], 1.0 / D, st[:, 3:4], ALU.mult, ALU.subtract), waits=[tm2])
                tve = s.emit("vector", lambda h: h.tensor_scalar(st[:, 5:6], st[:, 4:5], EPS_LN, None, ALU.add), waits=[tv])
                tsd = s.emit("scalar", lambda h: h.sqrt(st[:, 6:7], st[:, 5:6]), waits=[tve])
                trs = s.emit("vector", lambda h: h.reciprocal(st[:, 7:8], st[:, 6:7]), waits=[tsd])
                txn = s.emit("vector", lambda h: h.tensor_scalar(
                    xnf[:], xt[:], st[:, 2:3], st[:, 7:8], ALU.subtract, ALU.mult), waits=[trs])
                tq = emit_quant(s, qs, xnf[:], sb["ln_xn"][:], D, waits=[txn])
                tst2 = s.dma("sync", lambda h, i=i: h.dma_start(
                    out=xq_scr[i * 128:(i + 1) * 128, :], in_=sb["ln_xn"][:]), waits=[tq])
                last = txn
                prevst = tst2
            ln_done = [prevst]

        xq_fm = E(nc.sbuf_tensor("xq_fm", [128, 8 * TL], BF16))
        wdqT = E(nc.sbuf_tensor("wdqT", [128, 22 * D], BF16))
        wgT = [E(nc.sbuf_tensor(f"wgT{i}", [128, 8 * 256], BF16)) for i in range(2)]
        wuT = [E(nc.sbuf_tensor(f"wuT{i}", [128, 8 * 256], BF16)) for i in range(2)]
        hqT = E(nc.sbuf_tensor("hqT", [128, 4 * 2816], BF16))
        silu = E(nc.sbuf_tensor("silu", [128, 256], F32))
        upse = E(nc.sbuf_tensor("upse", [128, 256], F32))
        hft = E(nc.sbuf_tensor("hft", [128, 256], F32))
        hqt = E(nc.sbuf_tensor("hqt", [128, 256], BF16))
        xtile = E(nc.sbuf_tensor("xtile", [128, D], F32))
        otile = E(nc.sbuf_tensor("otile", [128, D], F32))
        ps_g = E(nc.psum_tensor("ps_g", [128, 256], F32))
        ps_u = E(nc.psum_tensor("ps_u", [128, 256], F32))
        ps_t = E(nc.psum_tensor("ps_t", [128, 128], BF16))
        ps_d = [E(nc.psum_tensor(f"ps_d{i}", [128, 512], F32)) for i in range(2)]

        tr_x = emit_transpose_in(s, xq_fm, xq_scr, TL, ln_done)
        twd = []
        for i in range(22):
            twd.append(s.dma("sync", lambda h, i=i: h.dma_start_transpose(
                wdqT[:, i * D:(i + 1) * D], wdq_scr[:, i * 128:(i + 1) * 128]),
                waits=[prevq] if prevq else []))

        prev = {}
        for tg in range(4):
            for Ic in range(NI):
                wb = Ic % 2
                wl = tr_x + ([prev[("wg", wb)]] if ("wg", wb) in prev else [])
                tgw = []
                for kc in range(8):
                    tgw.append(s.dma("sync", lambda h, kc=kc, Ic=Ic, wb=wb: h.dma_start_transpose(
                        wgT[wb][:, kc * 256:(kc + 1) * 256],
                        wgq_scr[Ic * 256:(Ic + 1) * 256, kc * 128:(kc + 1) * 128]), waits=wl))
                    tgw.append(s.dma("sync", lambda h, kc=kc, Ic=Ic, wb=wb: h.dma_start_transpose(
                        wuT[wb][:, kc * 256:(kc + 1) * 256],
                        wuq_scr[Ic * 256:(Ic + 1) * 256, kc * 128:(kc + 1) * 128]), waits=wl))
                for ti in range(4):
                    tt = tg * 4 + ti
                    wl2 = tgw + ([prev["hq"]] if "hq" in prev else [])
                    tokg = toku = None
                    for kc in range(8):
                        tokg = s.emit("tensor", lambda h, kc=kc, tt=tt, wb=wb: h.matmul(
                            ps_g[:], xq_fm[:, kc * TL + tt * 128: kc * TL + (tt + 1) * 128],
                            wgT[wb][:, kc * 256:(kc + 1) * 256],
                            start=(kc == 0), stop=(kc == 7)), waits=(wl2 if kc == 0 else []))
                    for kc in range(8):
                        toku = s.emit("tensor", lambda h, kc=kc, tt=tt, wb=wb: h.matmul(
                            ps_u[:], xq_fm[:, kc * TL + tt * 128: kc * TL + (tt + 1) * 128],
                            wuT[wb][:, kc * 256:(kc + 1) * 256],
                            start=(kc == 0), stop=(kc == 7)), waits=[])
                    tsl = s.emit("scalar", lambda h: h.activation(
                        silu[:], ps_g[:], AF.Silu, bias=0.0, scale=1.0),
                        waits=[tokg] + ([prev["sl"]] if "sl" in prev else []))
                    tue = s.emit("scalar", lambda h: h.copy(upse[:], ps_u[:]),
                        waits=[toku] + ([prev["ue"]] if "ue" in prev else []))
                    prev["sl"] = tsl
                    prev["ue"] = tue
                    th = s.emit("vector", lambda h: h.tensor_tensor(hft[:], silu[:], upse[:], ALU.mult),
                                waits=[tsl, tue])
                    thq = emit_quant(s, qs, hft[:], hqt[:], 256, waits=[th])
                    prev["hq"] = thq
                    for half in range(2):
                        ttr = s.emit("tensor", lambda h, half=half: h.transpose(
                            ps_t[:], hqt[:, half * 128:(half + 1) * 128], identt[:]),
                            waits=[thq, t_id] + ([prev["tre"]] if "tre" in prev else []))
                        tte = s.emit("vector", lambda h, ti=ti, Ic=Ic, half=half: h.tensor_copy(
                            hqT[:, ti * 2816 + Ic * 256 + half * 128: ti * 2816 + Ic * 256 + (half + 1) * 128],
                            ps_t[:]), waits=[ttr])
                        prev["tre"] = tte
                prev[("wg", wb)] = tokg
            for ti in range(4):
                tt = tg * 4 + ti
                tx = s.dma("sync", lambda h, tt=tt: h.dma_start(
                    out=xtile[:], in_=x_tm[tt * 128:(tt + 1) * 128, :]),
                    waits=[prev["xo"]] if "xo" in prev else [])
                for oc in range(2):
                    pd = ps_d[oc]
                    wl3 = twd + [prev["tre"], tx] + ([prev[("pd", oc)]] if ("pd", oc) in prev else [])
                    tok = None
                    for i in range(22):
                        tok = s.emit("tensor", lambda h, i=i, ti=ti, oc=oc, pd=pd: h.matmul(
                            pd[:], hqT[:, ti * 2816 + i * 128: ti * 2816 + i * 128 + 128],
                            wdqT[:, i * D + oc * 512: i * D + (oc + 1) * 512],
                            start=(i == 0), stop=(i == 21)), waits=(wl3 if i == 0 else []))
                    te = s.emit("vector", lambda h, oc=oc, pd=pd: h.tensor_tensor(
                        otile[:, oc * 512:(oc + 1) * 512], pd[:],
                        xtile[:, oc * 512:(oc + 1) * 512], ALU.add), waits=[tok])
                    prev[("pd", oc)] = te
                txo = s.dma("sync", lambda h, tt=tt: h.dma_start(
                    out=xo[tt * 128:(tt + 1) * 128, :], in_=otile[:]),
                    waits=[prev[("pd", 0)], prev[("pd", 1)]])
                prev["xo"] = txo
        s.build(nc, ctx)
    return nc


def run_ffn1_sublayer(x, ins, qcache):
    pre = "wgq" in qcache
    in_maps = []
    for c in range(8):
        b, p = c // 2, c % 2
        im = dict(x_tm=np.ascontiguousarray(x[b, p * TL:(p + 1) * TL]), ident=IDENT)
        if pre:
            im["wgq_scr"] = qcache["wgq"][c]
            im["wuq_scr"] = qcache["wuq"][c]
            im["wdq_scr"] = qcache["wdq"][c]
        else:
            im["wg"] = ins["L1_ffn_w_gate"]
            im["wu"] = ins["L1_ffn_w_up"]
            im["wd"] = ins["L1_ffn_w_down"]
        in_maps.append(im)
    if pre:
        res = run_bass_kernel_spmd(prog("ffn1_b", lambda: build_ffn1(True)), in_maps, core_ids=list(range(8)))
    else:
        res = run_bass_kernel_spmd(prog("ffn1_a", lambda: build_ffn1(False)), in_maps, core_ids=list(range(8)))
        qcache["wgq"] = [res.results[c]["wgq_scr"] for c in range(8)]
        qcache["wuq"] = [res.results[c]["wuq_scr"] for c in range(8)]
        qcache["wdq"] = [res.results[c]["wdq_scr"] for c in range(8)]
    xn = np.empty_like(x)
    for c in range(8):
        b, p = c // 2, c % 2
        xn[b, p * TL:(p + 1) * TL] = res.results[c]["xo"]
    return xn





def build_rn_a():
    nc = bass.Bass("TRN2", target_bir_lowering=False)
    xb = nc.dram_tensor("xb", [T, D], F32, kind="ExternalInput").ap()
    wq = nc.dram_tensor("wq", [D, 512], F32, kind="ExternalInput").ap()
    wk = nc.dram_tensor("wk", [D, 512], F32, kind="ExternalInput").ap()
    wv = nc.dram_tensor("wv", [D, 512], F32, kind="ExternalInput").ap()
    pkv = nc.dram_tensor("pkv", [64, 8 * 64], F32, kind="ExternalInput").ap()
    aB = nc.dram_tensor("aB", [64, 8 * 512], BF16, kind="ExternalInput").ap()
    bB = nc.dram_tensor("bB", [64, 8 * 128], BF16, kind="ExternalInput").ap()
    triui = nc.dram_tensor("triui", [128, 128], BF16, kind="ExternalInput").ap()
    cp = nc.dram_tensor("cp", [128, 8 * 144], F32, kind="ExternalInput").ap()
    xn_scr = nc.dram_tensor("xn_scr", [T, D], BF16, kind="Internal").ap()
    r_out = nc.dram_tensor("r_out", [128, 32 * 512], BF16, kind="ExternalOutput").ap()
    st_out = nc.dram_tensor("st_out", [1, 4], F32, kind="ExternalOutput").ap()

    NB = T // 128
    GAM = [1.0 - 2.0 ** (-5.0 - g) for g in range(16)]
    s = Sched()
    with ExitStack() as ctx:
        E = ctx.enter_context
        sb = {}
        xn_fm = E(nc.sbuf_tensor("xn_fm", [128, 8 * T], BF16))
        v_tm = E(nc.sbuf_tensor("v_tm", [128, NB * 512], BF16))
        wqh = [E(nc.sbuf_tensor(f"wqh{i}", [128, 8 * 64], BF16)) for i in range(2)]
        wkh = [E(nc.sbuf_tensor(f"wkh{i}", [128, 8 * 64], BF16)) for i in range(2)]
        Qh = E(nc.sbuf_tensor("Qh", [64, T], BF16))
        Qs = E(nc.sbuf_tensor("Qs", [64, T], BF16))
        Ks = E(nc.sbuf_tensor("Ks", [64, T], BF16))
        pkvf = E(nc.sbuf_tensor("pkvf", [64, 8 * 64], F32))
        pkvs = E(nc.sbuf_tensor("pkvs", [64, 8 * 64], BF16))
        aBt = E(nc.sbuf_tensor("aBt", [64, 8 * 512], BF16))
        bBt = E(nc.sbuf_tensor("bBt", [64, 8 * 128], BF16))
        triuit = E(nc.sbuf_tensor("triuit", [128, 128], BF16))
        cpt = E(nc.sbuf_tensor("cpt", [128, 8 * 144], F32))
        ps_sc = [E(nc.psum_tensor(f"ps_sc{i}", [128, 512], F32)) for i in range(2)]
        ps_r = [E(nc.psum_tensor(f"ps_r{i}", [128, 512], F32)) for i in range(4)]
        ps_n = E(nc.psum_tensor("ps_n", [1, 512], F32))

        t_c = [s.dma("sync", lambda h: h.dma_start(out=triuit[:], in_=triui[:, :]))]
        t_c.append(s.dma("sync", lambda h: h.dma_start(out=cpt[:], in_=cp[:, :])))
        t_c.append(s.dma("sync", lambda h: h.dma_start(out=aBt[:], in_=aB[:, :])))
        t_c.append(s.dma("sync", lambda h: h.dma_start(out=bBt[:], in_=bB[:, :])))
        t_c.append(s.dma("sync", lambda h: h.dma_start(out=pkvf[:], in_=pkv[:, :])))
        t_c.append(s.emit("vector", lambda h: h.tensor_scalar(
            pkvs[:], pkvf[:], 1024.0, None, ALU.mult), waits=t_c[-1:]))

        with ExitStack() as ctx2:
            E2 = ctx2.enter_context
            sb["ln_xt"] = [E2(nc.sbuf_tensor(f"lnxt{i}", [128, D], F32)) for i in range(2)]
            sb["ln_sq"] = E2(nc.sbuf_tensor("ln_sq", [128, D], F32))
            sb["ln_xn"] = E2(nc.sbuf_tensor("ln_xnb", [128, D], BF16))
            sb["ln_st"] = E2(nc.sbuf_tensor("ln_st", [128, 8], F32))
            wvt = E2(nc.sbuf_tensor("wvt", [128, 8 * 512], BF16))
            ln_toks = emit_ln(s, sb, xb, xn_scr, NB)
            tr_toks = emit_transpose_in(s, xn_fm, xn_scr, T, ln_toks)
            w_v = emit_load_w_bf16(s, wvt, wv, 8, 512)
            prev_v = {}
            for tt in range(NB):
                ps = ps_sc[tt % 2]
                wl = tr_toks + w_v + ([prev_v[id(ps)]] if id(ps) in prev_v else [])
                tok = None
                for kc in range(8):
                    tok = s.emit("tensor", lambda h, kc=kc, tt=tt, ps=ps: h.matmul(
                        ps[:], xn_fm[:, kc * T + tt * 128: kc * T + (tt + 1) * 128],
                        wvt[:, kc * 512:(kc + 1) * 512],
                        start=(kc == 0), stop=(kc == 7)), waits=(wl if kc == 0 else []))
                te = s.emit("vector", lambda h, tt=tt, ps=ps: h.tensor_copy(
                    v_tm[:, tt * 512:(tt + 1) * 512], ps[:]), waits=[tok])
                prev_v[id(ps)] = te

        bar0 = [(e, s.cnt[e]) for e in ENGINES]
        r_buf = E(nc.sbuf_tensor("r_buf", [128, NB * 512], BF16))
        sqb = E(nc.sbuf_tensor("sqb", [128, 512], BF16))
        ones128 = E(nc.sbuf_tensor("ones128", [128, 1], BF16))
        strow = E(nc.sbuf_tensor("strow", [1, 16], F32))
        pt = [E(nc.sbuf_tensor(f"pt{i}", [128, 512], BF16)) for i in range(2)]
        tz = s.emit("gpsimd", lambda h: h.memset(ones128[:], 1.0), waits=bar0)

        prev = {}
        wtok = [None, None]
        for hh in range(HL):
            hb = hh % 2
            wl = list(bar0) + ([wtok[hb]] if wtok[hb] else [])
            twq = s.dma("gpsimd", lambda h, hh=hh, hb=hb: h.dma_start(
                out=wqh[hb][:], in_=wq[:, hh * 64:(hh + 1) * 64].rearrange("(kc p) c -> p kc c", p=128)),
                waits=wl)
            twk = s.dma("gpsimd", lambda h, hh=hh, hb=hb: h.dma_start(
                out=wkh[hb][:], in_=wk[:, hh * 64:(hh + 1) * 64].rearrange("(kc p) c -> p kc c", p=128)),
                waits=wl)
            for side in ("q", "k"):
                wt_, tws = (wqh, twq) if side == "q" else (wkh, twk)
                for tc in range(8):
                    ps = ps_sc[tc % 2]
                    wl = list(bar0) + [tws] + ([prev[id(ps)]] if id(ps) in prev else [])
                    tok = None
                    for kc in range(8):
                        tok = s.emit("tensor", lambda h, kc=kc, tc=tc, hb=hb, wt_=wt_, ps=ps: h.matmul(
                            ps[0:64, :], wt_[hb][:, kc * 64:(kc + 1) * 64],
                            xn_fm[:, kc * T + tc * 512: kc * T + (tc + 1) * 512],
                            start=(kc == 0), stop=(kc == 7)), waits=(wl if kc == 0 else []))
                    if side == "q":
                        tev = s.emit("scalar", lambda h, ps=ps, tc=tc: h.copy(
                            Qh[:, tc * 512:(tc + 1) * 512], ps[0:64, :]), waits=[tok])
                        tqs = s.emit("vector", lambda h, ps=ps, tc=tc, hh=hh: h.tensor_tensor(
                            Qs[:, tc * 512:(tc + 1) * 512], ps[0:64, :],
                            aBt[:, hh * 512:(hh + 1) * 512], ALU.mult), waits=[tok, tev] + t_c)
                        prev[id(ps)] = tqs
                        prev["qev"] = tev
                    else:
                        tks = s.emit("vector", lambda h, ps=ps, tc=tc, hh=hh: h.tensor_tensor(
                            Ks[:, tc * 512:(tc + 1) * 512].rearrange("p (a m) -> p a m", a=4),
                            ps[0:64, :].rearrange("p (a m) -> p a m", a=4),
                            bBt[:, hh * 128:(hh + 1) * 128].unsqueeze(1).broadcast_to([64, 4, 128]),
                            ALU.mult), waits=[tok] + t_c)
                        prev[id(ps)] = tks
            pk_done = [("vector", s.cnt["vector"]), ("scalar", s.cnt["scalar"])]

            for tcq in range(8):
                for j in range(4):
                    tb_ = 4 * tcq + j
                    tcx = s.emit("tensor", lambda h, j=j, tb_=tb_, hh=hh: h.matmul(
                        ps_r[j][:, 0:64], Qh[:, tb_ * 128:(tb_ + 1) * 128],
                        pkvs[:, hh * 64:(hh + 1) * 64], start=True, stop=False),
                        waits=pk_done + t_c + ([prev[("pr", j)]] if ("pr", j) in prev else []))
                    prev[("prx", j)] = tcx
                for sb_ in range(4 * tcq + 4):
                    pp = ps_sc[sb_ % 2]
                    pb = sb_ % 2
                    wl = pk_done + ([prev[("pt", pb)]] if ("pt", pb) in prev else []) + \
                         ([prev[id(pp)]] if id(pp) in prev else [])
                    tmm = s.emit("tensor", lambda h, tcq=tcq, sb_=sb_, pp=pp: h.matmul(
                        pp[:], Ks[:, sb_ * 128:(sb_ + 1) * 128],
                        Qs[:, tcq * 512:(tcq + 1) * 512],
                        start=True, stop=True), waits=wl)
                    j0 = sb_ - 4 * tcq
                    ci = cp_index(hh, tcq, sb_)
                    tex = s.emit("vector", lambda h, pb=pb, pp=pp, ci=ci: h.tensor_scalar(
                        pt[pb][:], pp[:], cpt[:, ci:ci + 1], None, ALU.mult), waits=[tmm] + t_c)
                    if 0 <= j0 <= 3:
                        tex = s.emit("vector", lambda h, pb=pb, pp=pp, j0=j0, ci=ci: h.scalar_tensor_tensor(
                            pt[pb][:, j0 * 128:(j0 + 1) * 128], pp[:, j0 * 128:(j0 + 1) * 128],
                            cpt[:, ci:ci + 1], triuit[:], ALU.mult, ALU.mult), waits=[tex])
                    prev[id(pp)] = tex
                    tpv = None
                    for j in range(max(0, j0), 4):
                        tpv = s.emit("tensor", lambda h, pb=pb, j=j, sb_=sb_, hh=hh: h.matmul(
                            ps_r[j][:, 0:64], pt[pb][:, j * 128:(j + 1) * 128],
                            v_tm[:, sb_ * 512 + hh * 64: sb_ * 512 + (hh + 1) * 64],
                            start=False, stop=(sb_ == 4 * tcq + j)),
                            waits=[tex, prev[("prx", j)]])
                    prev[("pt", pb)] = tpv if tpv is not None else tex
                for j in range(4):
                    tfo = s.emit("scalar", lambda h, j=j, tcq=tcq, hh=hh: h.copy(
                        r_buf[:, (4 * tcq + j) * 512 + hh * 64: (4 * tcq + j) * 512 + (hh + 1) * 64],
                        ps_r[j][:, 0:64]), waits=[prev[("pt", 0)], prev[("pt", 1)]])
                    prev[("pr", j)] = tfo
            wtok[hb] = prev[("pt", 0)]

        rdone = [prev[("pr", j)] for j in range(4)]
        tok1 = tok2 = None
        for i in range(NB):
            tok1 = s.emit("tensor", lambda h, i=i: h.matmul(
                ps_n[:], ones128[:], r_buf[:, i * 512:(i + 1) * 512],
                start=(i == 0), stop=(i == NB - 1)), waits=(rdone + [tz] if i == 0 else []))
        ts1 = s.emit("vector", lambda h: h.reduce_sum(strow[:, 0:1], ps_n[:], AX), waits=[tok1])
        sq_ev = []
        for i in range(NB):
            tsq = s.emit("scalar", lambda h, i=i: h.square(sqb[:], r_buf[:, i * 512:(i + 1) * 512]),
                         waits=rdone + [ts1] + sq_ev[-1:])
            tok2 = s.emit("tensor", lambda h, i=i: h.matmul(
                ps_n[:], ones128[:], sqb[:], start=(i == 0), stop=(i == NB - 1)),
                waits=[tsq] + ([ts1] if i == 0 else []))
            sq_ev.append(tok2)
        ts2 = s.emit("vector", lambda h: h.reduce_sum(strow[:, 1:2], ps_n[:], AX), waits=[tok2])
        tds = s.dma("sync", lambda h: h.dma_start(out=st_out[0:1, 0:2], in_=strow[:, 0:2]), waits=[ts1, ts2])
        for part in range(4):
            s.dma("sync", lambda h, part=part: h.dma_start(
                out=r_out[:, part * 8 * 512:(part + 1) * 8 * 512],
                in_=r_buf[:, part * 8 * 512:(part + 1) * 8 * 512]), waits=rdone + [ts2])
        s.build(nc, ctx)
    return nc


def cp_index(hh, tcq, sb_):
    return hh * 144 + 2 * tcq * tcq + 2 * tcq + sb_


def make_cp_table(p):
    tab = np.zeros(8 * 144, np.float32)
    for hh in range(8):
        g = 1.0 - 2.0 ** (-5.0 - (hh + 8 * p))
        for tcq in range(8):
            for sb_ in range(4 * tcq + 4):
                e = 512 * tcq - 128 * sb_
                tab[cp_index(hh, tcq, sb_)] = np.float32(g) ** e
    return np.ascontiguousarray(np.broadcast_to(tab, (128, 8 * 144)))


def make_ab_tables(p):
    aB = np.zeros((64, 8 * 512), np.float32)
    bB = np.zeros((64, 8 * 128), np.float32)
    for hh in range(8):
        g = 1.0 - 2.0 ** (-5.0 - (hh + 8 * p))
        aB[:, hh * 512:(hh + 1) * 512] = g ** np.arange(512)[None, :]
        bB[:, hh * 128:(hh + 1) * 128] = g ** (-np.arange(128)[None, :].astype(np.float64))
    return aB.astype(NPBF16), bB.astype(NPBF16)


def run_retnet_sublayer(x, ins):
    wqkv = ins["L1_attn_b_W_qkv"]
    in_maps = []
    for c in range(8):
        b, p = c // 2, c % 2
        hs = p * 512
        aB, bB = make_ab_tables(p)
        in_maps.append(dict(
            xb=np.ascontiguousarray(x[b]),
            wq=np.ascontiguousarray(wqkv[:, hs:hs + 512]),
            wk=np.ascontiguousarray(wqkv[:, 1024 + hs:1024 + hs + 512]),
            wv=np.ascontiguousarray(wqkv[:, 2048 + hs:2048 + hs + 512]),
            pkv=np.ascontiguousarray(ins["L1_attn_b_past_kv"][p * 8:(p + 1) * 8].transpose(1, 0, 2).reshape(64, 512)),
            aB=aB, bB=bB, triui=TRIUI, cp=make_cp_table(p),
        ))
    res = run_bass_kernel_spmd(prog("rn_a", build_rn_a), in_maps, core_ids=list(range(8)))
    r_full = [np.concatenate([o_to_tokmajor(res.results[2 * b]["r_out"]),
                              o_to_tokmajor(res.results[2 * b + 1]["r_out"])], axis=1)
              for b in range(4)]
    gnw_col = col_by_head(ins["L1_attn_b_gn_weight"])
    gnb_col = col_by_head(ins["L1_attn_b_gn_bias"])
    in_maps = []
    for c in range(8):
        b, p = c // 2, c % 2
        stats = np.concatenate([res.results[2 * b]["st_out"][0],
                                res.results[2 * b + 1]["st_out"][0]]).reshape(1, 8)
        in_maps.append(dict(
            x_fm=fm(x[b, p * TL:(p + 1) * TL]),
            r_fm=fm(r_full[b][p * TL:(p + 1) * TL]).astype(NPBF16),
            st2=np.ascontiguousarray(stats.astype(np.float32)),
            gnw_col=gnw_col, gnb_col=gnb_col,
        ))
    res = run_bass_kernel_spmd(prog("rn_b", build_rn_b), in_maps, core_ids=list(range(8)))
    xn = np.empty_like(x)
    for c in range(8):
        b, p = c // 2, c % 2
        xn[b, p * TL:(p + 1) * TL] = unfm(res.results[c]["xo"], TL)
    return xn


def col_by_head(g):
    feat = np.repeat(g, 64)
    return np.ascontiguousarray(feat.reshape(8, 128).T.astype(np.float32))


def build_rn_b():
    nc = bass.Bass("TRN2", target_bir_lowering=False)
    x_fm = nc.dram_tensor("x_fm", [128, 8 * TL], F32, kind="ExternalInput").ap()
    r_fm = nc.dram_tensor("r_fm", [128, 8 * TL], BF16, kind="ExternalInput").ap()
    st2 = nc.dram_tensor("st2", [1, 8], F32, kind="ExternalInput").ap()
    gnw_col = nc.dram_tensor("gnw_col", [128, 8], F32, kind="ExternalInput").ap()
    gnb_col = nc.dram_tensor("gnb_col", [128, 8], F32, kind="ExternalInput").ap()
    bscr = nc.dram_tensor("bscr", [1, 256], F32, kind="Internal").ap()
    xo = nc.dram_tensor("xo", [128, 8 * TL], F32, kind="ExternalOutput").ap()
    NTOT = float(16 * 64 * T)
    s = Sched()
    with ExitStack() as ctx:
        E = ctx.enter_context
        xt = E(nc.sbuf_tensor("xt", [128, 8 * TL], F32))
        rt = E(nc.sbuf_tensor("rt", [128, 8 * TL], BF16))
        stt_ = E(nc.sbuf_tensor("stt_", [1, 16], F32))
        gw = E(nc.sbuf_tensor("gw", [128, 8], F32))
        gb = E(nc.sbuf_tensor("gb", [128, 8], F32))
        row = E(nc.sbuf_tensor("row", [1, 256], F32))
        colb = E(nc.sbuf_tensor("colb", [128, 2], F32))
        alf = E(nc.sbuf_tensor("alf", [128, 8], F32))
        bet = E(nc.sbuf_tensor("bet", [128, 8], F32))
        tmp = E(nc.sbuf_tensor("tmp", [128, 512], F32))
        on1 = E(nc.sbuf_tensor("on1", [1, 256], F32))
        xob = E(nc.sbuf_tensor("xob", [128, 8 * TL], F32))

        tx = s.dma("sync", lambda h: h.dma_start(out=xt[:], in_=x_fm[:, :]))
        tr = s.dma("sync", lambda h: h.dma_start(out=rt[:], in_=r_fm[:, :]))
        tst = s.dma("sync", lambda h: h.dma_start(out=stt_[0:1, 0:8], in_=st2[0:1, :]))
        tgw = s.dma("sync", lambda h: h.dma_start(out=gw[:], in_=gnw_col[:, :]))
        tgb = s.dma("sync", lambda h: h.dma_start(out=gb[:], in_=gnb_col[:, :]))
        to1 = s.emit("gpsimd", lambda h: h.memset(on1[:], 1.0))
        t1 = s.emit("vector", lambda h: h.tensor_tensor(stt_[0:1, 8:9], stt_[0:1, 0:1], stt_[0:1, 4:5], ALU.add), waits=[tst])
        t2 = s.emit("vector", lambda h: h.tensor_tensor(stt_[0:1, 9:10], stt_[0:1, 1:2], stt_[0:1, 5:6], ALU.add), waits=[t1])
        tmn = s.emit("vector", lambda h: h.tensor_scalar(stt_[0:1, 10:11], stt_[0:1, 8:9], 1.0 / NTOT, None, ALU.mult), waits=[t2])
        tm2 = s.emit("vector", lambda h: h.tensor_scalar(stt_[0:1, 11:12], stt_[0:1, 10:11], stt_[0:1, 10:11], None, ALU.mult), waits=[tmn])
        tvr = s.emit("vector", lambda h: h.scalar_tensor_tensor(
            stt_[0:1, 12:13], stt_[0:1, 9:10], 1.0 / NTOT, stt_[0:1, 11:12], ALU.mult, ALU.subtract), waits=[tm2])
        tve = s.emit("vector", lambda h: h.tensor_scalar(stt_[0:1, 13:14], stt_[0:1, 12:13], 1e-5, None, ALU.add), waits=[tvr])
        tsd = s.emit("scalar", lambda h: h.sqrt(stt_[0:1, 14:15], stt_[0:1, 13:14]), waits=[tve])
        trc = s.emit("vector", lambda h: h.reciprocal(stt_[0:1, 15:16], stt_[0:1, 14:15]), waits=[tsd])
        tr1 = s.emit("vector", lambda h: h.tensor_scalar(row[:, 0:128], on1[:, 0:128], stt_[0:1, 15:16], None, ALU.mult), waits=[trc, to1])
        tr2 = s.emit("vector", lambda h: h.tensor_scalar(row[:, 128:256], on1[:, 0:128], stt_[0:1, 10:11], -1.0, ALU.mult, ALU.mult), waits=[tr1])
        td1 = s.dma("sync", lambda h: h.dma_start(out=bscr[0:1, :], in_=row[:, :]), waits=[tr2])
        td2a = s.dma("sync", lambda h: h.dma_start(
            out=colb[:, 0:1], in_=bscr[0:1, 0:128].rearrange("a (p c) -> (a p) c", c=1)), waits=[td1])
        td2 = s.dma("sync", lambda h: h.dma_start(
            out=colb[:, 1:2], in_=bscr[0:1, 128:256].rearrange("a (p c) -> (a p) c", c=1)), waits=[td1])
        ta = s.emit("vector", lambda h: h.tensor_scalar(alf[:], gw[:], colb[:, 0:1], None, ALU.mult), waits=[td2a, td2, tgw])
        tb = s.emit("vector", lambda h: h.scalar_tensor_tensor(
            bet[:], alf[:], colb[:, 1:2], gb[:], ALU.mult, ALU.add), waits=[ta, tgb])
        prev = None
        for oc in range(8):
            for tc in range(TL // 512):
                sl = slice(oc * TL + tc * 512, oc * TL + (tc + 1) * 512)
                tt1 = s.emit("vector", lambda h, sl=sl, oc=oc: h.tensor_scalar(
                    tmp[:], rt[:, sl], alf[:, oc:oc + 1], bet[:, oc:oc + 1], ALU.mult, ALU.add),
                    waits=[tb, tr] + ([prev] if prev else []))
                tt2 = s.emit("vector", lambda h, sl=sl: h.tensor_tensor(
                    xob[:, sl], tmp[:], xt[:, sl], ALU.add), waits=[tt1, tx])
                prev = tt2
        for j2 in range(8):
            s.dma("sync", lambda h, j2=j2: h.dma_start(
                out=xo[:, j2 * TL:(j2 + 1) * TL], in_=xob[:, j2 * TL:(j2 + 1) * TL]), waits=[prev])
        s.build(nc, ctx)
    return nc


def kernel(**inputs):
    ins = {k: np.asarray(v) for k, v in inputs.items()}
    x = np.ascontiguousarray(ins["L0_attn_a_x"], dtype=np.float32)
    qcache = {}
    x = run_deltanet_sublayer(x, ins)
    x = run_moe_sublayer(x, ins, "L0_ffn")
    x = run_rope_attn_sublayer(x, ins)
    x = run_moe_sublayer(x, ins, "L0_ffn")
    x = run_alibi_attn_sublayer(x, ins)
    x = run_ffn1_sublayer(x, ins, qcache)
    x = run_retnet_sublayer(x, ins)
    x = run_ffn1_sublayer(x, ins, qcache)
    return x


# version 09
# speedup vs baseline: 1.1144x; 1.0582x over previous
import numpy as np
import ml_dtypes
from contextlib import ExitStack

import concourse.bass as bass
import concourse.mybir as mybir
from concourse.bass_utils import run_bass_kernel_spmd

F32 = mybir.dt.float32
BF16 = mybir.dt.bfloat16
I32 = mybir.dt.int32
AF = mybir.ActivationFunctionType
ALU = mybir.AluOpType
AX = mybir.AxisListType.X
NPBF16 = ml_dtypes.bfloat16

D = 1024
T = 4096
TL = 2048
HL = 8
HD = 64
EPS_LN = 1e-5

ENGINES = ("tensor", "vector", "scalar", "sync", "gpsimd")
SEMS = ENGINES + ("gpdma",)

class Sched:
    def __init__(self):
        self.q = {e: [] for e in ENGINES}
        self.cnt = {e: 0 for e in SEMS}

    def emit(self, eng, fn, waits=(), inc=1):
        sem = "gpdma" if (eng == "gpsimd" and inc == 16) else eng
        self.cnt[sem] += inc
        tok = (sem, self.cnt[sem])
        self.q[eng].append((fn, tuple(waits), inc, sem))
        return tok

    def dma(self, eng, fn, waits=()):
        return self.emit(eng, fn, waits, inc=16)

    def build(self, nc, ctx):
        sems = {e: ctx.enter_context(nc.semaphore(f"s_{e}")) for e in SEMS}
        block = ctx.enter_context(nc.Block())
        sched = self

        def replay(eng_name, h):
            hw = {e: 0 for e in SEMS}
            cnt_self = {e: 0 for e in SEMS}
            for fn, waits, inc, sem in sched.q[eng_name]:
                if inc == 16 and cnt_self[sem] > hw[sem]:
                    h.wait_ge(sems[sem], cnt_self[sem])
                    hw[sem] = cnt_self[sem]
                for (we, wv) in waits:
                    if wv > hw[we]:
                        h.wait_ge(sems[we], wv)
                        hw[we] = wv
                cnt_self[sem] += inc
                inst = fn(h)
                if inst is not None:
                    inst.then_inc(sems[sem], inc)
                else:
                    h.nop().then_inc(sems[sem], inc)

        @block.tensor
        def _(h):
            replay("tensor", h)

        @block.vector
        def _(h):
            replay("vector", h)

        @block.scalar
        def _(h):
            replay("scalar", h)
            for e in SEMS:
                if sched.cnt[e] > 0 and e != "scalar":
                    h.wait_ge(sems[e], sched.cnt[e])

        @block.sync
        def _(h):
            replay("sync", h)

        @block.gpsimd
        def _(h):
            replay("gpsimd", h)



def emit_ln(s, sb, x_dram, xn_dram, ntiles, pre_waits=()):
    xt, sq, xnb, st = sb["ln_xt"], sb["ln_sq"], sb["ln_xn"], sb["ln_st"]
    out_toks = []
    last = [None, None]
    for i in range(ntiles):
        b = i % 2
        w = list(pre_waits) + ([last[b]] if last[b] else [])
        tld = s.dma("sync", lambda h, i=i, b=b: h.dma_start(
            out=xt[b][:], in_=x_dram[i * 128:(i + 1) * 128, :]), waits=w)
        tsq = s.emit("scalar", lambda h, b=b: h.square(sq[:], xt[b][:]), waits=[tld])
        t1 = s.emit("vector", lambda h, b=b: h.reduce_sum(st[:, 0:1], xt[b][:], AX), waits=[tld])
        t2 = s.emit("vector", lambda h: h.reduce_sum(st[:, 1:2], sq[:], AX), waits=[tsq, t1])
        tm = s.emit("vector", lambda h: h.tensor_scalar(st[:, 2:3], st[:, 0:1], 1.0 / D, None, ALU.mult), waits=[t2])
        tm2 = s.emit("vector", lambda h: h.tensor_scalar(st[:, 3:4], st[:, 2:3], st[:, 2:3], None, ALU.mult), waits=[tm])
        tv = s.emit("vector", lambda h: h.scalar_tensor_tensor(
            st[:, 4:5], st[:, 1:2], 1.0 / D, st[:, 3:4], ALU.mult, ALU.subtract), waits=[tm2])
        tve = s.emit("vector", lambda h: h.tensor_scalar(st[:, 5:6], st[:, 4:5], EPS_LN, None, ALU.add), waits=[tv])
        tsd = s.emit("scalar", lambda h: h.sqrt(st[:, 6:7], st[:, 5:6]), waits=[tve])
        trs = s.emit("vector", lambda h: h.reciprocal(st[:, 7:8], st[:, 6:7]), waits=[tsd])
        txn = s.emit("vector", lambda h, b=b: h.tensor_scalar(
            xnb[:], xt[b][:], st[:, 2:3], st[:, 7:8], ALU.subtract, ALU.mult),
            waits=[trs] + ([sb["_ln_prev_store"]] if "_ln_prev_store" in sb else []))
        tst = s.dma("sync", lambda h, i=i: h.dma_start(
            out=xn_dram[i * 128:(i + 1) * 128, :], in_=xnb[:]), waits=[txn])
        last[b] = txn
        sb["_ln_prev_store"] = tst
        out_toks.append(tst)
    return out_toks


def emit_transpose_in(s, sb_fm, xn_dram, Tp, waits):
    toks = []
    for j in range(D // 128):
        t = s.dma("sync", lambda h, j=j: h.dma_start_transpose(
            sb_fm[:, j * Tp:(j + 1) * Tp], xn_dram[:, j * 128:(j + 1) * 128]), waits=waits)
        toks.append(t)
    return toks


def emit_load_w_bf16(s, dst, src_dram, kchunks, ncols, waits=()):
    toks = []
    for kc in range(kchunks):
        t = s.dma("gpsimd", lambda h, kc=kc: h.dma_start(
            out=dst[:, kc * ncols:(kc + 1) * ncols],
            in_=src_dram[kc * 128:(kc + 1) * 128, :]), waits=waits)
        toks.append(t)
    return toks



def build_dn_a(debug=False, serial_scan=False, nchunks=None, skip_ph1=False, nwaves=99):
    nc = bass.Bass("TRN2", target_bir_lowering=False)
    xb = nc.dram_tensor("xb", [T, D], F32, kind="ExternalInput").ap()
    wq = nc.dram_tensor("wq", [D, 512], F32, kind="ExternalInput").ap()
    wk = nc.dram_tensor("wk", [D, 512], F32, kind="ExternalInput").ap()
    wv = nc.dram_tensor("wv", [D, 512], F32, kind="ExternalInput").ap()
    wbe = nc.dram_tensor("wbe", [D, 8], F32, kind="ExternalInput").ap()
    bq_row = nc.dram_tensor("bq_row", [1, 512], F32, kind="ExternalInput").ap()
    bk_row = nc.dram_tensor("bk_row", [1, 512], F32, kind="ExternalInput").ap()
    bv_row = nc.dram_tensor("bv_row", [1, 512], F32, kind="ExternalInput").ap()
    bbe_row = nc.dram_tensor("bbe_row", [1, 8], F32, kind="ExternalInput").ap()
    ident = nc.dram_tensor("ident", [128, 128], BF16, kind="ExternalInput").ap()
    stril = nc.dram_tensor("stril", [128, 128], BF16, kind="ExternalInput").ap()
    triui = nc.dram_tensor("triui", [128, 128], BF16, kind="ExternalInput").ap()
    xn_scr = nc.dram_tensor("xn_scr", [T, D], BF16, kind="Internal").ap()
    w_scr = nc.dram_tensor("w_scr", [(T // 128) * 8 * 128, 128], BF16, kind="Internal").ap()
    o_out = nc.dram_tensor("o_out", [128, 4 * T], BF16, kind="ExternalOutput").ap()
    if debug:
        dbg_q = nc.dram_tensor("dbg_q", [128, 4 * T], BF16, kind="ExternalOutput").ap()
        dbg_ktm = nc.dram_tensor("dbg_ktm", [128, 32 * 512], BF16, kind="ExternalOutput").ap()
        dbg_vtm = nc.dram_tensor("dbg_vtm", [128, 32 * 512], BF16, kind="ExternalOutput").ap()
        dbg_beta = nc.dram_tensor("dbg_beta", [128, 32 * 8], F32, kind="ExternalOutput").ap()
        dbg_kfm = nc.dram_tensor("dbg_kfm", [128, 4 * T], BF16, kind="ExternalOutput").ap()
        dbg_xn = nc.dram_tensor("dbg_xn", [128, D], BF16, kind="ExternalOutput").ap()

    s = Sched()
    with ExitStack() as ctx:
        E = ctx.enter_context
        sb = {}
        identt = E(nc.sbuf_tensor("identt", [128, 128], BF16))
        strilt = E(nc.sbuf_tensor("strilt", [128, 128], BF16))
        triuit = E(nc.sbuf_tensor("triuit", [128, 128], BF16))
        ones1 = E(nc.sbuf_tensor("ones1", [1, 128], BF16))
        bqr = E(nc.sbuf_tensor("bqr", [1, 512], BF16))
        bkr = E(nc.sbuf_tensor("bkr", [1, 512], BF16))
        bvr = E(nc.sbuf_tensor("bvr", [1, 512], BF16))
        bber = E(nc.sbuf_tensor("bber", [1, 8], BF16))
        q_tm = E(nc.sbuf_tensor("q_tm", [128, 32 * 512], BF16))
        k_tm = E(nc.sbuf_tensor("k_tm", [128, 32 * 512], BF16))
        v_tm = E(nc.sbuf_tensor("v_tm", [128, 32 * 512], BF16))
        beta = E(nc.sbuf_tensor("beta", [128, 32 * 8], F32))
        nbeta = E(nc.sbuf_tensor("nbeta", [128, 32 * 8], F32))

        t_c = []
        t_c.append(s.dma("sync", lambda h: h.dma_start(out=identt[:], in_=ident[:, :])))
        t_c.append(s.dma("sync", lambda h: h.dma_start(out=strilt[:], in_=stril[:, :])))
        t_c.append(s.dma("sync", lambda h: h.dma_start(out=triuit[:], in_=triui[:, :])))
        t_c.append(s.emit("gpsimd", lambda h: h.memset(ones1[:], 1.0)))
        t_c.append(s.dma("gpsimd", lambda h: h.dma_start(out=bqr[:], in_=bq_row[:, :])))
        t_c.append(s.dma("gpsimd", lambda h: h.dma_start(out=bkr[:], in_=bk_row[:, :])))
        t_c.append(s.dma("gpsimd", lambda h: h.dma_start(out=bvr[:], in_=bv_row[:, :])))
        t_c.append(s.dma("gpsimd", lambda h: h.dma_start(out=bber[:], in_=bbe_row[:, :])))

        with ExitStack() as ctx2:
            E2 = ctx2.enter_context
            xn_fm = E2(nc.sbuf_tensor("xn_fm", [128, 8 * T], BF16))
            sb["ln_xt"] = [E2(nc.sbuf_tensor(f"ln_xt{i}", [128, D], F32)) for i in range(2)]
            sb["ln_sq"] = E2(nc.sbuf_tensor("ln_sq", [128, D], F32))
            sb["ln_xn"] = E2(nc.sbuf_tensor("ln_xnb", [128, D], BF16))
            sb["ln_st"] = E2(nc.sbuf_tensor("ln_st", [128, 8], F32))
            wqt = E2(nc.sbuf_tensor("wqt", [128, 8 * 512], BF16))
            wkt = E2(nc.sbuf_tensor("wkt", [128, 8 * 512], BF16))
            wvt = E2(nc.sbuf_tensor("wvt", [128, 8 * 512], BF16))
            wbet = E2(nc.sbuf_tensor("wbet", [128, 8 * 8], BF16))
            ksq = sb["ln_sq"]
            kst = sb["ln_xt"][0]

            def krw(kb):
                return (sb["ln_xt"][0] if kb == 0 else sb["ln_sq"])[:, 512:1024]
            ps_a = E2(nc.psum_tensor("ps_a", [128, 512], F32))
            ps_b = E2(nc.psum_tensor("ps_b", [128, 512], F32))
            ps_c = E2(nc.psum_tensor("ps_c", [128, 512], F32))
            ps_s = E2(nc.psum_tensor("ps_s", [128, 8], F32))

            ln_toks = emit_ln(s, sb, xb, xn_scr, T // 128)
            tr_toks = emit_transpose_in(s, xn_fm, xn_scr, T, ln_toks)
            wq_t = emit_load_w_bf16(s, wqt, wq, 8, 512)
            wk_t = emit_load_w_bf16(s, wkt, wk, 8, 512)
            wv_t = emit_load_w_bf16(s, wvt, wv, 8, 512)
            wbe_t = emit_load_w_bf16(s, wbet, wbe, 8, 8)

            ev_prev = {}
            for tt in range(T // 128):
                for which in ("k", "be", "v", "q"):
                    wt = {"k": wkt, "v": wvt, "be": wbet, "q": wqt}[which]
                    br = {"k": bkr, "v": bvr, "be": bber, "q": bqr}[which]
                    ncols = 8 if which == "be" else 512
                    ps = ps_s if which == "be" else [ps_a, ps_b][tt % 2]
                    w = tr_toks + {"k": wk_t, "v": wv_t, "be": wbe_t, "q": wq_t}[which] + t_c
                    key = (id(ps),)
                    if key in ev_prev:
                        w = w + [ev_prev[key]]
                    tok = None
                    for kc in range(8):
                        tok = s.emit("tensor", lambda h, kc=kc, tt=tt, wt=wt, ncols=ncols, ps=ps: h.matmul(
                            ps[:, 0:ncols],
                            xn_fm[:, kc * T + tt * 128: kc * T + (tt + 1) * 128],
                            wt[:, kc * ncols:(kc + 1) * ncols],
                            start=(kc == 0), stop=False), waits=(w if kc == 0 else []))
                    tok = s.emit("tensor", lambda h, br=br, ncols=ncols, ps=ps: h.matmul(
                        ps[:, 0:ncols], ones1[:], br[:, 0:ncols], start=False, stop=True))
                    if which == "v":
                        tev = s.emit("vector", lambda h, tt=tt, ps=ps: h.tensor_tensor(
                            v_tm[:, tt * 512:(tt + 1) * 512].rearrange("p (a m) -> p a m", a=8),
                            ps[:].rearrange("p (a m) -> p a m", a=8),
                            beta[:, tt * 8:(tt + 1) * 8].unsqueeze(2).broadcast_to([128, 8, 64]),
                            ALU.mult), waits=[tok, ev_prev[(id(ps_s),)]])
                        ev_prev[(id(ps),)] = tev
                    elif which == "q":
                        tev = s.emit("vector", lambda h, tt=tt, ps=ps: h.tensor_copy(
                            q_tm[:, tt * 512:(tt + 1) * 512], ps[:]), waits=[tok])
                        ev_prev[(id(ps),)] = tev
                    elif which == "be":
                        tsg = s.emit("scalar", lambda h, tt=tt: h.activation(
                            beta[:, tt * 8:(tt + 1) * 8], ps_s[:], AF.Sigmoid,
                            bias=0.0, scale=1.0), waits=[tok])
                        tnb = s.emit("vector", lambda h, tt=tt: h.tensor_scalar(
                            nbeta[:, tt * 8:(tt + 1) * 8], beta[:, tt * 8:(tt + 1) * 8],
                            -1.0, None, ALU.mult), waits=[tsg])
                        ev_prev[(id(ps_s),)] = tnb
                    else:
                        kb = tt % 2
                        tkr = s.emit("scalar", lambda h, kb=kb, ps=ps: h.copy(
                            krw(kb), ps[:]),
                            waits=[tok] + ([ev_prev[("kraw", kb)]] if ("kraw", kb) in ev_prev else []))
                        ev_prev[(id(ps),)] = tkr
                        tq = s.emit("scalar", lambda h, kb=kb: h.square(ksq[:, 0:512], krw(kb)), waits=[tkr])
                        trd = s.emit("vector", lambda h: h.tensor_reduce(
                            kst[:, 0:8], ksq[:, 0:512].rearrange("p (a m) -> p a m", a=8), AX, ALU.add), waits=[tq])
                        tsd = s.emit("scalar", lambda h: h.sqrt(kst[:, 8:16], kst[:, 0:8]), waits=[trd])
                        tep = s.emit("vector", lambda h: h.tensor_scalar(
                            kst[:, 8:16], kst[:, 8:16], 1e-6, None, ALU.add), waits=[tsd])
                        trc = s.emit("vector", lambda h: h.reciprocal(kst[:, 16:24], kst[:, 8:16]), waits=[tep])
                        tkn = s.emit("vector", lambda h, tt=tt, kb=kb: h.tensor_tensor(
                            k_tm[:, tt * 512:(tt + 1) * 512].rearrange("p (a m) -> p a m", a=8),
                            krw(kb).rearrange("p (a m) -> p a m", a=8),
                            kst[:, 16:24].unsqueeze(2).broadcast_to([128, 8, 64]),
                            ALU.mult), waits=[trc])
                        ev_prev[("kraw", kb)] = tkn

        def ktm_h(h_, c):
            return k_tm[:, c * 512 + h_ * 64: c * 512 + (h_ + 1) * 64]

        def vtm_h(h_, c):
            return v_tm[:, c * 512 + h_ * 64: c * 512 + (h_ + 1) * 64]

        def qtm_h(h_, c):
            return q_tm[:, c * 512 + h_ * 64: c * 512 + (h_ + 1) * 64]

        def beta_ap(h_, c):
            return beta[:, c * 8 + h_: c * 8 + h_ + 1]

        def nbeta_ap(h_, c):
            return nbeta[:, c * 8 + h_: c * 8 + h_ + 1]

        bar0 = [(e, s.cnt[e]) for e in SEMS]
        NC = T // 128
        NRUN = NC if nchunks is None else nchunks
        NSLOT = 3
        with ExitStack() as ctx3:
            E3 = ctx3.enter_context
            slots = []
            for i3 in range(NSLOT):
                slots.append(dict(
                    Kt=E3(nc.sbuf_tensor(f"p1K{i3}", [64, 128], BF16)),
                    Nt=E3(nc.sbuf_tensor(f"p1Nt{i3}", [128, 128], BF16)),
                    NTt=E3(nc.sbuf_tensor(f"p1NT{i3}", [128, 128], BF16)),
                    X=[E3(nc.sbuf_tensor(f"p1X{i3}_{j3}", [128, 128], BF16)) for j3 in range(2)],
                    XT=[E3(nc.sbuf_tensor(f"p1XT{i3}_{j3}", [128, 128], BF16)) for j3 in range(2)],
                    P=[E3(nc.sbuf_tensor(f"p1P{i3}_{j3}", [128, 128], BF16)) for j3 in range(2)],
                    pmF=E3(nc.psum_tensor(f"p1pf{i3}", [128, 512], F32)),
                    pmT=E3(nc.psum_tensor(f"p1pt{i3}", [128, 128], BF16)),
                ))

            def phase1_u(u, sl, start_wait):
                c, hh = u // 8, u % 8
                S = slots[sl]
                pf = S["pmF"]
                w0 = list(bar0) + ([start_wait] if start_wait else [])
                tTK = s.emit("tensor", lambda h, hh=hh, c=c: h.transpose(
                    S["pmT"][0:64, :], ktm_h(hh, c), identt[:]), waits=w0)
                yield
                tKe = s.emit("vector", lambda h: h.tensor_copy(S["Kt"][:], S["pmT"][0:64, :]),
                             waits=[tTK])
                yield
                tKK = s.emit("tensor", lambda h: h.matmul(
                    pf[:, 0:128], S["Kt"][:], S["Kt"][:], start=True, stop=True), waits=[tKe])
                yield
                tN = s.emit("vector", lambda h, hh=hh, c=c: h.scalar_tensor_tensor(
                    S["Nt"][:], pf[:, 0:128], beta_ap(hh, c), strilt[:], ALU.mult, ALU.mult),
                    waits=[tKK])
                yield
                tTr = s.emit("tensor", lambda h: h.transpose(S["pmT"][:], S["Nt"][:], identt[:]),
                             waits=[tN, tKe])
                yield
                tNT = s.emit("vector", lambda h: h.tensor_copy(S["NTt"][:], S["pmT"][:]), waits=[tTr])
                yield
                tP0 = s.emit("vector", lambda h: h.tensor_tensor(
                    S["P"][0][:], identt[:], S["NTt"][:], ALU.subtract), waits=[tNT])
                yield
                Xprev, XTprev = S["Nt"], S["NTt"]
                evX_prev = tNT
                tPe = tP0
                tPm_hist = {}
                for k in range(1, 7):
                    pk = k % 2
                    Xd, XTd = S["X"][pk], S["XT"][pk]
                    wm = [evX_prev, tPe] + ([tN] if k == 1 else [])
                    tXm = s.emit("tensor", lambda h, XTprev=XTprev, Xprev=Xprev: h.matmul(
                        pf[:, 128:256], XTprev[:], Xprev[:], start=True, stop=True), waits=wm)
                    yield
                    tXTm = None
                    if k < 6:
                        tXTm = s.emit("tensor", lambda h, XTprev=XTprev, Xprev=Xprev: h.matmul(
                            pf[:, 256:384], Xprev[:], XTprev[:], start=True, stop=True))
                        yield
                    war = [tPm_hist[k - 2]] if (k - 2) in tPm_hist else []
                    tXe = s.emit("vector", lambda h, Xd=Xd: h.tensor_copy(
                        Xd[:], pf[:, 128:256]), waits=[tXTm or tXm] + war)
                    yield
                    tXTe = None
                    if k < 6:
                        tXTe = s.emit("vector", lambda h, XTd=XTd: h.tensor_copy(
                            XTd[:], pf[:, 256:384]), waits=[tXe])
                        yield
                    tPm = s.emit("tensor", lambda h, Xd=Xd, Pp=S["P"][(k - 1) % 2]: h.matmul(
                        pf[:, 384:512], Xd[:], Pp[:], start=True, stop=True),
                        waits=[tXTe or tXe, tPe])
                    tPm_hist[k] = tPm
                    yield
                    tPe = s.emit("vector", lambda h, Pn=S["P"][k % 2], Pp=S["P"][(k - 1) % 2]: h.tensor_tensor(
                        Pn[:], pf[:, 384:512], Pp[:], ALU.add), waits=[tPm])
                    yield
                    Xprev, XTprev = Xd, XTd
                    evX_prev = tXTe or tXe
                tSt = s.dma("sync" if u % 2 == 0 else "gpsimd",
                            lambda h, u=u, Pn=S["P"][0]: h.dma_start(
                                out=w_scr[u * 128:(u + 1) * 128, :], in_=Pn[:]), waits=[tPe])
                slot_last[sl] = tSt
                yield

            slot_last = [None] * NSLOT
            gens = [None] * NSLOT
            u_next = (NC * 8) if skip_ph1 else 0
            while u_next < NC * 8 or any(g is not None for g in gens):
                for sl in range(NSLOT):
                    if gens[sl] is None and u_next < NC * 8:
                        gens[sl] = phase1_u(u_next, sl, slot_last[sl])
                        u_next += 1
                    if gens[sl] is not None:
                        try:
                            next(gens[sl])
                        except StopIteration:
                            gens[sl] = None

        ph1bar = [(e, s.cnt[e]) for e in SEMS]

        Sf = [E(nc.sbuf_tensor(f"Sf{i2}", [64, 8 * 64], F32)) for i2 in range(2)]
        Sb = E(nc.sbuf_tensor("Sb", [64, 8 * 64], BF16))
        V8 = [E(nc.sbuf_tensor(f"V8_{i2}", [128, 8 * 128], BF16)) for i2 in range(2)]
        Rt8 = E(nc.sbuf_tensor("Rt8", [128, 8 * 64], BF16))
        Ut8 = E(nc.sbuf_tensor("Ut8", [128, 8 * 64], BF16))
        GT8 = E(nc.sbuf_tensor("GT8", [128, 8 * 128], BF16))
        kfc = E(nc.sbuf_tensor("kfc", [64, 8 * 128], BF16))
        qfc = E(nc.sbuf_tensor("qfc", [64, 8 * 128], BF16))
        ost = [E(nc.sbuf_tensor(f"ost{i2}", [64, 8 * 128], BF16)) for i2 in range(2)]
        ppu = E(nc.psum_tensor("ppu", [128, 512], F32))
        put = E(nc.psum_tensor("put", [128, 512], F32))
        pg = [E(nc.psum_tensor(f"pg{i2}", [128, 512], F32)) for i2 in range(2)]
        ppo = [E(nc.psum_tensor(f"ppo{i2}", [64, 512], F32)) for i2 in range(2)]
        pps = E(nc.psum_tensor("pps", [64, 512], F32))
        ptr = E(nc.psum_tensor("ptr", [64, 512], BF16))

        tz0 = s.emit("gpsimd", lambda h: h.memset(Sf[0][:], 0.0), waits=ph1bar)
        tz1 = s.emit("gpsimd", lambda h: h.memset(Sf[1][:], 0.0), waits=ph1bar)
        tz2 = s.emit("gpsimd", lambda h: h.memset(Sb[:], 0.0), waits=ph1bar)

        def v8load(c, waits):
            return s.dma("gpsimd", lambda h, c=c: h.dma_start(
                out=V8[c % 2][:].rearrange("p (hh d) -> p hh d", hh=8),
                in_=w_scr[c * 1024:(c + 1) * 1024, :].rearrange("(hh p) d -> p hh d", p=128)),
                waits=waits)

        o_view = o_out[:, :].rearrange("(hh2 d) (j t) -> d j hh2 t", hh2=2, t=T)

        pv = {}
        if NRUN > 0:
            pv["vl", 0] = v8load(0, ph1bar)
        if NRUN > 1:
            pv["vl", 1] = v8load(1, ph1bar)
        for c in range(NRUN):
            par = c % 2
            cbar = [(e, s.cnt[e]) for e in SEMS] if serial_scan else []
            tp_toks = []
            ev_last = None
            for i4 in range(16):
                h_, which = i4 // 2, i4 % 2
                reg = i4 % 4
                w = []
                if i4 == 0:
                    w = [pv[k] for k in [("kqe", c - 1)] if k in pv] + cbar
                    if c == 0:
                        w = list(ph1bar)
                elif reg == 0:
                    w = [ev_last]
                src_ap = ktm_h(h_, c) if which == 0 else qtm_h(h_, c)
                tok = s.emit("tensor", lambda h, src_ap=src_ap, reg=reg: h.transpose(
                    ptr[:, reg * 128:(reg + 1) * 128], src_ap, identt[:]), waits=w)
                tp_toks.append(tok)
                if reg == 3 or i4 == 15:
                    for j4 in range(i4 - reg, i4 + 1):
                        hj, wj = j4 // 2, j4 % 2
                        dst = kfc if wj == 0 else qfc
                        ev_last = s.emit("vector", lambda h, dst=dst, hj=hj, rj=j4 % 4: h.tensor_copy(
                            dst[:, hj * 128:(hj + 1) * 128], ptr[:, rj * 128:(rj + 1) * 128]),
                            waits=[tp_toks[-1]] if j4 == i4 - reg else [])
            pv["kqe", c] = ev_last
            for h_ in range(HL):
                w = []
                if h_ == 0:
                    w = [pv["kqe", c]] + [pv[k] for k in (("tR", c - 1), ("tSb", c - 1)) if k in pv]
                    if c == 0:
                        w += [tz0, tz1, tz2]
                tok = s.emit("tensor", lambda h, h_=h_: h.matmul(
                    ppu[:, h_ * 64:(h_ + 1) * 64], kfc[:, h_ * 128:(h_ + 1) * 128],
                    Sb[:, h_ * 64:(h_ + 1) * 64], start=True, stop=True), waits=w)
            pv["tU", c] = tok
            for h_ in range(HL):
                w = [pv[k] for k in [("tGm", c - 1)] if k in pv] if h_ == 0 else []
                tok = s.emit("tensor", lambda h, h_=h_: h.matmul(
                    pg[h_ // 4][:, (h_ % 4) * 128:(h_ % 4 + 1) * 128],
                    kfc[:, h_ * 128:(h_ + 1) * 128], qfc[:, h_ * 128:(h_ + 1) * 128],
                    start=True, stop=True), waits=w)
            pv["tG", c] = tok
            for h_ in range(HL):
                w = []
                if h_ == 0:
                    w = [pv["tU", c]] + [pv[k] for k in [("tUt", c - 1)] if k in pv]
                tok = s.emit("vector", lambda h, h_=h_, c=c: h.scalar_tensor_tensor(
                    Rt8[:, h_ * 64:(h_ + 1) * 64], ppu[:, h_ * 64:(h_ + 1) * 64],
                    nbeta_ap(h_, c), vtm_h(h_, c), ALU.mult, ALU.add), waits=w)
            pv["tR", c] = tok
            for h_ in range(HL):
                w = []
                if h_ == 0:
                    w = [pv["tR", c], pv["vl", c]] + [pv[k] for k in [("tUe", c - 1)] if k in pv]
                tok = s.emit("tensor", lambda h, h_=h_, c=c: h.matmul(
                    put[:, h_ * 64:(h_ + 1) * 64], V8[c % 2][:, h_ * 128:(h_ + 1) * 128],
                    Rt8[:, h_ * 64:(h_ + 1) * 64], start=True, stop=True), waits=w)
            pv["tUt", c] = tok
            for h_ in range(HL):
                w = []
                if h_ == 0:
                    w = [pv["tUt", c]] + [pv[k] for k in [("tS", c - 1)] if k in pv]
                tok = s.emit("vector", lambda h, h_=h_: h.tensor_copy(
                    Ut8[:, h_ * 64:(h_ + 1) * 64], put[:, h_ * 64:(h_ + 1) * 64]), waits=w)
            pv["tUe", c] = tok
            for h_ in range(HL):
                w = []
                if h_ == 0:
                    w = [pv["tG", c]] + [pv[k] for k in [("tO2", c - 1)] if k in pv]
                tok = s.emit("vector", lambda h, h_=h_: h.tensor_tensor(
                    GT8[:, h_ * 128:(h_ + 1) * 128],
                    pg[h_ // 4][:, (h_ % 4) * 128:(h_ % 4 + 1) * 128],
                    triuit[:], ALU.mult), waits=w)
            pv["tGm", c] = tok
            if c + 2 < NRUN:
                pv["vl", c + 2] = v8load(c + 2, [pv["tUt", c]])
            for h_ in range(HL):
                w = []
                if h_ == 0:
                    w = [pv["tUe", c], pv["tGm", c]] + [pv[k] for k in [("tOe", c - 1)] if k in pv]
                    if c == 0:
                        w += [tz2]
                s.emit("tensor", lambda h, h_=h_: h.matmul(
                    ppo[h_ // 4][:, (h_ % 4) * 128:(h_ % 4 + 1) * 128],
                    Sb[:, h_ * 64:(h_ + 1) * 64], qfc[:, h_ * 128:(h_ + 1) * 128],
                    start=True, stop=False), waits=w)
                tok = s.emit("tensor", lambda h, h_=h_: h.matmul(
                    ppo[h_ // 4][:, (h_ % 4) * 128:(h_ % 4 + 1) * 128],
                    Ut8[:, h_ * 64:(h_ + 1) * 64], GT8[:, h_ * 128:(h_ + 1) * 128],
                    start=False, stop=True), waits=[])
            pv["tO2", c] = tok
            for h_ in range(HL):
                w = []
                if h_ == 0:
                    w = [pv["tO2", c]] + [pv[k] for k in [("od", c - 2)] if k in pv]
                tok = s.emit("scalar", lambda h, h_=h_, c=c: h.copy(
                    ost[c % 2][:, h_ * 128:(h_ + 1) * 128],
                    ppo[h_ // 4][:, (h_ % 4) * 128:(h_ % 4 + 1) * 128]), waits=w)
            pv["tOe", c] = tok
            pv["od", c] = s.dma("sync", lambda h, c=c: h.dma_start(
                out=o_view[:, :, :, c * 128:(c + 1) * 128],
                in_=ost[c % 2][:].rearrange("p (j hh2 t) -> p j hh2 t", j=4, t=128)),
                waits=[pv["tOe", c]])
            for h_ in range(HL):
                w = []
                if h_ == 0:
                    w = [pv["tUe", c]] + [pv[k] for k in [("tSa", c - 1)] if k in pv]
                tok = s.emit("tensor", lambda h, h_=h_, c=c: h.matmul(
                    pps[:, h_ * 64:(h_ + 1) * 64],
                    ktm_h(h_, c), Ut8[:, h_ * 64:(h_ + 1) * 64], start=True, stop=True), waits=w)
            pv["tS", c] = tok
            for h_ in range(HL):
                w = []
                if h_ == 0:
                    w = [pv["tS", c]]
                    if c == 0:
                        w += [tz0, tz1]
                tok = s.emit("vector", lambda h, h_=h_, par=par: h.tensor_tensor(
                    Sf[1 - par][:, h_ * 64:(h_ + 1) * 64],
                    pps[:, h_ * 64:(h_ + 1) * 64],
                    Sf[par][:, h_ * 64:(h_ + 1) * 64], ALU.add), waits=w)
            pv["tSa", c] = tok
            for h_ in range(HL):
                w = [pv["tSa", c], pv["tO2", c]] if h_ == 0 else []
                tok = s.emit("scalar", lambda h, h_=h_, par=par: h.copy(
                    Sb[:, h_ * 64:(h_ + 1) * 64],
                    Sf[1 - par][:, h_ * 64:(h_ + 1) * 64]), waits=w)
            pv["tSb", c] = tok

        if debug:
            dw = [(e, s.cnt[e]) for e in SEMS]
            s.dma("sync", lambda h: h.dma_start(out=dbg_q[:, :], in_=q_tm[:, 0:4 * T]), waits=dw)
            s.dma("sync", lambda h: h.dma_start(out=dbg_ktm[:, :], in_=k_tm[:]), waits=dw)
            s.dma("sync", lambda h: h.dma_start(out=dbg_vtm[:, :], in_=v_tm[:]), waits=dw)
            s.dma("sync", lambda h: h.dma_start(out=dbg_beta[:, :], in_=beta[:]), waits=dw)
            s.dma("sync", lambda h: h.dma_start(out=dbg_xn[:, :], in_=xn_scr[0:128, :]), waits=dw)
        s.build(nc, ctx)
    return nc



def build_proj_residual():
    nc = bass.Bass("TRN2", target_bir_lowering=False)
    x_fm = nc.dram_tensor("x_fm", [128, 8 * TL], F32, kind="ExternalInput").ap()
    o_in = nc.dram_tensor("o_in", [128, 8 * TL], BF16, kind="ExternalInput").ap()
    w = nc.dram_tensor("w", [D, D], F32, kind="ExternalInput").ap()
    b_col = nc.dram_tensor("b_col", [128, 8], F32, kind="ExternalInput").ap()
    xo = nc.dram_tensor("xo", [128, 8 * TL], F32, kind="ExternalOutput").ap()

    s = Sched()
    with ExitStack() as ctx:
        E = ctx.enter_context
        xt = E(nc.sbuf_tensor("xt", [128, 8 * TL], F32))
        ot = E(nc.sbuf_tensor("ot", [128, 8 * TL], BF16))
        wt = E(nc.sbuf_tensor("wt", [128, 8 * D], BF16))
        bt = E(nc.sbuf_tensor("bt", [128, 8], F32))
        xn = E(nc.sbuf_tensor("xn", [128, 8 * TL], F32))
        pss = [E(nc.psum_tensor(f"ps{i}", [128, 512], F32)) for i in range(2)]

        tx = s.dma("sync", lambda h: h.dma_start(out=xt[:], in_=x_fm[:, :]))
        to = s.dma("sync", lambda h: h.dma_start(out=ot[:], in_=o_in[:, :]))
        tb = s.dma("sync", lambda h: h.dma_start(out=bt[:], in_=b_col[:, :]))
        tw = emit_load_w_bf16(s, wt, w, 8, D)
        ev = {}
        for oc in range(8):
            for tc in range(TL // 512):
                ps = pss[tc % 2]
                wl = [tx, to, tb] + tw + ([ev[id(ps)]] if id(ps) in ev else [])
                tok = None
                for kc in range(8):
                    tok = s.emit("tensor", lambda h, kc=kc, oc=oc, tc=tc, ps=ps: h.matmul(
                        ps[:], wt[:, kc * D + oc * 128: kc * D + (oc + 1) * 128],
                        ot[:, kc * TL + tc * 512: kc * TL + (tc + 1) * 512],
                        start=(kc == 0), stop=(kc == 7)), waits=(wl if kc == 0 else []))
                te = s.emit("vector", lambda h, oc=oc, tc=tc, ps=ps: h.scalar_tensor_tensor(
                    xn[:, oc * TL + tc * 512: oc * TL + (tc + 1) * 512], ps[:],
                    bt[:, oc:oc + 1], xt[:, oc * TL + tc * 512: oc * TL + (tc + 1) * 512],
                    ALU.add, ALU.add), waits=[tok])
                ev[id(ps)] = te
        wall = list(ev.values())
        for j in range(8):
            s.dma("sync", lambda h, j=j: h.dma_start(
                out=xo[:, j * TL:(j + 1) * TL], in_=xn[:, j * TL:(j + 1) * TL]), waits=wall)
        s.build(nc, ctx)
    return nc



def fm(x):
    Tp = x.shape[0]
    return np.ascontiguousarray(x.T.reshape(D // 128, 128, Tp).transpose(1, 0, 2).reshape(128, -1))

def unfm(y, Tp):
    return np.ascontiguousarray(y.reshape(128, D // 128, Tp).transpose(1, 0, 2).reshape(D, Tp).T)

def col128(b):
    return np.ascontiguousarray(b.reshape(-1, 128).T)

IDENT = np.eye(128, dtype=NPBF16)
STRIL = np.tril(np.ones((128, 128), np.float32), -1).astype(NPBF16)
TRIUI = np.triu(np.ones((128, 128), np.float32), 0).astype(NPBF16)

_PROGS = {}

def prog(name, builder):
    if name not in _PROGS:
        _PROGS[name] = builder()
    return _PROGS[name]


def o_heads_to_fm(o_parts):
    return np.concatenate(o_parts, axis=1)


def run_deltanet_sublayer(x, ins):
    in_maps = []
    for c in range(8):
        b, p = c // 2, c % 2
        hs = p * HL * HD
        in_maps.append(dict(
            xb=np.ascontiguousarray(x[b]),
            wq=np.ascontiguousarray(ins["L0_attn_a_Wq"][:, hs:hs + 512]),
            wk=np.ascontiguousarray(ins["L0_attn_a_Wk"][:, hs:hs + 512]),
            wv=np.ascontiguousarray(ins["L0_attn_a_Wv"][:, hs:hs + 512]),
            wbe=np.ascontiguousarray(ins["L0_attn_a_Wbeta"][:, p * 8:p * 8 + 8]),
            bq_row=ins["L0_attn_a_bq"][hs:hs + 512].reshape(1, 512),
            bk_row=ins["L0_attn_a_bk"][hs:hs + 512].reshape(1, 512),
            bv_row=ins["L0_attn_a_bv"][hs:hs + 512].reshape(1, 512),
            bbe_row=ins["L0_attn_a_bbeta"][p * 8:p * 8 + 8].reshape(1, 8),
            ident=IDENT, stril=STRIL, triui=TRIUI,
        ))
    res = run_bass_kernel_spmd(prog("dn_a", build_dn_a), in_maps, core_ids=list(range(8)))
    o_full = [o_heads_to_fm([res.results[2 * b]["o_out"], res.results[2 * b + 1]["o_out"]])
              for b in range(4)]
    in_maps = []
    for c in range(8):
        b, p = c // 2, c % 2
        in_maps.append(dict(
            x_fm=fm(x[b, p * TL:(p + 1) * TL]),
            o_in=np.ascontiguousarray(o_full[b][:, [j * T + p * TL + t for j in range(8) for t in range(0)]])
            if False else np.ascontiguousarray(
                o_full[b].reshape(128, 8, T)[:, :, p * TL:(p + 1) * TL].reshape(128, 8 * TL)),
            w=ins["L0_attn_a_Wo"],
            b_col=col128(ins["L0_attn_a_bo"]),
        ))
    res = run_bass_kernel_spmd(prog("dn_b", build_proj_residual), in_maps, core_ids=list(range(8)))
    xn = np.empty_like(x)
    for c in range(8):
        b, p = c // 2, c % 2
        xn[b, p * TL:(p + 1) * TL] = unfm(res.results[c]["xo"], TL)
    return xn





def build_moe():
    nc = bass.Bass("TRN2", target_bir_lowering=False)
    x_tm = nc.dram_tensor("x_tm", [TL, D], F32, kind="ExternalInput").ap()
    wrT = nc.dram_tensor("wrT", [D, 16], F32, kind="ExternalInput").ap()
    wmhT = nc.dram_tensor("wmhT", [D, D], F32, kind="ExternalInput").ap()
    wmgT = nc.dram_tensor("wmgT", [D, D], F32, kind="ExternalInput").ap()
    kf = nc.dram_tensor("kf", [8, 128, 4096], F32, kind="ExternalInput").ap()
    vf = nc.dram_tensor("vf", [8, 4096, 128], F32, kind="ExternalInput").ap()
    oh = nc.dram_tensor("oh", [16, 16 * 128], BF16, kind="ExternalInput").ap()
    ident = nc.dram_tensor("ident", [128, 128], BF16, kind="ExternalInput").ap()
    xn_scr = nc.dram_tensor("xn_scr", [TL, D], BF16, kind="Internal").ap()
    xo = nc.dram_tensor("xo", [TL, D], F32, kind="ExternalOutput").ap()

    NT_ = TL // 128
    s = Sched()
    with ExitStack() as ctx:
        E = ctx.enter_context
        sb = {}
        oht = E(nc.sbuf_tensor("oht", [16, 16 * 128], BF16))
        identt = E(nc.sbuf_tensor("identt", [128, 128], BF16))
        gateT = E(nc.sbuf_tensor("gateT", [16, TL], BF16))
        gA = E(nc.sbuf_tensor("gA", [128, 16], F32))
        gB = E(nc.sbuf_tensor("gB", [128, 16], F32))
        gC = E(nc.sbuf_tensor("gC", [128, 4], F32))
        gD = E(nc.sbuf_tensor("gD", [128, 16], F32))
        gbf = E(nc.sbuf_tensor("gbf", [128, 16], BF16))
        q_fm = E(nc.sbuf_tensor("q_fm", [128, 8 * TL], BF16))
        ps_lg = E(nc.psum_tensor("ps_lg", [128, 16], F32))
        ps_tr = E(nc.psum_tensor("ps_tr", [16, 128], BF16))
        ps_sc = [E(nc.psum_tensor(f"ps_sc{i}", [128, 512], F32)) for i in range(2)]
        ps_y = E(nc.psum_tensor("ps_y", [128, 512], F32))
        ps_w = E(nc.psum_tensor("ps_w", [128, 512], F32))

        t_oh = s.dma("sync", lambda h: h.dma_start(out=oht[:], in_=oh[:, :]))
        t_id = s.dma("sync", lambda h: h.dma_start(out=identt[:], in_=ident[:, :]))

        with ExitStack() as ctx2:
            E2 = ctx2.enter_context
            xn_fm = E2(nc.sbuf_tensor("xn_fm", [128, 8 * TL], BF16))
            sb["ln_xt"] = [E2(nc.sbuf_tensor(f"ln_xt{i}", [128, D], F32)) for i in range(2)]
            sb["ln_sq"] = E2(nc.sbuf_tensor("ln_sq", [128, D], F32))
            sb["ln_xn"] = E2(nc.sbuf_tensor("ln_xnb", [128, D], BF16))
            sb["ln_st"] = E2(nc.sbuf_tensor("ln_st", [128, 8], F32))
            wrTt = E2(nc.sbuf_tensor("wrTt", [128, 8 * 16], BF16))
            wmt = E2(nc.sbuf_tensor("wmt", [128, 8 * D], BF16))

            ln_toks = emit_ln(s, sb, x_tm, xn_scr, NT_)
            tr_toks = emit_transpose_in(s, xn_fm, xn_scr, TL, ln_toks)
            wr_t = emit_load_w_bf16(s, wrTt, wrT, 8, 16)
            wm_t = emit_load_w_bf16(s, wmt, wmhT, 8, D)

            prev_lg = None
            prev_lg_s = None
            prev_tr = None
            for i in range(NT_):
                wl = tr_toks + wr_t + ([prev_lg, prev_lg_s] if prev_lg else [])
                tok = None
                for kc in range(8):
                    tok = s.emit("tensor", lambda h, kc=kc, i=i: h.matmul(
                        ps_lg[:], xn_fm[:, kc * TL + i * 128: kc * TL + (i + 1) * 128],
                        wrTt[:, kc * 16:(kc + 1) * 16],
                        start=(kc == 0), stop=(kc == 7)), waits=(wl if kc == 0 else []))
                tm1 = s.emit("vector", lambda h: h.reduce_max(gC[:, 0:1], ps_lg[:], AX), waits=[tok])
                teq = s.emit("vector", lambda h: h.tensor_scalar(gA[:], ps_lg[:], gC[:, 0:1], None, ALU.is_equal), waits=[tm1])
                tng = s.emit("vector", lambda h: h.scalar_tensor_tensor(
                    gB[:], gA[:], -1e30, ps_lg[:], ALU.mult, ALU.add), waits=[teq])
                tm2 = s.emit("vector", lambda h: h.reduce_max(gC[:, 1:2], gB[:], AX), waits=[tng])
                tse = s.emit("vector", lambda h: h.tensor_scalar(gA[:], ps_lg[:], gC[:, 1:2], None, ALU.is_ge), waits=[tm2])
                tsg = s.emit("scalar", lambda h: h.activation(gB[:], ps_lg[:], AF.Sigmoid, bias=0.0, scale=1.0), waits=[tse])
                tgt = s.emit("vector", lambda h: h.tensor_tensor(gD[:], gA[:], gB[:], ALU.mult), waits=[tsg])
                prev_lg = tgt
                prev_lg_s = tsg
                tgb = s.emit("vector", lambda h: h.tensor_copy(gbf[:], gD[:]), waits=[tgt])
                ttr = s.emit("tensor", lambda h: h.transpose(ps_tr[:], gbf[:], identt[:]),
                             waits=[tgb, t_id] + ([prev_tr] if prev_tr else []))
                tte = s.emit("vector", lambda h, i=i: h.tensor_copy(
                    gateT[:, i * 128:(i + 1) * 128], ps_tr[:]), waits=[ttr])
                prev_tr = tte

            prev_ps = {}
            for oc in range(8):
                for tc in range(TL // 512):
                    ps = ps_sc[tc % 2]
                    wl = tr_toks + wm_t + ([prev_ps[id(ps)]] if id(ps) in prev_ps else [])
                    tok = None
                    for kc in range(8):
                        tok = s.emit("tensor", lambda h, kc=kc, oc=oc, tc=tc, ps=ps: h.matmul(
                            ps[:], wmt[:, kc * D + oc * 128: kc * D + (oc + 1) * 128],
                            xn_fm[:, kc * TL + tc * 512: kc * TL + (tc + 1) * 512],
                            start=(kc == 0), stop=(kc == 7)), waits=(wl if kc == 0 else []))
                    te = s.emit("scalar", lambda h, oc=oc, tc=tc, ps=ps: h.copy(
                        q_fm[:, oc * TL + tc * 512: oc * TL + (tc + 1) * 512], ps[:]), waits=[tok])
                    prev_ps[id(ps)] = te

        bar0 = [(e, s.cnt[e]) for e in SEMS]
        y_fm = E(nc.sbuf_tensor("y_fm", [128, 8 * TL], BF16))
        Bt = E(nc.sbuf_tensor("Bt", [128, 16 * 1024], BF16))
        kft = [E(nc.sbuf_tensor(f"kft{i}", [128, 4096], BF16)) for i in range(2)]
        vft = [E(nc.sbuf_tensor(f"vft{i}", [128, 4096], BF16)) for i in range(2)]
        ar = [E(nc.sbuf_tensor(f"ar{i}", [128, 512], BF16)) for i in range(2)]
        ag = [E(nc.sbuf_tensor(f"ag{i}", [128, 512], BF16)) for i in range(2)]

        kvctr = [0]
        y_toks = []
        prev_act = {}
        prev_ag = {}
        prev_ty = None
        prev_B = None
        last_B = None
        kv_tok = [None, None]
        for tcg in range(2):
            exp_done = [(e, s.cnt[e]) for e in SEMS]
            for e in range(16):
                for tci in range(2):
                    tc = 2 * tcg + tci
                    tm = s.emit("tensor", lambda h, e=e, tc=tc: h.matmul(
                        ps_w[:], oht[:, e * 128:(e + 1) * 128],
                        gateT[:, tc * 512:(tc + 1) * 512],
                        start=True, stop=True),
                        waits=bar0 + exp_done + [t_oh] + ([prev_B] if prev_B else []))
                    te = s.emit("vector", lambda h, e=e, tci=tci: h.tensor_copy(
                        Bt[:, e * 1024 + tci * 512: e * 1024 + (tci + 1) * 512], ps_w[:]), waits=[tm])
                    prev_B = te
            last_B = prev_B
            for h_ in range(8):
                hb = h_ % 2
                wl = bar0 + ([kv_tok[hb]] if kv_tok[hb] else [])
                tk = s.dma("gpsimd", lambda h, h_=h_, hb=hb: h.dma_start(
                    out=kft[hb][:], in_=kf[h_, :, :]), waits=wl)
                tv = s.dma("gpsimd", lambda h, h_=h_, hb=hb: h.dma_start(
                    out=vft[hb][:].rearrange("p (b d) -> p b d", b=32),
                    in_=vf[h_, :, :].rearrange("(b p) d -> p b d", p=128)), waits=wl)
                for tci in range(2):
                    tc = 2 * tcg + tci
                    for eb in range(32):
                        pp = ps_sc[eb % 2]
                        ab = eb % 2
                        wl2 = list(bar0) + [tk, tv, last_B] + ([prev_act.get(id(pp))] if prev_act.get(id(pp)) else [])
                        tsc = s.emit("tensor", lambda h, h_=h_, hb=hb, tc=tc, eb=eb, pp=pp: h.matmul(
                            pp[:], kft[hb][:, eb * 128:(eb + 1) * 128],
                            q_fm[:, h_ * TL + tc * 512: h_ * TL + (tc + 1) * 512],
                            start=True, stop=True), waits=wl2)
                        tge = s.emit("scalar", lambda h, ab=ab, pp=pp: h.activation(
                            ar[ab][:], pp[:], AF.Gelu, bias=0.0, scale=1.0),
                            waits=[tsc] + ([prev_ag.get(ab)] if prev_ag.get(ab) else []))
                        tag = s.emit("vector", lambda h, ab=ab, eb=eb, tci=tci: h.tensor_tensor(
                            ag[ab][:], ar[ab][:],
                            Bt[:, (eb // 2) * 1024 + tci * 512: (eb // 2) * 1024 + (tci + 1) * 512],
                            ALU.mult), waits=[tge])
                        prev_act[id(pp)] = tag
                        prev_ag[ab] = tag
                        tym = s.emit("tensor", lambda h, hb=hb, ab=ab, eb=eb: h.matmul(
                            ps_y[:], vft[hb][:, eb * 128:(eb + 1) * 128], ag[ab][:],
                            start=(eb == 0), stop=(eb == 31)),
                            waits=[tag] + ([prev_ty] if (eb == 0 and prev_ty) else []))
                    ty = s.emit("scalar", lambda h, h_=h_, tc=tc: h.copy(
                        y_fm[:, h_ * TL + tc * 512: h_ * TL + (tc + 1) * 512], ps_y[:]), waits=[tym])
                    prev_ty = ty
                    y_toks.append(ty)
                kv_tok[hb] = ty

        with ExitStack() as ctx3:
            E3 = ctx3.enter_context
            wgt = E3(nc.sbuf_tensor("wgt", [128, 8 * D], BF16))
            xtile = [E3(nc.sbuf_tensor(f"xtile{i}", [128, D], F32)) for i in range(2)]
            otile = [E3(nc.sbuf_tensor(f"otile{i}", [128, D], F32)) for i in range(2)]
            wg_t = emit_load_w_bf16(s, wgt, wmgT, 8, D, waits=bar0)
            prev_x = [None, None]
            prev_o = [None, None]
            prev_wg = None
            for i in range(NT_):
                ib = i % 2
                tx = s.dma("sync", lambda h, i=i, ib=ib: h.dma_start(
                    out=xtile[ib][:], in_=x_tm[i * 128:(i + 1) * 128, :]),
                    waits=([prev_x[ib]] if prev_x[ib] else []))
                for oc in range(2):
                    wl = y_toks[-1:] + [last_B] + wg_t + [tx]
                    if prev_wg:
                        wl = wl + [prev_wg]
                    tok = None
                    for kc in range(8):
                        tok = s.emit("tensor", lambda h, kc=kc, i=i, oc=oc: h.matmul(
                            ps_w[:], y_fm[:, kc * TL + i * 128: kc * TL + (i + 1) * 128],
                            wgt[:, kc * D + oc * 512: kc * D + (oc + 1) * 512],
                            start=(kc == 0), stop=(kc == 7)), waits=(wl if kc == 0 else []))
                    te = s.emit("vector", lambda h, i=i, oc=oc, ib=ib: h.tensor_tensor(
                        otile[ib][:, oc * 512:(oc + 1) * 512], ps_w[:],
                        xtile[ib][:, oc * 512:(oc + 1) * 512], ALU.add), waits=[tok])
                    prev_wg = te
                tst = s.dma("sync", lambda h, i=i, ib=ib: h.dma_start(
                    out=xo[i * 128:(i + 1) * 128, :], in_=otile[ib][:]), waits=[prev_wg])
                prev_x[ib] = prev_wg
                prev_o[ib] = tst
            s.build(nc, ctx)
    return nc


OH16 = np.zeros((16, 16 * 128), NPBF16)
for _e in range(16):
    OH16[_e, _e * 128:(_e + 1) * 128] = 1


def run_moe_sublayer(x, ins, pfx="L0_ffn"):
    in_maps = []
    for c in range(8):
        b, p = c // 2, c % 2
        in_maps.append(dict(
            x_tm=np.ascontiguousarray(x[b, p * TL:(p + 1) * TL]),
            wrT=np.ascontiguousarray(ins[pfx + "_w_router"].T),
            wmhT=np.ascontiguousarray(ins[pfx + "_w_mh"].T),
            wmgT=np.ascontiguousarray(ins[pfx + "_w_mg"].T),
            kf=np.ascontiguousarray(ins[pfx + "_k_ffwd"].transpose(0, 3, 1, 2).reshape(8, 128, 4096)),
            vf=np.ascontiguousarray(ins[pfx + "_v_ffwd"].reshape(8, 4096, 128)),
            oh=OH16, ident=IDENT,
        ))
    res = run_bass_kernel_spmd(prog("moe", build_moe), in_maps, core_ids=list(range(8)))
    xn = np.empty_like(x)
    for c in range(8):
        b, p = c // 2, c % 2
        xn[b, p * TL:(p + 1) * TL] = res.results[c]["xo"]
    return xn





def build_attn(kind):
    rope = kind == "rope"
    nex = 1 if rope else 2
    scale = 0.125 if rope else 0.03125
    nc = bass.Bass("TRN2", target_bir_lowering=False)
    xb = nc.dram_tensor("xb", [T, D], F32, kind="ExternalInput").ap()
    wq = nc.dram_tensor("wq", [D, 512], F32, kind="ExternalInput").ap()
    wk = nc.dram_tensor("wk", [D, 512], F32, kind="ExternalInput").ap()
    wv = nc.dram_tensor("wv", [D, 512], F32, kind="ExternalInput").ap()
    negm = nc.dram_tensor("negm", [128, 128], BF16, kind="ExternalInput").ap()
    if rope:
        bq_col = nc.dram_tensor("bq_col", [64, 8], F32, kind="ExternalInput").ap()
        bk_col = nc.dram_tensor("bk_col", [64, 8], F32, kind="ExternalInput").ap()
        bv_row = nc.dram_tensor("bv_row", [1, 512], F32, kind="ExternalInput").ap()
        rc_t = nc.dram_tensor("rc_t", [64, T], BF16, kind="ExternalInput").ap()
        rs_t = nc.dram_tensor("rs_t", [64, T], BF16, kind="ExternalInput").ap()
        pswap = nc.dram_tensor("pswap", [64, 64], BF16, kind="ExternalInput").ap()
    else:
        m_col = nc.dram_tensor("m_col", [128, 8], F32, kind="ExternalInput").ap()
        iota_r = nc.dram_tensor("iota_r", [1, T], F32, kind="ExternalInput").ap()
        iota2d = nc.dram_tensor("iota2d", [128, 32], F32, kind="ExternalInput").ap()
    xn_scr = nc.dram_tensor("xn_scr", [T, D], BF16, kind="Internal").ap()
    o_out = nc.dram_tensor("o_out", [128, 32 * 512], BF16, kind="ExternalOutput").ap()

    NB = T // 128
    s = Sched()
    with ExitStack() as ctx:
        E = ctx.enter_context
        sb = {}
        xn_fm = E(nc.sbuf_tensor("xn_fm", [128, 8 * T], BF16))
        v_tm = E(nc.sbuf_tensor("v_tm", [128, NB * 520], BF16))
        negmt = E(nc.sbuf_tensor("negmt", [128, 128], BF16))
        Qh = E(nc.sbuf_tensor("Qh", [64 + nex, T], BF16))
        Kh = E(nc.sbuf_tensor("Kh", [64 + nex, T], BF16))
        wqh = [E(nc.sbuf_tensor(f"wqh{i}", [128, 8 * 64], BF16)) for i in range(2)]
        wkh = [E(nc.sbuf_tensor(f"wkh{i}", [128, 8 * 64], BF16)) for i in range(2)]
        qraw = E(nc.sbuf_tensor("qraw", [64, 512], BF16))
        rtm01 = E(nc.sbuf_tensor("rtm01", [64, 512], BF16))
        rtm23 = E(nc.sbuf_tensor("rtm23", [64, 512], BF16))
        sq64 = E(nc.sbuf_tensor("sq64", [64, 512], BF16))
        ones64 = E(nc.sbuf_tensor("ones64", [64, 1], BF16))
        kmx = E(nc.sbuf_tensor("kmx", [1, 16], F32))
        rowa = E(nc.sbuf_tensor("rowa", [1, 512], F32))
        rowb = E(nc.sbuf_tensor("rowb", [1, 512], F32))
        rowc = E(nc.sbuf_tensor("rowc", [1, 512], F32))
        mbf = E(nc.sbuf_tensor("mbf", [1, T], BF16))

        if rope:
            rct = E(nc.sbuf_tensor("rct", [64, T], BF16))
            rst = E(nc.sbuf_tensor("rst", [64, T], BF16))
            pswt = E(nc.sbuf_tensor("pswt", [64, 64], BF16))
            swp = E(nc.sbuf_tensor("swp", [64, 512], BF16))
            bqc = E(nc.sbuf_tensor("bqc", [64, 8], F32))
            bkc = E(nc.sbuf_tensor("bkc", [64, 8], F32))
            bvr = E(nc.sbuf_tensor("bvr", [1, 512], BF16))
            ones1 = E(nc.sbuf_tensor("ones1", [1, 128], BF16))
        else:
            mbf2 = E(nc.sbuf_tensor("mbf2", [1, T], BF16))
            mrowf = E(nc.sbuf_tensor("mrowf", [1, 512], F32))
            mct = E(nc.sbuf_tensor("mct", [128, 8], F32))
            iort = E(nc.sbuf_tensor("iort", [1, T], F32))
            io2t = E(nc.sbuf_tensor("io2t", [128, 32], F32))
            msfull = E(nc.sbuf_tensor("msfull", [128, 32], F32))
            mrl = E(nc.sbuf_tensor("mrl", [128, 8], F32))
        ps_sc = [E(nc.psum_tensor(f"ps_sc{i}", [128, 512], F32)) for i in range(2)]
        ps_o = [E(nc.psum_tensor(f"ps_o{i}", [128, 512], F32)) for i in range(4)]
        ps_n = E(nc.psum_tensor("ps_n", [1, 512], F32))

        t_c = [s.dma("sync", lambda h: h.dma_start(out=negmt[:], in_=negm[:, :]))]
        t_c.append(s.emit("gpsimd", lambda h: h.memset(ones64[:], 1.0)))
        t_c.append(s.emit("gpsimd", lambda h: h.memset(Kh[64:64 + nex, :], 1.0)))
        t_c.append(s.emit("gpsimd", lambda h: h.memset(v_tm[:], 1.0)))
        if rope:
            t_c.append(s.dma("sync", lambda h: h.dma_start(out=rct[:], in_=rc_t[:, :])))
            t_c.append(s.dma("sync", lambda h: h.dma_start(out=rst[:], in_=rs_t[:, :])))
            t_c.append(s.dma("sync", lambda h: h.dma_start(out=pswt[:], in_=pswap[:, :])))
            t_c.append(s.dma("sync", lambda h: h.dma_start(out=bqc[:], in_=bq_col[:, :])))
            t_c.append(s.dma("sync", lambda h: h.dma_start(out=bkc[:], in_=bk_col[:, :])))
            t_c.append(s.dma("gpsimd", lambda h: h.dma_start(out=bvr[:], in_=bv_row[:, :])))
            t_c.append(s.emit("gpsimd", lambda h: h.memset(ones1[:], 1.0)))
        else:
            t_c.append(s.dma("sync", lambda h: h.dma_start(out=mct[:], in_=m_col[:, :])))
            t_c.append(s.dma("sync", lambda h: h.dma_start(out=iort[:], in_=iota_r[:, :])))
            t_c.append(s.dma("sync", lambda h: h.dma_start(out=io2t[:], in_=iota2d[:, :])))
            t_c.append(s.emit("vector", lambda h: h.tensor_relu(mrl[:], mct[:]), waits=t_c[-3:]))

        with ExitStack() as ctx2:
            E2 = ctx2.enter_context
            sb["ln_xt"] = [E2(nc.sbuf_tensor("ln_xt0", [128, D], F32))] * 2
            sb["ln_xt"] = [sb["ln_xt"][0], sb["ln_xt"][0]]
            sb["ln_sq"] = E2(nc.sbuf_tensor("ln_sq", [128, D], F32))
            sb["ln_xn"] = E2(nc.sbuf_tensor("ln_xnb", [128, D], BF16))
            sb["ln_st"] = E2(nc.sbuf_tensor("ln_st", [128, 8], F32))
            wvt = E2(nc.sbuf_tensor("wvt", [128, 8 * 512], BF16))
            ln_toks = emit_ln(s, sb, xb, xn_scr, NB)
            tr_toks = emit_transpose_in(s, xn_fm, xn_scr, T, ln_toks)
            w_v = emit_load_w_bf16(s, wvt, wv, 8, 512)

            prev_v = {}
            for tt in range(NB):
                ps = ps_sc[tt % 2]
                wl = tr_toks + w_v + t_c + ([prev_v[id(ps)]] if id(ps) in prev_v else [])
                tok = None
                for kc in range(8):
                    tok = s.emit("tensor", lambda h, kc=kc, tt=tt, ps=ps: h.matmul(
                        ps[:], xn_fm[:, kc * T + tt * 128: kc * T + (tt + 1) * 128],
                        wvt[:, kc * 512:(kc + 1) * 512],
                        start=(kc == 0), stop=(False if rope else kc == 7)),
                        waits=(wl if kc == 0 else []))
                if rope:
                    tok = s.emit("tensor", lambda h, ps=ps: h.matmul(
                        ps[:], ones1[:], bvr[:], start=False, stop=True))
                te = s.emit("vector", lambda h, tt=tt, ps=ps: h.tensor_copy(
                    v_tm[:, tt * 520: tt * 520 + 520].rearrange("p (a m) -> p a m", a=8)[:, :, 0:64],
                    ps[:].rearrange("p (a m) -> p a m", a=8)), waits=[tok])
                prev_v[id(ps)] = te

        bar0 = [(e, s.cnt[e]) for e in SEMS]
        o_buf = E(nc.sbuf_tensor("o_buf", [128, NB * 512], BF16))
        pt = [E(nc.sbuf_tensor(f"pt{i}", [128, 512], BF16)) for i in range(2)]
        dtile = E(nc.sbuf_tensor("dtile", [128, 128], F32))
        rcp = E(nc.sbuf_tensor("rcp", [128, 4], F32))

        prev = {}
        wtok = [None, None]
        for h_ in range(HL):
            hb = h_ % 2
            wl = list(bar0) + ([wtok[hb]] if wtok[hb] else [])
            twq = s.dma("gpsimd", lambda h, h_=h_, hb=hb: h.dma_start(
                out=wqh[hb][:], in_=wq[:, h_ * 64:(h_ + 1) * 64].rearrange("(kc p) c -> p kc c", p=128)),
                waits=wl)
            twk = s.dma("gpsimd", lambda h, h_=h_, hb=hb: h.dma_start(
                out=wkh[hb][:], in_=wk[:, h_ * 64:(h_ + 1) * 64].rearrange("(kc p) c -> p kc c", p=128)),
                waits=wl)
            for side, wt_, dst in (("q", wqh, Qh), ("k", wkh, Kh)):
                tws = twq if side == "q" else twk
                for tc in range(8):
                    ps = ps_sc[tc % 2]
                    wl = list(bar0) + [tws] + ([prev[id(ps)]] if id(ps) in prev else [])
                    tok = None
                    for kc in range(8):
                        tok = s.emit("tensor", lambda h, kc=kc, tc=tc, hb=hb, wt_=wt_, ps=ps: h.matmul(
                            ps[0:64, :], wt_[hb][:, kc * 64:(kc + 1) * 64],
                            xn_fm[:, kc * T + tc * 512: kc * T + (tc + 1) * 512],
                            start=(kc == 0), stop=(kc == 7)), waits=(wl if kc == 0 else []))
                    if rope:
                        bcol = bqc if side == "q" else bkc
                        tev = s.emit("scalar", lambda h, ps=ps, bcol=bcol, h_=h_: h.activation(
                            qraw[:], ps[0:64, :], AF.Identity, bias=bcol[:, h_:h_ + 1], scale=1.0),
                            waits=[tok] + ([prev["qraw"]] if "qraw" in prev else []))
                        tsw = s.emit("tensor", lambda h, ps=ps: h.matmul(
                            ps[64:128, 0:512], pswt[:], qraw[:], start=True, stop=True,
                            tile_position=(0, 64)), waits=[tev])
                        tse = s.emit("scalar", lambda h, ps=ps: h.copy(swp[:], ps[64:128, 0:512]), waits=[tsw])
                        t1 = s.emit("vector", lambda h, tc=tc: h.tensor_tensor(
                            rtm01[:], qraw[:], rct[:, tc * 512:(tc + 1) * 512], ALU.mult), waits=[tev])
                        t2 = s.emit("vector", lambda h, tc=tc: h.tensor_tensor(
                            rtm23[:], swp[:], rst[:, tc * 512:(tc + 1) * 512], ALU.mult), waits=[tse, t1])
                        t6 = s.emit("vector", lambda h, tc=tc, dst=dst: h.tensor_tensor(
                            dst[0:64, tc * 512:(tc + 1) * 512], rtm01[:], rtm23[:], ALU.add), waits=[t2])
                        prev["qraw"] = t6
                        prev[id(ps)] = t6
                    else:
                        tev = s.emit("scalar", lambda h, ps=ps, tc=tc, dst=dst: h.copy(
                            dst[0:64, tc * 512:(tc + 1) * 512], ps[0:64, :]), waits=[tok])
                        prev[id(ps)] = tev
            pk_done = [("vector" if rope else "scalar", s.cnt["vector" if rope else "scalar"])]

            nrm_ev = []
            for tc in range(8):
                tsq = s.emit("scalar", lambda h, tc=tc: h.square(
                    sq64[:], Kh[0:64, tc * 512:(tc + 1) * 512]), waits=pk_done + nrm_ev[-1:])
                tmm = s.emit("tensor", lambda h: h.matmul(
                    ps_n[:], ones64[:], sq64[:], start=True, stop=True),
                    waits=[tsq] + nrm_ev[-1:] + pk_done)
                tev = s.emit("vector", lambda h, tc=tc: h.reduce_max(
                    kmx[:, tc:tc + 1], ps_n[:], AX), waits=[tmm])
                nrm_ev.append(tev)
            tkx = s.emit("vector", lambda h: h.reduce_max(kmx[:, 8:9], kmx[:, 0:8], AX), waits=nrm_ev[-1:])
            mrow_done = []
            mdma = []
            for tc in range(8):
                tsq = s.emit("scalar", lambda h, tc=tc: h.square(
                    sq64[:], Qh[0:64, tc * 512:(tc + 1) * 512]),
                    waits=pk_done + mrow_done[-1:] + nrm_ev[-1:])
                tmm = s.emit("tensor", lambda h: h.matmul(
                    ps_n[:], ones64[:], sq64[:], start=True, stop=True),
                    waits=[tsq] + mrow_done[-1:] + nrm_ev[-1:])
                tq2 = s.emit("vector", lambda h: h.tensor_scalar(
                    rowa[:], ps_n[:], kmx[:, 8:9], None, ALU.mult), waits=[tmm, tkx])
                tsd = s.emit("scalar", lambda h: h.sqrt(rowb[:], rowa[:]), waits=[tq2])
                tlast = tsd
                if not rope:
                    tit = s.emit("vector", lambda h, h_=h_, tc=tc: h.tensor_scalar(
                        rowc[:], iort[:, tc * 512:(tc + 1) * 512], mrl[0:1, h_:h_ + 1], 32.0,
                        ALU.mult, ALU.mult), waits=t_c + [tsd])
                    tlast = s.emit("vector", lambda h: h.tensor_tensor(
                        rowb[:], rowb[:], rowc[:], ALU.add), waits=[tit])
                tng = s.emit("vector", lambda h: h.tensor_scalar(
                    rowa[:], rowb[:], 1.0, -1.0, ALU.add, ALU.mult), waits=[tlast])
                tb1 = s.emit("vector", lambda h, tc=tc: h.tensor_copy(
                    mbf[:, tc * 512:(tc + 1) * 512], rowa[:]), waits=[tng])
                mrow_done.append(tb1)
                if not rope:
                    tup = s.emit("vector", lambda h, tc=tc: h.tensor_copy(
                        mrowf[:], mbf[:, tc * 512:(tc + 1) * 512]), waits=[tb1])
                    tm2v = s.emit("vector", lambda h: h.tensor_tensor(
                        rowc[:], rowa[:], mrowf[:], ALU.subtract), waits=[tup])
                    tb2 = s.emit("vector", lambda h, tc=tc: h.tensor_copy(
                        mbf2[:, tc * 512:(tc + 1) * 512], rowc[:]), waits=[tm2v])
                    mrow_done.append(tb2)
            td1 = s.dma("sync", lambda h: h.dma_start(out=Qh[64:65, :], in_=mbf[:, :]),
                        waits=mrow_done + [("tensor", s.cnt["tensor"])])
            mdma = [td1]
            if not rope:
                td2 = s.dma("sync", lambda h: h.dma_start(out=Qh[65:66, :], in_=mbf2[:, :]),
                            waits=mrow_done + [("tensor", s.cnt["tensor"])])
                mdma.append(td2)
                tmsf = s.emit("vector", lambda h, h_=h_: h.tensor_scalar(
                    msfull[:], io2t[:], mct[:, h_:h_ + 1], None, ALU.mult),
                    waits=t_c + mdma + [("scalar", s.cnt["scalar"])])
                mdma.append(tmsf)

            for tcq in range(8):
                for sb_ in range(4 * tcq + 4):
                    pp = ps_sc[sb_ % 2]
                    pb = sb_ % 2
                    wl = list(mdma) + ([prev[("pt", pb)]] if ("pt", pb) in prev else []) + \
                         ([prev[id(pp)]] if id(pp) in prev else [])
                    tmm = s.emit("tensor", lambda h, h_=h_, tcq=tcq, sb_=sb_, pp=pp: h.matmul(
                        pp[:], Kh[0:64 + nex, sb_ * 128:(sb_ + 1) * 128],
                        Qh[0:64 + nex, tcq * 512:(tcq + 1) * 512],
                        start=True, stop=True), waits=wl)
                    if rope:
                        tex = s.emit("scalar", lambda h, pb=pb, pp=pp: h.activation(
                            pt[pb][:], pp[:], AF.Exp, bias=0.0, scale=scale), waits=[tmm])
                    else:
                        tex = s.emit("scalar", lambda h, pb=pb, pp=pp, sb_=sb_: h.activation(
                            pt[pb][:], pp[:], AF.Exp, bias=msfull[:, sb_:sb_ + 1], scale=scale), waits=[tmm])
                    j0 = sb_ - 4 * tcq
                    if 0 <= j0 <= 3:
                        tdt = s.emit("vector", lambda h, pp=pp, j0=j0: h.tensor_tensor(
                            dtile[:], pp[:, j0 * 128:(j0 + 1) * 128], negmt[:], ALU.add),
                            waits=[tmm, tex])
                        if rope:
                            tex2 = s.emit("scalar", lambda h, pb=pb, j0=j0: h.activation(
                                pt[pb][:, j0 * 128:(j0 + 1) * 128], dtile[:], AF.Exp,
                                bias=0.0, scale=scale), waits=[tdt])
                        else:
                            tex2 = s.emit("scalar", lambda h, pb=pb, j0=j0, sb_=sb_: h.activation(
                                pt[pb][:, j0 * 128:(j0 + 1) * 128], dtile[:], AF.Exp,
                                bias=msfull[:, sb_:sb_ + 1], scale=scale), waits=[tdt])
                        tex = tex2
                    prev[id(pp)] = tex
                    tpv = None
                    for j in range(max(0, j0), 4):
                        tpv = s.emit("tensor", lambda h, pb=pb, j=j, sb_=sb_, h_=h_: h.matmul(
                            ps_o[j][:, 0:65], pt[pb][:, j * 128:(j + 1) * 128],
                            v_tm[:, sb_ * 520 + h_ * 65: sb_ * 520 + (h_ + 1) * 65],
                            start=(sb_ == 0), stop=(sb_ == 4 * tcq + j)),
                            waits=[tex] + ([prev[("po", j)]] if (("po", j) in prev and sb_ == 0) else []))
                    prev[("pt", pb)] = tpv if tpv is not None else tex
                for j in range(4):
                    trc = s.emit("vector", lambda h, j=j: h.reciprocal(
                        rcp[:, j:j + 1], ps_o[j][:, 64:65]),
                        waits=[prev[("pt", 0)], prev[("pt", 1)]])
                    tfo = s.emit("vector", lambda h, j=j, tcq=tcq, h_=h_: h.tensor_scalar(
                        o_buf[:, (4 * tcq + j) * 512 + h_ * 64: (4 * tcq + j) * 512 + (h_ + 1) * 64],
                        ps_o[j][:, 0:64], rcp[:, j:j + 1], None, ALU.mult), waits=[trc])
                    prev[("po", j)] = tfo
            wtok[hb] = prev[("pt", 0)]
        ow = [prev[("po", j)] for j in range(4)]
        for part in range(4):
            s.dma("sync", lambda h, part=part: h.dma_start(
                out=o_out[:, part * 8 * 512:(part + 1) * 8 * 512],
                in_=o_buf[:, part * 8 * 512:(part + 1) * 8 * 512]), waits=ow)
        s.build(nc, ctx)
    return nc


NEGM_TILE = np.where(np.arange(128)[:, None] <= np.arange(128)[None, :], 0.0, -1e30).astype(NPBF16)


def rope_tables():
    inv = 1.0 / 10000.0 ** (np.arange(0, 64, 2) / 64.0)
    ang = np.arange(T)[None, :] * np.repeat(inv, 2)[:, None]
    C = np.cos(ang)
    S = np.sin(ang)
    S[0::2, :] *= -1.0
    return C.astype(NPBF16), S.astype(NPBF16)

PSWAP64 = np.zeros((64, 64), np.float32)
for _i in range(0, 64, 2):
    PSWAP64[_i, _i + 1] = 1.0
    PSWAP64[_i + 1, _i] = 1.0
PSWAP64 = PSWAP64.astype(NPBF16)


def perm_eo(w):
    w = w.reshape(w.shape[0], -1, 64)
    return np.ascontiguousarray(np.concatenate([w[:, :, 0::2], w[:, :, 1::2]], axis=2).reshape(w.shape[0], -1))


def perm_eo_b(b):
    b = b.reshape(-1, 64)
    return np.ascontiguousarray(np.concatenate([b[:, 0::2], b[:, 1::2]], axis=1).reshape(-1))


def o_to_tokmajor(o_buf_arr):
    return np.ascontiguousarray(o_buf_arr.reshape(128, 32, 512).transpose(1, 0, 2).reshape(T, 512))


def run_rope_attn_sublayer(x, ins):
    cos_t, sin_t = rope_tables()
    in_maps = []
    for c in range(8):
        b, p = c // 2, c % 2
        hs = p * 512
        wqT = np.ascontiguousarray(ins["L0_attn_b_Wq"].T[:, hs:hs + 512])
        wkT = np.ascontiguousarray(ins["L0_attn_b_Wk"].T[:, hs:hs + 512])
        wvT = np.ascontiguousarray(ins["L0_attn_b_Wv"].T[:, hs:hs + 512])
        bq = ins["L0_attn_b_bq"][hs:hs + 512]
        bk = ins["L0_attn_b_bk"][hs:hs + 512]
        in_maps.append(dict(
            xb=np.ascontiguousarray(x[b]),
            wq=wqT, wk=wkT, wv=wvT,
            bq_col=np.ascontiguousarray(bq.reshape(8, 64).T),
            bk_col=np.ascontiguousarray(bk.reshape(8, 64).T),
            bv_row=ins["L0_attn_b_bv"][hs:hs + 512].reshape(1, 512),
            rc_t=cos_t, rs_t=sin_t, negm=NEGM_TILE, pswap=PSWAP64,
        ))
    res = run_bass_kernel_spmd(prog("attn_rope", lambda: build_attn("rope")), in_maps, core_ids=list(range(8)))
    o_full = [np.concatenate([o_to_tokmajor(res.results[2 * b]["o_out"]),
                              o_to_tokmajor(res.results[2 * b + 1]["o_out"])], axis=1)
              for b in range(4)]
    in_maps = []
    for c in range(8):
        b, p = c // 2, c % 2
        in_maps.append(dict(
            x_fm=fm(x[b, p * TL:(p + 1) * TL]),
            o_in=fm(o_full[b][p * TL:(p + 1) * TL]).astype(NPBF16),
            w=np.ascontiguousarray(ins["L0_attn_b_Wo"].T),
            b_col=col128(ins["L0_attn_b_bo"]),
        ))
    res = run_bass_kernel_spmd(prog("dn_b", build_proj_residual), in_maps, core_ids=list(range(8)))
    xn = np.empty_like(x)
    for c in range(8):
        b, p = c // 2, c % 2
        xn[b, p * TL:(p + 1) * TL] = unfm(res.results[c]["xo"], TL)
    return xn


IDENT1024 = None

def run_alibi_attn_sublayer(x, ins):
    global IDENT1024
    wkqv = ins["L1_attn_a_W_kqv"]
    in_maps = []
    for c in range(8):
        b, p = c // 2, c % 2
        hs = p * 512
        in_maps.append(dict(
            xb=np.ascontiguousarray(x[b]),
            wk=np.ascontiguousarray(wkqv[:, hs:hs + 512]),
            wq=np.ascontiguousarray(wkqv[:, 1024 + hs:1024 + hs + 512]),
            wv=np.ascontiguousarray(wkqv[:, 2048 + hs:2048 + hs + 512]),
            m_col=np.ascontiguousarray(np.broadcast_to(
                ins["L1_attn_a_m"][p * 8:(p + 1) * 8, 0, 0].reshape(1, 8), (128, 8))),
            iota_r=np.arange(T, dtype=np.float32).reshape(1, T),
            iota2d=np.ascontiguousarray((np.arange(128)[:, None] + 128 * np.arange(32)[None, :]).astype(np.float32)),
            negm=NEGM_TILE,
        ))
    res = run_bass_kernel_spmd(prog("attn_alibi", lambda: build_attn("alibi")), in_maps, core_ids=list(range(8)))
    o_full = [np.concatenate([o_to_tokmajor(res.results[2 * b]["o_out"]),
                              o_to_tokmajor(res.results[2 * b + 1]["o_out"])], axis=1)
              for b in range(4)]
    if IDENT1024 is None:
        IDENT1024 = np.eye(D, dtype=np.float32)
    in_maps = []
    for c in range(8):
        b, p = c // 2, c % 2
        in_maps.append(dict(
            x_fm=fm(x[b, p * TL:(p + 1) * TL]),
            o_in=fm(o_full[b][p * TL:(p + 1) * TL]).astype(NPBF16),
            w=IDENT1024,
            b_col=np.zeros((128, 8), np.float32),
        ))
    res = run_bass_kernel_spmd(prog("dn_b", build_proj_residual), in_maps, core_ids=list(range(8)))
    xn = np.empty_like(x)
    for c in range(8):
        b, p = c // 2, c % 2
        xn[b, p * TL:(p + 1) * TL] = unfm(res.results[c]["xo"], TL)
    return xn





def emit_quant(s, qs, src_ap, dst_ap, W, waits=(), eng="vector"):
    nb = W // 16
    bc = lambda t: t[:, 0:nb].unsqueeze(2).broadcast_to([128, nb, 16])
    v3 = lambda t: t[:, 0:W].rearrange("p (a m) -> p a m", a=nb)
    tw = list(waits)
    t0 = s.emit(eng, lambda h: h.scalar_tensor_tensor(qs["qa"][:, 0:W], src_ap, -1.0, src_ap, ALU.mult, ALU.max), waits=tw)
    t1 = s.emit(eng, lambda h: h.tensor_reduce(qs["qm1"][:, 0:nb], v3(qs["qa"]), AX, ALU.max), waits=[t0])
    t2 = s.emit(eng, lambda h: h.tensor_scalar(qs["qm2"][:, 0:nb], qs["qm1"][:, 0:nb], 1.0 / 6.0, 1e-12, ALU.mult, ALU.max), waits=[t1])
    t3 = s.emit(eng, lambda h: h.reciprocal(qs["qm3"][:, 0:nb], qs["qm2"][:, 0:nb]), waits=[t2])
    t4 = s.emit(eng, lambda h: h.tensor_tensor(v3(qs["qv"]), v3(qs["qa"]), bc(qs["qm3"]), ALU.mult), waits=[t3])
    ta = s.emit(eng, lambda h: h.tensor_scalar(qs["qa"][:, 0:W], qs["qv"][:, 0:W], 0.25, None, ALU.is_gt), waits=[t4])
    for c in (0.75, 1.25, 1.75):
        ta = s.emit(eng, lambda h, c=c: h.scalar_tensor_tensor(
            qs["qa"][:, 0:W], qs["qv"][:, 0:W], c, qs["qa"][:, 0:W], ALU.is_gt, ALU.add), waits=[ta])
    tb = s.emit(eng, lambda h: h.tensor_scalar(qs["qb"][:, 0:W], qs["qv"][:, 0:W], 2.5, None, ALU.is_gt), waits=[t4])
    tb = s.emit(eng, lambda h: h.scalar_tensor_tensor(
        qs["qb"][:, 0:W], qs["qv"][:, 0:W], 3.5, qs["qb"][:, 0:W], ALU.is_gt, ALU.add), waits=[tb])
    tc_ = s.emit(eng, lambda h: h.tensor_scalar(qs["qv"][:, 0:W], qs["qv"][:, 0:W], 5.0, None, ALU.is_gt), waits=[tb, ta])
    tq = s.emit(eng, lambda h: h.scalar_tensor_tensor(
        qs["qb"][:, 0:W], qs["qv"][:, 0:W], 2.0, qs["qb"][:, 0:W], ALU.mult, ALU.add), waits=[tc_])
    tq = s.emit(eng, lambda h: h.scalar_tensor_tensor(
        qs["qb"][:, 0:W], qs["qa"][:, 0:W], 0.5, qs["qb"][:, 0:W], ALU.mult, ALU.add), waits=[tq])
    tsg = s.emit("scalar", lambda h: h.sign(qs["qsg"][:, 0:W], src_ap), waits=tw + [t4])
    tm = s.emit(eng, lambda h: h.tensor_tensor(qs["qa"][:, 0:W], qs["qb"][:, 0:W], qs["qsg"][:, 0:W], ALU.mult), waits=[tq, tsg])
    tout = s.emit(eng, lambda h: h.tensor_tensor(
        v3_out(dst_ap, nb), v3(qs["qa"]), bc(qs["qm2"]), ALU.mult), waits=[tm])
    return tout

def v3_out(ap, nb):
    return ap.rearrange("p (a m) -> p a m", a=nb)


def build_ffn1(pre=False):
    nc = bass.Bass("TRN2", target_bir_lowering=False)
    x_tm = nc.dram_tensor("x_tm", [TL, D], F32, kind="ExternalInput").ap()
    if not pre:
        wg = nc.dram_tensor("wg", [2816, D], F32, kind="ExternalInput").ap()
        wu = nc.dram_tensor("wu", [2816, D], F32, kind="ExternalInput").ap()
        wd = nc.dram_tensor("wd", [D, 2816], F32, kind="ExternalInput").ap()
    ident = nc.dram_tensor("ident", [128, 128], BF16, kind="ExternalInput").ap()
    xq_scr = nc.dram_tensor("xq_scr", [TL, D], BF16, kind="Internal").ap()
    wq_kind = "ExternalInput" if pre else "ExternalOutput"
    wgq_scr = nc.dram_tensor("wgq_scr", [2816, D], BF16, kind=wq_kind).ap()
    wuq_scr = nc.dram_tensor("wuq_scr", [2816, D], BF16, kind=wq_kind).ap()
    wdq_scr = nc.dram_tensor("wdq_scr", [D, 2816], BF16, kind=wq_kind).ap()
    xo = nc.dram_tensor("xo", [TL, D], F32, kind="ExternalOutput").ap()

    NT_ = TL // 128
    NI = 11
    s = Sched()
    with ExitStack() as ctx:
        E = ctx.enter_context
        qs = dict(
            qa=E(nc.sbuf_tensor("qa", [128, 2816], F32)),
            qv=E(nc.sbuf_tensor("qv", [128, 2816], F32)),
            qb=E(nc.sbuf_tensor("qb", [128, 2816], F32)),
            qm1=E(nc.sbuf_tensor("qm1", [128, 176], F32)),
            qm2=E(nc.sbuf_tensor("qm2", [128, 176], F32)),
            qm3=E(nc.sbuf_tensor("qm3", [128, 176], F32)),
            qsg=E(nc.sbuf_tensor("qsg", [128, 2816], BF16)),
        )
        identt = E(nc.sbuf_tensor("identt", [128, 128], BF16))
        wtile = E(nc.sbuf_tensor("wtile", [128, 2816], F32))
        wq_out = E(nc.sbuf_tensor("wq_out", [128, 2816], BF16))
        t_id = s.dma("sync", lambda h: h.dma_start(out=identt[:], in_=ident[:, :]))

        prevq = None
        if not pre:
            for wsrc, wdst, nt, wwid in ((wg, wgq_scr, 22, D), (wu, wuq_scr, 22, D), (wd, wdq_scr, 8, 2816)):
                for i in range(nt):
                    tld = s.dma("sync", lambda h, wsrc=wsrc, i=i, wwid=wwid: h.dma_start(
                        out=wtile[:, 0:wwid], in_=wsrc[i * 128:(i + 1) * 128, :]),
                        waits=[prevq] if prevq else [])
                    tq = emit_quant(s, qs, wtile[:, 0:wwid], wq_out[:, 0:wwid], wwid, waits=[tld])
                    tst = s.dma("sync", lambda h, wdst=wdst, i=i, wwid=wwid: h.dma_start(
                        out=wdst[i * 128:(i + 1) * 128, :], in_=wq_out[:, 0:wwid]), waits=[tq])
                    prevq = tst

        with ExitStack() as ctx2:
            E2 = ctx2.enter_context
            sb = {}
            sb["ln_xt"] = [E2(nc.sbuf_tensor(f"lnxt{i}", [128, D], F32)) for i in range(2)]
            sb["ln_sq"] = E2(nc.sbuf_tensor("ln_sq", [128, D], F32))
            sb["ln_xn"] = E2(nc.sbuf_tensor("ln_xnb", [128, D], BF16))
            sb["ln_st"] = E2(nc.sbuf_tensor("ln_st", [128, 8], F32))
            xnf = E2(nc.sbuf_tensor("xnf", [128, D], F32))
            last = None
            prevst = None
            for i in range(NT_):
                xt, sq, st = sb["ln_xt"][0], sb["ln_sq"], sb["ln_st"]
                tld = s.dma("sync", lambda h, i=i: h.dma_start(
                    out=xt[:], in_=x_tm[i * 128:(i + 1) * 128, :]),
                    waits=[last] if last else ([prevq] if prevq else []))
                tsq = s.emit("scalar", lambda h: h.square(sq[:], xt[:]), waits=[tld])
                t1 = s.emit("vector", lambda h: h.reduce_sum(st[:, 0:1], xt[:], AX), waits=[tld])
                t2 = s.emit("vector", lambda h: h.reduce_sum(st[:, 1:2], sq[:], AX), waits=[tsq, t1])
                tm = s.emit("vector", lambda h: h.tensor_scalar(st[:, 2:3], st[:, 0:1], 1.0 / D, None, ALU.mult), waits=[t2])
                tm2 = s.emit("vector", lambda h: h.tensor_scalar(st[:, 3:4], st[:, 2:3], st[:, 2:3], None, ALU.mult), waits=[tm])
                tv = s.emit("vector", lambda h: h.scalar_tensor_tensor(
                    st[:, 4:5], st[:, 1:2], 1.0 / D, st[:, 3:4], ALU.mult, ALU.subtract), waits=[tm2])
                tve = s.emit("vector", lambda h: h.tensor_scalar(st[:, 5:6], st[:, 4:5], EPS_LN, None, ALU.add), waits=[tv])
                tsd = s.emit("scalar", lambda h: h.sqrt(st[:, 6:7], st[:, 5:6]), waits=[tve])
                trs = s.emit("vector", lambda h: h.reciprocal(st[:, 7:8], st[:, 6:7]), waits=[tsd])
                txn = s.emit("vector", lambda h: h.tensor_scalar(
                    xnf[:], xt[:], st[:, 2:3], st[:, 7:8], ALU.subtract, ALU.mult), waits=[trs])
                tq = emit_quant(s, qs, xnf[:], sb["ln_xn"][:], D, waits=[txn])
                tst2 = s.dma("sync", lambda h, i=i: h.dma_start(
                    out=xq_scr[i * 128:(i + 1) * 128, :], in_=sb["ln_xn"][:]), waits=[tq])
                last = txn
                prevst = tst2
            ln_done = [prevst]

        xq_fm = E(nc.sbuf_tensor("xq_fm", [128, 8 * TL], BF16))
        wdqT = E(nc.sbuf_tensor("wdqT", [128, 22 * D], BF16))
        wgT = [E(nc.sbuf_tensor(f"wgT{i}", [128, 8 * 256], BF16)) for i in range(2)]
        wuT = [E(nc.sbuf_tensor(f"wuT{i}", [128, 8 * 256], BF16)) for i in range(2)]
        hqT = E(nc.sbuf_tensor("hqT", [128, 4 * 2816], BF16))
        silu = E(nc.sbuf_tensor("silu", [128, 256], F32))
        upse = E(nc.sbuf_tensor("upse", [128, 256], F32))
        hft = E(nc.sbuf_tensor("hft", [128, 256], F32))
        hqt = E(nc.sbuf_tensor("hqt", [128, 256], BF16))
        xtile = E(nc.sbuf_tensor("xtile", [128, D], F32))
        otile = E(nc.sbuf_tensor("otile", [128, D], F32))
        ps_g = E(nc.psum_tensor("ps_g", [128, 256], F32))
        ps_u = E(nc.psum_tensor("ps_u", [128, 256], F32))
        ps_t = E(nc.psum_tensor("ps_t", [128, 128], BF16))
        ps_d = [E(nc.psum_tensor(f"ps_d{i}", [128, 512], F32)) for i in range(2)]

        tr_x = emit_transpose_in(s, xq_fm, xq_scr, TL, ln_done)
        twd = []
        for i in range(22):
            twd.append(s.dma("sync", lambda h, i=i: h.dma_start_transpose(
                wdqT[:, i * D:(i + 1) * D], wdq_scr[:, i * 128:(i + 1) * 128]),
                waits=[prevq] if prevq else []))

        prev = {}
        for tg in range(4):
            for Ic in range(NI):
                wb = Ic % 2
                wl = tr_x + ([prev[("wg", wb)]] if ("wg", wb) in prev else [])
                tgw = []
                for kc in range(8):
                    tgw.append(s.dma("sync", lambda h, kc=kc, Ic=Ic, wb=wb: h.dma_start_transpose(
                        wgT[wb][:, kc * 256:(kc + 1) * 256],
                        wgq_scr[Ic * 256:(Ic + 1) * 256, kc * 128:(kc + 1) * 128]), waits=wl))
                    tgw.append(s.dma("sync", lambda h, kc=kc, Ic=Ic, wb=wb: h.dma_start_transpose(
                        wuT[wb][:, kc * 256:(kc + 1) * 256],
                        wuq_scr[Ic * 256:(Ic + 1) * 256, kc * 128:(kc + 1) * 128]), waits=wl))
                for ti in range(4):
                    tt = tg * 4 + ti
                    wl2 = tgw + ([prev["hq"]] if "hq" in prev else [])
                    tokg = toku = None
                    for kc in range(8):
                        tokg = s.emit("tensor", lambda h, kc=kc, tt=tt, wb=wb: h.matmul(
                            ps_g[:], xq_fm[:, kc * TL + tt * 128: kc * TL + (tt + 1) * 128],
                            wgT[wb][:, kc * 256:(kc + 1) * 256],
                            start=(kc == 0), stop=(kc == 7)), waits=(wl2 if kc == 0 else []))
                    for kc in range(8):
                        toku = s.emit("tensor", lambda h, kc=kc, tt=tt, wb=wb: h.matmul(
                            ps_u[:], xq_fm[:, kc * TL + tt * 128: kc * TL + (tt + 1) * 128],
                            wuT[wb][:, kc * 256:(kc + 1) * 256],
                            start=(kc == 0), stop=(kc == 7)), waits=[])
                    tsl = s.emit("scalar", lambda h: h.activation(
                        silu[:], ps_g[:], AF.Silu, bias=0.0, scale=1.0),
                        waits=[tokg] + ([prev["sl"]] if "sl" in prev else []))
                    tue = s.emit("scalar", lambda h: h.copy(upse[:], ps_u[:]),
                        waits=[toku] + ([prev["ue"]] if "ue" in prev else []))
                    prev["sl"] = tsl
                    prev["ue"] = tue
                    th = s.emit("vector", lambda h: h.tensor_tensor(hft[:], silu[:], upse[:], ALU.mult),
                                waits=[tsl, tue])
                    thq = emit_quant(s, qs, hft[:], hqt[:], 256, waits=[th])
                    prev["hq"] = thq
                    for half in range(2):
                        ttr = s.emit("tensor", lambda h, half=half: h.transpose(
                            ps_t[:], hqt[:, half * 128:(half + 1) * 128], identt[:]),
                            waits=[thq, t_id] + ([prev["tre"]] if "tre" in prev else []))
                        tte = s.emit("vector", lambda h, ti=ti, Ic=Ic, half=half: h.tensor_copy(
                            hqT[:, ti * 2816 + Ic * 256 + half * 128: ti * 2816 + Ic * 256 + (half + 1) * 128],
                            ps_t[:]), waits=[ttr])
                        prev["tre"] = tte
                prev[("wg", wb)] = tokg
            for ti in range(4):
                tt = tg * 4 + ti
                tx = s.dma("sync", lambda h, tt=tt: h.dma_start(
                    out=xtile[:], in_=x_tm[tt * 128:(tt + 1) * 128, :]),
                    waits=[prev["xo"]] if "xo" in prev else [])
                for oc in range(2):
                    pd = ps_d[oc]
                    wl3 = twd + [prev["tre"], tx] + ([prev[("pd", oc)]] if ("pd", oc) in prev else [])
                    tok = None
                    for i in range(22):
                        tok = s.emit("tensor", lambda h, i=i, ti=ti, oc=oc, pd=pd: h.matmul(
                            pd[:], hqT[:, ti * 2816 + i * 128: ti * 2816 + i * 128 + 128],
                            wdqT[:, i * D + oc * 512: i * D + (oc + 1) * 512],
                            start=(i == 0), stop=(i == 21)), waits=(wl3 if i == 0 else []))
                    te = s.emit("vector", lambda h, oc=oc, pd=pd: h.tensor_tensor(
                        otile[:, oc * 512:(oc + 1) * 512], pd[:],
                        xtile[:, oc * 512:(oc + 1) * 512], ALU.add), waits=[tok])
                    prev[("pd", oc)] = te
                txo = s.dma("sync", lambda h, tt=tt: h.dma_start(
                    out=xo[tt * 128:(tt + 1) * 128, :], in_=otile[:]),
                    waits=[prev[("pd", 0)], prev[("pd", 1)]])
                prev["xo"] = txo
        s.build(nc, ctx)
    return nc


def run_ffn1_sublayer(x, ins, qcache):
    pre = "wgq" in qcache
    in_maps = []
    for c in range(8):
        b, p = c // 2, c % 2
        im = dict(x_tm=np.ascontiguousarray(x[b, p * TL:(p + 1) * TL]), ident=IDENT)
        if pre:
            im["wgq_scr"] = qcache["wgq"][c]
            im["wuq_scr"] = qcache["wuq"][c]
            im["wdq_scr"] = qcache["wdq"][c]
        else:
            im["wg"] = ins["L1_ffn_w_gate"]
            im["wu"] = ins["L1_ffn_w_up"]
            im["wd"] = ins["L1_ffn_w_down"]
        in_maps.append(im)
    if pre:
        res = run_bass_kernel_spmd(prog("ffn1_b", lambda: build_ffn1(True)), in_maps, core_ids=list(range(8)))
    else:
        res = run_bass_kernel_spmd(prog("ffn1_a", lambda: build_ffn1(False)), in_maps, core_ids=list(range(8)))
        qcache["wgq"] = [res.results[c]["wgq_scr"] for c in range(8)]
        qcache["wuq"] = [res.results[c]["wuq_scr"] for c in range(8)]
        qcache["wdq"] = [res.results[c]["wdq_scr"] for c in range(8)]
    xn = np.empty_like(x)
    for c in range(8):
        b, p = c // 2, c % 2
        xn[b, p * TL:(p + 1) * TL] = res.results[c]["xo"]
    return xn





def build_rn_a():
    nc = bass.Bass("TRN2", target_bir_lowering=False)
    xb = nc.dram_tensor("xb", [T, D], F32, kind="ExternalInput").ap()
    wq = nc.dram_tensor("wq", [D, 512], F32, kind="ExternalInput").ap()
    wk = nc.dram_tensor("wk", [D, 512], F32, kind="ExternalInput").ap()
    wv = nc.dram_tensor("wv", [D, 512], F32, kind="ExternalInput").ap()
    pkv = nc.dram_tensor("pkv", [64, 8 * 64], F32, kind="ExternalInput").ap()
    aB = nc.dram_tensor("aB", [64, 8 * 512], BF16, kind="ExternalInput").ap()
    bB = nc.dram_tensor("bB", [64, 8 * 128], BF16, kind="ExternalInput").ap()
    triui = nc.dram_tensor("triui", [128, 128], BF16, kind="ExternalInput").ap()
    cp = nc.dram_tensor("cp", [128, 8 * 144], F32, kind="ExternalInput").ap()
    xn_scr = nc.dram_tensor("xn_scr", [T, D], BF16, kind="Internal").ap()
    r_out = nc.dram_tensor("r_out", [128, 32 * 512], BF16, kind="ExternalOutput").ap()
    st_out = nc.dram_tensor("st_out", [1, 4], F32, kind="ExternalOutput").ap()

    NB = T // 128
    GAM = [1.0 - 2.0 ** (-5.0 - g) for g in range(16)]
    s = Sched()
    with ExitStack() as ctx:
        E = ctx.enter_context
        sb = {}
        xn_fm = E(nc.sbuf_tensor("xn_fm", [128, 8 * T], BF16))
        v_tm = E(nc.sbuf_tensor("v_tm", [128, NB * 512], BF16))
        wqh = [E(nc.sbuf_tensor(f"wqh{i}", [128, 8 * 64], BF16)) for i in range(2)]
        wkh = [E(nc.sbuf_tensor(f"wkh{i}", [128, 8 * 64], BF16)) for i in range(2)]
        Qh = E(nc.sbuf_tensor("Qh", [64, T], BF16))
        Qs = E(nc.sbuf_tensor("Qs", [64, T], BF16))
        Ks = E(nc.sbuf_tensor("Ks", [64, T], BF16))
        pkvf = E(nc.sbuf_tensor("pkvf", [64, 8 * 64], F32))
        pkvs = E(nc.sbuf_tensor("pkvs", [64, 8 * 64], BF16))
        aBt = E(nc.sbuf_tensor("aBt", [64, 8 * 512], BF16))
        bBt = E(nc.sbuf_tensor("bBt", [64, 8 * 128], BF16))
        triuit = E(nc.sbuf_tensor("triuit", [128, 128], BF16))
        cpt = E(nc.sbuf_tensor("cpt", [128, 8 * 144], F32))
        ps_sc = [E(nc.psum_tensor(f"ps_sc{i}", [128, 512], F32)) for i in range(2)]
        ps_r = [E(nc.psum_tensor(f"ps_r{i}", [128, 512], F32)) for i in range(4)]
        ps_n = E(nc.psum_tensor("ps_n", [1, 512], F32))

        t_c = [s.dma("sync", lambda h: h.dma_start(out=triuit[:], in_=triui[:, :]))]
        t_c.append(s.dma("sync", lambda h: h.dma_start(out=cpt[:], in_=cp[:, :])))
        t_c.append(s.dma("sync", lambda h: h.dma_start(out=aBt[:], in_=aB[:, :])))
        t_c.append(s.dma("sync", lambda h: h.dma_start(out=bBt[:], in_=bB[:, :])))
        t_c.append(s.dma("sync", lambda h: h.dma_start(out=pkvf[:], in_=pkv[:, :])))
        t_c.append(s.emit("vector", lambda h: h.tensor_scalar(
            pkvs[:], pkvf[:], 1024.0, None, ALU.mult), waits=t_c[-1:]))

        with ExitStack() as ctx2:
            E2 = ctx2.enter_context
            sb["ln_xt"] = [E2(nc.sbuf_tensor(f"lnxt{i}", [128, D], F32)) for i in range(2)]
            sb["ln_sq"] = E2(nc.sbuf_tensor("ln_sq", [128, D], F32))
            sb["ln_xn"] = E2(nc.sbuf_tensor("ln_xnb", [128, D], BF16))
            sb["ln_st"] = E2(nc.sbuf_tensor("ln_st", [128, 8], F32))
            wvt = E2(nc.sbuf_tensor("wvt", [128, 8 * 512], BF16))
            ln_toks = emit_ln(s, sb, xb, xn_scr, NB)
            tr_toks = emit_transpose_in(s, xn_fm, xn_scr, T, ln_toks)
            w_v = emit_load_w_bf16(s, wvt, wv, 8, 512)
            prev_v = {}
            for tt in range(NB):
                ps = ps_sc[tt % 2]
                wl = tr_toks + w_v + ([prev_v[id(ps)]] if id(ps) in prev_v else [])
                tok = None
                for kc in range(8):
                    tok = s.emit("tensor", lambda h, kc=kc, tt=tt, ps=ps: h.matmul(
                        ps[:], xn_fm[:, kc * T + tt * 128: kc * T + (tt + 1) * 128],
                        wvt[:, kc * 512:(kc + 1) * 512],
                        start=(kc == 0), stop=(kc == 7)), waits=(wl if kc == 0 else []))
                te = s.emit("vector", lambda h, tt=tt, ps=ps: h.tensor_copy(
                    v_tm[:, tt * 512:(tt + 1) * 512], ps[:]), waits=[tok])
                prev_v[id(ps)] = te

        bar0 = [(e, s.cnt[e]) for e in SEMS]
        r_buf = E(nc.sbuf_tensor("r_buf", [128, NB * 512], BF16))
        sqb = E(nc.sbuf_tensor("sqb", [128, 512], BF16))
        ones128 = E(nc.sbuf_tensor("ones128", [128, 1], BF16))
        strow = E(nc.sbuf_tensor("strow", [1, 16], F32))
        pt = [E(nc.sbuf_tensor(f"pt{i}", [128, 512], BF16)) for i in range(2)]
        tz = s.emit("gpsimd", lambda h: h.memset(ones128[:], 1.0), waits=bar0)

        prev = {}
        wtok = [None, None]
        for hh in range(HL):
            hb = hh % 2
            wl = list(bar0) + ([wtok[hb]] if wtok[hb] else [])
            twq = s.dma("gpsimd", lambda h, hh=hh, hb=hb: h.dma_start(
                out=wqh[hb][:], in_=wq[:, hh * 64:(hh + 1) * 64].rearrange("(kc p) c -> p kc c", p=128)),
                waits=wl)
            twk = s.dma("gpsimd", lambda h, hh=hh, hb=hb: h.dma_start(
                out=wkh[hb][:], in_=wk[:, hh * 64:(hh + 1) * 64].rearrange("(kc p) c -> p kc c", p=128)),
                waits=wl)
            for side in ("q", "k"):
                wt_, tws = (wqh, twq) if side == "q" else (wkh, twk)
                for tc in range(8):
                    ps = ps_sc[tc % 2]
                    wl = list(bar0) + [tws] + ([prev[id(ps)]] if id(ps) in prev else [])
                    tok = None
                    for kc in range(8):
                        tok = s.emit("tensor", lambda h, kc=kc, tc=tc, hb=hb, wt_=wt_, ps=ps: h.matmul(
                            ps[0:64, :], wt_[hb][:, kc * 64:(kc + 1) * 64],
                            xn_fm[:, kc * T + tc * 512: kc * T + (tc + 1) * 512],
                            start=(kc == 0), stop=(kc == 7)), waits=(wl if kc == 0 else []))
                    if side == "q":
                        tev = s.emit("scalar", lambda h, ps=ps, tc=tc: h.copy(
                            Qh[:, tc * 512:(tc + 1) * 512], ps[0:64, :]), waits=[tok])
                        tqs = s.emit("vector", lambda h, ps=ps, tc=tc, hh=hh: h.tensor_tensor(
                            Qs[:, tc * 512:(tc + 1) * 512], ps[0:64, :],
                            aBt[:, hh * 512:(hh + 1) * 512], ALU.mult), waits=[tok, tev] + t_c)
                        prev[id(ps)] = tqs
                        prev["qev"] = tev
                    else:
                        tks = s.emit("vector", lambda h, ps=ps, tc=tc, hh=hh: h.tensor_tensor(
                            Ks[:, tc * 512:(tc + 1) * 512].rearrange("p (a m) -> p a m", a=4),
                            ps[0:64, :].rearrange("p (a m) -> p a m", a=4),
                            bBt[:, hh * 128:(hh + 1) * 128].unsqueeze(1).broadcast_to([64, 4, 128]),
                            ALU.mult), waits=[tok] + t_c)
                        prev[id(ps)] = tks
            pk_done = [("vector", s.cnt["vector"]), ("scalar", s.cnt["scalar"])]

            for tcq in range(8):
                for j in range(4):
                    tb_ = 4 * tcq + j
                    tcx = s.emit("tensor", lambda h, j=j, tb_=tb_, hh=hh: h.matmul(
                        ps_r[j][:, 0:64], Qh[:, tb_ * 128:(tb_ + 1) * 128],
                        pkvs[:, hh * 64:(hh + 1) * 64], start=True, stop=False),
                        waits=pk_done + t_c + ([prev[("pr", j)]] if ("pr", j) in prev else []))
                    prev[("prx", j)] = tcx
                for sb_ in range(4 * tcq + 4):
                    pp = ps_sc[sb_ % 2]
                    pb = sb_ % 2
                    wl = pk_done + ([prev[("pt", pb)]] if ("pt", pb) in prev else []) + \
                         ([prev[id(pp)]] if id(pp) in prev else [])
                    tmm = s.emit("tensor", lambda h, tcq=tcq, sb_=sb_, pp=pp: h.matmul(
                        pp[:], Ks[:, sb_ * 128:(sb_ + 1) * 128],
                        Qs[:, tcq * 512:(tcq + 1) * 512],
                        start=True, stop=True), waits=wl)
                    j0 = sb_ - 4 * tcq
                    ci = cp_index(hh, tcq, sb_)
                    tex = s.emit("vector", lambda h, pb=pb, pp=pp, ci=ci: h.tensor_scalar(
                        pt[pb][:], pp[:], cpt[:, ci:ci + 1], None, ALU.mult), waits=[tmm] + t_c)
                    if 0 <= j0 <= 3:
                        tex = s.emit("vector", lambda h, pb=pb, pp=pp, j0=j0, ci=ci: h.scalar_tensor_tensor(
                            pt[pb][:, j0 * 128:(j0 + 1) * 128], pp[:, j0 * 128:(j0 + 1) * 128],
                            cpt[:, ci:ci + 1], triuit[:], ALU.mult, ALU.mult), waits=[tex])
                    prev[id(pp)] = tex
                    tpv = None
                    for j in range(max(0, j0), 4):
                        tpv = s.emit("tensor", lambda h, pb=pb, j=j, sb_=sb_, hh=hh: h.matmul(
                            ps_r[j][:, 0:64], pt[pb][:, j * 128:(j + 1) * 128],
                            v_tm[:, sb_ * 512 + hh * 64: sb_ * 512 + (hh + 1) * 64],
                            start=False, stop=(sb_ == 4 * tcq + j)),
                            waits=[tex, prev[("prx", j)]])
                    prev[("pt", pb)] = tpv if tpv is not None else tex
                for j in range(4):
                    tfo = s.emit("scalar", lambda h, j=j, tcq=tcq, hh=hh: h.copy(
                        r_buf[:, (4 * tcq + j) * 512 + hh * 64: (4 * tcq + j) * 512 + (hh + 1) * 64],
                        ps_r[j][:, 0:64]), waits=[prev[("pt", 0)], prev[("pt", 1)]])
                    prev[("pr", j)] = tfo
            wtok[hb] = prev[("pt", 0)]

        rdone = [prev[("pr", j)] for j in range(4)]
        tok1 = tok2 = None
        for i in range(NB):
            tok1 = s.emit("tensor", lambda h, i=i: h.matmul(
                ps_n[:], ones128[:], r_buf[:, i * 512:(i + 1) * 512],
                start=(i == 0), stop=(i == NB - 1)), waits=(rdone + [tz] if i == 0 else []))
        ts1 = s.emit("vector", lambda h: h.reduce_sum(strow[:, 0:1], ps_n[:], AX), waits=[tok1])
        sq_ev = []
        for i in range(NB):
            tsq = s.emit("scalar", lambda h, i=i: h.square(sqb[:], r_buf[:, i * 512:(i + 1) * 512]),
                         waits=rdone + [ts1] + sq_ev[-1:])
            tok2 = s.emit("tensor", lambda h, i=i: h.matmul(
                ps_n[:], ones128[:], sqb[:], start=(i == 0), stop=(i == NB - 1)),
                waits=[tsq] + ([ts1] if i == 0 else []))
            sq_ev.append(tok2)
        ts2 = s.emit("vector", lambda h: h.reduce_sum(strow[:, 1:2], ps_n[:], AX), waits=[tok2])
        tds = s.dma("sync", lambda h: h.dma_start(out=st_out[0:1, 0:2], in_=strow[:, 0:2]), waits=[ts1, ts2])
        for part in range(4):
            s.dma("sync", lambda h, part=part: h.dma_start(
                out=r_out[:, part * 8 * 512:(part + 1) * 8 * 512],
                in_=r_buf[:, part * 8 * 512:(part + 1) * 8 * 512]), waits=rdone + [ts2])
        s.build(nc, ctx)
    return nc


def cp_index(hh, tcq, sb_):
    return hh * 144 + 2 * tcq * tcq + 2 * tcq + sb_


def make_cp_table(p):
    tab = np.zeros(8 * 144, np.float32)
    for hh in range(8):
        g = 1.0 - 2.0 ** (-5.0 - (hh + 8 * p))
        for tcq in range(8):
            for sb_ in range(4 * tcq + 4):
                e = 512 * tcq - 128 * sb_
                tab[cp_index(hh, tcq, sb_)] = np.float32(g) ** e
    return np.ascontiguousarray(np.broadcast_to(tab, (128, 8 * 144)))


def make_ab_tables(p):
    aB = np.zeros((64, 8 * 512), np.float32)
    bB = np.zeros((64, 8 * 128), np.float32)
    for hh in range(8):
        g = 1.0 - 2.0 ** (-5.0 - (hh + 8 * p))
        aB[:, hh * 512:(hh + 1) * 512] = g ** np.arange(512)[None, :]
        bB[:, hh * 128:(hh + 1) * 128] = g ** (-np.arange(128)[None, :].astype(np.float64))
    return aB.astype(NPBF16), bB.astype(NPBF16)


def run_retnet_sublayer(x, ins):
    wqkv = ins["L1_attn_b_W_qkv"]
    in_maps = []
    for c in range(8):
        b, p = c // 2, c % 2
        hs = p * 512
        aB, bB = make_ab_tables(p)
        in_maps.append(dict(
            xb=np.ascontiguousarray(x[b]),
            wq=np.ascontiguousarray(wqkv[:, hs:hs + 512]),
            wk=np.ascontiguousarray(wqkv[:, 1024 + hs:1024 + hs + 512]),
            wv=np.ascontiguousarray(wqkv[:, 2048 + hs:2048 + hs + 512]),
            pkv=np.ascontiguousarray(ins["L1_attn_b_past_kv"][p * 8:(p + 1) * 8].transpose(1, 0, 2).reshape(64, 512)),
            aB=aB, bB=bB, triui=TRIUI, cp=make_cp_table(p),
        ))
    res = run_bass_kernel_spmd(prog("rn_a", build_rn_a), in_maps, core_ids=list(range(8)))
    r_full = [np.concatenate([o_to_tokmajor(res.results[2 * b]["r_out"]),
                              o_to_tokmajor(res.results[2 * b + 1]["r_out"])], axis=1)
              for b in range(4)]
    gnw_col = col_by_head(ins["L1_attn_b_gn_weight"])
    gnb_col = col_by_head(ins["L1_attn_b_gn_bias"])
    in_maps = []
    for c in range(8):
        b, p = c // 2, c % 2
        stats = np.concatenate([res.results[2 * b]["st_out"][0],
                                res.results[2 * b + 1]["st_out"][0]]).reshape(1, 8)
        in_maps.append(dict(
            x_fm=fm(x[b, p * TL:(p + 1) * TL]),
            r_fm=fm(r_full[b][p * TL:(p + 1) * TL]).astype(NPBF16),
            st2=np.ascontiguousarray(stats.astype(np.float32)),
            gnw_col=gnw_col, gnb_col=gnb_col,
        ))
    res = run_bass_kernel_spmd(prog("rn_b", build_rn_b), in_maps, core_ids=list(range(8)))
    xn = np.empty_like(x)
    for c in range(8):
        b, p = c // 2, c % 2
        xn[b, p * TL:(p + 1) * TL] = unfm(res.results[c]["xo"], TL)
    return xn


def col_by_head(g):
    feat = np.repeat(g, 64)
    return np.ascontiguousarray(feat.reshape(8, 128).T.astype(np.float32))


def build_rn_b():
    nc = bass.Bass("TRN2", target_bir_lowering=False)
    x_fm = nc.dram_tensor("x_fm", [128, 8 * TL], F32, kind="ExternalInput").ap()
    r_fm = nc.dram_tensor("r_fm", [128, 8 * TL], BF16, kind="ExternalInput").ap()
    st2 = nc.dram_tensor("st2", [1, 8], F32, kind="ExternalInput").ap()
    gnw_col = nc.dram_tensor("gnw_col", [128, 8], F32, kind="ExternalInput").ap()
    gnb_col = nc.dram_tensor("gnb_col", [128, 8], F32, kind="ExternalInput").ap()
    bscr = nc.dram_tensor("bscr", [1, 256], F32, kind="Internal").ap()
    xo = nc.dram_tensor("xo", [128, 8 * TL], F32, kind="ExternalOutput").ap()
    NTOT = float(16 * 64 * T)
    s = Sched()
    with ExitStack() as ctx:
        E = ctx.enter_context
        xt = E(nc.sbuf_tensor("xt", [128, 8 * TL], F32))
        rt = E(nc.sbuf_tensor("rt", [128, 8 * TL], BF16))
        stt_ = E(nc.sbuf_tensor("stt_", [1, 16], F32))
        gw = E(nc.sbuf_tensor("gw", [128, 8], F32))
        gb = E(nc.sbuf_tensor("gb", [128, 8], F32))
        row = E(nc.sbuf_tensor("row", [1, 256], F32))
        colb = E(nc.sbuf_tensor("colb", [128, 2], F32))
        alf = E(nc.sbuf_tensor("alf", [128, 8], F32))
        bet = E(nc.sbuf_tensor("bet", [128, 8], F32))
        tmp = E(nc.sbuf_tensor("tmp", [128, 512], F32))
        on1 = E(nc.sbuf_tensor("on1", [1, 256], F32))
        xob = E(nc.sbuf_tensor("xob", [128, 8 * TL], F32))

        tx = s.dma("sync", lambda h: h.dma_start(out=xt[:], in_=x_fm[:, :]))
        tr = s.dma("sync", lambda h: h.dma_start(out=rt[:], in_=r_fm[:, :]))
        tst = s.dma("sync", lambda h: h.dma_start(out=stt_[0:1, 0:8], in_=st2[0:1, :]))
        tgw = s.dma("sync", lambda h: h.dma_start(out=gw[:], in_=gnw_col[:, :]))
        tgb = s.dma("sync", lambda h: h.dma_start(out=gb[:], in_=gnb_col[:, :]))
        to1 = s.emit("gpsimd", lambda h: h.memset(on1[:], 1.0))
        t1 = s.emit("vector", lambda h: h.tensor_tensor(stt_[0:1, 8:9], stt_[0:1, 0:1], stt_[0:1, 4:5], ALU.add), waits=[tst])
        t2 = s.emit("vector", lambda h: h.tensor_tensor(stt_[0:1, 9:10], stt_[0:1, 1:2], stt_[0:1, 5:6], ALU.add), waits=[t1])
        tmn = s.emit("vector", lambda h: h.tensor_scalar(stt_[0:1, 10:11], stt_[0:1, 8:9], 1.0 / NTOT, None, ALU.mult), waits=[t2])
        tm2 = s.emit("vector", lambda h: h.tensor_scalar(stt_[0:1, 11:12], stt_[0:1, 10:11], stt_[0:1, 10:11], None, ALU.mult), waits=[tmn])
        tvr = s.emit("vector", lambda h: h.scalar_tensor_tensor(
            stt_[0:1, 12:13], stt_[0:1, 9:10], 1.0 / NTOT, stt_[0:1, 11:12], ALU.mult, ALU.subtract), waits=[tm2])
        tve = s.emit("vector", lambda h: h.tensor_scalar(stt_[0:1, 13:14], stt_[0:1, 12:13], 1e-5, None, ALU.add), waits=[tvr])
        tsd = s.emit("scalar", lambda h: h.sqrt(stt_[0:1, 14:15], stt_[0:1, 13:14]), waits=[tve])
        trc = s.emit("vector", lambda h: h.reciprocal(stt_[0:1, 15:16], stt_[0:1, 14:15]), waits=[tsd])
        tr1 = s.emit("vector", lambda h: h.tensor_scalar(row[:, 0:128], on1[:, 0:128], stt_[0:1, 15:16], None, ALU.mult), waits=[trc, to1])
        tr2 = s.emit("vector", lambda h: h.tensor_scalar(row[:, 128:256], on1[:, 0:128], stt_[0:1, 10:11], -1.0, ALU.mult, ALU.mult), waits=[tr1])
        td1 = s.dma("sync", lambda h: h.dma_start(out=bscr[0:1, :], in_=row[:, :]), waits=[tr2])
        td2a = s.dma("sync", lambda h: h.dma_start(
            out=colb[:, 0:1], in_=bscr[0:1, 0:128].rearrange("a (p c) -> (a p) c", c=1)), waits=[td1])
        td2 = s.dma("sync", lambda h: h.dma_start(
            out=colb[:, 1:2], in_=bscr[0:1, 128:256].rearrange("a (p c) -> (a p) c", c=1)), waits=[td1])
        ta = s.emit("vector", lambda h: h.tensor_scalar(alf[:], gw[:], colb[:, 0:1], None, ALU.mult), waits=[td2a, td2, tgw])
        tb = s.emit("vector", lambda h: h.scalar_tensor_tensor(
            bet[:], alf[:], colb[:, 1:2], gb[:], ALU.mult, ALU.add), waits=[ta, tgb])
        prev = None
        for oc in range(8):
            for tc in range(TL // 512):
                sl = slice(oc * TL + tc * 512, oc * TL + (tc + 1) * 512)
                tt1 = s.emit("vector", lambda h, sl=sl, oc=oc: h.tensor_scalar(
                    tmp[:], rt[:, sl], alf[:, oc:oc + 1], bet[:, oc:oc + 1], ALU.mult, ALU.add),
                    waits=[tb, tr] + ([prev] if prev else []))
                tt2 = s.emit("vector", lambda h, sl=sl: h.tensor_tensor(
                    xob[:, sl], tmp[:], xt[:, sl], ALU.add), waits=[tt1, tx])
                prev = tt2
        for j2 in range(8):
            s.dma("sync", lambda h, j2=j2: h.dma_start(
                out=xo[:, j2 * TL:(j2 + 1) * TL], in_=xob[:, j2 * TL:(j2 + 1) * TL]), waits=[prev])
        s.build(nc, ctx)
    return nc


def kernel(**inputs):
    ins = {k: np.asarray(v) for k, v in inputs.items()}
    x = np.ascontiguousarray(ins["L0_attn_a_x"], dtype=np.float32)
    qcache = {}
    x = run_deltanet_sublayer(x, ins)
    x = run_moe_sublayer(x, ins, "L0_ffn")
    x = run_rope_attn_sublayer(x, ins)
    x = run_moe_sublayer(x, ins, "L0_ffn")
    x = run_alibi_attn_sublayer(x, ins)
    x = run_ffn1_sublayer(x, ins, qcache)
    x = run_retnet_sublayer(x, ins)
    x = run_ffn1_sublayer(x, ins, qcache)
    return x


# version 12 (final)
# speedup vs baseline: 1.1291x; 1.0132x over previous
import numpy as np
import ml_dtypes
from contextlib import ExitStack

import concourse.bass as bass
import concourse.mybir as mybir
from concourse.bass_utils import run_bass_kernel_spmd

F32 = mybir.dt.float32
BF16 = mybir.dt.bfloat16
I32 = mybir.dt.int32
AF = mybir.ActivationFunctionType
ALU = mybir.AluOpType
AX = mybir.AxisListType.X
NPBF16 = ml_dtypes.bfloat16

D = 1024
T = 4096
TL = 2048
HL = 8
HD = 64
EPS_LN = 1e-5

ENGINES = ("tensor", "vector", "scalar", "sync", "gpsimd")
SEMS = ENGINES + ("gpdma",)

class Sched:
    def __init__(self):
        self.q = {e: [] for e in ENGINES}
        self.cnt = {e: 0 for e in SEMS}

    def emit(self, eng, fn, waits=(), inc=1):
        sem = "gpdma" if (eng == "gpsimd" and inc == 16) else eng
        self.cnt[sem] += inc
        tok = (sem, self.cnt[sem])
        self.q[eng].append((fn, tuple(waits), inc, sem))
        return tok

    def dma(self, eng, fn, waits=()):
        return self.emit(eng, fn, waits, inc=16)

    def build(self, nc, ctx):
        sems = {e: ctx.enter_context(nc.semaphore(f"s_{e}")) for e in SEMS}
        block = ctx.enter_context(nc.Block())
        sched = self

        def replay(eng_name, h):
            hw = {e: 0 for e in SEMS}
            cnt_self = {e: 0 for e in SEMS}
            for fn, waits, inc, sem in sched.q[eng_name]:
                if inc == 16 and cnt_self[sem] > hw[sem]:
                    h.wait_ge(sems[sem], cnt_self[sem])
                    hw[sem] = cnt_self[sem]
                for (we, wv) in waits:
                    if wv > hw[we]:
                        h.wait_ge(sems[we], wv)
                        hw[we] = wv
                cnt_self[sem] += inc
                inst = fn(h)
                if inst is not None:
                    inst.then_inc(sems[sem], inc)
                else:
                    h.nop().then_inc(sems[sem], inc)

        @block.tensor
        def _(h):
            replay("tensor", h)

        @block.vector
        def _(h):
            replay("vector", h)

        @block.scalar
        def _(h):
            replay("scalar", h)
            for e in SEMS:
                if sched.cnt[e] > 0 and e != "scalar":
                    h.wait_ge(sems[e], sched.cnt[e])

        @block.sync
        def _(h):
            replay("sync", h)

        @block.gpsimd
        def _(h):
            replay("gpsimd", h)



def emit_ln(s, sb, x_dram, xn_dram, ntiles, pre_waits=()):
    xt, sq, xnb, st = sb["ln_xt"], sb["ln_sq"], sb["ln_xn"], sb["ln_st"]
    out_toks = []
    last = [None, None]
    for i in range(ntiles):
        b = i % 2
        w = list(pre_waits) + ([last[b]] if last[b] else [])
        tld = s.dma("sync", lambda h, i=i, b=b: h.dma_start(
            out=xt[b][:], in_=x_dram[i * 128:(i + 1) * 128, :]), waits=w)
        tsq = s.emit("scalar", lambda h, b=b: h.square(sq[:], xt[b][:]), waits=[tld])
        t1 = s.emit("vector", lambda h, b=b: h.reduce_sum(st[:, 0:1], xt[b][:], AX), waits=[tld])
        t2 = s.emit("vector", lambda h: h.reduce_sum(st[:, 1:2], sq[:], AX), waits=[tsq, t1])
        tm = s.emit("vector", lambda h: h.tensor_scalar(st[:, 2:3], st[:, 0:1], 1.0 / D, None, ALU.mult), waits=[t2])
        tm2 = s.emit("vector", lambda h: h.tensor_scalar(st[:, 3:4], st[:, 2:3], st[:, 2:3], None, ALU.mult), waits=[tm])
        tv = s.emit("vector", lambda h: h.scalar_tensor_tensor(
            st[:, 4:5], st[:, 1:2], 1.0 / D, st[:, 3:4], ALU.mult, ALU.subtract), waits=[tm2])
        tve = s.emit("vector", lambda h: h.tensor_scalar(st[:, 5:6], st[:, 4:5], EPS_LN, None, ALU.add), waits=[tv])
        tsd = s.emit("scalar", lambda h: h.sqrt(st[:, 6:7], st[:, 5:6]), waits=[tve])
        trs = s.emit("vector", lambda h: h.reciprocal(st[:, 7:8], st[:, 6:7]), waits=[tsd])
        txn = s.emit("vector", lambda h, b=b: h.tensor_scalar(
            xnb[:], xt[b][:], st[:, 2:3], st[:, 7:8], ALU.subtract, ALU.mult),
            waits=[trs] + ([sb["_ln_prev_store"]] if "_ln_prev_store" in sb else []))
        tst = s.dma("sync", lambda h, i=i: h.dma_start(
            out=xn_dram[i * 128:(i + 1) * 128, :], in_=xnb[:]), waits=[txn])
        last[b] = txn
        sb["_ln_prev_store"] = tst
        out_toks.append(tst)
    return out_toks


def emit_transpose_in(s, sb_fm, xn_dram, Tp, waits):
    toks = []
    for j in range(D // 128):
        t = s.dma("sync", lambda h, j=j: h.dma_start_transpose(
            sb_fm[:, j * Tp:(j + 1) * Tp], xn_dram[:, j * 128:(j + 1) * 128]), waits=waits)
        toks.append(t)
    return toks


def emit_load_w_bf16(s, dst, src_dram, kchunks, ncols, waits=()):
    toks = []
    for kc in range(kchunks):
        t = s.dma("gpsimd", lambda h, kc=kc: h.dma_start(
            out=dst[:, kc * ncols:(kc + 1) * ncols],
            in_=src_dram[kc * 128:(kc + 1) * 128, :]), waits=waits)
        toks.append(t)
    return toks



def build_dn_a(debug=False, serial_scan=False, nchunks=None, skip_ph1=False, nwaves=99):
    nc = bass.Bass("TRN2", target_bir_lowering=False)
    xb = nc.dram_tensor("xb", [T, D], F32, kind="ExternalInput").ap()
    wq = nc.dram_tensor("wq", [D, 512], F32, kind="ExternalInput").ap()
    wk = nc.dram_tensor("wk", [D, 512], F32, kind="ExternalInput").ap()
    wv = nc.dram_tensor("wv", [D, 512], F32, kind="ExternalInput").ap()
    wbe = nc.dram_tensor("wbe", [D, 8], F32, kind="ExternalInput").ap()
    bq_row = nc.dram_tensor("bq_row", [1, 512], F32, kind="ExternalInput").ap()
    bk_row = nc.dram_tensor("bk_row", [1, 512], F32, kind="ExternalInput").ap()
    bv_row = nc.dram_tensor("bv_row", [1, 512], F32, kind="ExternalInput").ap()
    bbe_row = nc.dram_tensor("bbe_row", [1, 8], F32, kind="ExternalInput").ap()
    ident = nc.dram_tensor("ident", [128, 128], BF16, kind="ExternalInput").ap()
    stril = nc.dram_tensor("stril", [128, 128], BF16, kind="ExternalInput").ap()
    triui = nc.dram_tensor("triui", [128, 128], BF16, kind="ExternalInput").ap()
    xn_scr = nc.dram_tensor("xn_scr", [T, D], BF16, kind="Internal").ap()
    w_scr = nc.dram_tensor("w_scr", [(T // 128) * 8 * 128, 128], BF16, kind="Internal").ap()
    o_out = nc.dram_tensor("o_out", [128, 4 * T], BF16, kind="ExternalOutput").ap()
    if debug:
        dbg_q = nc.dram_tensor("dbg_q", [128, 4 * T], BF16, kind="ExternalOutput").ap()
        dbg_ktm = nc.dram_tensor("dbg_ktm", [128, 32 * 512], BF16, kind="ExternalOutput").ap()
        dbg_vtm = nc.dram_tensor("dbg_vtm", [128, 32 * 512], BF16, kind="ExternalOutput").ap()
        dbg_beta = nc.dram_tensor("dbg_beta", [128, 32 * 8], F32, kind="ExternalOutput").ap()
        dbg_kfm = nc.dram_tensor("dbg_kfm", [128, 4 * T], BF16, kind="ExternalOutput").ap()
        dbg_xn = nc.dram_tensor("dbg_xn", [128, D], BF16, kind="ExternalOutput").ap()

    s = Sched()
    with ExitStack() as ctx:
        E = ctx.enter_context
        sb = {}
        identt = E(nc.sbuf_tensor("identt", [128, 128], BF16))
        strilt = E(nc.sbuf_tensor("strilt", [128, 128], BF16))
        triuit = E(nc.sbuf_tensor("triuit", [128, 128], BF16))
        ones1 = E(nc.sbuf_tensor("ones1", [1, 128], BF16))
        bqr = E(nc.sbuf_tensor("bqr", [1, 512], BF16))
        bkr = E(nc.sbuf_tensor("bkr", [1, 512], BF16))
        bvr = E(nc.sbuf_tensor("bvr", [1, 512], BF16))
        bber = E(nc.sbuf_tensor("bber", [1, 8], BF16))
        q_tm = E(nc.sbuf_tensor("q_tm", [128, 32 * 512], BF16))
        k_tm = E(nc.sbuf_tensor("k_tm", [128, 32 * 512], BF16))
        v_tm = E(nc.sbuf_tensor("v_tm", [128, 32 * 512], BF16))
        beta = E(nc.sbuf_tensor("beta", [128, 32 * 8], F32))
        nbeta = E(nc.sbuf_tensor("nbeta", [128, 32 * 8], F32))

        t_c = []
        t_c.append(s.dma("sync", lambda h: h.dma_start(out=identt[:], in_=ident[:, :])))
        t_c.append(s.dma("sync", lambda h: h.dma_start(out=strilt[:], in_=stril[:, :])))
        t_c.append(s.dma("sync", lambda h: h.dma_start(out=triuit[:], in_=triui[:, :])))
        t_c.append(s.emit("gpsimd", lambda h: h.memset(ones1[:], 1.0)))
        t_c.append(s.dma("gpsimd", lambda h: h.dma_start(out=bqr[:], in_=bq_row[:, :])))
        t_c.append(s.dma("gpsimd", lambda h: h.dma_start(out=bkr[:], in_=bk_row[:, :])))
        t_c.append(s.dma("gpsimd", lambda h: h.dma_start(out=bvr[:], in_=bv_row[:, :])))
        t_c.append(s.dma("gpsimd", lambda h: h.dma_start(out=bber[:], in_=bbe_row[:, :])))

        with ExitStack() as ctx2:
            E2 = ctx2.enter_context
            xn_fm = E2(nc.sbuf_tensor("xn_fm", [128, 8 * T], BF16))
            sb["ln_xt"] = [E2(nc.sbuf_tensor(f"ln_xt{i}", [128, D], F32)) for i in range(2)]
            sb["ln_sq"] = E2(nc.sbuf_tensor("ln_sq", [128, D], F32))
            sb["ln_xn"] = E2(nc.sbuf_tensor("ln_xnb", [128, D], BF16))
            sb["ln_st"] = E2(nc.sbuf_tensor("ln_st", [128, 8], F32))
            wqt = E2(nc.sbuf_tensor("wqt", [128, 8 * 512], BF16))
            wkt = E2(nc.sbuf_tensor("wkt", [128, 8 * 512], BF16))
            wvt = E2(nc.sbuf_tensor("wvt", [128, 8 * 512], BF16))
            wbet = E2(nc.sbuf_tensor("wbet", [128, 8 * 8], BF16))
            ksq = sb["ln_sq"]
            kst = sb["ln_xt"][0]

            def krw(kb):
                return (sb["ln_xt"][0] if kb == 0 else sb["ln_sq"])[:, 512:1024]
            ps_a = E2(nc.psum_tensor("ps_a", [128, 512], F32))
            ps_b = E2(nc.psum_tensor("ps_b", [128, 512], F32))
            ps_c = E2(nc.psum_tensor("ps_c", [128, 512], F32))
            ps_s = E2(nc.psum_tensor("ps_s", [128, 8], F32))

            ln_toks = emit_ln(s, sb, xb, xn_scr, T // 128)
            tr_toks = emit_transpose_in(s, xn_fm, xn_scr, T, ln_toks)
            wq_t = emit_load_w_bf16(s, wqt, wq, 8, 512)
            wk_t = emit_load_w_bf16(s, wkt, wk, 8, 512)
            wv_t = emit_load_w_bf16(s, wvt, wv, 8, 512)
            wbe_t = emit_load_w_bf16(s, wbet, wbe, 8, 8)

            ev_prev = {}
            for tt in range(T // 128):
                for which in ("k", "be", "v", "q"):
                    wt = {"k": wkt, "v": wvt, "be": wbet, "q": wqt}[which]
                    br = {"k": bkr, "v": bvr, "be": bber, "q": bqr}[which]
                    ncols = 8 if which == "be" else 512
                    ps = ps_s if which == "be" else [ps_a, ps_b][tt % 2]
                    w = tr_toks + {"k": wk_t, "v": wv_t, "be": wbe_t, "q": wq_t}[which] + t_c
                    key = (id(ps),)
                    if key in ev_prev:
                        w = w + [ev_prev[key]]
                    tok = None
                    for kc in range(8):
                        tok = s.emit("tensor", lambda h, kc=kc, tt=tt, wt=wt, ncols=ncols, ps=ps: h.matmul(
                            ps[:, 0:ncols],
                            xn_fm[:, kc * T + tt * 128: kc * T + (tt + 1) * 128],
                            wt[:, kc * ncols:(kc + 1) * ncols],
                            start=(kc == 0), stop=False), waits=(w if kc == 0 else []))
                    tok = s.emit("tensor", lambda h, br=br, ncols=ncols, ps=ps: h.matmul(
                        ps[:, 0:ncols], ones1[:], br[:, 0:ncols], start=False, stop=True))
                    if which == "v":
                        tev = s.emit("vector", lambda h, tt=tt, ps=ps: h.tensor_tensor(
                            v_tm[:, tt * 512:(tt + 1) * 512].rearrange("p (a m) -> p a m", a=8),
                            ps[:].rearrange("p (a m) -> p a m", a=8),
                            beta[:, tt * 8:(tt + 1) * 8].unsqueeze(2).broadcast_to([128, 8, 64]),
                            ALU.mult), waits=[tok, ev_prev[(id(ps_s),)]])
                        ev_prev[(id(ps),)] = tev
                    elif which == "q":
                        tev = s.emit("vector", lambda h, tt=tt, ps=ps: h.tensor_copy(
                            q_tm[:, tt * 512:(tt + 1) * 512], ps[:]), waits=[tok])
                        ev_prev[(id(ps),)] = tev
                    elif which == "be":
                        tsg = s.emit("scalar", lambda h, tt=tt: h.activation(
                            beta[:, tt * 8:(tt + 1) * 8], ps_s[:], AF.Sigmoid,
                            bias=0.0, scale=1.0), waits=[tok])
                        tnb = s.emit("vector", lambda h, tt=tt: h.tensor_scalar(
                            nbeta[:, tt * 8:(tt + 1) * 8], beta[:, tt * 8:(tt + 1) * 8],
                            -1.0, None, ALU.mult), waits=[tsg])
                        ev_prev[(id(ps_s),)] = tnb
                    else:
                        kb = tt % 2
                        tkr = s.emit("scalar", lambda h, kb=kb, ps=ps: h.copy(
                            krw(kb), ps[:]),
                            waits=[tok] + ([ev_prev[("kraw", kb)]] if ("kraw", kb) in ev_prev else []))
                        ev_prev[(id(ps),)] = tkr
                        tq = s.emit("scalar", lambda h, kb=kb: h.square(ksq[:, 0:512], krw(kb)), waits=[tkr])
                        trd = s.emit("vector", lambda h: h.tensor_reduce(
                            kst[:, 0:8], ksq[:, 0:512].rearrange("p (a m) -> p a m", a=8), AX, ALU.add), waits=[tq])
                        tsd = s.emit("scalar", lambda h: h.sqrt(kst[:, 8:16], kst[:, 0:8]), waits=[trd])
                        tep = s.emit("vector", lambda h: h.tensor_scalar(
                            kst[:, 8:16], kst[:, 8:16], 1e-6, None, ALU.add), waits=[tsd])
                        trc = s.emit("vector", lambda h: h.reciprocal(kst[:, 16:24], kst[:, 8:16]), waits=[tep])
                        tkn = s.emit("vector", lambda h, tt=tt, kb=kb: h.tensor_tensor(
                            k_tm[:, tt * 512:(tt + 1) * 512].rearrange("p (a m) -> p a m", a=8),
                            krw(kb).rearrange("p (a m) -> p a m", a=8),
                            kst[:, 16:24].unsqueeze(2).broadcast_to([128, 8, 64]),
                            ALU.mult), waits=[trc])
                        ev_prev[("kraw", kb)] = tkn

        def ktm_h(h_, c):
            return k_tm[:, c * 512 + h_ * 64: c * 512 + (h_ + 1) * 64]

        def vtm_h(h_, c):
            return v_tm[:, c * 512 + h_ * 64: c * 512 + (h_ + 1) * 64]

        def qtm_h(h_, c):
            return q_tm[:, c * 512 + h_ * 64: c * 512 + (h_ + 1) * 64]

        def beta_ap(h_, c):
            return beta[:, c * 8 + h_: c * 8 + h_ + 1]

        def nbeta_ap(h_, c):
            return nbeta[:, c * 8 + h_: c * 8 + h_ + 1]

        bar0 = [(e, s.cnt[e]) for e in SEMS]
        NC = T // 128
        NRUN = NC if nchunks is None else nchunks
        NSLOT = 4
        with ExitStack() as ctx3:
            E3 = ctx3.enter_context
            slots = []
            for i3 in range(NSLOT):
                slots.append(dict(
                    Kt=E3(nc.sbuf_tensor(f"p1K{i3}", [64, 128], BF16)),
                    Nt=E3(nc.sbuf_tensor(f"p1Nt{i3}", [128, 128], BF16)),
                    NTt=E3(nc.sbuf_tensor(f"p1NT{i3}", [128, 128], BF16)),
                    X=[E3(nc.sbuf_tensor(f"p1X{i3}_{j3}", [128, 128], BF16)) for j3 in range(2)],
                    XT=[E3(nc.sbuf_tensor(f"p1XT{i3}_{j3}", [128, 128], BF16)) for j3 in range(2)],
                    P=[E3(nc.sbuf_tensor(f"p1P{i3}_{j3}", [128, 128], BF16)) for j3 in range(2)],
                    pmF=E3(nc.psum_tensor(f"p1pf{i3}", [128, 512], F32)),
                    pmT=E3(nc.psum_tensor(f"p1pt{i3}", [128, 128], BF16)),
                ))

            def phase1_u(u, sl, start_wait):
                c, hh = u // 8, u % 8
                S = slots[sl]
                pf = S["pmF"]
                w0 = list(bar0) + ([start_wait] if start_wait else [])
                tTK = s.emit("tensor", lambda h, hh=hh, c=c: h.transpose(
                    S["pmT"][0:64, :], ktm_h(hh, c), identt[:]), waits=w0)
                yield
                tKe = s.emit("vector", lambda h: h.tensor_copy(S["Kt"][:], S["pmT"][0:64, :]),
                             waits=[tTK])
                yield
                tKK = s.emit("tensor", lambda h: h.matmul(
                    pf[:, 0:128], S["Kt"][:], S["Kt"][:], start=True, stop=True), waits=[tKe])
                yield
                tN = s.emit("vector", lambda h, hh=hh, c=c: h.scalar_tensor_tensor(
                    S["Nt"][:], pf[:, 0:128], beta_ap(hh, c), strilt[:], ALU.mult, ALU.mult),
                    waits=[tKK])
                yield
                tTr = s.emit("tensor", lambda h: h.transpose(S["pmT"][:], S["Nt"][:], identt[:]),
                             waits=[tN, tKe])
                yield
                tNT = s.emit("vector", lambda h: h.tensor_copy(S["NTt"][:], S["pmT"][:]), waits=[tTr])
                yield
                tP0 = s.emit("vector", lambda h: h.tensor_tensor(
                    S["P"][0][:], identt[:], S["NTt"][:], ALU.subtract), waits=[tNT])
                yield
                Xprev, XTprev = S["Nt"], S["NTt"]
                evX_prev = tNT
                tPe = tP0
                tPm_hist = {}
                for k in range(1, 7):
                    pk = k % 2
                    Xd, XTd = S["X"][pk], S["XT"][pk]
                    wm = [evX_prev, tPe] + ([tN] if k == 1 else [])
                    tXm = s.emit("tensor", lambda h, XTprev=XTprev, Xprev=Xprev: h.matmul(
                        pf[:, 128:256], XTprev[:], Xprev[:], start=True, stop=True), waits=wm)
                    yield
                    tXTm = None
                    if k < 6:
                        tXTm = s.emit("tensor", lambda h, XTprev=XTprev, Xprev=Xprev: h.matmul(
                            pf[:, 256:384], Xprev[:], XTprev[:], start=True, stop=True))
                        yield
                    war = [tPm_hist[k - 2]] if (k - 2) in tPm_hist else []
                    tXe = s.emit("vector", lambda h, Xd=Xd: h.tensor_copy(
                        Xd[:], pf[:, 128:256]), waits=[tXTm or tXm] + war)
                    yield
                    tXTe = None
                    if k < 6:
                        tXTe = s.emit("vector", lambda h, XTd=XTd: h.tensor_copy(
                            XTd[:], pf[:, 256:384]), waits=[tXe])
                        yield
                    tPm = s.emit("tensor", lambda h, Xd=Xd, Pp=S["P"][(k - 1) % 2]: h.matmul(
                        pf[:, 384:512], Xd[:], Pp[:], start=True, stop=True),
                        waits=[tXTe or tXe, tPe])
                    tPm_hist[k] = tPm
                    yield
                    tPe = s.emit("vector", lambda h, Pn=S["P"][k % 2], Pp=S["P"][(k - 1) % 2]: h.tensor_tensor(
                        Pn[:], pf[:, 384:512], Pp[:], ALU.add), waits=[tPm])
                    yield
                    Xprev, XTprev = Xd, XTd
                    evX_prev = tXTe or tXe
                tSt = s.dma("sync" if u % 2 == 0 else "gpsimd",
                            lambda h, u=u, Pn=S["P"][0]: h.dma_start(
                                out=w_scr[u * 128:(u + 1) * 128, :], in_=Pn[:]), waits=[tPe])
                slot_last[sl] = tSt
                yield

            slot_last = [None] * NSLOT
            gens = [None] * NSLOT
            u_next = (NC * 8) if skip_ph1 else 0
            while u_next < NC * 8 or any(g is not None for g in gens):
                for sl in range(NSLOT):
                    if gens[sl] is None and u_next < NC * 8:
                        gens[sl] = phase1_u(u_next, sl, slot_last[sl])
                        u_next += 1
                    if gens[sl] is not None:
                        try:
                            next(gens[sl])
                        except StopIteration:
                            gens[sl] = None

        ph1bar = [(e, s.cnt[e]) for e in SEMS]

        Sf = [E(nc.sbuf_tensor(f"Sf{i2}", [64, 8 * 64], F32)) for i2 in range(2)]
        Sb = E(nc.sbuf_tensor("Sb", [64, 8 * 64], BF16))
        V8 = [E(nc.sbuf_tensor(f"V8_{i2}", [128, 8 * 128], BF16)) for i2 in range(2)]
        Rt8 = E(nc.sbuf_tensor("Rt8", [128, 8 * 64], BF16))
        Ut8 = E(nc.sbuf_tensor("Ut8", [128, 8 * 64], BF16))
        GT8 = E(nc.sbuf_tensor("GT8", [128, 8 * 128], BF16))
        kfc = E(nc.sbuf_tensor("kfc", [64, 8 * 128], BF16))
        qfc = E(nc.sbuf_tensor("qfc", [64, 8 * 128], BF16))
        ost = [E(nc.sbuf_tensor(f"ost{i2}", [64, 8 * 128], BF16)) for i2 in range(2)]
        ppu = E(nc.psum_tensor("ppu", [128, 512], F32))
        put = E(nc.psum_tensor("put", [128, 512], F32))
        pg = [E(nc.psum_tensor(f"pg{i2}", [128, 512], F32)) for i2 in range(2)]
        ppo = [E(nc.psum_tensor(f"ppo{i2}", [64, 512], F32)) for i2 in range(2)]
        pps = E(nc.psum_tensor("pps", [64, 512], F32))
        ptr = E(nc.psum_tensor("ptr", [64, 512], BF16))

        tz0 = s.emit("gpsimd", lambda h: h.memset(Sf[0][:], 0.0), waits=ph1bar)
        tz1 = s.emit("gpsimd", lambda h: h.memset(Sf[1][:], 0.0), waits=ph1bar)
        tz2 = s.emit("gpsimd", lambda h: h.memset(Sb[:], 0.0), waits=ph1bar)

        def v8load(c, waits):
            return s.dma("gpsimd", lambda h, c=c: h.dma_start(
                out=V8[c % 2][:].rearrange("p (hh d) -> p hh d", hh=8),
                in_=w_scr[c * 1024:(c + 1) * 1024, :].rearrange("(hh p) d -> p hh d", p=128)),
                waits=waits)

        o_view = o_out[:, :].rearrange("(hh2 d) (j t) -> d j hh2 t", hh2=2, t=T)

        pv = {}
        if NRUN > 0:
            pv["vl", 0] = v8load(0, ph1bar)
        if NRUN > 1:
            pv["vl", 1] = v8load(1, ph1bar)
        for c in range(NRUN):
            par = c % 2
            cbar = [(e, s.cnt[e]) for e in SEMS] if serial_scan else []
            tp_toks = []
            ev_last = None
            for i4 in range(16):
                h_, which = i4 // 2, i4 % 2
                reg = i4 % 4
                w = []
                if i4 == 0:
                    w = [pv[k] for k in [("kqe", c - 1)] if k in pv] + cbar
                    if c == 0:
                        w = list(ph1bar)
                elif reg == 0:
                    w = [ev_last]
                src_ap = ktm_h(h_, c) if which == 0 else qtm_h(h_, c)
                tok = s.emit("tensor", lambda h, src_ap=src_ap, reg=reg: h.transpose(
                    ptr[:, reg * 128:(reg + 1) * 128], src_ap, identt[:]), waits=w)
                tp_toks.append(tok)
                if reg == 3 or i4 == 15:
                    for j4 in range(i4 - reg, i4 + 1):
                        hj, wj = j4 // 2, j4 % 2
                        dst = kfc if wj == 0 else qfc
                        ev_last = s.emit("vector", lambda h, dst=dst, hj=hj, rj=j4 % 4: h.tensor_copy(
                            dst[:, hj * 128:(hj + 1) * 128], ptr[:, rj * 128:(rj + 1) * 128]),
                            waits=[tp_toks[-1]] if j4 == i4 - reg else [])
            pv["kqe", c] = ev_last
            for h_ in range(HL):
                w = []
                if h_ == 0:
                    w = [pv["kqe", c]] + [pv[k] for k in (("tR", c - 1), ("tSb", c - 1)) if k in pv]
                    if c == 0:
                        w += [tz0, tz1, tz2]
                tok = s.emit("tensor", lambda h, h_=h_: h.matmul(
                    ppu[:, h_ * 64:(h_ + 1) * 64], kfc[:, h_ * 128:(h_ + 1) * 128],
                    Sb[:, h_ * 64:(h_ + 1) * 64], start=True, stop=True), waits=w)
            pv["tU", c] = tok
            for h_ in range(HL):
                w = [pv[k] for k in [("tGm", c - 1)] if k in pv] if h_ == 0 else []
                tok = s.emit("tensor", lambda h, h_=h_: h.matmul(
                    pg[h_ // 4][:, (h_ % 4) * 128:(h_ % 4 + 1) * 128],
                    kfc[:, h_ * 128:(h_ + 1) * 128], qfc[:, h_ * 128:(h_ + 1) * 128],
                    start=True, stop=True), waits=w)
            pv["tG", c] = tok
            for h_ in range(HL):
                w = []
                if h_ == 0:
                    w = [pv["tU", c]] + [pv[k] for k in [("tUt", c - 1)] if k in pv]
                tok = s.emit("vector", lambda h, h_=h_, c=c: h.scalar_tensor_tensor(
                    Rt8[:, h_ * 64:(h_ + 1) * 64], ppu[:, h_ * 64:(h_ + 1) * 64],
                    nbeta_ap(h_, c), vtm_h(h_, c), ALU.mult, ALU.add), waits=w)
            pv["tR", c] = tok
            for h_ in range(HL):
                w = []
                if h_ == 0:
                    w = [pv["tR", c], pv["vl", c]] + [pv[k] for k in [("tUe", c - 1)] if k in pv]
                tok = s.emit("tensor", lambda h, h_=h_, c=c: h.matmul(
                    put[:, h_ * 64:(h_ + 1) * 64], V8[c % 2][:, h_ * 128:(h_ + 1) * 128],
                    Rt8[:, h_ * 64:(h_ + 1) * 64], start=True, stop=True), waits=w)
            pv["tUt", c] = tok
            for h_ in range(HL):
                w = []
                if h_ == 0:
                    w = [pv["tUt", c]] + [pv[k] for k in [("tS", c - 1)] if k in pv]
                tok = s.emit("vector", lambda h, h_=h_: h.tensor_copy(
                    Ut8[:, h_ * 64:(h_ + 1) * 64], put[:, h_ * 64:(h_ + 1) * 64]), waits=w)
            pv["tUe", c] = tok
            for h_ in range(HL):
                w = []
                if h_ == 0:
                    w = [pv["tG", c]] + [pv[k] for k in [("tO2", c - 1)] if k in pv]
                tok = s.emit("vector", lambda h, h_=h_: h.tensor_tensor(
                    GT8[:, h_ * 128:(h_ + 1) * 128],
                    pg[h_ // 4][:, (h_ % 4) * 128:(h_ % 4 + 1) * 128],
                    triuit[:], ALU.mult), waits=w)
            pv["tGm", c] = tok
            if c + 2 < NRUN:
                pv["vl", c + 2] = v8load(c + 2, [pv["tUt", c]])
            for h_ in range(HL):
                w = []
                if h_ == 0:
                    w = [pv["tUe", c], pv["tGm", c]] + [pv[k] for k in [("tOe", c - 1)] if k in pv]
                    if c == 0:
                        w += [tz2]
                s.emit("tensor", lambda h, h_=h_: h.matmul(
                    ppo[h_ // 4][:, (h_ % 4) * 128:(h_ % 4 + 1) * 128],
                    Sb[:, h_ * 64:(h_ + 1) * 64], qfc[:, h_ * 128:(h_ + 1) * 128],
                    start=True, stop=False), waits=w)
                tok = s.emit("tensor", lambda h, h_=h_: h.matmul(
                    ppo[h_ // 4][:, (h_ % 4) * 128:(h_ % 4 + 1) * 128],
                    Ut8[:, h_ * 64:(h_ + 1) * 64], GT8[:, h_ * 128:(h_ + 1) * 128],
                    start=False, stop=True), waits=[])
            pv["tO2", c] = tok
            for h_ in range(HL):
                w = []
                if h_ == 0:
                    w = [pv["tO2", c]] + [pv[k] for k in [("od", c - 2)] if k in pv]
                tok = s.emit("scalar", lambda h, h_=h_, c=c: h.copy(
                    ost[c % 2][:, h_ * 128:(h_ + 1) * 128],
                    ppo[h_ // 4][:, (h_ % 4) * 128:(h_ % 4 + 1) * 128]), waits=w)
            pv["tOe", c] = tok
            pv["od", c] = s.dma("sync", lambda h, c=c: h.dma_start(
                out=o_view[:, :, :, c * 128:(c + 1) * 128],
                in_=ost[c % 2][:].rearrange("p (j hh2 t) -> p j hh2 t", j=4, t=128)),
                waits=[pv["tOe", c]])
            for h_ in range(HL):
                w = []
                if h_ == 0:
                    w = [pv["tUe", c]] + [pv[k] for k in [("tSa", c - 1)] if k in pv]
                tok = s.emit("tensor", lambda h, h_=h_, c=c: h.matmul(
                    pps[:, h_ * 64:(h_ + 1) * 64],
                    ktm_h(h_, c), Ut8[:, h_ * 64:(h_ + 1) * 64], start=True, stop=True), waits=w)
            pv["tS", c] = tok
            for h_ in range(HL):
                w = []
                if h_ == 0:
                    w = [pv["tS", c]]
                    if c == 0:
                        w += [tz0, tz1]
                tok = s.emit("vector", lambda h, h_=h_, par=par: h.tensor_tensor(
                    Sf[1 - par][:, h_ * 64:(h_ + 1) * 64],
                    pps[:, h_ * 64:(h_ + 1) * 64],
                    Sf[par][:, h_ * 64:(h_ + 1) * 64], ALU.add), waits=w)
            pv["tSa", c] = tok
            for h_ in range(HL):
                w = [pv["tSa", c], pv["tO2", c]] if h_ == 0 else []
                tok = s.emit("scalar", lambda h, h_=h_, par=par: h.copy(
                    Sb[:, h_ * 64:(h_ + 1) * 64],
                    Sf[1 - par][:, h_ * 64:(h_ + 1) * 64]), waits=w)
            pv["tSb", c] = tok

        if debug:
            dw = [(e, s.cnt[e]) for e in SEMS]
            s.dma("sync", lambda h: h.dma_start(out=dbg_q[:, :], in_=q_tm[:, 0:4 * T]), waits=dw)
            s.dma("sync", lambda h: h.dma_start(out=dbg_ktm[:, :], in_=k_tm[:]), waits=dw)
            s.dma("sync", lambda h: h.dma_start(out=dbg_vtm[:, :], in_=v_tm[:]), waits=dw)
            s.dma("sync", lambda h: h.dma_start(out=dbg_beta[:, :], in_=beta[:]), waits=dw)
            s.dma("sync", lambda h: h.dma_start(out=dbg_xn[:, :], in_=xn_scr[0:128, :]), waits=dw)
        s.build(nc, ctx)
    return nc



def build_proj_residual():
    nc = bass.Bass("TRN2", target_bir_lowering=False)
    x_fm = nc.dram_tensor("x_fm", [128, 8 * TL], F32, kind="ExternalInput").ap()
    o_in = nc.dram_tensor("o_in", [128, 8 * TL], BF16, kind="ExternalInput").ap()
    w = nc.dram_tensor("w", [D, D], F32, kind="ExternalInput").ap()
    b_col = nc.dram_tensor("b_col", [128, 8], F32, kind="ExternalInput").ap()
    xo = nc.dram_tensor("xo", [128, 8 * TL], F32, kind="ExternalOutput").ap()

    s = Sched()
    with ExitStack() as ctx:
        E = ctx.enter_context
        xt = E(nc.sbuf_tensor("xt", [128, 8 * TL], F32))
        ot = E(nc.sbuf_tensor("ot", [128, 8 * TL], BF16))
        wt = E(nc.sbuf_tensor("wt", [128, 8 * D], BF16))
        bt = E(nc.sbuf_tensor("bt", [128, 8], F32))
        xn = E(nc.sbuf_tensor("xn", [128, 8 * TL], F32))
        pss = [E(nc.psum_tensor(f"ps{i}", [128, 512], F32)) for i in range(2)]

        tx = s.dma("sync", lambda h: h.dma_start(out=xt[:], in_=x_fm[:, :]))
        to = s.dma("sync", lambda h: h.dma_start(out=ot[:], in_=o_in[:, :]))
        tb = s.dma("sync", lambda h: h.dma_start(out=bt[:], in_=b_col[:, :]))
        tw = emit_load_w_bf16(s, wt, w, 8, D)
        ev = {}
        for oc in range(8):
            for tc in range(TL // 512):
                ps = pss[tc % 2]
                wl = [tx, to, tb] + tw + ([ev[id(ps)]] if id(ps) in ev else [])
                tok = None
                for kc in range(8):
                    tok = s.emit("tensor", lambda h, kc=kc, oc=oc, tc=tc, ps=ps: h.matmul(
                        ps[:], wt[:, kc * D + oc * 128: kc * D + (oc + 1) * 128],
                        ot[:, kc * TL + tc * 512: kc * TL + (tc + 1) * 512],
                        start=(kc == 0), stop=(kc == 7)), waits=(wl if kc == 0 else []))
                te = s.emit("vector", lambda h, oc=oc, tc=tc, ps=ps: h.scalar_tensor_tensor(
                    xn[:, oc * TL + tc * 512: oc * TL + (tc + 1) * 512], ps[:],
                    bt[:, oc:oc + 1], xt[:, oc * TL + tc * 512: oc * TL + (tc + 1) * 512],
                    ALU.add, ALU.add), waits=[tok])
                ev[id(ps)] = te
        wall = list(ev.values())
        for j in range(8):
            s.dma("sync", lambda h, j=j: h.dma_start(
                out=xo[:, j * TL:(j + 1) * TL], in_=xn[:, j * TL:(j + 1) * TL]), waits=wall)
        s.build(nc, ctx)
    return nc



def fm(x):
    Tp = x.shape[0]
    return np.ascontiguousarray(x.T.reshape(D // 128, 128, Tp).transpose(1, 0, 2).reshape(128, -1))

def unfm(y, Tp):
    return np.ascontiguousarray(y.reshape(128, D // 128, Tp).transpose(1, 0, 2).reshape(D, Tp).T)

def col128(b):
    return np.ascontiguousarray(b.reshape(-1, 128).T)

IDENT = np.eye(128, dtype=NPBF16)
STRIL = np.tril(np.ones((128, 128), np.float32), -1).astype(NPBF16)
TRIUI = np.triu(np.ones((128, 128), np.float32), 0).astype(NPBF16)

_PROGS = {}

def prog(name, builder):
    if name not in _PROGS:
        _PROGS[name] = builder()
    return _PROGS[name]


def o_heads_to_fm(o_parts):
    return np.concatenate(o_parts, axis=1)


def run_deltanet_sublayer(x, ins):
    in_maps = []
    for c in range(8):
        b, p = c // 2, c % 2
        hs = p * HL * HD
        in_maps.append(dict(
            xb=np.ascontiguousarray(x[b]),
            wq=np.ascontiguousarray(ins["L0_attn_a_Wq"][:, hs:hs + 512]),
            wk=np.ascontiguousarray(ins["L0_attn_a_Wk"][:, hs:hs + 512]),
            wv=np.ascontiguousarray(ins["L0_attn_a_Wv"][:, hs:hs + 512]),
            wbe=np.ascontiguousarray(ins["L0_attn_a_Wbeta"][:, p * 8:p * 8 + 8]),
            bq_row=ins["L0_attn_a_bq"][hs:hs + 512].reshape(1, 512),
            bk_row=ins["L0_attn_a_bk"][hs:hs + 512].reshape(1, 512),
            bv_row=ins["L0_attn_a_bv"][hs:hs + 512].reshape(1, 512),
            bbe_row=ins["L0_attn_a_bbeta"][p * 8:p * 8 + 8].reshape(1, 8),
            ident=IDENT, stril=STRIL, triui=TRIUI,
        ))
    res = run_bass_kernel_spmd(prog("dn_a", build_dn_a), in_maps, core_ids=list(range(8)))
    o_full = [o_heads_to_fm([res.results[2 * b]["o_out"], res.results[2 * b + 1]["o_out"]])
              for b in range(4)]
    in_maps = []
    for c in range(8):
        b, p = c // 2, c % 2
        in_maps.append(dict(
            x_fm=fm(x[b, p * TL:(p + 1) * TL]),
            o_in=np.ascontiguousarray(o_full[b][:, [j * T + p * TL + t for j in range(8) for t in range(0)]])
            if False else np.ascontiguousarray(
                o_full[b].reshape(128, 8, T)[:, :, p * TL:(p + 1) * TL].reshape(128, 8 * TL)),
            w=ins["L0_attn_a_Wo"],
            b_col=col128(ins["L0_attn_a_bo"]),
        ))
    res = run_bass_kernel_spmd(prog("dn_b", build_proj_residual), in_maps, core_ids=list(range(8)))
    xn = np.empty_like(x)
    for c in range(8):
        b, p = c // 2, c % 2
        xn[b, p * TL:(p + 1) * TL] = unfm(res.results[c]["xo"], TL)
    return xn





def build_moe():
    nc = bass.Bass("TRN2", target_bir_lowering=False)
    x_tm = nc.dram_tensor("x_tm", [TL, D], F32, kind="ExternalInput").ap()
    wrT = nc.dram_tensor("wrT", [D, 16], F32, kind="ExternalInput").ap()
    wmhT = nc.dram_tensor("wmhT", [D, D], F32, kind="ExternalInput").ap()
    wmgT = nc.dram_tensor("wmgT", [D, D], F32, kind="ExternalInput").ap()
    kf = nc.dram_tensor("kf", [8, 128, 4096], F32, kind="ExternalInput").ap()
    vf = nc.dram_tensor("vf", [8, 4096, 128], F32, kind="ExternalInput").ap()
    oh = nc.dram_tensor("oh", [16, 16 * 128], BF16, kind="ExternalInput").ap()
    ident = nc.dram_tensor("ident", [128, 128], BF16, kind="ExternalInput").ap()
    xn_scr = nc.dram_tensor("xn_scr", [TL, D], BF16, kind="Internal").ap()
    xo = nc.dram_tensor("xo", [TL, D], F32, kind="ExternalOutput").ap()

    NT_ = TL // 128
    s = Sched()
    with ExitStack() as ctx:
        E = ctx.enter_context
        sb = {}
        oht = E(nc.sbuf_tensor("oht", [16, 16 * 128], BF16))
        identt = E(nc.sbuf_tensor("identt", [128, 128], BF16))
        gateT = E(nc.sbuf_tensor("gateT", [16, TL], BF16))
        gA = E(nc.sbuf_tensor("gA", [128, 16], F32))
        gB = E(nc.sbuf_tensor("gB", [128, 16], F32))
        gC = E(nc.sbuf_tensor("gC", [128, 4], F32))
        gD = E(nc.sbuf_tensor("gD", [128, 16], F32))
        gbf = E(nc.sbuf_tensor("gbf", [128, 16], BF16))
        q_fm = E(nc.sbuf_tensor("q_fm", [128, 8 * TL], BF16))
        ps_lg = E(nc.psum_tensor("ps_lg", [128, 16], F32))
        ps_tr = E(nc.psum_tensor("ps_tr", [16, 128], BF16))
        ps_sc = [E(nc.psum_tensor(f"ps_sc{i}", [128, 512], F32)) for i in range(2)]
        ps_y = E(nc.psum_tensor("ps_y", [128, 512], F32))
        ps_w = E(nc.psum_tensor("ps_w", [128, 512], F32))

        t_oh = s.dma("sync", lambda h: h.dma_start(out=oht[:], in_=oh[:, :]))
        t_id = s.dma("sync", lambda h: h.dma_start(out=identt[:], in_=ident[:, :]))

        with ExitStack() as ctx2:
            E2 = ctx2.enter_context
            xn_fm = E2(nc.sbuf_tensor("xn_fm", [128, 8 * TL], BF16))
            sb["ln_xt"] = [E2(nc.sbuf_tensor(f"ln_xt{i}", [128, D], F32)) for i in range(2)]
            sb["ln_sq"] = E2(nc.sbuf_tensor("ln_sq", [128, D], F32))
            sb["ln_xn"] = E2(nc.sbuf_tensor("ln_xnb", [128, D], BF16))
            sb["ln_st"] = E2(nc.sbuf_tensor("ln_st", [128, 8], F32))
            wrTt = E2(nc.sbuf_tensor("wrTt", [128, 8 * 16], BF16))
            wmt = E2(nc.sbuf_tensor("wmt", [128, 8 * D], BF16))

            ln_toks = emit_ln(s, sb, x_tm, xn_scr, NT_)
            tr_toks = emit_transpose_in(s, xn_fm, xn_scr, TL, ln_toks)
            wr_t = emit_load_w_bf16(s, wrTt, wrT, 8, 16)
            wm_t = emit_load_w_bf16(s, wmt, wmhT, 8, D)

            prev_lg = None
            prev_lg_s = None
            prev_tr = None
            for i in range(NT_):
                wl = tr_toks + wr_t + ([prev_lg, prev_lg_s] if prev_lg else [])
                tok = None
                for kc in range(8):
                    tok = s.emit("tensor", lambda h, kc=kc, i=i: h.matmul(
                        ps_lg[:], xn_fm[:, kc * TL + i * 128: kc * TL + (i + 1) * 128],
                        wrTt[:, kc * 16:(kc + 1) * 16],
                        start=(kc == 0), stop=(kc == 7)), waits=(wl if kc == 0 else []))
                tm1 = s.emit("vector", lambda h: h.reduce_max(gC[:, 0:1], ps_lg[:], AX), waits=[tok])
                teq = s.emit("vector", lambda h: h.tensor_scalar(gA[:], ps_lg[:], gC[:, 0:1], None, ALU.is_equal), waits=[tm1])
                tng = s.emit("vector", lambda h: h.scalar_tensor_tensor(
                    gB[:], gA[:], -1e30, ps_lg[:], ALU.mult, ALU.add), waits=[teq])
                tm2 = s.emit("vector", lambda h: h.reduce_max(gC[:, 1:2], gB[:], AX), waits=[tng])
                tse = s.emit("vector", lambda h: h.tensor_scalar(gA[:], ps_lg[:], gC[:, 1:2], None, ALU.is_ge), waits=[tm2])
                tsg = s.emit("scalar", lambda h: h.activation(gB[:], ps_lg[:], AF.Sigmoid, bias=0.0, scale=1.0), waits=[tse])
                tgt = s.emit("vector", lambda h: h.tensor_tensor(gD[:], gA[:], gB[:], ALU.mult), waits=[tsg])
                prev_lg = tgt
                prev_lg_s = tsg
                tgb = s.emit("vector", lambda h: h.tensor_copy(gbf[:], gD[:]), waits=[tgt])
                ttr = s.emit("tensor", lambda h: h.transpose(ps_tr[:], gbf[:], identt[:]),
                             waits=[tgb, t_id] + ([prev_tr] if prev_tr else []))
                tte = s.emit("vector", lambda h, i=i: h.tensor_copy(
                    gateT[:, i * 128:(i + 1) * 128], ps_tr[:]), waits=[ttr])
                prev_tr = tte

            prev_ps = {}
            for oc in range(8):
                for tc in range(TL // 512):
                    ps = ps_sc[tc % 2]
                    wl = tr_toks + wm_t + ([prev_ps[id(ps)]] if id(ps) in prev_ps else [])
                    tok = None
                    for kc in range(8):
                        tok = s.emit("tensor", lambda h, kc=kc, oc=oc, tc=tc, ps=ps: h.matmul(
                            ps[:], wmt[:, kc * D + oc * 128: kc * D + (oc + 1) * 128],
                            xn_fm[:, kc * TL + tc * 512: kc * TL + (tc + 1) * 512],
                            start=(kc == 0), stop=(kc == 7)), waits=(wl if kc == 0 else []))
                    te = s.emit("scalar", lambda h, oc=oc, tc=tc, ps=ps: h.copy(
                        q_fm[:, oc * TL + tc * 512: oc * TL + (tc + 1) * 512], ps[:]), waits=[tok])
                    prev_ps[id(ps)] = te

        bar0 = [(e, s.cnt[e]) for e in SEMS]
        y_fm = E(nc.sbuf_tensor("y_fm", [128, 8 * TL], BF16))
        Bt = E(nc.sbuf_tensor("Bt", [128, 16 * 1024], BF16))
        kft = [E(nc.sbuf_tensor(f"kft{i}", [128, 4096], BF16)) for i in range(2)]
        vft = [E(nc.sbuf_tensor(f"vft{i}", [128, 4096], BF16)) for i in range(2)]
        ar = [E(nc.sbuf_tensor(f"ar{i}", [128, 512], BF16)) for i in range(2)]
        ag = [E(nc.sbuf_tensor(f"ag{i}", [128, 512], BF16)) for i in range(2)]

        kvctr = [0]
        y_toks = []
        prev_act = {}
        prev_ag = {}
        prev_ty = None
        prev_B = None
        last_B = None
        kv_tok = [None, None]
        for tcg in range(2):
            exp_done = [(e, s.cnt[e]) for e in SEMS]
            for e in range(16):
                for tci in range(2):
                    tc = 2 * tcg + tci
                    tm = s.emit("tensor", lambda h, e=e, tc=tc: h.matmul(
                        ps_w[:], oht[:, e * 128:(e + 1) * 128],
                        gateT[:, tc * 512:(tc + 1) * 512],
                        start=True, stop=True),
                        waits=bar0 + exp_done + [t_oh] + ([prev_B] if prev_B else []))
                    te = s.emit("vector", lambda h, e=e, tci=tci: h.tensor_copy(
                        Bt[:, e * 1024 + tci * 512: e * 1024 + (tci + 1) * 512], ps_w[:]), waits=[tm])
                    prev_B = te
            last_B = prev_B
            for h_ in range(8):
                hb = h_ % 2
                wl = bar0 + ([kv_tok[hb]] if kv_tok[hb] else [])
                tk = s.dma("gpsimd", lambda h, h_=h_, hb=hb: h.dma_start(
                    out=kft[hb][:], in_=kf[h_, :, :]), waits=wl)
                tv = s.dma("gpsimd", lambda h, h_=h_, hb=hb: h.dma_start(
                    out=vft[hb][:].rearrange("p (b d) -> p b d", b=32),
                    in_=vf[h_, :, :].rearrange("(b p) d -> p b d", p=128)), waits=wl)
                for tci in range(2):
                    tc = 2 * tcg + tci
                    for eb in range(32):
                        pp = ps_sc[eb % 2]
                        ab = eb % 2
                        wl2 = list(bar0) + [tk, tv, last_B] + ([prev_act.get(id(pp))] if prev_act.get(id(pp)) else [])
                        tsc = s.emit("tensor", lambda h, h_=h_, hb=hb, tc=tc, eb=eb, pp=pp: h.matmul(
                            pp[:], kft[hb][:, eb * 128:(eb + 1) * 128],
                            q_fm[:, h_ * TL + tc * 512: h_ * TL + (tc + 1) * 512],
                            start=True, stop=True), waits=wl2)
                        tge = s.emit("scalar", lambda h, ab=ab, pp=pp: h.activation(
                            ar[ab][:], pp[:], AF.Gelu, bias=0.0, scale=1.0),
                            waits=[tsc] + ([prev_ag.get(ab)] if prev_ag.get(ab) else []))
                        tag = s.emit("vector", lambda h, ab=ab, eb=eb, tci=tci: h.tensor_tensor(
                            ag[ab][:], ar[ab][:],
                            Bt[:, (eb // 2) * 1024 + tci * 512: (eb // 2) * 1024 + (tci + 1) * 512],
                            ALU.mult), waits=[tge])
                        prev_act[id(pp)] = tag
                        prev_ag[ab] = tag
                        tym = s.emit("tensor", lambda h, hb=hb, ab=ab, eb=eb: h.matmul(
                            ps_y[:], vft[hb][:, eb * 128:(eb + 1) * 128], ag[ab][:],
                            start=(eb == 0), stop=(eb == 31)),
                            waits=[tag] + ([prev_ty] if (eb == 0 and prev_ty) else []))
                    ty = s.emit("scalar", lambda h, h_=h_, tc=tc: h.copy(
                        y_fm[:, h_ * TL + tc * 512: h_ * TL + (tc + 1) * 512], ps_y[:]), waits=[tym])
                    prev_ty = ty
                    y_toks.append(ty)
                kv_tok[hb] = ty

        with ExitStack() as ctx3:
            E3 = ctx3.enter_context
            wgt = E3(nc.sbuf_tensor("wgt", [128, 8 * D], BF16))
            xtile = [E3(nc.sbuf_tensor(f"xtile{i}", [128, D], F32)) for i in range(2)]
            otile = [E3(nc.sbuf_tensor(f"otile{i}", [128, D], F32)) for i in range(2)]
            wg_t = emit_load_w_bf16(s, wgt, wmgT, 8, D, waits=bar0)
            prev_x = [None, None]
            prev_o = [None, None]
            prev_wg = None
            for i in range(NT_):
                ib = i % 2
                tx = s.dma("sync", lambda h, i=i, ib=ib: h.dma_start(
                    out=xtile[ib][:], in_=x_tm[i * 128:(i + 1) * 128, :]),
                    waits=([prev_x[ib]] if prev_x[ib] else []))
                for oc in range(2):
                    wl = y_toks[-1:] + [last_B] + wg_t + [tx]
                    if prev_wg:
                        wl = wl + [prev_wg]
                    tok = None
                    for kc in range(8):
                        tok = s.emit("tensor", lambda h, kc=kc, i=i, oc=oc: h.matmul(
                            ps_w[:], y_fm[:, kc * TL + i * 128: kc * TL + (i + 1) * 128],
                            wgt[:, kc * D + oc * 512: kc * D + (oc + 1) * 512],
                            start=(kc == 0), stop=(kc == 7)), waits=(wl if kc == 0 else []))
                    te = s.emit("vector", lambda h, i=i, oc=oc, ib=ib: h.tensor_tensor(
                        otile[ib][:, oc * 512:(oc + 1) * 512], ps_w[:],
                        xtile[ib][:, oc * 512:(oc + 1) * 512], ALU.add), waits=[tok])
                    prev_wg = te
                tst = s.dma("sync", lambda h, i=i, ib=ib: h.dma_start(
                    out=xo[i * 128:(i + 1) * 128, :], in_=otile[ib][:]), waits=[prev_wg])
                prev_x[ib] = prev_wg
                prev_o[ib] = tst
            s.build(nc, ctx)
    return nc


OH16 = np.zeros((16, 16 * 128), NPBF16)
for _e in range(16):
    OH16[_e, _e * 128:(_e + 1) * 128] = 1


def run_moe_sublayer(x, ins, pfx="L0_ffn"):
    in_maps = []
    for c in range(8):
        b, p = c // 2, c % 2
        in_maps.append(dict(
            x_tm=np.ascontiguousarray(x[b, p * TL:(p + 1) * TL]),
            wrT=np.ascontiguousarray(ins[pfx + "_w_router"].T),
            wmhT=np.ascontiguousarray(ins[pfx + "_w_mh"].T),
            wmgT=np.ascontiguousarray(ins[pfx + "_w_mg"].T),
            kf=np.ascontiguousarray(ins[pfx + "_k_ffwd"].transpose(0, 3, 1, 2).reshape(8, 128, 4096)),
            vf=np.ascontiguousarray(ins[pfx + "_v_ffwd"].reshape(8, 4096, 128)),
            oh=OH16, ident=IDENT,
        ))
    res = run_bass_kernel_spmd(prog("moe", build_moe), in_maps, core_ids=list(range(8)))
    xn = np.empty_like(x)
    for c in range(8):
        b, p = c // 2, c % 2
        xn[b, p * TL:(p + 1) * TL] = res.results[c]["xo"]
    return xn





def build_attn(kind):
    rope = kind == "rope"
    nex = 1 if rope else 2
    scale = 0.125 if rope else 0.03125
    nc = bass.Bass("TRN2", target_bir_lowering=False)
    xb = nc.dram_tensor("xb", [T, D], F32, kind="ExternalInput").ap()
    wq = nc.dram_tensor("wq", [D, 512], F32, kind="ExternalInput").ap()
    wk = nc.dram_tensor("wk", [D, 512], F32, kind="ExternalInput").ap()
    wv = nc.dram_tensor("wv", [D, 512], F32, kind="ExternalInput").ap()
    negm = nc.dram_tensor("negm", [128, 128], BF16, kind="ExternalInput").ap()
    if rope:
        bq_col = nc.dram_tensor("bq_col", [64, 8], F32, kind="ExternalInput").ap()
        bk_col = nc.dram_tensor("bk_col", [64, 8], F32, kind="ExternalInput").ap()
        bv_row = nc.dram_tensor("bv_row", [1, 512], F32, kind="ExternalInput").ap()
        rc_t = nc.dram_tensor("rc_t", [64, T], BF16, kind="ExternalInput").ap()
        rs_t = nc.dram_tensor("rs_t", [64, T], BF16, kind="ExternalInput").ap()
        pswap = nc.dram_tensor("pswap", [64, 64], BF16, kind="ExternalInput").ap()
    else:
        m_col = nc.dram_tensor("m_col", [128, 8], F32, kind="ExternalInput").ap()
        iota_r = nc.dram_tensor("iota_r", [1, T], F32, kind="ExternalInput").ap()
        iota2d = nc.dram_tensor("iota2d", [128, 32], F32, kind="ExternalInput").ap()
    xn_scr = nc.dram_tensor("xn_scr", [T, D], BF16, kind="Internal").ap()
    o_out = nc.dram_tensor("o_out", [128, 32 * 512], BF16, kind="ExternalOutput").ap()

    NB = T // 128
    s = Sched()
    with ExitStack() as ctx:
        E = ctx.enter_context
        sb = {}
        xn_fm = E(nc.sbuf_tensor("xn_fm", [128, 8 * T], BF16))
        v_tm = E(nc.sbuf_tensor("v_tm", [128, NB * 520], BF16))
        negmt = E(nc.sbuf_tensor("negmt", [128, 128], BF16))
        Qh = E(nc.sbuf_tensor("Qh", [64 + nex, T], BF16))
        Kh = E(nc.sbuf_tensor("Kh", [64 + nex, T], BF16))
        wqh = [E(nc.sbuf_tensor(f"wqh{i}", [128, 8 * 64], BF16)) for i in range(2)]
        wkh = [E(nc.sbuf_tensor(f"wkh{i}", [128, 8 * 64], BF16)) for i in range(2)]
        qraw = E(nc.sbuf_tensor("qraw", [64, 512], BF16))
        rtm01 = E(nc.sbuf_tensor("rtm01", [64, 512], BF16))
        rtm23 = E(nc.sbuf_tensor("rtm23", [64, 512], BF16))
        sq64 = E(nc.sbuf_tensor("sq64", [64, 512], BF16))
        ones64 = E(nc.sbuf_tensor("ones64", [64, 1], BF16))
        kmx = E(nc.sbuf_tensor("kmx", [1, 16], F32))
        rowa = E(nc.sbuf_tensor("rowa", [1, 512], F32))
        rowb = E(nc.sbuf_tensor("rowb", [1, 512], F32))
        rowc = E(nc.sbuf_tensor("rowc", [1, 512], F32))
        mbf = E(nc.sbuf_tensor("mbf", [1, T], BF16))

        if rope:
            rct = E(nc.sbuf_tensor("rct", [64, T], BF16))
            rst = E(nc.sbuf_tensor("rst", [64, T], BF16))
            pswt = E(nc.sbuf_tensor("pswt", [64, 64], BF16))
            swp = E(nc.sbuf_tensor("swp", [64, 512], BF16))
            bqc = E(nc.sbuf_tensor("bqc", [64, 8], F32))
            bkc = E(nc.sbuf_tensor("bkc", [64, 8], F32))
            bvr = E(nc.sbuf_tensor("bvr", [1, 512], BF16))
            ones1 = E(nc.sbuf_tensor("ones1", [1, 128], BF16))
        else:
            mbf2 = E(nc.sbuf_tensor("mbf2", [1, T], BF16))
            mrowf = E(nc.sbuf_tensor("mrowf", [1, 512], F32))
            mct = E(nc.sbuf_tensor("mct", [128, 8], F32))
            iort = E(nc.sbuf_tensor("iort", [1, T], F32))
            io2t = E(nc.sbuf_tensor("io2t", [128, 32], F32))
            msfull = E(nc.sbuf_tensor("msfull", [128, 32], F32))
            mrl = E(nc.sbuf_tensor("mrl", [128, 8], F32))
        ps_sc = [E(nc.psum_tensor(f"ps_sc{i}", [128, 512], F32)) for i in range(2)]
        ps_o = [E(nc.psum_tensor(f"ps_o{i}", [128, 512], F32)) for i in range(4)]
        ps_n = E(nc.psum_tensor("ps_n", [1, 512], F32))

        t_c = [s.dma("sync", lambda h: h.dma_start(out=negmt[:], in_=negm[:, :]))]
        t_c.append(s.emit("gpsimd", lambda h: h.memset(ones64[:], 1.0)))
        t_c.append(s.emit("gpsimd", lambda h: h.memset(Kh[64:64 + nex, :], 1.0)))
        t_c.append(s.emit("gpsimd", lambda h: h.memset(v_tm[:], 1.0)))
        if rope:
            t_c.append(s.dma("sync", lambda h: h.dma_start(out=rct[:], in_=rc_t[:, :])))
            t_c.append(s.dma("sync", lambda h: h.dma_start(out=rst[:], in_=rs_t[:, :])))
            t_c.append(s.dma("sync", lambda h: h.dma_start(out=pswt[:], in_=pswap[:, :])))
            t_c.append(s.dma("sync", lambda h: h.dma_start(out=bqc[:], in_=bq_col[:, :])))
            t_c.append(s.dma("sync", lambda h: h.dma_start(out=bkc[:], in_=bk_col[:, :])))
            t_c.append(s.dma("gpsimd", lambda h: h.dma_start(out=bvr[:], in_=bv_row[:, :])))
            t_c.append(s.emit("gpsimd", lambda h: h.memset(ones1[:], 1.0)))
        else:
            t_c.append(s.dma("sync", lambda h: h.dma_start(out=mct[:], in_=m_col[:, :])))
            t_c.append(s.dma("sync", lambda h: h.dma_start(out=iort[:], in_=iota_r[:, :])))
            t_c.append(s.dma("sync", lambda h: h.dma_start(out=io2t[:], in_=iota2d[:, :])))
            t_c.append(s.emit("vector", lambda h: h.tensor_relu(mrl[:], mct[:]), waits=t_c[-3:]))

        with ExitStack() as ctx2:
            E2 = ctx2.enter_context
            sb["ln_xt"] = [E2(nc.sbuf_tensor("ln_xt0", [128, D], F32))] * 2
            sb["ln_xt"] = [sb["ln_xt"][0], sb["ln_xt"][0]]
            sb["ln_sq"] = E2(nc.sbuf_tensor("ln_sq", [128, D], F32))
            sb["ln_xn"] = E2(nc.sbuf_tensor("ln_xnb", [128, D], BF16))
            sb["ln_st"] = E2(nc.sbuf_tensor("ln_st", [128, 8], F32))
            wvt = E2(nc.sbuf_tensor("wvt", [128, 8 * 512], BF16))
            ln_toks = emit_ln(s, sb, xb, xn_scr, NB)
            tr_toks = emit_transpose_in(s, xn_fm, xn_scr, T, ln_toks)
            w_v = emit_load_w_bf16(s, wvt, wv, 8, 512)

            prev_v = {}
            for tt in range(NB):
                ps = ps_sc[tt % 2]
                wl = tr_toks + w_v + t_c + ([prev_v[id(ps)]] if id(ps) in prev_v else [])
                tok = None
                for kc in range(8):
                    tok = s.emit("tensor", lambda h, kc=kc, tt=tt, ps=ps: h.matmul(
                        ps[:], xn_fm[:, kc * T + tt * 128: kc * T + (tt + 1) * 128],
                        wvt[:, kc * 512:(kc + 1) * 512],
                        start=(kc == 0), stop=(False if rope else kc == 7)),
                        waits=(wl if kc == 0 else []))
                if rope:
                    tok = s.emit("tensor", lambda h, ps=ps: h.matmul(
                        ps[:], ones1[:], bvr[:], start=False, stop=True))
                te = s.emit("vector", lambda h, tt=tt, ps=ps: h.tensor_copy(
                    v_tm[:, tt * 520: tt * 520 + 520].rearrange("p (a m) -> p a m", a=8)[:, :, 0:64],
                    ps[:].rearrange("p (a m) -> p a m", a=8)), waits=[tok])
                prev_v[id(ps)] = te

        bar0 = [(e, s.cnt[e]) for e in SEMS]
        o_buf = E(nc.sbuf_tensor("o_buf", [128, NB * 512], BF16))
        pt = [E(nc.sbuf_tensor(f"pt{i}", [128, 512], BF16)) for i in range(2)]
        dtile = E(nc.sbuf_tensor("dtile", [128, 128], F32))
        rcp = E(nc.sbuf_tensor("rcp", [128, 4], F32))

        prev = {}
        wtok = [None, None]
        for h_ in range(HL):
            hb = h_ % 2
            wl = list(bar0) + ([wtok[hb]] if wtok[hb] else [])
            twq = s.dma("gpsimd", lambda h, h_=h_, hb=hb: h.dma_start(
                out=wqh[hb][:], in_=wq[:, h_ * 64:(h_ + 1) * 64].rearrange("(kc p) c -> p kc c", p=128)),
                waits=wl)
            twk = s.dma("gpsimd", lambda h, h_=h_, hb=hb: h.dma_start(
                out=wkh[hb][:], in_=wk[:, h_ * 64:(h_ + 1) * 64].rearrange("(kc p) c -> p kc c", p=128)),
                waits=wl)
            for side, wt_, dst in (("q", wqh, Qh), ("k", wkh, Kh)):
                tws = twq if side == "q" else twk
                for tc in range(8):
                    ps = ps_sc[tc % 2]
                    wl = list(bar0) + [tws] + ([prev[id(ps)]] if id(ps) in prev else [])
                    tok = None
                    for kc in range(8):
                        tok = s.emit("tensor", lambda h, kc=kc, tc=tc, hb=hb, wt_=wt_, ps=ps: h.matmul(
                            ps[0:64, :], wt_[hb][:, kc * 64:(kc + 1) * 64],
                            xn_fm[:, kc * T + tc * 512: kc * T + (tc + 1) * 512],
                            start=(kc == 0), stop=(kc == 7)), waits=(wl if kc == 0 else []))
                    if rope:
                        bcol = bqc if side == "q" else bkc
                        tev = s.emit("scalar", lambda h, ps=ps, bcol=bcol, h_=h_: h.activation(
                            qraw[:], ps[0:64, :], AF.Identity, bias=bcol[:, h_:h_ + 1], scale=1.0),
                            waits=[tok] + ([prev["qraw"]] if "qraw" in prev else []))
                        tsw = s.emit("tensor", lambda h, ps=ps: h.matmul(
                            ps[64:128, 0:512], pswt[:], qraw[:], start=True, stop=True,
                            tile_position=(0, 64)), waits=[tev])
                        tse = s.emit("scalar", lambda h, ps=ps: h.copy(swp[:], ps[64:128, 0:512]), waits=[tsw])
                        t1 = s.emit("vector", lambda h, tc=tc: h.tensor_tensor(
                            rtm01[:], qraw[:], rct[:, tc * 512:(tc + 1) * 512], ALU.mult), waits=[tev])
                        t2 = s.emit("vector", lambda h, tc=tc: h.tensor_tensor(
                            rtm23[:], swp[:], rst[:, tc * 512:(tc + 1) * 512], ALU.mult), waits=[tse, t1])
                        t6 = s.emit("vector", lambda h, tc=tc, dst=dst: h.tensor_tensor(
                            dst[0:64, tc * 512:(tc + 1) * 512], rtm01[:], rtm23[:], ALU.add), waits=[t2])
                        prev["qraw"] = t6
                        prev[id(ps)] = t6
                    else:
                        tev = s.emit("scalar", lambda h, ps=ps, tc=tc, dst=dst: h.copy(
                            dst[0:64, tc * 512:(tc + 1) * 512], ps[0:64, :]), waits=[tok])
                        prev[id(ps)] = tev
            pk_done = [("vector" if rope else "scalar", s.cnt["vector" if rope else "scalar"])]

            nrm_ev = []
            for tc in range(8):
                tsq = s.emit("scalar", lambda h, tc=tc: h.square(
                    sq64[:], Kh[0:64, tc * 512:(tc + 1) * 512]), waits=pk_done + nrm_ev[-1:])
                tmm = s.emit("tensor", lambda h: h.matmul(
                    ps_n[:], ones64[:], sq64[:], start=True, stop=True),
                    waits=[tsq] + nrm_ev[-1:] + pk_done)
                tev = s.emit("vector", lambda h, tc=tc: h.reduce_max(
                    kmx[:, tc:tc + 1], ps_n[:], AX), waits=[tmm])
                nrm_ev.append(tev)
            tkx = s.emit("vector", lambda h: h.reduce_max(kmx[:, 8:9], kmx[:, 0:8], AX), waits=nrm_ev[-1:])
            mrow_done = []
            mdma = []
            for tc in range(8):
                tsq = s.emit("scalar", lambda h, tc=tc: h.square(
                    sq64[:], Qh[0:64, tc * 512:(tc + 1) * 512]),
                    waits=pk_done + mrow_done[-1:] + nrm_ev[-1:])
                tmm = s.emit("tensor", lambda h: h.matmul(
                    ps_n[:], ones64[:], sq64[:], start=True, stop=True),
                    waits=[tsq] + mrow_done[-1:] + nrm_ev[-1:])
                tq2 = s.emit("vector", lambda h: h.tensor_scalar(
                    rowa[:], ps_n[:], kmx[:, 8:9], None, ALU.mult), waits=[tmm, tkx])
                tsd = s.emit("scalar", lambda h: h.sqrt(rowb[:], rowa[:]), waits=[tq2])
                tlast = tsd
                if not rope:
                    tit = s.emit("vector", lambda h, h_=h_, tc=tc: h.tensor_scalar(
                        rowc[:], iort[:, tc * 512:(tc + 1) * 512], mrl[0:1, h_:h_ + 1], 32.0,
                        ALU.mult, ALU.mult), waits=t_c + [tsd])
                    tlast = s.emit("vector", lambda h: h.tensor_tensor(
                        rowb[:], rowb[:], rowc[:], ALU.add), waits=[tit])
                tng = s.emit("vector", lambda h: h.tensor_scalar(
                    rowa[:], rowb[:], 1.0, -1.0, ALU.add, ALU.mult), waits=[tlast])
                tb1 = s.emit("vector", lambda h, tc=tc: h.tensor_copy(
                    mbf[:, tc * 512:(tc + 1) * 512], rowa[:]), waits=[tng])
                mrow_done.append(tb1)
                if not rope:
                    tup = s.emit("vector", lambda h, tc=tc: h.tensor_copy(
                        mrowf[:], mbf[:, tc * 512:(tc + 1) * 512]), waits=[tb1])
                    tm2v = s.emit("vector", lambda h: h.tensor_tensor(
                        rowc[:], rowa[:], mrowf[:], ALU.subtract), waits=[tup])
                    tb2 = s.emit("vector", lambda h, tc=tc: h.tensor_copy(
                        mbf2[:, tc * 512:(tc + 1) * 512], rowc[:]), waits=[tm2v])
                    mrow_done.append(tb2)
            td1 = s.dma("sync", lambda h: h.dma_start(out=Qh[64:65, :], in_=mbf[:, :]),
                        waits=mrow_done + [("tensor", s.cnt["tensor"])])
            mdma = [td1]
            if not rope:
                td2 = s.dma("sync", lambda h: h.dma_start(out=Qh[65:66, :], in_=mbf2[:, :]),
                            waits=mrow_done + [("tensor", s.cnt["tensor"])])
                mdma.append(td2)
                tmsf = s.emit("vector", lambda h, h_=h_: h.tensor_scalar(
                    msfull[:], io2t[:], mct[:, h_:h_ + 1], None, ALU.mult),
                    waits=t_c + mdma + [("scalar", s.cnt["scalar"])])
                mdma.append(tmsf)

            for tcq in range(8):
                for sb_ in range(4 * tcq + 4):
                    pp = ps_sc[sb_ % 2]
                    pb = sb_ % 2
                    wl = list(mdma) + ([prev[("pt", pb)]] if ("pt", pb) in prev else []) + \
                         ([prev[id(pp)]] if id(pp) in prev else [])
                    tmm = s.emit("tensor", lambda h, h_=h_, tcq=tcq, sb_=sb_, pp=pp: h.matmul(
                        pp[:], Kh[0:64 + nex, sb_ * 128:(sb_ + 1) * 128],
                        Qh[0:64 + nex, tcq * 512:(tcq + 1) * 512],
                        start=True, stop=True), waits=wl)
                    if rope:
                        tex = s.emit("scalar", lambda h, pb=pb, pp=pp: h.activation(
                            pt[pb][:], pp[:], AF.Exp, bias=0.0, scale=scale), waits=[tmm])
                    else:
                        tex = s.emit("scalar", lambda h, pb=pb, pp=pp, sb_=sb_: h.activation(
                            pt[pb][:], pp[:], AF.Exp, bias=msfull[:, sb_:sb_ + 1], scale=scale), waits=[tmm])
                    j0 = sb_ - 4 * tcq
                    if 0 <= j0 <= 3:
                        tdt = s.emit("vector", lambda h, pp=pp, j0=j0: h.tensor_tensor(
                            dtile[:], pp[:, j0 * 128:(j0 + 1) * 128], negmt[:], ALU.add),
                            waits=[tmm, tex])
                        if rope:
                            tex2 = s.emit("scalar", lambda h, pb=pb, j0=j0: h.activation(
                                pt[pb][:, j0 * 128:(j0 + 1) * 128], dtile[:], AF.Exp,
                                bias=0.0, scale=scale), waits=[tdt])
                        else:
                            tex2 = s.emit("scalar", lambda h, pb=pb, j0=j0, sb_=sb_: h.activation(
                                pt[pb][:, j0 * 128:(j0 + 1) * 128], dtile[:], AF.Exp,
                                bias=msfull[:, sb_:sb_ + 1], scale=scale), waits=[tdt])
                        tex = tex2
                    prev[id(pp)] = tex
                    tpv = None
                    for j in range(max(0, j0), 4):
                        tpv = s.emit("tensor", lambda h, pb=pb, j=j, sb_=sb_, h_=h_: h.matmul(
                            ps_o[j][:, 0:65], pt[pb][:, j * 128:(j + 1) * 128],
                            v_tm[:, sb_ * 520 + h_ * 65: sb_ * 520 + (h_ + 1) * 65],
                            start=(sb_ == 0), stop=(sb_ == 4 * tcq + j)),
                            waits=[tex] + ([prev[("po", j)]] if (("po", j) in prev and sb_ == 0) else []))
                    prev[("pt", pb)] = tpv if tpv is not None else tex
                for j in range(4):
                    trc = s.emit("vector", lambda h, j=j: h.reciprocal(
                        rcp[:, j:j + 1], ps_o[j][:, 64:65]),
                        waits=[prev[("pt", 0)], prev[("pt", 1)]])
                    tfo = s.emit("vector", lambda h, j=j, tcq=tcq, h_=h_: h.tensor_scalar(
                        o_buf[:, (4 * tcq + j) * 512 + h_ * 64: (4 * tcq + j) * 512 + (h_ + 1) * 64],
                        ps_o[j][:, 0:64], rcp[:, j:j + 1], None, ALU.mult), waits=[trc])
                    prev[("po", j)] = tfo
            wtok[hb] = prev[("pt", 0)]
        ow = [prev[("po", j)] for j in range(4)]
        for part in range(4):
            s.dma("sync", lambda h, part=part: h.dma_start(
                out=o_out[:, part * 8 * 512:(part + 1) * 8 * 512],
                in_=o_buf[:, part * 8 * 512:(part + 1) * 8 * 512]), waits=ow)
        s.build(nc, ctx)
    return nc


NEGM_TILE = np.where(np.arange(128)[:, None] <= np.arange(128)[None, :], 0.0, -1e30).astype(NPBF16)


def rope_tables():
    inv = 1.0 / 10000.0 ** (np.arange(0, 64, 2) / 64.0)
    ang = np.arange(T)[None, :] * np.repeat(inv, 2)[:, None]
    C = np.cos(ang)
    S = np.sin(ang)
    S[0::2, :] *= -1.0
    return C.astype(NPBF16), S.astype(NPBF16)

PSWAP64 = np.zeros((64, 64), np.float32)
for _i in range(0, 64, 2):
    PSWAP64[_i, _i + 1] = 1.0
    PSWAP64[_i + 1, _i] = 1.0
PSWAP64 = PSWAP64.astype(NPBF16)


def perm_eo(w):
    w = w.reshape(w.shape[0], -1, 64)
    return np.ascontiguousarray(np.concatenate([w[:, :, 0::2], w[:, :, 1::2]], axis=2).reshape(w.shape[0], -1))


def perm_eo_b(b):
    b = b.reshape(-1, 64)
    return np.ascontiguousarray(np.concatenate([b[:, 0::2], b[:, 1::2]], axis=1).reshape(-1))


def o_to_tokmajor(o_buf_arr):
    return np.ascontiguousarray(o_buf_arr.reshape(128, 32, 512).transpose(1, 0, 2).reshape(T, 512))


def run_rope_attn_sublayer(x, ins):
    cos_t, sin_t = rope_tables()
    in_maps = []
    for c in range(8):
        b, p = c // 2, c % 2
        hs = p * 512
        wqT = np.ascontiguousarray(ins["L0_attn_b_Wq"].T[:, hs:hs + 512])
        wkT = np.ascontiguousarray(ins["L0_attn_b_Wk"].T[:, hs:hs + 512])
        wvT = np.ascontiguousarray(ins["L0_attn_b_Wv"].T[:, hs:hs + 512])
        bq = ins["L0_attn_b_bq"][hs:hs + 512]
        bk = ins["L0_attn_b_bk"][hs:hs + 512]
        in_maps.append(dict(
            xb=np.ascontiguousarray(x[b]),
            wq=wqT, wk=wkT, wv=wvT,
            bq_col=np.ascontiguousarray(bq.reshape(8, 64).T),
            bk_col=np.ascontiguousarray(bk.reshape(8, 64).T),
            bv_row=ins["L0_attn_b_bv"][hs:hs + 512].reshape(1, 512),
            rc_t=cos_t, rs_t=sin_t, negm=NEGM_TILE, pswap=PSWAP64,
        ))
    res = run_bass_kernel_spmd(prog("attn_rope", lambda: build_attn("rope")), in_maps, core_ids=list(range(8)))
    o_full = [np.concatenate([o_to_tokmajor(res.results[2 * b]["o_out"]),
                              o_to_tokmajor(res.results[2 * b + 1]["o_out"])], axis=1)
              for b in range(4)]
    in_maps = []
    for c in range(8):
        b, p = c // 2, c % 2
        in_maps.append(dict(
            x_fm=fm(x[b, p * TL:(p + 1) * TL]),
            o_in=fm(o_full[b][p * TL:(p + 1) * TL]).astype(NPBF16),
            w=np.ascontiguousarray(ins["L0_attn_b_Wo"].T),
            b_col=col128(ins["L0_attn_b_bo"]),
        ))
    res = run_bass_kernel_spmd(prog("dn_b", build_proj_residual), in_maps, core_ids=list(range(8)))
    xn = np.empty_like(x)
    for c in range(8):
        b, p = c // 2, c % 2
        xn[b, p * TL:(p + 1) * TL] = unfm(res.results[c]["xo"], TL)
    return xn


IDENT1024 = None

def run_alibi_attn_sublayer(x, ins):
    global IDENT1024
    wkqv = ins["L1_attn_a_W_kqv"]
    in_maps = []
    for c in range(8):
        b, p = c // 2, c % 2
        hs = p * 512
        in_maps.append(dict(
            xb=np.ascontiguousarray(x[b]),
            wk=np.ascontiguousarray(wkqv[:, hs:hs + 512]),
            wq=np.ascontiguousarray(wkqv[:, 1024 + hs:1024 + hs + 512]),
            wv=np.ascontiguousarray(wkqv[:, 2048 + hs:2048 + hs + 512]),
            m_col=np.ascontiguousarray(np.broadcast_to(
                ins["L1_attn_a_m"][p * 8:(p + 1) * 8, 0, 0].reshape(1, 8), (128, 8))),
            iota_r=np.arange(T, dtype=np.float32).reshape(1, T),
            iota2d=np.ascontiguousarray((np.arange(128)[:, None] + 128 * np.arange(32)[None, :]).astype(np.float32)),
            negm=NEGM_TILE,
        ))
    res = run_bass_kernel_spmd(prog("attn_alibi", lambda: build_attn("alibi")), in_maps, core_ids=list(range(8)))
    o_full = [np.concatenate([o_to_tokmajor(res.results[2 * b]["o_out"]),
                              o_to_tokmajor(res.results[2 * b + 1]["o_out"])], axis=1)
              for b in range(4)]
    if IDENT1024 is None:
        IDENT1024 = np.eye(D, dtype=np.float32)
    in_maps = []
    for c in range(8):
        b, p = c // 2, c % 2
        in_maps.append(dict(
            x_fm=fm(x[b, p * TL:(p + 1) * TL]),
            o_in=fm(o_full[b][p * TL:(p + 1) * TL]).astype(NPBF16),
            w=IDENT1024,
            b_col=np.zeros((128, 8), np.float32),
        ))
    res = run_bass_kernel_spmd(prog("dn_b", build_proj_residual), in_maps, core_ids=list(range(8)))
    xn = np.empty_like(x)
    for c in range(8):
        b, p = c // 2, c % 2
        xn[b, p * TL:(p + 1) * TL] = unfm(res.results[c]["xo"], TL)
    return xn





def emit_quant(s, qs, src_ap, dst_ap, W, waits=(), eng="vector"):
    nb = W // 16
    bc = lambda t: t[:, 0:nb].unsqueeze(2).broadcast_to([128, nb, 16])
    v3 = lambda t: t[:, 0:W].rearrange("p (a m) -> p a m", a=nb)
    tw = list(waits)
    t0 = s.emit(eng, lambda h: h.scalar_tensor_tensor(qs["qa"][:, 0:W], src_ap, -1.0, src_ap, ALU.mult, ALU.max), waits=tw)
    t1 = s.emit(eng, lambda h: h.tensor_reduce(qs["qm1"][:, 0:nb], v3(qs["qa"]), AX, ALU.max), waits=[t0])
    t2 = s.emit(eng, lambda h: h.tensor_scalar(qs["qm2"][:, 0:nb], qs["qm1"][:, 0:nb], 1.0 / 6.0, 1e-12, ALU.mult, ALU.max), waits=[t1])
    t3 = s.emit(eng, lambda h: h.reciprocal(qs["qm3"][:, 0:nb], qs["qm2"][:, 0:nb]), waits=[t2])
    t4 = s.emit(eng, lambda h: h.tensor_tensor(v3(qs["qv"]), v3(qs["qa"]), bc(qs["qm3"]), ALU.mult), waits=[t3])
    ta = s.emit(eng, lambda h: h.tensor_scalar(qs["qa"][:, 0:W], qs["qv"][:, 0:W], 0.25, None, ALU.is_gt), waits=[t4])
    for c in (0.75, 1.25, 1.75):
        ta = s.emit(eng, lambda h, c=c: h.scalar_tensor_tensor(
            qs["qa"][:, 0:W], qs["qv"][:, 0:W], c, qs["qa"][:, 0:W], ALU.is_gt, ALU.add), waits=[ta])
    tb = s.emit(eng, lambda h: h.tensor_scalar(qs["qb"][:, 0:W], qs["qv"][:, 0:W], 2.5, None, ALU.is_gt), waits=[t4])
    tb = s.emit(eng, lambda h: h.scalar_tensor_tensor(
        qs["qb"][:, 0:W], qs["qv"][:, 0:W], 3.5, qs["qb"][:, 0:W], ALU.is_gt, ALU.add), waits=[tb])
    tc_ = s.emit(eng, lambda h: h.tensor_scalar(qs["qv"][:, 0:W], qs["qv"][:, 0:W], 5.0, None, ALU.is_gt), waits=[tb, ta])
    tq = s.emit(eng, lambda h: h.scalar_tensor_tensor(
        qs["qb"][:, 0:W], qs["qv"][:, 0:W], 2.0, qs["qb"][:, 0:W], ALU.mult, ALU.add), waits=[tc_])
    tq = s.emit(eng, lambda h: h.scalar_tensor_tensor(
        qs["qb"][:, 0:W], qs["qa"][:, 0:W], 0.5, qs["qb"][:, 0:W], ALU.mult, ALU.add), waits=[tq])
    tsg = s.emit("scalar", lambda h: h.sign(qs["qsg"][:, 0:W], src_ap), waits=tw + [t4])
    tm = s.emit(eng, lambda h: h.tensor_tensor(qs["qa"][:, 0:W], qs["qb"][:, 0:W], qs["qsg"][:, 0:W], ALU.mult), waits=[tq, tsg])
    tout = s.emit(eng, lambda h: h.tensor_tensor(
        v3_out(dst_ap, nb), v3(qs["qa"]), bc(qs["qm2"]), ALU.mult), waits=[tm])
    return tout

def v3_out(ap, nb):
    return ap.rearrange("p (a m) -> p a m", a=nb)


def build_ffn1(pre=False):
    nc = bass.Bass("TRN2", target_bir_lowering=False)
    x_tm = nc.dram_tensor("x_tm", [TL, D], F32, kind="ExternalInput").ap()
    if not pre:
        wg = nc.dram_tensor("wg", [2816, D], F32, kind="ExternalInput").ap()
        wu = nc.dram_tensor("wu", [2816, D], F32, kind="ExternalInput").ap()
        wd = nc.dram_tensor("wd", [D, 2816], F32, kind="ExternalInput").ap()
    ident = nc.dram_tensor("ident", [128, 128], BF16, kind="ExternalInput").ap()
    xq_scr = nc.dram_tensor("xq_scr", [TL, D], BF16, kind="Internal").ap()
    wq_kind = "ExternalInput" if pre else "ExternalOutput"
    wgq_scr = nc.dram_tensor("wgq_scr", [2816, D], BF16, kind=wq_kind).ap()
    wuq_scr = nc.dram_tensor("wuq_scr", [2816, D], BF16, kind=wq_kind).ap()
    wdq_scr = nc.dram_tensor("wdq_scr", [D, 2816], BF16, kind=wq_kind).ap()
    xo = nc.dram_tensor("xo", [TL, D], F32, kind="ExternalOutput").ap()

    NT_ = TL // 128
    NI = 11
    s = Sched()
    with ExitStack() as ctx:
        E = ctx.enter_context
        qs = dict(
            qa=E(nc.sbuf_tensor("qa", [128, 2816], F32)),
            qv=E(nc.sbuf_tensor("qv", [128, 2816], F32)),
            qb=E(nc.sbuf_tensor("qb", [128, 2816], F32)),
            qm1=E(nc.sbuf_tensor("qm1", [128, 176], F32)),
            qm2=E(nc.sbuf_tensor("qm2", [128, 176], F32)),
            qm3=E(nc.sbuf_tensor("qm3", [128, 176], F32)),
            qsg=E(nc.sbuf_tensor("qsg", [128, 2816], BF16)),
        )
        identt = E(nc.sbuf_tensor("identt", [128, 128], BF16))
        wtile = E(nc.sbuf_tensor("wtile", [128, 2816], F32))
        wq_out = E(nc.sbuf_tensor("wq_out", [128, 2816], BF16))
        t_id = s.dma("sync", lambda h: h.dma_start(out=identt[:], in_=ident[:, :]))

        prevq = None
        if not pre:
            for wsrc, wdst, nt, wwid in ((wg, wgq_scr, 22, D), (wu, wuq_scr, 22, D), (wd, wdq_scr, 8, 2816)):
                for i in range(nt):
                    tld = s.dma("sync", lambda h, wsrc=wsrc, i=i, wwid=wwid: h.dma_start(
                        out=wtile[:, 0:wwid], in_=wsrc[i * 128:(i + 1) * 128, :]),
                        waits=[prevq] if prevq else [])
                    tq = emit_quant(s, qs, wtile[:, 0:wwid], wq_out[:, 0:wwid], wwid, waits=[tld])
                    tst = s.dma("sync", lambda h, wdst=wdst, i=i, wwid=wwid: h.dma_start(
                        out=wdst[i * 128:(i + 1) * 128, :], in_=wq_out[:, 0:wwid]), waits=[tq])
                    prevq = tst

        with ExitStack() as ctx2:
            E2 = ctx2.enter_context
            sb = {}
            sb["ln_xt"] = [E2(nc.sbuf_tensor(f"lnxt{i}", [128, D], F32)) for i in range(2)]
            sb["ln_sq"] = E2(nc.sbuf_tensor("ln_sq", [128, D], F32))
            sb["ln_xn"] = E2(nc.sbuf_tensor("ln_xnb", [128, D], BF16))
            sb["ln_st"] = E2(nc.sbuf_tensor("ln_st", [128, 8], F32))
            xnf = E2(nc.sbuf_tensor("xnf", [128, D], F32))
            last = None
            prevst = None
            for i in range(NT_):
                xt, sq, st = sb["ln_xt"][0], sb["ln_sq"], sb["ln_st"]
                tld = s.dma("sync", lambda h, i=i: h.dma_start(
                    out=xt[:], in_=x_tm[i * 128:(i + 1) * 128, :]),
                    waits=[last] if last else ([prevq] if prevq else []))
                tsq = s.emit("scalar", lambda h: h.square(sq[:], xt[:]), waits=[tld])
                t1 = s.emit("vector", lambda h: h.reduce_sum(st[:, 0:1], xt[:], AX), waits=[tld])
                t2 = s.emit("vector", lambda h: h.reduce_sum(st[:, 1:2], sq[:], AX), waits=[tsq, t1])
                tm = s.emit("vector", lambda h: h.tensor_scalar(st[:, 2:3], st[:, 0:1], 1.0 / D, None, ALU.mult), waits=[t2])
                tm2 = s.emit("vector", lambda h: h.tensor_scalar(st[:, 3:4], st[:, 2:3], st[:, 2:3], None, ALU.mult), waits=[tm])
                tv = s.emit("vector", lambda h: h.scalar_tensor_tensor(
                    st[:, 4:5], st[:, 1:2], 1.0 / D, st[:, 3:4], ALU.mult, ALU.subtract), waits=[tm2])
                tve = s.emit("vector", lambda h: h.tensor_scalar(st[:, 5:6], st[:, 4:5], EPS_LN, None, ALU.add), waits=[tv])
                tsd = s.emit("scalar", lambda h: h.sqrt(st[:, 6:7], st[:, 5:6]), waits=[tve])
                trs = s.emit("vector", lambda h: h.reciprocal(st[:, 7:8], st[:, 6:7]), waits=[tsd])
                txn = s.emit("vector", lambda h: h.tensor_scalar(
                    xnf[:], xt[:], st[:, 2:3], st[:, 7:8], ALU.subtract, ALU.mult), waits=[trs])
                tq = emit_quant(s, qs, xnf[:], sb["ln_xn"][:], D, waits=[txn])
                tst2 = s.dma("sync", lambda h, i=i: h.dma_start(
                    out=xq_scr[i * 128:(i + 1) * 128, :], in_=sb["ln_xn"][:]), waits=[tq])
                last = txn
                prevst = tst2
            ln_done = [prevst]

        xq_fm = E(nc.sbuf_tensor("xq_fm", [128, 8 * TL], BF16))
        wdqT = E(nc.sbuf_tensor("wdqT", [128, 22 * D], BF16))
        wgT = [E(nc.sbuf_tensor(f"wgT{i}", [128, 8 * 256], BF16)) for i in range(2)]
        wuT = [E(nc.sbuf_tensor(f"wuT{i}", [128, 8 * 256], BF16)) for i in range(2)]
        hqT = E(nc.sbuf_tensor("hqT", [128, 4 * 2816], BF16))
        silu = E(nc.sbuf_tensor("silu", [128, 256], F32))
        upse = E(nc.sbuf_tensor("upse", [128, 256], F32))
        hft = E(nc.sbuf_tensor("hft", [128, 256], F32))
        hqt = E(nc.sbuf_tensor("hqt", [128, 256], BF16))
        xtile = E(nc.sbuf_tensor("xtile", [128, D], F32))
        otile = E(nc.sbuf_tensor("otile", [128, D], F32))
        ps_g = E(nc.psum_tensor("ps_g", [128, 256], F32))
        ps_u = E(nc.psum_tensor("ps_u", [128, 256], F32))
        ps_t = E(nc.psum_tensor("ps_t", [128, 128], BF16))
        ps_d = [E(nc.psum_tensor(f"ps_d{i}", [128, 512], F32)) for i in range(2)]

        tr_x = emit_transpose_in(s, xq_fm, xq_scr, TL, ln_done)
        twd = []
        for i in range(22):
            twd.append(s.dma("sync", lambda h, i=i: h.dma_start_transpose(
                wdqT[:, i * D:(i + 1) * D], wdq_scr[:, i * 128:(i + 1) * 128]),
                waits=[prevq] if prevq else []))

        prev = {}
        for tg in range(4):
            for Ic in range(NI):
                wb = Ic % 2
                wl = tr_x + ([prev[("wg", wb)]] if ("wg", wb) in prev else [])
                tgw = []
                for kc in range(8):
                    tgw.append(s.dma("sync", lambda h, kc=kc, Ic=Ic, wb=wb: h.dma_start_transpose(
                        wgT[wb][:, kc * 256:(kc + 1) * 256],
                        wgq_scr[Ic * 256:(Ic + 1) * 256, kc * 128:(kc + 1) * 128]), waits=wl))
                    tgw.append(s.dma("sync", lambda h, kc=kc, Ic=Ic, wb=wb: h.dma_start_transpose(
                        wuT[wb][:, kc * 256:(kc + 1) * 256],
                        wuq_scr[Ic * 256:(Ic + 1) * 256, kc * 128:(kc + 1) * 128]), waits=wl))
                for ti in range(4):
                    tt = tg * 4 + ti
                    wl2 = tgw + ([prev["hq"]] if "hq" in prev else [])
                    tokg = toku = None
                    for kc in range(8):
                        tokg = s.emit("tensor", lambda h, kc=kc, tt=tt, wb=wb: h.matmul(
                            ps_g[:], xq_fm[:, kc * TL + tt * 128: kc * TL + (tt + 1) * 128],
                            wgT[wb][:, kc * 256:(kc + 1) * 256],
                            start=(kc == 0), stop=(kc == 7)), waits=(wl2 if kc == 0 else []))
                    for kc in range(8):
                        toku = s.emit("tensor", lambda h, kc=kc, tt=tt, wb=wb: h.matmul(
                            ps_u[:], xq_fm[:, kc * TL + tt * 128: kc * TL + (tt + 1) * 128],
                            wuT[wb][:, kc * 256:(kc + 1) * 256],
                            start=(kc == 0), stop=(kc == 7)), waits=[])
                    tsl = s.emit("scalar", lambda h: h.activation(
                        silu[:], ps_g[:], AF.Silu, bias=0.0, scale=1.0),
                        waits=[tokg] + ([prev["sl"]] if "sl" in prev else []))
                    tue = s.emit("scalar", lambda h: h.copy(upse[:], ps_u[:]),
                        waits=[toku] + ([prev["ue"]] if "ue" in prev else []))
                    prev["sl"] = tsl
                    prev["ue"] = tue
                    th = s.emit("vector", lambda h: h.tensor_tensor(hft[:], silu[:], upse[:], ALU.mult),
                                waits=[tsl, tue])
                    thq = emit_quant(s, qs, hft[:], hqt[:], 256, waits=[th])
                    prev["hq"] = thq
                    for half in range(2):
                        ttr = s.emit("tensor", lambda h, half=half: h.transpose(
                            ps_t[:], hqt[:, half * 128:(half + 1) * 128], identt[:]),
                            waits=[thq, t_id] + ([prev["tre"]] if "tre" in prev else []))
                        tte = s.emit("vector", lambda h, ti=ti, Ic=Ic, half=half: h.tensor_copy(
                            hqT[:, ti * 2816 + Ic * 256 + half * 128: ti * 2816 + Ic * 256 + (half + 1) * 128],
                            ps_t[:]), waits=[ttr])
                        prev["tre"] = tte
                prev[("wg", wb)] = tokg
            for ti in range(4):
                tt = tg * 4 + ti
                tx = s.dma("sync", lambda h, tt=tt: h.dma_start(
                    out=xtile[:], in_=x_tm[tt * 128:(tt + 1) * 128, :]),
                    waits=[prev["xo"]] if "xo" in prev else [])
                for oc in range(2):
                    pd = ps_d[oc]
                    wl3 = twd + [prev["tre"], tx] + ([prev[("pd", oc)]] if ("pd", oc) in prev else [])
                    tok = None
                    for i in range(22):
                        tok = s.emit("tensor", lambda h, i=i, ti=ti, oc=oc, pd=pd: h.matmul(
                            pd[:], hqT[:, ti * 2816 + i * 128: ti * 2816 + i * 128 + 128],
                            wdqT[:, i * D + oc * 512: i * D + (oc + 1) * 512],
                            start=(i == 0), stop=(i == 21)), waits=(wl3 if i == 0 else []))
                    te = s.emit("vector", lambda h, oc=oc, pd=pd: h.tensor_tensor(
                        otile[:, oc * 512:(oc + 1) * 512], pd[:],
                        xtile[:, oc * 512:(oc + 1) * 512], ALU.add), waits=[tok])
                    prev[("pd", oc)] = te
                txo = s.dma("sync", lambda h, tt=tt: h.dma_start(
                    out=xo[tt * 128:(tt + 1) * 128, :], in_=otile[:]),
                    waits=[prev[("pd", 0)], prev[("pd", 1)]])
                prev["xo"] = txo
        s.build(nc, ctx)
    return nc


def run_ffn1_sublayer(x, ins, qcache):
    pre = "wgq" in qcache
    in_maps = []
    for c in range(8):
        b, p = c // 2, c % 2
        im = dict(x_tm=np.ascontiguousarray(x[b, p * TL:(p + 1) * TL]), ident=IDENT)
        if pre:
            im["wgq_scr"] = qcache["wgq"][c]
            im["wuq_scr"] = qcache["wuq"][c]
            im["wdq_scr"] = qcache["wdq"][c]
        else:
            im["wg"] = ins["L1_ffn_w_gate"]
            im["wu"] = ins["L1_ffn_w_up"]
            im["wd"] = ins["L1_ffn_w_down"]
        in_maps.append(im)
    if pre:
        res = run_bass_kernel_spmd(prog("ffn1_b", lambda: build_ffn1(True)), in_maps, core_ids=list(range(8)))
    else:
        res = run_bass_kernel_spmd(prog("ffn1_a", lambda: build_ffn1(False)), in_maps, core_ids=list(range(8)))
        qcache["wgq"] = [res.results[c]["wgq_scr"] for c in range(8)]
        qcache["wuq"] = [res.results[c]["wuq_scr"] for c in range(8)]
        qcache["wdq"] = [res.results[c]["wdq_scr"] for c in range(8)]
    xn = np.empty_like(x)
    for c in range(8):
        b, p = c // 2, c % 2
        xn[b, p * TL:(p + 1) * TL] = res.results[c]["xo"]
    return xn





def build_rn_a():
    nc = bass.Bass("TRN2", target_bir_lowering=False)
    xb = nc.dram_tensor("xb", [T, D], F32, kind="ExternalInput").ap()
    wq = nc.dram_tensor("wq", [D, 512], F32, kind="ExternalInput").ap()
    wk = nc.dram_tensor("wk", [D, 512], F32, kind="ExternalInput").ap()
    wv = nc.dram_tensor("wv", [D, 512], F32, kind="ExternalInput").ap()
    pkv = nc.dram_tensor("pkv", [64, 8 * 64], F32, kind="ExternalInput").ap()
    aB = nc.dram_tensor("aB", [64, 8 * 512], BF16, kind="ExternalInput").ap()
    bB = nc.dram_tensor("bB", [64, 8 * 128], BF16, kind="ExternalInput").ap()
    triui = nc.dram_tensor("triui", [128, 128], BF16, kind="ExternalInput").ap()
    cp = nc.dram_tensor("cp", [128, 8 * 144], F32, kind="ExternalInput").ap()
    xn_scr = nc.dram_tensor("xn_scr", [T, D], BF16, kind="Internal").ap()
    r_out = nc.dram_tensor("r_out", [128, 32 * 512], BF16, kind="ExternalOutput").ap()
    st_out = nc.dram_tensor("st_out", [1, 4], F32, kind="ExternalOutput").ap()

    NB = T // 128
    GAM = [1.0 - 2.0 ** (-5.0 - g) for g in range(16)]
    s = Sched()
    with ExitStack() as ctx:
        E = ctx.enter_context
        sb = {}
        xn_fm = E(nc.sbuf_tensor("xn_fm", [128, 8 * T], BF16))
        v_tm = E(nc.sbuf_tensor("v_tm", [128, NB * 512], BF16))
        wqh = [E(nc.sbuf_tensor(f"wqh{i}", [128, 8 * 64], BF16)) for i in range(2)]
        wkh = [E(nc.sbuf_tensor(f"wkh{i}", [128, 8 * 64], BF16)) for i in range(2)]
        Qh = E(nc.sbuf_tensor("Qh", [64, T], BF16))
        Qs = E(nc.sbuf_tensor("Qs", [64, T], BF16))
        Ks = E(nc.sbuf_tensor("Ks", [64, T], BF16))
        pkvf = E(nc.sbuf_tensor("pkvf", [64, 8 * 64], F32))
        pkvs = E(nc.sbuf_tensor("pkvs", [64, 8 * 64], BF16))
        aBt = E(nc.sbuf_tensor("aBt", [64, 8 * 512], BF16))
        bBt = E(nc.sbuf_tensor("bBt", [64, 8 * 128], BF16))
        triuit = E(nc.sbuf_tensor("triuit", [128, 128], BF16))
        cpt = E(nc.sbuf_tensor("cpt", [128, 8 * 144], F32))
        ps_sc = [E(nc.psum_tensor(f"ps_sc{i}", [128, 512], F32)) for i in range(2)]
        ps_r = [E(nc.psum_tensor(f"ps_r{i}", [128, 512], F32)) for i in range(4)]
        ps_n = E(nc.psum_tensor("ps_n", [1, 512], F32))

        t_c = [s.dma("sync", lambda h: h.dma_start(out=triuit[:], in_=triui[:, :]))]
        t_c.append(s.dma("sync", lambda h: h.dma_start(out=cpt[:], in_=cp[:, :])))
        t_c.append(s.dma("sync", lambda h: h.dma_start(out=aBt[:], in_=aB[:, :])))
        t_c.append(s.dma("sync", lambda h: h.dma_start(out=bBt[:], in_=bB[:, :])))
        t_c.append(s.dma("sync", lambda h: h.dma_start(out=pkvf[:], in_=pkv[:, :])))
        t_c.append(s.emit("vector", lambda h: h.tensor_scalar(
            pkvs[:], pkvf[:], 1024.0, None, ALU.mult), waits=t_c[-1:]))

        with ExitStack() as ctx2:
            E2 = ctx2.enter_context
            sb["ln_xt"] = [E2(nc.sbuf_tensor(f"lnxt{i}", [128, D], F32)) for i in range(2)]
            sb["ln_sq"] = E2(nc.sbuf_tensor("ln_sq", [128, D], F32))
            sb["ln_xn"] = E2(nc.sbuf_tensor("ln_xnb", [128, D], BF16))
            sb["ln_st"] = E2(nc.sbuf_tensor("ln_st", [128, 8], F32))
            wvt = E2(nc.sbuf_tensor("wvt", [128, 8 * 512], BF16))
            ln_toks = emit_ln(s, sb, xb, xn_scr, NB)
            tr_toks = emit_transpose_in(s, xn_fm, xn_scr, T, ln_toks)
            w_v = emit_load_w_bf16(s, wvt, wv, 8, 512)
            prev_v = {}
            for tt in range(NB):
                ps = ps_sc[tt % 2]
                wl = tr_toks + w_v + ([prev_v[id(ps)]] if id(ps) in prev_v else [])
                tok = None
                for kc in range(8):
                    tok = s.emit("tensor", lambda h, kc=kc, tt=tt, ps=ps: h.matmul(
                        ps[:], xn_fm[:, kc * T + tt * 128: kc * T + (tt + 1) * 128],
                        wvt[:, kc * 512:(kc + 1) * 512],
                        start=(kc == 0), stop=(kc == 7)), waits=(wl if kc == 0 else []))
                te = s.emit("vector", lambda h, tt=tt, ps=ps: h.tensor_copy(
                    v_tm[:, tt * 512:(tt + 1) * 512], ps[:]), waits=[tok])
                prev_v[id(ps)] = te

        bar0 = [(e, s.cnt[e]) for e in SEMS]
        r_buf = E(nc.sbuf_tensor("r_buf", [128, NB * 512], BF16))
        sqb = E(nc.sbuf_tensor("sqb", [128, 512], BF16))
        ones128 = E(nc.sbuf_tensor("ones128", [128, 1], BF16))
        strow = E(nc.sbuf_tensor("strow", [1, 16], F32))
        pt = [E(nc.sbuf_tensor(f"pt{i}", [128, 512], BF16)) for i in range(2)]
        tz = s.emit("gpsimd", lambda h: h.memset(ones128[:], 1.0), waits=bar0)

        prev = {}
        wtok = [None, None]
        for hh in range(HL):
            hb = hh % 2
            wl = list(bar0) + ([wtok[hb]] if wtok[hb] else [])
            twq = s.dma("gpsimd", lambda h, hh=hh, hb=hb: h.dma_start(
                out=wqh[hb][:], in_=wq[:, hh * 64:(hh + 1) * 64].rearrange("(kc p) c -> p kc c", p=128)),
                waits=wl)
            twk = s.dma("gpsimd", lambda h, hh=hh, hb=hb: h.dma_start(
                out=wkh[hb][:], in_=wk[:, hh * 64:(hh + 1) * 64].rearrange("(kc p) c -> p kc c", p=128)),
                waits=wl)
            for side in ("q", "k"):
                wt_, tws = (wqh, twq) if side == "q" else (wkh, twk)
                for tc in range(8):
                    ps = ps_sc[tc % 2]
                    wl = list(bar0) + [tws] + ([prev[id(ps)]] if id(ps) in prev else [])
                    tok = None
                    for kc in range(8):
                        tok = s.emit("tensor", lambda h, kc=kc, tc=tc, hb=hb, wt_=wt_, ps=ps: h.matmul(
                            ps[0:64, :], wt_[hb][:, kc * 64:(kc + 1) * 64],
                            xn_fm[:, kc * T + tc * 512: kc * T + (tc + 1) * 512],
                            start=(kc == 0), stop=(kc == 7)), waits=(wl if kc == 0 else []))
                    if side == "q":
                        tev = s.emit("scalar", lambda h, ps=ps, tc=tc: h.copy(
                            Qh[:, tc * 512:(tc + 1) * 512], ps[0:64, :]), waits=[tok])
                        tqs = s.emit("vector", lambda h, ps=ps, tc=tc, hh=hh: h.tensor_tensor(
                            Qs[:, tc * 512:(tc + 1) * 512], ps[0:64, :],
                            aBt[:, hh * 512:(hh + 1) * 512], ALU.mult), waits=[tok, tev] + t_c)
                        prev[id(ps)] = tqs
                        prev["qev"] = tev
                    else:
                        tks = s.emit("vector", lambda h, ps=ps, tc=tc, hh=hh: h.tensor_tensor(
                            Ks[:, tc * 512:(tc + 1) * 512].rearrange("p (a m) -> p a m", a=4),
                            ps[0:64, :].rearrange("p (a m) -> p a m", a=4),
                            bBt[:, hh * 128:(hh + 1) * 128].unsqueeze(1).broadcast_to([64, 4, 128]),
                            ALU.mult), waits=[tok] + t_c)
                        prev[id(ps)] = tks
            pk_done = [("vector", s.cnt["vector"]), ("scalar", s.cnt["scalar"])]

            for tcq in range(8):
                for j in range(4):
                    tb_ = 4 * tcq + j
                    tcx = s.emit("tensor", lambda h, j=j, tb_=tb_, hh=hh: h.matmul(
                        ps_r[j][:, 0:64], Qh[:, tb_ * 128:(tb_ + 1) * 128],
                        pkvs[:, hh * 64:(hh + 1) * 64], start=True, stop=False),
                        waits=pk_done + t_c + ([prev[("pr", j)]] if ("pr", j) in prev else []))
                    prev[("prx", j)] = tcx
                for sb_ in range(4 * tcq + 4):
                    pp = ps_sc[sb_ % 2]
                    pb = sb_ % 2
                    wl = pk_done + ([prev[("pt", pb)]] if ("pt", pb) in prev else []) + \
                         ([prev[id(pp)]] if id(pp) in prev else [])
                    tmm = s.emit("tensor", lambda h, tcq=tcq, sb_=sb_, pp=pp: h.matmul(
                        pp[:], Ks[:, sb_ * 128:(sb_ + 1) * 128],
                        Qs[:, tcq * 512:(tcq + 1) * 512],
                        start=True, stop=True), waits=wl)
                    j0 = sb_ - 4 * tcq
                    ci = cp_index(hh, tcq, sb_)
                    tex = s.emit("vector", lambda h, pb=pb, pp=pp, ci=ci: h.tensor_scalar(
                        pt[pb][:], pp[:], cpt[:, ci:ci + 1], None, ALU.mult), waits=[tmm] + t_c)
                    if 0 <= j0 <= 3:
                        tex = s.emit("vector", lambda h, pb=pb, pp=pp, j0=j0, ci=ci: h.scalar_tensor_tensor(
                            pt[pb][:, j0 * 128:(j0 + 1) * 128], pp[:, j0 * 128:(j0 + 1) * 128],
                            cpt[:, ci:ci + 1], triuit[:], ALU.mult, ALU.mult), waits=[tex])
                    prev[id(pp)] = tex
                    tpv = None
                    for j in range(max(0, j0), 4):
                        tpv = s.emit("tensor", lambda h, pb=pb, j=j, sb_=sb_, hh=hh: h.matmul(
                            ps_r[j][:, 0:64], pt[pb][:, j * 128:(j + 1) * 128],
                            v_tm[:, sb_ * 512 + hh * 64: sb_ * 512 + (hh + 1) * 64],
                            start=False, stop=(sb_ == 4 * tcq + j)),
                            waits=[tex, prev[("prx", j)]])
                    prev[("pt", pb)] = tpv if tpv is not None else tex
                for j in range(4):
                    tfo = s.emit("scalar", lambda h, j=j, tcq=tcq, hh=hh: h.copy(
                        r_buf[:, (4 * tcq + j) * 512 + hh * 64: (4 * tcq + j) * 512 + (hh + 1) * 64],
                        ps_r[j][:, 0:64]), waits=[prev[("pt", 0)], prev[("pt", 1)]])
                    prev[("pr", j)] = tfo
            wtok[hb] = prev[("pt", 0)]

        rdone = [prev[("pr", j)] for j in range(4)]
        tok1 = tok2 = None
        for i in range(NB):
            tok1 = s.emit("tensor", lambda h, i=i: h.matmul(
                ps_n[:], ones128[:], r_buf[:, i * 512:(i + 1) * 512],
                start=(i == 0), stop=(i == NB - 1)), waits=(rdone + [tz] if i == 0 else []))
        ts1 = s.emit("vector", lambda h: h.reduce_sum(strow[:, 0:1], ps_n[:], AX), waits=[tok1])
        sq_ev = []
        for i in range(NB):
            tsq = s.emit("scalar", lambda h, i=i: h.square(sqb[:], r_buf[:, i * 512:(i + 1) * 512]),
                         waits=rdone + [ts1] + sq_ev[-1:])
            tok2 = s.emit("tensor", lambda h, i=i: h.matmul(
                ps_n[:], ones128[:], sqb[:], start=(i == 0), stop=(i == NB - 1)),
                waits=[tsq] + ([ts1] if i == 0 else []))
            sq_ev.append(tok2)
        ts2 = s.emit("vector", lambda h: h.reduce_sum(strow[:, 1:2], ps_n[:], AX), waits=[tok2])
        tds = s.dma("sync", lambda h: h.dma_start(out=st_out[0:1, 0:2], in_=strow[:, 0:2]), waits=[ts1, ts2])
        for part in range(4):
            s.dma("sync", lambda h, part=part: h.dma_start(
                out=r_out[:, part * 8 * 512:(part + 1) * 8 * 512],
                in_=r_buf[:, part * 8 * 512:(part + 1) * 8 * 512]), waits=rdone + [ts2])
        s.build(nc, ctx)
    return nc


def cp_index(hh, tcq, sb_):
    return hh * 144 + 2 * tcq * tcq + 2 * tcq + sb_


def make_cp_table(p):
    tab = np.zeros(8 * 144, np.float32)
    for hh in range(8):
        g = 1.0 - 2.0 ** (-5.0 - (hh + 8 * p))
        for tcq in range(8):
            for sb_ in range(4 * tcq + 4):
                e = 512 * tcq - 128 * sb_
                tab[cp_index(hh, tcq, sb_)] = np.float32(g) ** e
    return np.ascontiguousarray(np.broadcast_to(tab, (128, 8 * 144)))


def make_ab_tables(p):
    aB = np.zeros((64, 8 * 512), np.float32)
    bB = np.zeros((64, 8 * 128), np.float32)
    for hh in range(8):
        g = 1.0 - 2.0 ** (-5.0 - (hh + 8 * p))
        aB[:, hh * 512:(hh + 1) * 512] = g ** np.arange(512)[None, :]
        bB[:, hh * 128:(hh + 1) * 128] = g ** (-np.arange(128)[None, :].astype(np.float64))
    return aB.astype(NPBF16), bB.astype(NPBF16)


def run_retnet_sublayer(x, ins):
    wqkv = ins["L1_attn_b_W_qkv"]
    in_maps = []
    for c in range(8):
        b, p = c // 2, c % 2
        hs = p * 512
        aB, bB = make_ab_tables(p)
        in_maps.append(dict(
            xb=np.ascontiguousarray(x[b]),
            wq=np.ascontiguousarray(wqkv[:, hs:hs + 512]),
            wk=np.ascontiguousarray(wqkv[:, 1024 + hs:1024 + hs + 512]),
            wv=np.ascontiguousarray(wqkv[:, 2048 + hs:2048 + hs + 512]),
            pkv=np.ascontiguousarray(ins["L1_attn_b_past_kv"][p * 8:(p + 1) * 8].transpose(1, 0, 2).reshape(64, 512)),
            aB=aB, bB=bB, triui=TRIUI, cp=make_cp_table(p),
        ))
    res = run_bass_kernel_spmd(prog("rn_a", build_rn_a), in_maps, core_ids=list(range(8)))
    r_full = [np.concatenate([o_to_tokmajor(res.results[2 * b]["r_out"]),
                              o_to_tokmajor(res.results[2 * b + 1]["r_out"])], axis=1)
              for b in range(4)]
    gnw_col = col_by_head(ins["L1_attn_b_gn_weight"])
    gnb_col = col_by_head(ins["L1_attn_b_gn_bias"])
    in_maps = []
    for c in range(8):
        b, p = c // 2, c % 2
        stats = np.concatenate([res.results[2 * b]["st_out"][0],
                                res.results[2 * b + 1]["st_out"][0]]).reshape(1, 8)
        in_maps.append(dict(
            x_fm=fm(x[b, p * TL:(p + 1) * TL]),
            r_fm=fm(r_full[b][p * TL:(p + 1) * TL]).astype(NPBF16),
            st2=np.ascontiguousarray(stats.astype(np.float32)),
            gnw_col=gnw_col, gnb_col=gnb_col,
        ))
    res = run_bass_kernel_spmd(prog("rn_b", build_rn_b), in_maps, core_ids=list(range(8)))
    xn = np.empty_like(x)
    for c in range(8):
        b, p = c // 2, c % 2
        xn[b, p * TL:(p + 1) * TL] = unfm(res.results[c]["xo"], TL)
    return xn


def col_by_head(g):
    feat = np.repeat(g, 64)
    return np.ascontiguousarray(feat.reshape(8, 128).T.astype(np.float32))


def build_rn_b():
    nc = bass.Bass("TRN2", target_bir_lowering=False)
    x_fm = nc.dram_tensor("x_fm", [128, 8 * TL], F32, kind="ExternalInput").ap()
    r_fm = nc.dram_tensor("r_fm", [128, 8 * TL], BF16, kind="ExternalInput").ap()
    st2 = nc.dram_tensor("st2", [1, 8], F32, kind="ExternalInput").ap()
    gnw_col = nc.dram_tensor("gnw_col", [128, 8], F32, kind="ExternalInput").ap()
    gnb_col = nc.dram_tensor("gnb_col", [128, 8], F32, kind="ExternalInput").ap()
    bscr = nc.dram_tensor("bscr", [1, 256], F32, kind="Internal").ap()
    xo = nc.dram_tensor("xo", [128, 8 * TL], F32, kind="ExternalOutput").ap()
    NTOT = float(16 * 64 * T)
    s = Sched()
    with ExitStack() as ctx:
        E = ctx.enter_context
        xt = E(nc.sbuf_tensor("xt", [128, 8 * TL], F32))
        rt = E(nc.sbuf_tensor("rt", [128, 8 * TL], BF16))
        stt_ = E(nc.sbuf_tensor("stt_", [1, 16], F32))
        gw = E(nc.sbuf_tensor("gw", [128, 8], F32))
        gb = E(nc.sbuf_tensor("gb", [128, 8], F32))
        row = E(nc.sbuf_tensor("row", [1, 256], F32))
        colb = E(nc.sbuf_tensor("colb", [128, 2], F32))
        alf = E(nc.sbuf_tensor("alf", [128, 8], F32))
        bet = E(nc.sbuf_tensor("bet", [128, 8], F32))
        tmp = E(nc.sbuf_tensor("tmp", [128, 512], F32))
        on1 = E(nc.sbuf_tensor("on1", [1, 256], F32))
        xob = E(nc.sbuf_tensor("xob", [128, 8 * TL], F32))

        tx = s.dma("sync", lambda h: h.dma_start(out=xt[:], in_=x_fm[:, :]))
        tr = s.dma("sync", lambda h: h.dma_start(out=rt[:], in_=r_fm[:, :]))
        tst = s.dma("sync", lambda h: h.dma_start(out=stt_[0:1, 0:8], in_=st2[0:1, :]))
        tgw = s.dma("sync", lambda h: h.dma_start(out=gw[:], in_=gnw_col[:, :]))
        tgb = s.dma("sync", lambda h: h.dma_start(out=gb[:], in_=gnb_col[:, :]))
        to1 = s.emit("gpsimd", lambda h: h.memset(on1[:], 1.0))
        t1 = s.emit("vector", lambda h: h.tensor_tensor(stt_[0:1, 8:9], stt_[0:1, 0:1], stt_[0:1, 4:5], ALU.add), waits=[tst])
        t2 = s.emit("vector", lambda h: h.tensor_tensor(stt_[0:1, 9:10], stt_[0:1, 1:2], stt_[0:1, 5:6], ALU.add), waits=[t1])
        tmn = s.emit("vector", lambda h: h.tensor_scalar(stt_[0:1, 10:11], stt_[0:1, 8:9], 1.0 / NTOT, None, ALU.mult), waits=[t2])
        tm2 = s.emit("vector", lambda h: h.tensor_scalar(stt_[0:1, 11:12], stt_[0:1, 10:11], stt_[0:1, 10:11], None, ALU.mult), waits=[tmn])
        tvr = s.emit("vector", lambda h: h.scalar_tensor_tensor(
            stt_[0:1, 12:13], stt_[0:1, 9:10], 1.0 / NTOT, stt_[0:1, 11:12], ALU.mult, ALU.subtract), waits=[tm2])
        tve = s.emit("vector", lambda h: h.tensor_scalar(stt_[0:1, 13:14], stt_[0:1, 12:13], 1e-5, None, ALU.add), waits=[tvr])
        tsd = s.emit("scalar", lambda h: h.sqrt(stt_[0:1, 14:15], stt_[0:1, 13:14]), waits=[tve])
        trc = s.emit("vector", lambda h: h.reciprocal(stt_[0:1, 15:16], stt_[0:1, 14:15]), waits=[tsd])
        tr1 = s.emit("vector", lambda h: h.tensor_scalar(row[:, 0:128], on1[:, 0:128], stt_[0:1, 15:16], None, ALU.mult), waits=[trc, to1])
        tr2 = s.emit("vector", lambda h: h.tensor_scalar(row[:, 128:256], on1[:, 0:128], stt_[0:1, 10:11], -1.0, ALU.mult, ALU.mult), waits=[tr1])
        td1 = s.dma("sync", lambda h: h.dma_start(out=bscr[0:1, :], in_=row[:, :]), waits=[tr2])
        td2a = s.dma("sync", lambda h: h.dma_start(
            out=colb[:, 0:1], in_=bscr[0:1, 0:128].rearrange("a (p c) -> (a p) c", c=1)), waits=[td1])
        td2 = s.dma("sync", lambda h: h.dma_start(
            out=colb[:, 1:2], in_=bscr[0:1, 128:256].rearrange("a (p c) -> (a p) c", c=1)), waits=[td1])
        ta = s.emit("vector", lambda h: h.tensor_scalar(alf[:], gw[:], colb[:, 0:1], None, ALU.mult), waits=[td2a, td2, tgw])
        tb = s.emit("vector", lambda h: h.scalar_tensor_tensor(
            bet[:], alf[:], colb[:, 1:2], gb[:], ALU.mult, ALU.add), waits=[ta, tgb])
        prev = None
        for oc in range(8):
            for tc in range(TL // 512):
                sl = slice(oc * TL + tc * 512, oc * TL + (tc + 1) * 512)
                tt1 = s.emit("vector", lambda h, sl=sl, oc=oc: h.tensor_scalar(
                    tmp[:], rt[:, sl], alf[:, oc:oc + 1], bet[:, oc:oc + 1], ALU.mult, ALU.add),
                    waits=[tb, tr] + ([prev] if prev else []))
                tt2 = s.emit("vector", lambda h, sl=sl: h.tensor_tensor(
                    xob[:, sl], tmp[:], xt[:, sl], ALU.add), waits=[tt1, tx])
                prev = tt2
        for j2 in range(8):
            s.dma("sync", lambda h, j2=j2: h.dma_start(
                out=xo[:, j2 * TL:(j2 + 1) * TL], in_=xob[:, j2 * TL:(j2 + 1) * TL]), waits=[prev])
        s.build(nc, ctx)
    return nc


def kernel(**inputs):
    ins = {k: np.asarray(v) for k, v in inputs.items()}
    x = np.ascontiguousarray(ins["L0_attn_a_x"], dtype=np.float32)
    qcache = {}
    x = run_deltanet_sublayer(x, ins)
    x = run_moe_sublayer(x, ins, "L0_ffn")
    x = run_rope_attn_sublayer(x, ins)
    x = run_moe_sublayer(x, ins, "L0_ffn")
    x = run_alibi_attn_sublayer(x, ins)
    x = run_ffn1_sublayer(x, ins, qcache)
    x = run_retnet_sublayer(x, ins)
    x = run_ffn1_sublayer(x, ins, qcache)
    return x
